# version 02
# baseline (speedup 1.0000x reference)
import math
import numpy as np
import ml_dtypes
import concourse.bass as bass
import concourse.mybir as mybir
from concourse.bass_utils import run_bass_kernel_spmd

F32 = mybir.dt.float32
BF16 = mybir.dt.bfloat16
AF = mybir.ActivationFunctionType
ALU = mybir.AluOpType
AX = mybir.AxisListType

D = 1024
T = 4096
NCHUNK = D // 128
NEG = -1.0e30
import os as _os
KNORM_PARANOID = bool(_os.environ.get("KNORM_PARANOID"))

ENGINES = ("t", "v", "sc", "g", "sy")
DMA_ENG = ("sy", "g", "sc", "v")


NWAVE = 8


class Prog:

    def __init__(self, nc):
        self.nc = nc
        self.items = {e: [] for e in ENGINES}
        self.c = {e: 0 for e in ENGINES}
        self.didx = {e: 0 for e in DMA_ENG}
        self.last_wait = {e: {} for e in ENGINES}

    def op(self, e, fn, inc=0, lp=False):
        if lp:
            base = fn
            def fn(h, _b=base, _nc=self.nc):
                with _nc.allow_low_precision(reason="intentional bf16 rounding"):
                    return _b(h)
        self.items[e].append(("op", fn, inc, False, None))
        if inc:
            self.c[e] += inc
        return (e, False, self.c[e])

    def dma(self, e, out, in_, **kw):
        def fn(h):
            return h.dma_start(out=out, in_=in_, **kw)
        idx = self.didx[e]
        k = idx % NWAVE
        wave = idx // NWAVE + 1
        if idx >= NWAVE:
            self.wait(e, ("dma", e, k, 16 * (wave - 1)))
        self.items[e].append(("op", fn, 16, True, k))
        self.didx[e] = idx + 1
        return ("dma", e, k, 16 * wave)

    def tok(self, e, dma=False):
        if dma:
            idx = self.didx[e]
            if idx == 0:
                return None
            k = (idx - 1) % NWAVE
            return ("dma", e, k, 16 * ((idx - 1) // NWAVE + 1))
        return (e, False, self.c[e])

    def _dma_all_toks(self, e):
        toks = []
        idx = self.didx[e]
        for k in range(NWAVE):
            n = (idx - 1 - k) // NWAVE + 1 if idx > k else 0
            if n > 0:
                toks.append(("dma", e, k, 16 * n))
        return toks

    def wait(self, e, tok, force=False):
        if tok is None:
            return
        if tok[0] == "dma":
            _, src, k, val = tok
            key = ("dma", src, k)
        else:
            src, dma, val = tok
            if src == e and not dma and not force:
                return
            key = (src, dma)
        if val <= 0 or val <= self.last_wait[e].get(key, 0):
            return
        self.last_wait[e][key] = val
        self.items[e].append(("wait", key, val))

    def waits(self, e, toks):
        for t in toks:
            self.wait(e, t)

    def barrier(self):
        snap_c = dict(self.c)
        dtoks = {s: self._dma_all_toks(s) for s in DMA_ENG}
        for e in ENGINES:
            for s in ENGINES:
                if s != e and snap_c[s]:
                    self.wait(e, (s, False, snap_c[s]))
            for s in DMA_ENG:
                for t in dtoks[s]:
                    self.wait(e, t)

    def build(self, block):
        nc = self.nc
        sems = {}
        for e in ENGINES:
            sems[(e, False)] = nc.alloc_semaphore(name=f"c_{e}")
        for e in DMA_ENG:
            for k in range(NWAVE):
                sems[("dma", e, k)] = nc.alloc_semaphore(name=f"d_{e}{k}")
        self.sems = sems

        def runner(e):
            def run(h):
                for it in self.items[e]:
                    if it[0] == "wait":
                        _, key, val = it
                        h.wait_ge(sems[key], val)
                    else:
                        _, fn, inc, dma, slot = it
                        ins = fn(h)
                        if inc:
                            ins.then_inc(sems[("dma", e, slot) if dma else (e, False)], inc)
            return run

        block.tensor(runner("t"))
        block.vector(runner("v"))
        block.scalar(runner("sc"))
        block.gpsimd(runner("g"))
        block.sync(runner("sy"))



def ceil_div(a, b):
    return (a + b - 1) // b


class Consts:

    def __init__(self, nc, values):
        self.values = sorted(set(float(v) for v in values))
        self.tile = None

    def alloc(self, stack, nc):
        n = max(1, len(self.values))
        self.tile = stack.enter_context(nc.sbuf_tensor("consts", [128, n], F32))
        for i, v in enumerate(self.values):
            nc.const_aps.register(F32, v, self.tile[:, i:i + 1])

    def emit_init(self, p):
        def mk(i, v):
            return lambda h: h.memset(self.tile[:, i:i + 1], v)
        for i, v in enumerate(self.values):
            p.op("v", mk(i, v), inc=(1 if i == len(self.values) - 1 else 0))
        return p.tok("v")



def _eo_perm(nheads, hd=64):
    idx = []
    for h in range(nheads):
        base = h * hd
        idx.extend(base + np.arange(0, hd, 2))
        idx.extend(base + np.arange(1, hd, 2))
    return np.asarray(idx)


def _swap_perm(nheads, hd=64):
    idx = []
    for h in range(nheads):
        base = h * hd
        idx.extend(base + np.arange(1, hd, 2))
        idx.extend(base + np.arange(0, hd, 2))
    return np.asarray(idx)


def _rope_tables(base, T_, hd=64):
    j = np.arange(hd // 2, dtype=np.float64)
    inv = 1.0 / (base ** (2 * j / hd))
    t = np.arange(T_, dtype=np.float64)
    fr = t[None, :] * inv[:, None]
    cos = np.cos(fr).astype(np.float32)
    sin = np.sin(fr).astype(np.float32)
    c1 = np.tile(cos, (4, 1))
    c2 = np.tile(np.concatenate([-sin, sin], 0), (2, 1))
    return c1.astype(ml_dtypes.bfloat16), c2.astype(ml_dtypes.bfloat16)


def _chunk_major(v):
    return np.ascontiguousarray(v.reshape(-1, 128).T)


def _causal_masks():
    q = np.arange(512)[None, :]
    k = np.arange(128)[:, None]
    out = []
    for off in range(4):
        m = np.where((q - k) >= 128 * off, 0.0, NEG).astype(np.float32)
        out.append(m)
    return np.stack(out)


def _local_masks(win=256):
    q = np.arange(512)[None, :]
    k = np.arange(128)[:, None]
    out = []
    for off in range(-2, 4):
        d = (q - k) - 128 * off
        m = np.where((d >= 0) & (d < win), 0.0, NEG).astype(np.float32)
        out.append(m)
    return np.stack(out)


def _selpair(nheads):
    nch = nheads // 2
    s = np.zeros((nheads, nch * 128), np.float32)
    for c in range(nch):
        s[2 * c, c * 128: c * 128 + 64] = 1.0
        s[2 * c + 1, c * 128 + 64: (c + 1) * 128] = 1.0
    return s.astype(ml_dtypes.bfloat16)


def _maskq(nheads):
    nch = nheads // 2
    m = np.zeros((nch, 128, nheads), np.float32)
    for c in range(nch):
        m[c, 0:64, 2 * c] = 1.0
        m[c, 64:128, 2 * c + 1] = 1.0
    return np.ascontiguousarray(m.transpose(1, 0, 2).reshape(128, nch * nheads)).astype(ml_dtypes.bfloat16)


def prep_inputs(I):
    B = 8
    x = np.asarray(I["L0_attn_hidden"])
    percore = []
    for b in range(B):
        percore.append({"x0": np.ascontiguousarray(x[b].T).astype(np.float32)})

    S = {}
    f32 = lambda a: np.ascontiguousarray(np.asarray(a, dtype=np.float32))
    pq = _eo_perm(16)
    sq = _swap_perm(16)
    S["g_wq"] = f32(np.concatenate([I["L0_attn_g_wq"][:, pq], I["L0_attn_g_wq"][:, sq]], axis=1))
    S["g_kvd"] = f32(I["L0_attn_g_kv_down"])
    ku = np.asarray(I["L0_attn_g_k_up"])
    pk = _eo_perm(4)
    sk = _swap_perm(4)
    kup_d = np.concatenate([np.concatenate([ku[:, pk[g * 64:(g + 1) * 64]]] * 2, axis=1) for g in range(4)], axis=1)
    kup_s = np.concatenate([np.concatenate([ku[:, sk[g * 64:(g + 1) * 64]]] * 2, axis=1) for g in range(4)], axis=1)
    S["g_kup"] = f32(np.concatenate([kup_d, kup_s], axis=1))
    S["g_vup"] = f32(I["L0_attn_g_v_up"])
    S["g_wo"] = f32(I["L0_attn_g_wo"])
    S["l_wq"] = f32(np.concatenate([I["L0_attn_l_wq"][:, pq], I["L0_attn_l_wq"][:, sq]], axis=1))
    lk = np.asarray(I["L0_attn_l_wk"])
    lk_d = np.concatenate([np.concatenate([lk[:, pk[g * 64:(g + 1) * 64]]] * 2, axis=1) for g in range(4)], axis=1)
    lk_s = np.concatenate([np.concatenate([lk[:, sk[g * 64:(g + 1) * 64]]] * 2, axis=1) for g in range(4)], axis=1)
    S["l_wk"] = f32(np.concatenate([lk_d, lk_s], axis=1))
    S["l_wv"] = f32(I["L0_attn_l_wv"])
    S["l_wo"] = f32(I["L0_attn_l_wo"])
    S["w1n"] = f32(_chunk_major(I["L0_attn_norm1_w"]))
    S["w2n"] = f32(_chunk_major(I["L0_attn_norm2_w"]))
    S["m0_wg"] = f32(I["L0_ffn_Wg"])
    S["m0_w1"] = f32(I["L0_ffn_W1"])
    S["m0_w2"] = f32(I["L0_ffn_W2"])
    S["a1_wq"] = f32(I["L1_attn_Wq"])
    S["a1_wk"] = f32(np.asarray(I["L1_attn_Wkv"])[:, :512])
    S["a1_wv"] = f32(np.asarray(I["L1_attn_Wkv"])[:, 512:])
    S["a1_wo"] = f32(I["L1_attn_Wo"])
    for nm, key in [("f1_wq", "L1_ffn_Wq"), ("f1_wk", "L1_ffn_Wk"), ("f1_wv", "L1_ffn_Wv"), ("f1_wo", "L1_ffn_Wo")]:
        S[nm] = f32(I[key])
    for nm, key in [("f1_bq", "L1_ffn_bq"), ("f1_bk", "L1_ffn_bk"), ("f1_bv", "L1_ffn_bv"), ("f1_bo", "L1_ffn_bo"),
                    ("f1_g1", "L1_ffn_ln1_g"), ("f1_b1n", "L1_ffn_ln1_b"), ("f1_g2", "L1_ffn_ln2_g"), ("f1_b2n", "L1_ffn_ln2_b")]:
        S[nm] = f32(_chunk_major(I[key]))
    S["f1_gw"] = f32(I["L1_ffn_gW"])
    S["f1_gb"] = f32(np.asarray(I["L1_ffn_gb"]).reshape(1, 8))
    S["f1_w1"] = f32(I["L1_ffn_W1"])
    S["f1_b1c"] = f32(np.asarray(I["L1_ffn_b1"]).reshape(8, 16, 128).transpose(0, 2, 1))
    S["f1_w2"] = f32(I["L1_ffn_W2"])
    S["f1_b2"] = f32(I["L1_ffn_b2"])
    gc1, gc2 = _rope_tables(1e6, T)
    lc1, lc2 = _rope_tables(1e4, T)
    S["ropeg"] = np.ascontiguousarray(np.concatenate([gc1, gc2], axis=1))
    S["ropel"] = np.ascontiguousarray(np.concatenate([lc1, lc2], axis=1))
    S["cmask"] = _causal_masks().reshape(4 * 128, 512)
    S["lmask"] = _local_masks().reshape(6 * 128, 512)
    S["ident"] = np.eye(128, dtype=np.float32).astype(ml_dtypes.bfloat16)
    S["selp16"] = _selpair(16)
    S["selp8"] = _selpair(8)
    S["maskq16"] = _maskq(16)
    S["maskq8"] = _maskq(8)
    mh = np.zeros((128, 3), np.float32)
    mh[0:64, 0] = 1.0
    mh[0:64, 1] = 1.0
    mh[64:128, 2] = 1.0
    S["maskh"] = mh.astype(ml_dtypes.bfloat16)
    sel8 = np.zeros((8, 8 * 128), np.float32)
    for e in range(8):
        sel8[e, e * 128:(e + 1) * 128] = 1.0
    S["sel8"] = sel8.astype(ml_dtypes.bfloat16)
    return S, percore


_DT = {np.dtype(np.float32): F32, np.dtype(ml_dtypes.bfloat16): BF16}


def declare_inputs(nc, shared, percore0):
    aps = {}
    for nm, arr in list(shared.items()) + list(percore0.items()):
        aps[nm] = nc.dram_tensor(nm, list(arr.shape), _DT[arr.dtype], kind="ExternalInput").ap()
    return aps


class Ctx:
    def __init__(self, nc, stack, aps, consts_vals=(), sbuf_spec=()):
        self.nc = nc
        self.aps = aps
        self.p = Prog(nc)
        self.pb = [stack.enter_context(nc.psum_tensor(f"pb{i}", [128, 512], F32)) for i in range(7)]
        self.pbbf = stack.enter_context(nc.psum_tensor("pbbf", [128, 1024], BF16))
        self.ones_bf = stack.enter_context(nc.sbuf_tensor("ones_bf", [128, 1], BF16))
        self.ones_row = stack.enter_context(nc.sbuf_tensor("ones_row", [1, 128], F32))
        self.ones_row_bf = stack.enter_context(nc.sbuf_tensor("ones_row_bf", [1, 128], BF16))
        self.consts = Consts(nc, consts_vals)
        self.consts.alloc(stack, nc)
        self.sb = {}
        for (nm, shape, dt) in sbuf_spec:
            self.sb[nm] = stack.enter_context(nc.sbuf_tensor(nm, list(shape), dt))
        self.wstage_tok = None

    def setup(self):
        p = self.p
        p.op("v", lambda h: h.memset(self.ones_bf[:, :], 1.0))
        p.op("v", lambda h: h.memset(self.ones_row[:, :], 1.0))
        p.op("v", lambda h: h.memset(self.ones_row_bf[:, :], 1.0), inc=1)
        self.consts.emit_init(p)
        self.t_setup = p.tok("v")
        for e in ("t", "sc", "v", "g"):
            p.wait(e, self.t_setup)


def load_bf16(p, dram_ap, sb_ap):
    return p.dma("g", sb_ap, dram_ap)



def emit_ln(ctx, x_dram, xn_sb, *, rs1_row=None, out_dram=None, sq_eps=1e-6,
            src_bf16=False, gamma=None, beta=None):
    p = ctx.p
    nc = ctx.nc
    S = ctx.sb
    xstg, sqt = S["ln_xstg"], S["ln_sqt"]
    rows = S["ln_rows"]
    abc, bbc = S["ln_abc"], S["ln_bbc"]
    ps_s, ps_q, ps_b1, ps_b2 = ctx.pb[0], ctx.pb[1], ctx.pb[2], ctx.pb[3]

    for blk in range(8):
        t0 = blk * 512
        v_done = []
        t_done = []
        for ch in range(8):
            sl = xstg[:, (ch % 2) * 512:(ch % 2) * 512 + 512]
            sq = sqt[:, (ch % 2) * 512:(ch % 2) * 512 + 512]
            if ch >= 2:
                p.wait("sy", v_done[ch - 2])
            if src_bf16:
                sl = sl.bitcast(BF16)[:, 0:512]
            tl = p.dma("sy", sl, x_dram[ch * 128:(ch + 1) * 128, t0:t0 + 512])
            xnsl = xn_sb[:, ch * T + t0: ch * T + t0 + 512]
            p.wait("v", tl)
            if ch >= 2:
                p.wait("v", t_done[ch - 2])
            p.op("v", lambda h, a=xnsl, b=sl: h.tensor_copy(a, b))
            tv = p.op("v", lambda h, a=sq, b=xnsl: h.tensor_tensor(a, b, b, ALU.mult), inc=1)
            v_done.append(tv)
            p.wait("t", tv)
            p.op("t", lambda h, b=xnsl, s=(ch == 0): h.matmul(ps_s[0:1, :], ctx.ones_bf[:, :], b, start=s, stop=(ch == 7)))
            tok_mm = p.op("t", lambda h, b=sq, s=(ch == 0): h.matmul(ps_q[0:1, :], ctx.ones_bf[:, :], b, start=s, stop=(ch == 7)), inc=1)
            t_done.append(tok_mm)
        mu, m2, var, sd, rstd, brow = (rows[0:1, i * 512:(i + 1) * 512] for i in range(6))
        tmp = rows[0:1, 3 * 512:4 * 512]
        p.wait("v", tok_mm)
        p.op("v", lambda h: h.tensor_scalar(mu, ps_s[0:1, :], 1.0 / D, None, ALU.mult))
        p.op("v", lambda h: h.tensor_scalar(m2, ps_q[0:1, :], 1.0 / D, None, ALU.mult))
        p.op("v", lambda h: h.tensor_tensor(tmp, mu, mu, ALU.mult))
        p.op("v", lambda h: h.tensor_tensor(var, m2, tmp, ALU.subtract))
        tv = p.op("v", lambda h: h.tensor_scalar(var, var, 1e-5, None, ALU.add), inc=1)
        p.wait("sc", tv)
        ts = p.op("sc", lambda h: h.activation(sd, var, AF.Sqrt), inc=1)
        p.wait("v", ts)
        trc0 = p.op("v", lambda h: h.reciprocal(rstd, sd), inc=1)
        p.op("v", lambda h: h.tensor_scalar(tmp, mu, -1.0, None, ALU.mult))
        p.wait("v", trc0, force=True)
        tv2 = p.op("v", lambda h: h.tensor_tensor(brow, tmp, rstd, ALU.mult), inc=1)
        if rs1_row is not None:
            r1, r2 = rows[0:1, 6 * 512:7 * 512], rows[0:1, 7 * 512:8 * 512]
            p.wait("v", tv, force=True)
            trr = p.op("v", lambda h: h.reciprocal(r1, var), inc=1)
            p.wait("v", trr, force=True)
            p.op("v", lambda h: h.tensor_scalar(r1, r1, -1e-5, None, ALU.mult))
            p.op("v", lambda h: h.tensor_scalar(r1, r1, 1.0, None, ALU.add))
            tv3 = p.op("v", lambda h: h.tensor_scalar(r1, r1, 1.0, sq_eps, ALU.mult, ALU.add), inc=1)
            p.wait("sc", tv3)
            ts2 = p.op("sc", lambda h: h.activation(r2, r1, AF.Sqrt), inc=1)
            p.wait("v", ts2)
            p.op("v", lambda h, o=rs1_row[0:1, t0:t0 + 512]: h.reciprocal(o, r2), inc=1)
        p.wait("t", tv2)
        tb1 = p.op("t", lambda h: h.matmul(ps_b1[:, :], ctx.ones_row[:, :], rstd, start=True, stop=True), inc=1)
        tb2 = p.op("t", lambda h: h.matmul(ps_b2[:, :], ctx.ones_row[:, :], brow, start=True, stop=True), inc=1)
        p.wait("sc", tb1)
        p.wait("sc", tb2)
        e1 = p.op("sc", lambda h: h.activation(abc[:, :], ps_b1[:, :], AF.Copy))
        e2 = p.op("sc", lambda h: h.activation(bbc[:, :], ps_b2[:, :], AF.Copy), inc=1)
        p.wait("v", e2)
        for ch in range(8):
            xnsl = xn_sb[:, ch * T + t0: ch * T + t0 + 512]
            p.op("v", lambda h, a=xnsl: h.tensor_tensor(a, a, abc[:, :], ALU.mult))
            tv = p.op("v", lambda h, a=xnsl: h.tensor_tensor(a, a, bbc[:, :], ALU.add), inc=1)
            if gamma is not None:
                tv = p.op("v", lambda h, a=xnsl, g=gamma[:, ch:ch + 1], b=beta[:, ch:ch + 1]:
                          h.tensor_scalar(a, a, g, b, ALU.mult, ALU.add), inc=1)
            if out_dram is not None:
                p.wait("sy", tv)
                p.dma("sy", out_dram[ch * 128:(ch + 1) * 128, t0:t0 + 512], xnsl)
    return p.tok("v")



def _declare(nc, shared, percore0, names, extra=None):
    aps = {}
    for nm in names:
        arr = shared[nm] if nm in shared else percore0[nm]
        aps[nm] = nc.dram_tensor(nm, list(arr.shape), _DT[np.asarray(arr).dtype], kind="ExternalInput").ap()
    for nm, (shape, dt) in (extra or {}).items():
        aps[nm] = nc.dram_tensor(nm, list(shape), dt, kind="ExternalInput").ap()
    return aps


def _outs(nc, spec):
    return {nm: nc.dram_tensor(nm, list(shape), dt, kind="ExternalOutput").ap()
            for nm, (shape, dt) in spec.items()}


def _finish(ctx):
    ctx.p.barrier()


def build_g1(shared, percore0):
    import contextlib
    nc = bass.Bass("TRN2", target_bir_lowering=False)
    stack = contextlib.ExitStack()
    nc._dev_stack = stack
    aps = _declare(nc, shared, percore0, ["x0"])
    aps.update(_outs(nc, {"xnT": ([D, T], BF16), "rs1": ([1, T], F32)}))
    sbuf_spec = [
        ("ln_xstg", (128, 1024), F32), ("ln_sqt", (128, 1024), BF16),
        ("ln_rows", (1, 4096), F32), ("ln_abc", (128, 512), BF16), ("ln_bbc", (128, 512), BF16),
        ("XN", (128, NCHUNK * T), BF16),
        ("rs1row", (1, T), F32),
    ]
    with nc.Block() as block:
        ctx = Ctx(nc, stack, aps, consts_vals=(), sbuf_spec=sbuf_spec)
        ctx.setup()
        emit_ln(ctx, aps["x0"], ctx.sb["XN"], rs1_row=ctx.sb["rs1row"], out_dram=aps["xnT"])
        tv = ctx.p.tok("v")
        ctx.p.wait("sy", tv)
        ctx.p.dma("sy", aps["rs1"][:, :], ctx.sb["rs1row"][:, :])
        _finish(ctx)
        ctx.p.build(block)
    return nc



def emit_rms_row(ctx, src_sb, out_row, eps=1e-6):
    p = ctx.p
    sqt = ctx.sb["ln_sqt"]
    rows = ctx.sb["ln_rows"]
    ps = ctx.pb[0]
    for blk in range(8):
        t0 = blk * 512
        t_done = []
        for ch in range(8):
            sq = sqt[:, (ch % 2) * 512:(ch % 2) * 512 + 512]
            sl = src_sb[:, ch * T + t0: ch * T + t0 + 512]
            if ch >= 2:
                p.wait("v", t_done[ch - 2])
            tv = p.op("v", lambda h, a=sq, b=sl: h.tensor_tensor(a, b, b, ALU.mult), inc=1)
            p.wait("t", tv)
            tm = p.op("t", lambda h, b=sq, s=(ch == 0): h.matmul(ps[0:1, :], ctx.ones_bf[:, :], b, start=s, stop=(ch == 7)), inc=1)
            t_done.append(tm)
        r1 = rows[0:1, 0:512]
        r2 = rows[0:1, 512:1024]
        p.wait("v", t_done[-1])
        tv = p.op("v", lambda h: h.tensor_scalar(r1, ps[0:1, :], 1.0 / D, eps, ALU.mult, ALU.add), inc=1)
        p.wait("sc", tv)
        ts = p.op("sc", lambda h: h.activation(r2, r1, AF.Sqrt), inc=1)
        p.wait("v", ts)
        p.op("v", lambda h, o=out_row[0:1, t0:t0 + 512]: h.reciprocal(o, r2), inc=1)
    return p.tok("v")



def load_rows(ctx, dram, dst, nkc, cols, scale=None):
    p = ctx.p
    toks = []
    if scale is None:
        for k in range(nkc):
            toks.append(p.dma("g", dst[:, k * cols:(k + 1) * cols], dram[k * 128:(k + 1) * 128, 0:cols]))
        return toks[-1]
    stg = ctx.sb["wstage"]
    for k in range(nkc):
        s0 = 0
        if k >= 1:
            p.wait("sy", toks[k - 1])
        else:
            p.wait("sy", ctx.wstage_tok)
        tl = p.dma("sy", stg[:, s0:s0 + cols], dram[k * 128:(k + 1) * 128, 0:cols])
        p.wait("v", tl)
        tv = p.op("v", lambda h, o=dst[:, k * cols:(k + 1) * cols], i=stg[:, s0:s0 + cols], s=scale[:, k:k + 1]:
                  h.tensor_scalar(o, i, s, None, ALU.mult), inc=1)
        toks.append(tv)
    ctx.wstage_tok = toks[-1]
    return toks[-1]


def emit_w_prep(ctx, src_dram, dst_dram, nkc, cols, scale):
    p = ctx.p
    stg = ctx.sb["wstage"]
    stgb = ctx.sb["wstageb"]
    vt, dt = [], []
    for k in range(nkc):
        s0 = 0
        if k >= 1:
            p.wait("sy", vt[k - 1])
        else:
            p.wait("sy", ctx.wstage_tok)
        tl = p.dma("sy", stg[:, s0:s0 + cols], src_dram[k * 128:(k + 1) * 128, 0:cols])
        p.wait("v", tl)
        if k >= 1:
            p.wait("v", dt[k - 1])
        tv = p.op("v", lambda h, o=stgb[:, s0:s0 + cols], i=stg[:, s0:s0 + cols], s=scale[:, k:k + 1]:
                  h.tensor_scalar(o, i, s, None, ALU.mult), inc=1)
        vt.append(tv)
        p.wait("sy", tv)
        dt.append(p.dma("sy", dst_dram[k * 128:(k + 1) * 128, 0:cols], stgb[:, s0:s0 + cols]))
    ctx.wstage_tok = vt[-1]
    return dt[-1]


class AttnCfg:
    def __init__(self, **kw):
        self.__dict__.update(kw)


def emit_attn(ctx, cfg, out_dram):
    p = ctx.p
    S = ctx.sb
    nq = cfg.nheads
    NQC = nq // 2
    rope = getattr(cfg, "rope", None)
    qk = getattr(cfg, "qk", None)
    dup = getattr(cfg, "kv_dup", False)
    nkvc = cfg.nkv_chunks
    nkvh = cfg.nkv_heads
    nho = 1 if dup else 2
    SCR = S["SCR"]
    KTS = S["KTS"]
    VTOK = S["VTOK"]
    qtile = lambda c: SCR[:, c * 512:(c + 1) * 512]
    qsw = lambda c: SCR[:, 4096 + c * 512:4096 + (c + 1) * 512]
    asm = lambda c: SCR[:, 8192 + c * 512:8192 + (c + 1) * 512]
    P_t = lambda i: SCR[:, 12288 + i * 512:12288 + (i + 1) * 512]
    nskt = S.get("nskt")
    rqrows = S.get("rqrows")
    sumsrow = S["sumsrow"]
    fscr = S["fscr"]
    bank_tok = {}

    def use_bank(e, b):
        tok = bank_tok.get(b)
        if tok is not None:
            p.wait(e, tok)

    if getattr(cfg, "latent", None) is not None:
        LATN = S["LATN"]
        WD = S["WD"]
        lat = cfg.latent
        bc_tok = {}
        for tb in range(8):
            t0 = tb * 512
            sl = tb % 2
            bcb = S["bcbuf"][:, sl * 512:sl * 512 + 512]
            bb = ctx.pb[2]
            use_bank("t", 2)
            tb1 = p.op("t", lambda h, r=lat["scale_row"][0:1, t0:t0 + 512]:
                       h.matmul(bb[:, :], ctx.ones_row_bf[:, :], r, start=True, stop=True), inc=1)
            p.wait("sc", tb1)
            p.wait("sc", bc_tok.get(sl))
            te = p.op("sc", lambda h, o=bcb: h.activation(o, bb[:, :], AF.Copy), inc=1)
            bank_tok[2] = te
            for fc in range(2):
                b = fc % 2
                pb = ctx.pb[b]
                use_bank("t", b)
                for kc in range(NCHUNK):
                    xap, tok = cfg.src(kc, t0, 512)
                    p.wait("t", tok)
                    tm = p.op("t", lambda h, w=WD[:, kc * 256 + fc * 128: kc * 256 + fc * 128 + 128],
                              x=xap, o=pb, s=(kc == 0), e=(kc == NCHUNK - 1):
                              h.matmul(o[:, :], w, x, start=s, stop=e), inc=(1 if kc == NCHUNK - 1 else 0))
                p.wait("v", tm)
                p.wait("v", te)
                tv = p.op("v", lambda h, o=LATN[:, fc * T + t0: fc * T + t0 + 512], i=pb, bcx=bcb:
                          h.tensor_tensor(o, i[:, :], bcx, ALU.mult), inc=1)
                bank_tok[b] = tv
                bc_tok[sl] = tv
        lat_tok = p.tok("v")
        kv_src = lambda kc, t0, w: (LATN[:, kc * T + t0: kc * T + t0 + w], lat_tok)
        kv_kc = 2
        kv_wcols = getattr(cfg, "wk_cols")
    else:
        kv_src = cfg.src
        kv_kc = NCHUNK
        kv_wcols = getattr(cfg, "wk_cols")

    WK = S["WK"]
    ksw_tok = {}
    for tb in range(8):
        for c in range(nkvc):
            t0 = tb * 512
            b = c % 2
            pbK = ctx.pb[b]
            use_bank("t", b)
            for kc in range(kv_kc):
                xap, tok = kv_src(kc, t0, 512)
                p.wait("t", tok)
                tm = p.op("t", lambda h, w=WK[:, kc * kv_wcols + c * 128: kc * kv_wcols + c * 128 + 128],
                          x=xap, o=pbK, s=(kc == 0), e=(kc == kv_kc - 1):
                          h.matmul(o[:, :], w, x, start=s, stop=e), inc=(1 if kc == kv_kc - 1 else 0))
            ksl = KTS[:, c * T + t0: c * T + t0 + 512]
            p.wait("sc", tm)
            if getattr(cfg, "bk", None) is not None:
                te = p.op("sc", lambda h, o=ksl, i=pbK, ba=cfg.bk[:, c:c + 1]: h.activation(o, i[:, :], AF.Identity, bias=ba), inc=1)
            else:
                te = p.op("sc", lambda h, o=ksl, i=pbK: h.activation(o, i[:, :], AF.Copy), inc=1)
            bank_tok[b] = te
            if rope is not None:
                b2 = 2 + c % 2
                pbS = ctx.pb[b2]
                use_bank("t", b2)
                for kc in range(kv_kc):
                    xap, tok = kv_src(kc, t0, 512)
                    p.wait("t", tok)
                    tm2 = p.op("t", lambda h, w=WK[:, kc * kv_wcols + nkvc * 128 + c * 128: kc * kv_wcols + nkvc * 128 + c * 128 + 128],
                               x=xap, o=pbS, s=(kc == 0), e=(kc == kv_kc - 1):
                               h.matmul(o[:, :], w, x, start=s, stop=e), inc=(1 if kc == kv_kc - 1 else 0))
                ks = asm(c % 2)
                p.wait("sc", tm2)
                p.wait("sc", ksw_tok.get(c % 2))
                te2 = p.op("sc", lambda h, o=ks, i=pbS: h.activation(o, i[:, :], AF.Copy), inc=1)
                bank_tok[b2] = te2
                rt = S["ROPE"]
                p.wait("v", te)
                p.wait("v", te2)
                if getattr(cfg, "no_krope", False):
                    tv = p.op("v", lambda h, a=ksl: h.tensor_copy(a, a), inc=1)
                else:
                    p.op("v", lambda h, a=ksl, t1=rt[:, t0:t0 + 512]: h.tensor_tensor(a, a, t1, ALU.mult))
                    p.op("v", lambda h, bx=ks, t2=rt[:, T + t0:T + t0 + 512]: h.tensor_tensor(bx, bx, t2, ALU.mult))
                    tv = p.op("v", lambda h, a=ksl, bx=ks: h.tensor_tensor(a, a, bx, ALU.add), inc=1)
                ksw_tok[c % 2] = tv
                bank_tok[b2] = tv

    if qk is not None:
        for c in range(nkvc):
            sq_tok = {}
            ev_tok = None
            for tb in range(8):
                t0 = tb * 512
                dq = P_t(tb % 2)
                ksl = KTS[:, c * T + t0: c * T + t0 + 512]
                p.wait("v", sq_tok.get(tb % 2))
                tv = p.op("v", lambda h, o=dq, i=ksl: h.tensor_tensor(o, i, i, ALU.mult), inc=1)
                p.wait("t", tv)
                use_bank("t", 4)
                rh = S["MASKH"][:, 0:1] if dup else S["MASKH"][:, 1:3]
                for tt in range(4):
                    tm = p.op("t", lambda h, l=dq[:, tt * 128:(tt + 1) * 128], r=rh,
                              o=ctx.pb[4][:, tt * nho:(tt + 1) * nho]:
                              h.matmul(o, l, r, start=True, stop=True), inc=(1 if tt == 3 else 0))
                sq_tok[tb % 2] = tm
                p.wait("v", tm)
                tvd = p.op("v", lambda h: h.memset(fscr[:, 255:256], 0.0), inc=1)
                p.wait("sc", tvd)
                ev_tok = p.op("sc", lambda h, o=fscr[:, 128 + tb * 4 * nho: 128 + (tb + 1) * 4 * nho],
                              i=ctx.pb[4][:, 0:4 * nho]: h.activation(o, i, AF.Copy), inc=1)
                bank_tok[4] = ev_tok
            width = 32 * nho
            p.wait("v", ev_tok)
            f1_ = fscr[:, 0:width]
            f2_ = fscr[:, 64:64 + width]
            inv_pm = 1.0 / cfg.exp_premul
            if qk == "rms":
                tv = p.op("v", lambda h, a=f1_, b=fscr[:, 128:128 + width]: h.tensor_scalar(a, b, 1.0 / 64.0, 1e-6, ALU.mult, ALU.add), inc=1)
                p.wait("sc", tv)
                ts = p.op("sc", lambda h, a=f2_, b=f1_: h.activation(a, b, AF.Sqrt), inc=1)
                p.wait("v", ts)
                tpm = p.op("v", lambda h, a=f2_, s=inv_pm: h.tensor_scalar(a, a, s, None, ALU.mult), inc=1)
                p.wait("v", tpm, force=True)
                tv2 = p.op("v", lambda h, o=nskt[:, c * width:(c + 1) * width], b=f2_: h.reciprocal(o, b), inc=1)
            else:
                tv = p.op("v", lambda h, a=f1_, b=fscr[:, 128:128 + width]: h.tensor_copy(a, b), inc=1)
                p.wait("sc", tv)
                ts = p.op("sc", lambda h, a=f2_, b=f1_: h.activation(a, b, AF.Sqrt), inc=1)
                p.wait("v", ts)
                p.op("v", lambda h, a=f2_: h.tensor_scalar_max(a, a, 1e-12))
                tpm = p.op("v", lambda h, a=f2_, s=inv_pm: h.tensor_scalar(a, a, s, None, ALU.mult), inc=1)
                p.wait("v", tpm, force=True)
                tv2 = p.op("v", lambda h, o=nskt[:, c * width:(c + 1) * width], b=f2_: h.reciprocal(o, b), inc=1)
            bank_tok[4] = tv2

    if getattr(cfg, "dbg", None) is not None and "knorm_aps" in cfg.dbg:
        p.barrier()
        for nm, (sap, dap) in cfg.dbg["knorm_aps"].items():
            p.dma("sy", dap, sap)
        p.barrier()

    if getattr(cfg, "load_wv", None) is not None:
        p.barrier()
        cfg.load_wv()
        p.barrier()
    WV = S.get("WV") or S["WK"]
    vone = VTOK.rearrange("p (n s) -> p n s", s=65)
    tvo = p.op("v", lambda h: h.memset(vone[:, :, 64:65], 1.0), inc=1)
    nvc = nkvh * 64 // 128
    wv_cols = cfg.wv_cols
    vstg_tok = {}
    vbc_tok = {}
    for tb in range(8):
        for c in range(nvc):
            t0 = tb * 512
            b = c % 2
            pbV = ctx.pb[b]
            use_bank("t", b)
            for kc in range(kv_kc):
                xap, tok = kv_src(kc, t0, 512)
                p.wait("t", tok)
                tm = p.op("t", lambda h, w=WV[:, kc * wv_cols + c * 128: kc * wv_cols + c * 128 + 128],
                          x=xap, o=pbV, s=(kc == 0), e=(kc == kv_kc - 1):
                          h.matmul(o[:, :], w, x, start=s, stop=e), inc=(1 if kc == kv_kc - 1 else 0))
            vstg = S["VSTG"][:, b * 512: b * 512 + 512]
            p.wait("sc", tm)
            p.wait("sc", vstg_tok.get(b))
            if getattr(cfg, "bv", None) is not None:
                te = p.op("sc", lambda h, o=vstg, i=pbV, ba=cfg.bv[:, c:c + 1]: h.activation(o, i[:, :], AF.Identity, bias=ba), inc=1)
            elif getattr(cfg, "v_scale_row", None) is not None:
                sl = tb % 2
                bcb = S["bcbuf"][:, sl * 512:sl * 512 + 512]
                use_bank("t", 5)
                tb1 = p.op("t", lambda h, r=cfg.v_scale_row[0:1, t0:t0 + 512]:
                           h.matmul(ctx.pb[5][:, :], ctx.ones_row_bf[:, :], r, start=True, stop=True), inc=1)
                p.wait("sc", tb1)
                p.wait("sc", vbc_tok.get(sl))
                p.op("sc", lambda h, o=bcb: h.activation(o, ctx.pb[5][:, :], AF.Copy))
                te0 = p.op("sc", lambda h, o=vstg, i=pbV: h.activation(o, i[:, :], AF.Copy), inc=1)
                p.wait("v", te0)
                te = p.op("v", lambda h, o=vstg, bcx=bcb: h.tensor_tensor(o, o, bcx, ALU.mult), inc=1)
                bank_tok[5] = te
                vbc_tok[sl] = te
            else:
                te = p.op("sc", lambda h, o=vstg, i=pbV: h.activation(o, i[:, :], AF.Copy), inc=1)
            bank_tok[b] = te
            p.wait("t", te)
            p.wait("v", tvo)
            last_v = None
            for tt in range(4):
                kt = tb * 4 + tt
                pt = ctx.pbbf[:, 0:128]
                use_bank("t", "bf")
                ttr = p.op("t", lambda h, i=vstg[:, tt * 128:(tt + 1) * 128], o=pt:
                           h.transpose(o, i, S["IDENT"][:, :]), inc=1)
                p.wait("v", ttr)
                bA = ((2 * c) * 32 + kt) * 65
                bB = ((2 * c + 1) * 32 + kt) * 65
                p.op("v", lambda h, o=VTOK[:, bA:bA + 64], i=pt[:, 0:64]: h.tensor_copy(o, i))
                tvv = p.op("v", lambda h, o=VTOK[:, bB:bB + 64], i=pt[:, 64:128]: h.tensor_copy(o, i), inc=1)
                bank_tok["bf"] = tvv
                last_v = tvv
            vstg_tok[b] = last_v

    p.barrier()
    if getattr(cfg, "dbg", None) is not None and "kv_aps" in cfg.dbg:
        for nm, (sap, dap) in cfg.dbg["kv_aps"].items():
            p.dma("sy", dap, sap)
        p.barrier()

    WQb = S["WQB"]
    WOS = S.get("WOS")
    FQ = NQC * (2 if rope else 1)
    recips = sumsrow[0:1, 1024:2048] if False else S["recips"]
    ostg = S["OSTG"]
    p_tok = {}
    asm_tok = None
    qe_tok = {}
    for qc in range(8):
        q0 = qc * 512
        sched = cfg.mask_sched(qc)
        wq_toks = {}
        for fc in range(FQ):
            slot = fc % 2
            b = fc % 2
            if fc >= 2:
                p.wait("g" if cfg.wq_bf else "g", wq_toks[fc - 2])
            tl = p.dma("g", WQb[:, slot * 1024:(slot + 1) * 1024].rearrange("p (k c) -> p k c", c=128),
                       cfg.wq[:, fc * 128:(fc + 1) * 128].rearrange("(k p) c -> p k c", p=128))
            pbQ = ctx.pb[b]
            use_bank("t", b)
            p.wait("t", tl)
            for kc in range(NCHUNK):
                xap, tok = cfg.src(kc, q0, 512)
                p.wait("t", tok)
                tm = p.op("t", lambda h, w=WQb[:, slot * 1024 + kc * 128: slot * 1024 + kc * 128 + 128],
                          x=xap, o=pbQ, s=(kc == 0), e=(kc == NCHUNK - 1):
                          h.matmul(o[:, :], w, x, start=s, stop=e), inc=(1 if kc == NCHUNK - 1 else 0))
            wq_toks[fc] = tm
            dst = qtile(fc) if fc < NQC else qsw(fc - NQC)
            p.wait("sc", tm)
            p.wait("sc", asm_tok)
            if getattr(cfg, "bq", None) is not None and fc < NQC:
                te = p.op("sc", lambda h, o=dst, i=pbQ, ba=cfg.bq[:, fc:fc + 1]: h.activation(o, i[:, :], AF.Identity, bias=ba), inc=1)
            else:
                te = p.op("sc", lambda h, o=dst, i=pbQ: h.activation(o, i[:, :], AF.Copy), inc=1)
            bank_tok[b] = te
            qe_tok[("e", fc)] = te
        if rope is not None:
            rt = S["ROPE"]
            for c in range(NQC):
                p.wait("v", qe_tok[("e", c)])
                p.wait("v", qe_tok[("e", c + NQC)])
                p.op("v", lambda h, a=qtile(c), t1=rt[:, q0:q0 + 512]: h.tensor_tensor(a, a, t1, ALU.mult))
                p.op("v", lambda h, bx=qsw(c), t2=rt[:, T + q0:T + q0 + 512]: h.tensor_tensor(bx, bx, t2, ALU.mult))
                p.op("v", lambda h, a=qtile(c), bx=qsw(c): h.tensor_tensor(a, a, bx, ALU.add), inc=1)
                qe_tok[("r", c)] = p.tok("v")
        if qk is not None:
            use_bank("t", 4)
            for c in range(NQC):
                src_t = qe_tok.get(("r", c), qe_tok[("e", c)])
                p.wait("v", src_t)
                tv = p.op("v", lambda h, o=qsw(c), a=qtile(c): h.tensor_tensor(o, a, a, ALU.mult), inc=1)
                p.wait("t", tv)
                for tt in range(4):
                    tm = p.op("t", lambda h, l=qsw(c)[:, tt * 128:(tt + 1) * 128], r=S["MASKH"][:, 1:3],
                              o=ctx.pb[4][:, tt * 2 * NQC + c * 2: tt * 2 * NQC + c * 2 + 2]:
                              h.matmul(o, l, r, start=True, stop=True), inc=(1 if (tt == 3 and c == NQC - 1) else 0))
            qn_w = 2 * NQC * 4
            p.wait("v", tm)
            f1q = fscr[:, 0:qn_w]
            f3q = fscr[:, 64:64 + qn_w]
            if qk == "rms":
                p.op("v", lambda h, a=f1q, b=ctx.pb[4][:, 0:qn_w]: h.tensor_scalar(a, b, 1.0 / 64.0, 1e-6, ALU.mult, ALU.add), inc=1)
                p.wait("sc", p.tok("v"))
                ts = p.op("sc", lambda h, a=f1q: h.activation(a, a, AF.Sqrt), inc=1)
                p.wait("v", ts)
                tvq0 = p.op("v", lambda h, o=f3q, b=f1q: h.reciprocal(o, b), inc=1)
            else:
                p.op("v", lambda h, a=f1q, b=ctx.pb[4][:, 0:qn_w]: h.tensor_copy(a, b), inc=1)
                p.wait("sc", p.tok("v"))
                ts = p.op("sc", lambda h, a=f1q: h.activation(a, a, AF.Sqrt), inc=1)
                p.wait("v", ts)
                tmx = p.op("v", lambda h, a=f1q: h.tensor_scalar_max(a, a, 1e-12), inc=1)
                p.wait("v", tmx, force=True)
                tvq0 = p.op("v", lambda h, o=f3q, b=f1q: h.reciprocal(o, b), inc=1)
            p.wait("v", tvq0, force=True)
            tvq = p.op("v", lambda h, o=fscr.bitcast(BF16)[:, 256:256 + qn_w], b=f3q: h.tensor_copy(o, b), inc=1)
            bank_tok[4] = tvq
            rqb = fscr.bitcast(BF16)[:, 256:512]
            for tt in range(4):
                pt = ctx.pbbf[:, 0:128]
                use_bank("t", "bf")
                p.wait("t", tvq)
                ttr = p.op("t", lambda h, i=rqb[:, tt * 2 * NQC:(tt + 1) * 2 * NQC], o=pt[0:2 * NQC, 0:128]:
                           h.transpose(o, i, S["IDENT"][:, :]), inc=1)
                p.wait("sc", ttr)
                tse = p.op("sc", lambda h, o=rqrows[0:2 * NQC, tt * 128:(tt + 1) * 128], i=pt[0:2 * NQC, 0:128]:
                           h.activation(o, i, AF.Copy), inc=1)
                bank_tok["bf"] = tse
            for c in range(NQC):
                use_bank("t", 5)
                p.wait("t", tse)
                tbq = p.op("t", lambda h, l=S["SELP"][0:nq, c * 128:(c + 1) * 128], r=rqrows[0:nq, 0:512]:
                           h.matmul(ctx.pb[5][:, :], l, r, start=True, stop=True), inc=1)
                p.wait("v", tbq)
                tvm = p.op("v", lambda h, a=qtile(c): h.tensor_tensor(a, a, ctx.pb[5][:, :], ALU.mult), inc=1)
                bank_tok[5] = tvm
                qe_tok[("n", c)] = tvm
        if getattr(cfg, "dbg", None) is not None and qc == cfg.dbg["qc"] and "q_ap" in cfg.dbg:
            for fc in range(FQ):
                p.wait("sy", qe_tok[("e", fc)])
            p.dma("sy", cfg.dbg["q_ap"], SCR[:, 0:4096])
            if "wq_ap" in cfg.dbg:
                p.dma("sy", cfg.dbg["wq_ap"], WQb[:, :])
            p.barrier()
        for c in range(NQC):
            g = (c // 2) if dup else c
            q_ready = qe_tok.get(("n", c)) or qe_tok.get(("r", c)) or qe_tok[("e", c)]
            pvA, pvB = ctx.pb[4], ctx.pb[5]
            use_bank("t", 4)
            use_bank("t", 5)
            p.wait("t", q_ready)
            nsched = len(sched)
            evA = evB = None
            for idx, (kt, mi) in enumerate(sched):
                bA = idx % 2
                sA, sB = ctx.pb[0 + 2 * bA], ctx.pb[1 + 2 * bA]
                use_bank("t", 0 + 2 * bA)
                use_bank("t", 1 + 2 * bA)
                p.op("t", lambda h, l=KTS[0:64, g * T + kt * 128: g * T + kt * 128 + 128],
                     r=qtile(c)[0:64, :], o=sA: h.matmul(o[:, :], l, r, start=True, stop=True, tile_position=(0, 0)))
                tSB = p.op("t", lambda h, l=KTS[64:128, g * T + kt * 128: g * T + kt * 128 + 128],
                           r=qtile(c)[64:128, :], o=sB: h.matmul(o[:, :], l, r, start=True, stop=True, tile_position=(64, 0)), inc=1)
                if mi is not None:
                    p.wait("v", tSB)
                    p.op("v", lambda h, o=sA, m=S["MASKS"][:, mi * 512:(mi + 1) * 512]: h.tensor_tensor(o[:, :], o[:, :], m, ALU.add))
                    tSB = p.op("v", lambda h, o=sB, m=S["MASKS"][:, mi * 512:(mi + 1) * 512]: h.tensor_tensor(o[:, :], o[:, :], m, ALU.add), inc=1)
                pA, pB = P_t(2 * bA), P_t(2 * bA + 1)
                p.wait("sc", tSB)
                p.wait("sc", p_tok.get(2 * bA))
                if qk is not None:
                    colA = g * 32 * nho + kt * nho
                    colB = colA if dup else colA + 1
                    p.op("sc", lambda h, o=pA, i=sA, scl=nskt[:, colA:colA + 1]:
                         h.activation(o, i[:, :], AF.Exp, bias=cfg.exp_bias, scale=scl))
                    teP = p.op("sc", lambda h, o=pB, i=sB, scl=nskt[:, colB:colB + 1]:
                               h.activation(o, i[:, :], AF.Exp, bias=cfg.exp_bias, scale=scl), inc=1)
                else:
                    p.op("sc", lambda h, o=pA, i=sA: h.activation(o, i[:, :], AF.Exp, bias=cfg.exp_bias, scale=cfg.exp_scale_const))
                    teP = p.op("sc", lambda h, o=pB, i=sB: h.activation(o, i[:, :], AF.Exp, bias=cfg.exp_bias, scale=cfg.exp_scale_const), inc=1)
                bank_tok[0 + 2 * bA] = teP
                bank_tok[1 + 2 * bA] = teP
                hA = (g if dup else 2 * c)
                hB = (g if dup else 2 * c + 1)
                vbA = (hA * 32 + kt) * 65
                vbB = (hB * 32 + kt) * 65
                p.wait("t", teP)
                p.op("t", lambda h, l=VTOK[:, vbA:vbA + 65], r=pA, o=pvA[0:65, :], s=(idx == 0), e=(idx == nsched - 1):
                     h.matmul(o, l, r, start=s, stop=e))
                tPV = p.op("t", lambda h, l=VTOK[:, vbB:vbB + 65], r=pB, o=pvB[0:65, :], s=(idx == 0), e=(idx == nsched - 1):
                           h.matmul(o, l, r, start=s, stop=e), inc=1)
                p_tok[2 * bA] = tPV
            p.wait("sc", tPV)
            p.wait("sc", qe_tok.get("rc"))
            p.op("sc", lambda h: h.activation(sumsrow[0:1, 0:512], pvA[64:65, :], AF.Copy))
            tsum = p.op("sc", lambda h: h.activation(sumsrow[0:1, 512:1024], pvB[64:65, :], AF.Copy), inc=1)
            p.wait("v", tsum)
            if getattr(cfg, "sums_clip", None) is not None:
                tcl = p.op("v", lambda h: h.tensor_scalar_max(sumsrow[0:1, :], sumsrow[0:1, :], cfg.sums_clip), inc=1)
                p.wait("v", tcl, force=True)
            trc = p.op("v", lambda h: h.reciprocal(recips[0:1, 0:1024], sumsrow[0:1, :]), inc=1)
            qe_tok["rc"] = trc
            use_bank("t", 6)
            p.wait("t", trc)
            p.op("t", lambda h: h.matmul(ctx.pb[6][0:64, :], ctx.ones_row[0:1, 0:64], recips[0:1, 0:512],
                                         start=True, stop=True, tile_position=(0, 0)))
            tbc = p.op("t", lambda h: h.matmul(ctx.pb[6][64:128, :], ctx.ones_row[0:1, 0:64], recips[0:1, 512:1024],
                                               start=True, stop=True, tile_position=(0, 64)), inc=1)
            p.wait("sc", asm_tok)
            p.op("sc", lambda h, o=asm(c)[0:64, :], i=pvA[0:64, :]: h.activation(o, i, AF.Copy))
            tev = p.op("sc", lambda h, o=asm(c)[64:128, :], i=pvB[0:64, :]: h.activation(o, i, AF.Copy), inc=1)
            bank_tok[4] = tev
            bank_tok[5] = tev
            p.wait("v", tev)
            p.wait("v", tbc)
            p.op("v", lambda h, a=asm(c)[0:64, :]: h.tensor_tensor(a, a, ctx.pb[6][0:64, :], ALU.mult))
            tno = p.op("v", lambda h, a=asm(c)[64:128, :]: h.tensor_tensor(a, a, ctx.pb[6][64:128, :], ALU.mult), inc=1)
            bank_tok[6] = tno
            qe_tok[("a", c)] = tno
        if getattr(cfg, "dbg", None) is not None and qc == cfg.dbg["qc"]:
            for nm, ap in cfg.dbg["aps"].items():
                for c2 in range(NQC):
                    p.wait("sy", qe_tok[("a", c2)])
                p.wait("sy", p.tok("v"))
                p.wait("sy", p.tok("sc"))
                p.dma("sy", ap[1], ap[0])
        for c in range(NQC):
            p.wait("t", qe_tok[("a", c)])
        IC = NQC
        for fc in range(NCHUNK):
            b = fc % 2
            pbO = ctx.pb[b]
            use_bank("t", b)
            if getattr(cfg, "wo_stream", None) is not None:
                p.wait("g", qe_tok.get(("wo", b)))
                two = p.dma("g", S["WOB"][:, b * 1024:(b + 1) * 1024].rearrange("p (k c) -> p k c", c=128),
                            cfg.wo_stream[:, fc * 128:(fc + 1) * 128].rearrange("(k p) c -> p k c", p=128))
                p.wait("t", two)
                wsrc = lambda ic, b=b: S["WOB"][:, b * 1024 + ic * 128: b * 1024 + ic * 128 + 128]
            else:
                wsrc = lambda ic, fc=fc: WOS[:, ic * 1024 + fc * 128: ic * 1024 + fc * 128 + 128]
            for ic in range(IC):
                tm = p.op("t", lambda h, w=wsrc(ic), x=asm(ic), o=pbO, s=(ic == 0), e=(ic == IC - 1):
                          h.matmul(o[:, :], w, x, start=s, stop=e), inc=(1 if ic == IC - 1 else 0))
            o_sl = ostg[:, b * 512: b * 512 + 512]
            res = cfg.res(fc, q0) if getattr(cfg, "res", None) is not None else None
            p.wait("sc" if getattr(cfg, "bo", None) is not None else "v", tm)
            if getattr(cfg, "bo", None) is not None:
                p.wait("sc", qe_tok.get(("o", b)))
                te = p.op("sc", lambda h, o=o_sl, i=pbO, ba=cfg.bo[:, fc:fc + 1]: h.activation(o, i[:, :], AF.Identity, bias=ba), inc=1)
                if res is not None:
                    rap, rtok = res
                    p.wait("v", te)
                    p.wait("v", rtok)
                    te = p.op("v", lambda h, o=o_sl, r=rap: h.tensor_tensor(o, o, r, ALU.add), inc=1)
            else:
                p.wait("v", tm)
                p.wait("v", qe_tok.get(("o", b)))
                if res is not None:
                    rap, rtok = res
                    p.wait("v", rtok)
                    te = p.op("v", lambda h, o=o_sl, i=pbO, r=rap: h.tensor_tensor(o, i[:, :], r, ALU.add), inc=1)
                else:
                    te = p.op("v", lambda h, o=o_sl, i=pbO: h.tensor_copy(o, i[:, :]), inc=1)
            bank_tok[b] = te
            qe_tok[("wo", b)] = tm
            p.wait("sy", te)
            td = p.dma("sy", out_dram[fc * 128:(fc + 1) * 128, q0:q0 + 512], o_sl)
            qe_tok[("o", b)] = td
        asm_tok = p.tok("t")
    p.barrier()


class SrcStream:

    def __init__(self, ctx, dram, buf, queue="sy"):
        self.ctx, self.dram, self.buf, self.q = ctx, dram, buf, queue
        self.key = {}

    def __call__(self, kc, t0, w=512):
        p = self.ctx.p
        cur = self.key.get(kc)
        if cur is not None and cur[0] == t0:
            return (self.buf[:, kc * 512:(kc + 1) * 512], cur[1])
        if cur is not None:
            p.barrier()
            p.wait(self.q, p.tok("t"))
            p.wait(self.q, p.tok("v"))
        tl = p.dma(self.q, self.buf[:, kc * 512:(kc + 1) * 512],
                   self.dram[kc * 128:(kc + 1) * 128, t0:t0 + w])
        self.key[kc] = (t0, tl)
        return (self.buf[:, kc * 512:(kc + 1) * 512], tl)


def emit_rms_row_stream(ctx, src, out_row, eps=1e-6):
    p = ctx.p
    S = ctx.sb
    rows = S["sumsrow"]
    ps = ctx.pb[6]
    for tb in range(8):
        t0 = tb * 512
        t_done = []
        for ch in range(NCHUNK):
            xap, tok = src(ch, t0)
            sq = S["VSTG"][:, (ch % 2) * 512:(ch % 2) * 512 + 512]
            p.wait("v", tok)
            if ch >= 2:
                p.wait("v", t_done[ch - 2])
            tv = p.op("v", lambda h, o=sq, i=xap: h.tensor_tensor(o, i, i, ALU.mult), inc=1)
            p.wait("t", tv)
            tm = p.op("t", lambda h, b=sq, s=(ch == 0): h.matmul(ps[0:1, :], ctx.ones_bf[:, :], b, start=s, stop=(ch == 7)), inc=1)
            t_done.append(tm)
        r1 = rows[0:1, 0:512]
        p.wait("v", t_done[-1])
        tv = p.op("v", lambda h: h.tensor_scalar(r1, ps[0:1, :], 1.0 / D, eps, ALU.mult, ALU.add), inc=1)
        p.wait("sc", tv)
        ts = p.op("sc", lambda h: h.activation(r1, r1, AF.Sqrt), inc=1)
        p.wait("v", ts)
        tvr = p.op("v", lambda h, o=out_row[0:1, t0:t0 + 512]: h.reciprocal(o, r1), inc=1, lp=True)
        p.wait("t", tvr)
    return p.tok("v")


def _attn_sbuf_spec(nq, nkvc, nkvh, nho, kv_kc, wk_cols, wv_cols, *, rope, nmask, latent,
                    wo_stream=False, qk=True, scaled=True, wv_shared=False, bcb=True):
    NQC = nq // 2
    spec = [
        ("SCR", (128, 14336), BF16),
        ("KTS", (128, nkvc * T), BF16),
        ("VTOK", (128, nkvh * 32 * 65), BF16),
        ("SRCB", (128, 4096), BF16),
        ("WK", (128, kv_kc * max(wk_cols, wv_cols if wv_shared else 0)), BF16),
        ("WQB", (128, 2048), BF16),
        ("MASKH", (128, 3), BF16),
        ("IDENT", (128, 128), BF16),
        ("sumsrow", (1, 1024), F32),
        ("recips", (1, 1024), F32),
        ("fscr", (128, 256), F32),
        ("VSTG", (128, 1024), BF16),
        ("OSTG", (128, 1024), BF16),
    ]
    if not wv_shared:
        spec.append(("WV", (128, kv_kc * wv_cols), BF16))
    if qk:
        spec += [("SELP", (nq, NQC * 128), BF16),
                 ("nskt", (128, max(1, nkvc * 32 * nho)), F32),
                 ("rqrows", (16, 512), BF16)]
    if bcb:
        spec.append(("bcbuf", (128, 1024), BF16))
    if scaled:
        spec += [("wstage", (128, 2048), F32), ("wstageb", (128, 2048), BF16)]
    spec.append(("WOB", (128, 2048), BF16) if wo_stream else ("WOS", (128, NQC * 1024), BF16))
    if rope:
        spec.append(("ROPE", (128, 2 * T), BF16))
    if nmask:
        spec.append(("MASKS", (128, nmask * 512), F32))
    if latent:
        spec.append(("WD", (128, NCHUNK * 256), BF16))
        spec.append(("LATN", (128, 2 * T), BF16))
    return spec


def _load_attn_consts(ctx, aps, *, rope_key=None, nmask=0, mask_key=None, rs1=None):
    p = ctx.p
    S = ctx.sb
    toks = []
    toks.append(p.dma("g", S["MASKH"][:, :], aps["maskh"][:, :]))
    toks.append(p.dma("g", S["IDENT"][:, :], aps["ident"][:, :]))
    if rope_key:
        toks.append(p.dma("g", S["ROPE"][:, :], aps[rope_key][:, :]))
    for mi in range(nmask):
        toks.append(p.dma("sy", S["MASKS"][:, mi * 512:(mi + 1) * 512], aps[mask_key][mi * 128:(mi + 1) * 128, :]))
    return toks


def _sched_causal(qc):
    return [(kt, (kt - qc * 4) if kt >= qc * 4 else None) for kt in range(qc * 4 + 4)]


def _sched_local(qc):
    out = []
    for kt in range(max(0, qc * 4 - 2), qc * 4 + 4):
        out.append((kt, kt - qc * 4 + 2))
    return out


def _sched_full(qc):
    return [(kt, None) for kt in range(32)]


def build_g2(shared, percore0, dbg_qc=None):
    import contextlib
    nc = bass.Bass("TRN2", target_bir_lowering=False)
    stack = contextlib.ExitStack()
    nc._dev_stack = stack
    aps = _declare(nc, shared, percore0, ["g_wq", "g_kvd", "g_kup", "g_vup", "g_wo",
                                         "w1n", "ropeg", "cmask", "ident", "selp16", "maskh"],
                   extra={"xnT_in": ([D, T], BF16), "rs1_in": ([1, T], F32)})
    outs = {"x1T": ([D, T], BF16)}
    if dbg_qc is not None:
        outs.update({"d_scr": ([128, 14336], BF16), "d_nskt": ([128, 128], F32),
                     "d_sums": ([1, 1024], F32), "d_recips": ([1, 1024], F32)})
    aps.update(_outs(nc, outs))
    wqbf = nc.dram_tensor("wqbf", [D, 2048], BF16, kind="Internal").ap()
    spec = _attn_sbuf_spec(16, 4, 4, 1, 2, 1024, 256, rope=True, nmask=4, latent=True)
    spec += [("W1N", (128, 8), F32), ("RS1", (1, T), BF16)]
    with nc.Block() as block:
        ctx = Ctx(nc, stack, aps, consts_vals=(-8.0,), sbuf_spec=spec)
        ctx.setup()
        p = ctx.p
        S = ctx.sb
        p.dma("g", S["W1N"][:, :], aps["w1n"][:, :])
        p.dma("g", S["RS1"][:, :], aps["rs1_in"][:, :])
        p.dma("g", S["SELP"][:, :], aps["selp16"][:, :])
        _load_attn_consts(ctx, aps, rope_key="ropeg", nmask=4, mask_key="cmask")
        p.barrier()
        emit_w_prep(ctx, aps["g_wq"], wqbf, 8, 2048, S["W1N"])
        load_rows(ctx, aps["g_kvd"], S["WD"], 8, 256, scale=S["W1N"])
        load_rows(ctx, aps["g_kup"], S["WK"], 2, 1024)
        load_rows(ctx, aps["g_vup"], S["WV"], 2, 256)
        load_rows(ctx, aps["g_wo"], S["WOS"], 8, 1024)
        p.barrier()
        src = SrcStream(ctx, aps["xnT_in"], S["SRCB"])
        cfg = AttnCfg(nheads=16, kv_dup=True, nkv_chunks=4, nkv_heads=4, rope="ROPE", qk="rms",
                      wq=wqbf, wq_bf=True, wk_cols=1024, wv_cols=256,
                      latent=dict(scale_row=S["RS1"]), exp_premul=0.125, exp_bias=-8.0,
                      mask_sched=_sched_causal, src=src,
                      res=lambda fc, q0: src(fc, q0))
        if dbg_qc is not None:
            cfg.dbg = {"qc": dbg_qc, "aps": {
                "scr": (S["SCR"][:, :], aps["d_scr"][:, :]),
                "nskt": (S["nskt"][:, :], aps["d_nskt"][:, :]),
                "sums": (S["sumsrow"][:, :], aps["d_sums"][:, :]),
                "recips": (S["recips"][:, :], aps["d_recips"][:, :]),
            }}
        emit_attn(ctx, cfg, aps["x1T"])
        _finish(ctx)
        ctx.p.build(block)
    return nc


class SrcResident:

    def __init__(self, ctx, arena, tok):
        self.arena, self.tok = arena, tok

    def __call__(self, kc, t0, w=512):
        return (self.arena[:, kc * T + t0: kc * T + t0 + w], self.tok)


def build_g3(shared, percore0, dbg_qc=None, resident=False):
    import contextlib
    nc = bass.Bass("TRN2", target_bir_lowering=False)
    stack = contextlib.ExitStack()
    nc._dev_stack = stack
    aps = _declare(nc, shared, percore0, ["l_wq", "l_wk", "l_wv", "l_wo",
                                         "w2n", "ropel", "lmask", "ident", "selp16", "maskh"],
                   extra={"x1T_in": ([D, T], BF16)})
    outs3 = {"x2T": ([D, T], BF16)}
    if dbg_qc is not None:
        outs3.update({"d_scr": ([128, 14336], BF16), "d_nskt": ([128, 128], F32),
                      "d_sums": ([1, 1024], F32), "d_rs2": ([1, T], BF16),
                      "d_kts": ([128, 4 * T], BF16), "d_wk": ([128, 8192], BF16),
                      "d_nskt0": ([128, 128], F32), "d_dq": ([128, 1024], BF16), "d_fscr": ([128, 256], F32)})
    aps.update(_outs(nc, outs3))
    wqbf = nc.dram_tensor("wqbf", [D, 2048], BF16, kind="Internal").ap()
    spec = _attn_sbuf_spec(16, 4, 4, 1, 8, 1024, 256, rope=True, nmask=6, latent=False)
    spec += [("W2N", (128, 8), F32), ("RS2", (1, T), BF16)]
    if resident:
        spec = [s for s in spec if s[0] != "SRCB"] + [("XA", (128, NCHUNK * T), BF16)]
    with nc.Block() as block:
        ctx = Ctx(nc, stack, aps, consts_vals=(-8.0,), sbuf_spec=spec)
        ctx.setup()
        p = ctx.p
        S = ctx.sb
        p.dma("g", S["W2N"][:, :], aps["w2n"][:, :])
        p.dma("g", S["SELP"][:, :], aps["selp16"][:, :])
        _load_attn_consts(ctx, aps, rope_key="ropel", nmask=6, mask_key="lmask")
        p.barrier()
        emit_w_prep(ctx, aps["l_wq"], wqbf, 8, 2048, S["W2N"])
        load_rows(ctx, aps["l_wk"], S["WK"], 8, 1024, scale=S["W2N"])
        load_rows(ctx, aps["l_wv"], S["WV"], 8, 256, scale=S["W2N"])
        load_rows(ctx, aps["l_wo"], S["WOS"], 8, 1024)
        p.barrier()
        if resident:
            for kc in range(NCHUNK):
                tok = p.dma("g", S["XA"][:, kc * T:(kc + 1) * T], aps["x1T_in"][kc * 128:(kc + 1) * 128, :])
            p.barrier()
            src = SrcResident(ctx, S["XA"], None)
        else:
            src = SrcStream(ctx, aps["x1T_in"], S["SRCB"])
            import os as _os
            if _os.environ.get("PARANOID"):
                src.paranoid = True
        emit_rms_row_stream(ctx, src, S["RS2"], eps=1e-6)
        cfg = AttnCfg(nheads=16, kv_dup=True, nkv_chunks=4, nkv_heads=4, rope="ROPE", qk="rms",
                      wq=wqbf, wq_bf=True, wk_cols=1024, wv_cols=256,
                      v_scale_row=S["RS2"], exp_premul=0.125, exp_bias=-8.0,
                      mask_sched=_sched_local, src=src,
                      res=lambda fc, q0: src(fc, q0))
        if dbg_qc is not None:
            cfg.dbg = {"qc": dbg_qc, "aps": {
                "scr": (S["SCR"][:, :], aps["d_scr"][:, :]),
                "nskt": (S["nskt"][:, :], aps["d_nskt"][:, :]),
                "sums": (S["sumsrow"][:, :], aps["d_sums"][:, :]),
                "rs2": (S["RS2"][:, :], aps["d_rs2"][:, :]),
            }, "kv_aps": {
                "kts": (S["KTS"][:, :], aps["d_kts"][:, :]),
                "wk": (S["WK"][:, :], aps["d_wk"][:, :]),
            }, "knorm_aps": {
                "nskt0": (S["nskt"][:, :], aps["d_nskt0"][:, :]),
                "dq": (S["SCR"][:, 12288:13312], aps["d_dq"][:, :]),
                "fscr": (S["fscr"][:, :], aps["d_fscr"][:, :]),
            }}
        emit_attn(ctx, cfg, aps["x2T"])
        _finish(ctx)
        ctx.p.build(block)
    return nc



def emit_moe(ctx, cfg, finish):
    p = ctx.p
    S = ctx.sb
    XMB = S["XMB"]
    W1S, W2S = S["W1S"], S["W2S"]
    HS = S["HS"]
    YA = S["YA"]
    CMB = S["CMB"]
    WTR = S["WTR"]
    WBC = S["WBC"]
    GSC = S["GSC"]
    renorm = cfg.renorm
    act = cfg.act

    wgs = S["WGS"]
    load_rows(ctx, cfg.wg, wgs, NCHUNK, 8)
    if cfg.gb is not None:
        tgb = p.dma("g", S["GB"][:, :], cfg.gb[0:1, :])
    p.barrier()
    tr_tok = None
    for tt in range(32):
        t0 = (tt // 4) * 512
        sub = (tt % 4) * 128
        pg = ctx.pb[6]
        tok_bank = p.tok("sc")
        for kc in range(NCHUNK):
            xap, tok = cfg.src(kc, t0)
            p.wait("t", tok)
            tm = p.op("t", lambda h, l=xap[:, sub:sub + 128], r=wgs[:, kc * 8:(kc + 1) * 8],
                      o=pg[:, 0:8], s=(kc == 0), e=(kc == NCHUNK - 1 and cfg.gb is None):
                      h.matmul(o, l, r, start=s, stop=e), inc=(1 if (kc == NCHUNK - 1 and cfg.gb is None) else 0))
        if cfg.gb is not None:
            p.wait("t", tgb)
            tm = p.op("t", lambda h: h.matmul(pg[:, 0:8], ctx.ones_row_bf[0:1, :], S["GB"][0:1, :],
                                              start=False, stop=True), inc=1)
        g8 = GSC[:, 0:8]
        mx = GSC[:, 8:9]
        nmx = GSC[:, 9:10]
        sm = GSC[:, 10:11]
        rs = GSC[:, 11:12]
        mx2 = GSC[:, 12:13]
        den = GSC[:, 13:14]
        w1c = GSC[:, 14:15]
        w2c = GSC[:, 15:16]
        oh1 = GSC[:, 16:24]
        msk = GSC[:, 24:32]
        p.wait("v", tm)

        def vs(fn):
            tk = p.op("v", fn, inc=1)
            p.wait("v", tk, force=True)
            return tk
        vs(lambda h: h.tensor_copy(g8, pg[:, 0:8]))
        vs(lambda h: h.reduce_max(mx, g8, AX.X))
        tvn = vs(lambda h: h.tensor_scalar(nmx, mx, -1.0, None, ALU.mult))
        p.wait("sc", tvn)
        tse = p.op("sc", lambda h: h.activation(g8, g8, AF.Exp, bias=nmx, scale=1.0), inc=1)
        p.wait("v", tse)
        vs(lambda h: h.reduce_sum(sm, g8, AX.X))
        vs(lambda h: h.reciprocal(rs, sm))
        vs(lambda h: h.tensor_scalar(g8, g8, rs, None, ALU.mult))
        vs(lambda h: h.reduce_max(mx, g8, AX.X))
        vs(lambda h: h.tensor_scalar(oh1, g8, mx, None, ALU.is_ge))
        vs(lambda h: h.tensor_tensor(msk, g8, oh1, ALU.mult))
        vs(lambda h: h.tensor_tensor(msk, g8, msk, ALU.subtract))
        vs(lambda h: h.reduce_max(mx2, msk, AX.X))
        if renorm:
            vs(lambda h: h.tensor_tensor(den, mx, mx2, ALU.add))
            vs(lambda h: h.tensor_scalar(den, den, 1e-8, None, ALU.add))
            vs(lambda h: h.reciprocal(rs, den))
            vs(lambda h: h.tensor_tensor(w1c, mx, rs, ALU.mult))
            vs(lambda h: h.tensor_tensor(w2c, mx2, rs, ALU.mult))
        else:
            vs(lambda h: h.tensor_copy(w1c, mx))
            vs(lambda h: h.tensor_copy(w2c, mx2))
        vs(lambda h: h.tensor_scalar(msk, msk, mx2, None, ALU.is_ge))
        vs(lambda h: h.tensor_scalar(oh1, oh1, w1c, None, ALU.mult))
        vs(lambda h: h.tensor_scalar(msk, msk, w2c, None, ALU.mult))
        tvc = vs(lambda h, o=CMB[:, tt * 8:(tt + 1) * 8]:
                 h.tensor_tensor(o, oh1, msk, ALU.add))
        p.wait("t", p.tok("v"))
        p.wait("t", tr_tok)
        ttr = p.op("t", lambda h, i=CMB[:, tt * 8:(tt + 1) * 8], o=ctx.pbbf[0:8, 0:128]:
                   h.transpose(o, i, S["IDENT"][:, :]), inc=1)
        p.wait("sc", ttr)
        tr_tok = p.op("sc", lambda h, o=WTR[0:8, tt * 128:(tt + 1) * 128], i=ctx.pbbf[0:8, 0:128]:
                      h.activation(o, i, AF.Copy), inc=1)
    p.barrier()

    for mega in range(2):
        tm0 = mega * 2048
        for kc in range(NCHUNK):
            for half in range(4):
                xap, tok = cfg.src(kc, tm0 + half * 512)
                p.wait("v", tok)
                p.op("v", lambda h, o=XMB[:, kc * 2048 + half * 512: kc * 2048 + half * 512 + 512], i=xap:
                     h.tensor_copy(o, i), inc=(1 if half == 3 else 0))
        xm_tok = p.tok("v")
        wbc_tok = {}
        E_LIST = [int(x) for x in _os.environ.get("MOE_EXPERTS", "0,1,2,3,4,5,6,7").split(",")]
        for ei, e in enumerate(E_LIST):
            p.barrier()
            tw1 = [p.dma("g", W1S[:, kc * 2048:(kc + 1) * 2048], cfg.w1[e, kc * 128:(kc + 1) * 128, :]) for kc in range(NCHUNK)]
            tw2 = [p.dma("g", W2S[:, hc * 1024:(hc + 1) * 1024], cfg.w2[e, hc * 128:(hc + 1) * 128, :]) for hc in range(16)]
            if cfg.b1 is not None:
                tb1 = p.dma("g", S["B1E"][:, :], cfg.b1[e, :, :])
            p.barrier()
            p.wait("t", xm_tok)
            for tc in range(4):
                sl = tc % 2
                tbc = p.op("t", lambda h, l=S["SEL8"][0:8, e * 128:(e + 1) * 128],
                           r=WTR[0:8, tm0 + tc * 512: tm0 + tc * 512 + 512]:
                           h.matmul(ctx.pb[5][:, :], l, r, start=True, stop=True), inc=1)
                p.wait("sc", tbc)
                p.wait("sc", wbc_tok.get(sl))
                twb = p.op("sc", lambda h, o=WBC[:, sl * 512: sl * 512 + 512]:
                           h.activation(o, ctx.pb[5][:, :], AF.Copy), inc=1)
                p.wait("t", twb)
                for fc in range(16):
                    b = fc % 2
                    pb = ctx.pb[b]
                    for kc in range(NCHUNK):
                        tm = p.op("t", lambda h, l=W1S[:, kc * 2048 + fc * 128: kc * 2048 + fc * 128 + 128],
                                  r=XMB[:, kc * 2048 + tc * 512: kc * 2048 + tc * 512 + 512],
                                  o=pb, s=(kc == 0), e2=(kc == NCHUNK - 1):
                                  h.matmul(o[:, :], l, r, start=s, stop=e2), inc=(1 if kc == NCHUNK - 1 else 0))
                    hs = HS[:, fc * 512:(fc + 1) * 512]
                    p.wait("v", tm)
                    p.wait("v", twb)
                    if act == "relu2":
                        p.op("v", lambda h, o=hs, i=pb: h.tensor_scalar_max(o, i[:, :], 0.0))
                        p.op("v", lambda h, o=S["HTMP"][:, 0:512], a=hs, w=WBC[:, sl * 512: sl * 512 + 512]:
                             h.tensor_tensor(o, a, w, ALU.mult))
                        tve = p.op("v", lambda h, o=hs, a=S["HTMP"][:, 0:512], b2=hs:
                                   h.tensor_tensor(o, a, b2, ALU.mult), inc=1)
                    else:
                        p.op("v", lambda h, o=hs, i=pb, b1a=S["B1E"][:, fc:fc + 1]:
                             h.tensor_scalar(o, i[:, :], b1a, 0.0, ALU.add, ALU.max))
                        tve = p.op("v", lambda h, o=hs, w=WBC[:, sl * 512: sl * 512 + 512]:
                                   h.tensor_tensor(o, o, w, ALU.mult), inc=1)
                    p.wait("t", tve)
                    wbc_tok[sl] = tve
                for dc in range(NCHUNK):
                    b = 2 + dc % 2
                    pb = ctx.pb[b]
                    first_extra = (ei == 0 and cfg.b2 is not None)
                    for hc in range(16):
                        tm = p.op("t", lambda h, l=W2S[:, hc * 1024 + dc * 128: hc * 1024 + dc * 128 + 128],
                                  r=HS[:, hc * 512:(hc + 1) * 512],
                                  o=pb, s=(hc == 0), e2=(hc == 15 and not first_extra):
                                  h.matmul(o[:, :], l, r, start=s, stop=e2),
                                  inc=(1 if (hc == 15 and not first_extra) else 0))
                    if first_extra:
                        tm = p.op("t", lambda h, l=S["B2S"][0:8, dc * 128:(dc + 1) * 128],
                                  r=WTR[0:8, tm0 + tc * 512: tm0 + tc * 512 + 512], o=pb:
                                  h.matmul(o[:, :], l, r, start=False, stop=True), inc=1)
                    ya = YA[:, dc * 2048 + tc * 512: dc * 2048 + tc * 512 + 512]
                    p.wait("v", tm)
                    if ei == 0:
                        tve = p.op("v", lambda h, o=ya, i=pb: h.tensor_copy(o, i[:, :]), inc=1)
                    else:
                        tcp = p.op("v", lambda h, o=S["YSTG"][:, :], i=pb: h.tensor_copy(o, i[:, :]), inc=1)
                        p.wait("v", tcp, force=True)
                        tve = p.op("v", lambda h, o=ya: h.tensor_tensor(o, o, S["YSTG"][:, :], ALU.add), inc=1)
                    p.wait("t", tve)
        finish(mega, tm0)
        p.barrier()


MOE_SBUF = [
    ("XMB", (128, 8 * 2048), BF16), ("W1S", (128, 8 * 2048), BF16), ("W2S", (128, 16 * 1024), BF16),
    ("HS", (128, 16 * 512), BF16), ("YA", (128, 8 * 2048), F32),
    ("CMB", (128, 8 * 32), BF16), ("WTR", (8, T), BF16), ("WBC", (128, 1024), BF16),
    ("GSC", (128, 32), F32), ("WGS", (128, 64), BF16), ("GB", (1, 8), BF16),
    ("B1E", (128, 16), F32), ("B2S", (8, 1024), BF16), ("HTMP", (128, 512), BF16),
    ("SEL8", (8, 8 * 128), BF16), ("IDENT", (128, 128), BF16),
    ("SRCB", (128, 4096), BF16), ("FSTG", (128, 1024), BF16), ("YSTG", (128, 512), F32),
]


def build_g4(shared, percore0, dbg=False):
    import contextlib
    nc = bass.Bass("TRN2", target_bir_lowering=False)
    stack = contextlib.ExitStack()
    nc._dev_stack = stack
    aps = _declare(nc, shared, percore0, ["m0_wg", "m0_w1", "m0_w2", "sel8", "ident", "x0"],
                   extra={"xnT_in": ([D, T], BF16), "x2T_in": ([D, T], BF16)})
    outs4 = {"xnewT": ([D, T], F32)}
    if dbg:
        outs4.update({"d_cmb": ([128, 256], BF16), "d_wtr": ([8, T], BF16),
                      "d_ya": ([128, 16384], F32), "d_hs": ([128, 16 * 512], BF16),
                      "d_xmb": ([128, 16384], BF16), "d_w1s": ([128, 16384], BF16)})
    aps.update(_outs(nc, outs4))
    with nc.Block() as block:
        ctx = Ctx(nc, stack, aps, consts_vals=(), sbuf_spec=MOE_SBUF)
        ctx.setup()
        p = ctx.p
        S = ctx.sb
        p.dma("g", S["SEL8"][:, :], aps["sel8"][:, :])
        p.dma("g", S["IDENT"][:, :], aps["ident"][:, :])
        p.barrier()
        src = SrcStream(ctx, aps["xnT_in"], S["SRCB"])
        cfg = AttnCfg(src=src, wg=aps["m0_wg"], gb=None, w1=aps["m0_w1"], w2=aps["m0_w2"],
                      b1=None, b2=None, act="relu2", renorm=True)

        def finish(mega, tm0):
            p.barrier()
            if dbg and mega == 0:
                p.dma("sy", aps["d_cmb"][:, :], S["CMB"][:, :])
                p.dma("sy", aps["d_wtr"][:, :], S["WTR"][:, :])
                p.dma("sy", aps["d_ya"][:, :], S["YA"][:, :])
                p.dma("sy", aps["d_hs"][:, :], S["HS"][:, :])
                p.dma("sy", aps["d_xmb"][:, :], S["XMB"][:, :])
                p.dma("sy", aps["d_w1s"][:, :], S["W1S"][:, :])
                p.barrier()
            for dc in range(NCHUNK):
                for tc in range(4):
                    ya = S["YA"][:, dc * 2048 + tc * 512: dc * 2048 + tc * 512 + 512]
                    gs = S["FSTG"][:, (tc % 2) * 512:(tc % 2) * 512 + 512]
                    t0 = tm0 + tc * 512
                    tls = p.dma("sy", S["HTMP"][:, 0:512], aps["x2T_in"][dc * 128:(dc + 1) * 128, t0:t0 + 512])
                    tlx = p.dma("sy", S["XMB"].bitcast(F32)[:, 0:512], aps["x0"][dc * 128:(dc + 1) * 128, t0:t0 + 512])
                    tsl = p.op("sc", lambda h, o=gs, i=ya: h.activation(o, i, AF.Silu), inc=1)
                    p.wait("v", tsl)
                    p.wait("v", tls)
                    p.wait("v", tlx)
                    p.op("v", lambda h, o=gs, s2=S["HTMP"][:, 0:512]: h.tensor_tensor(o, o, s2, ALU.mult))
                    tvf = p.op("v", lambda h, o=S["XMB"].bitcast(F32)[:, 512:1024], a=S["XMB"].bitcast(F32)[:, 0:512], g=gs:
                               h.tensor_tensor(o, a, g, ALU.add), inc=1)
                    p.wait("sy", tvf)
                    p.dma("sy", aps["xnewT"][dc * 128:(dc + 1) * 128, t0:t0 + 512], S["XMB"].bitcast(F32)[:, 512:1024])
                    p.barrier()

        emit_moe(ctx, cfg, finish)
        _finish(ctx)
        ctx.p.build(block)
    return nc


def build_g5(shared, percore0, key_in="xnewT_in", key_out="xn1T"):
    import contextlib
    nc = bass.Bass("TRN2", target_bir_lowering=False)
    stack = contextlib.ExitStack()
    nc._dev_stack = stack
    aps = _declare(nc, shared, percore0, [], extra={key_in: ([D, T], F32)})
    aps.update(_outs(nc, {key_out: ([D, T], BF16)}))
    sbuf_spec = [
        ("ln_xstg", (128, 1024), F32), ("ln_sqt", (128, 1024), BF16),
        ("ln_rows", (1, 4096), F32), ("ln_abc", (128, 512), BF16), ("ln_bbc", (128, 512), BF16),
        ("XN", (128, NCHUNK * T), BF16),
    ]
    with nc.Block() as block:
        ctx = Ctx(nc, stack, aps, consts_vals=(), sbuf_spec=sbuf_spec)
        ctx.setup()
        emit_ln(ctx, aps[key_in], ctx.sb["XN"], out_dram=aps[key_out])
        _finish(ctx)
        ctx.p.build(block)
    return nc


def build_g6(shared, percore0):
    import contextlib
    nc = bass.Bass("TRN2", target_bir_lowering=False)
    stack = contextlib.ExitStack()
    nc._dev_stack = stack
    aps = _declare(nc, shared, percore0, ["a1_wq", "a1_wk", "a1_wv", "a1_wo", "ident", "selp8", "maskh"],
                   extra={"xn1T_in": ([D, T], BF16)})
    aps.update(_outs(nc, {"s1T": ([D, T], BF16)}))
    spec = _attn_sbuf_spec(8, 4, 8, 2, 8, 512, 512, rope=False, nmask=0, latent=False, scaled=False, bcb=False)
    with nc.Block() as block:
        ctx = Ctx(nc, stack, aps, consts_vals=(-16.0,), sbuf_spec=spec)
        ctx.setup()
        p = ctx.p
        S = ctx.sb
        p.dma("g", S["SELP"][0:8, :], aps["selp8"][:, :])
        _load_attn_consts(ctx, aps)
        p.barrier()
        load_rows(ctx, aps["a1_wk"], S["WK"], 8, 512)
        load_rows(ctx, aps["a1_wv"], S["WV"], 8, 512)
        load_rows(ctx, aps["a1_wo"], S["WOS"], 4, 1024)
        p.barrier()
        src = SrcStream(ctx, aps["xn1T_in"], S["SRCB"])
        cfg = AttnCfg(nheads=8, kv_dup=False, nkv_chunks=4, nkv_heads=8, rope=None, qk="l2",
                      wq=aps["a1_wq"], wq_bf=False, wk_cols=512, wv_cols=512,
                      exp_premul=16.0, exp_bias=-16.0, sums_clip=1e-6,
                      mask_sched=_sched_full, src=src, res=None)
        emit_attn(ctx, cfg, aps["s1T"])
        _finish(ctx)
        ctx.p.build(block)
    return nc


def build_g7(shared, percore0, dbg=False):
    import contextlib
    nc = bass.Bass("TRN2", target_bir_lowering=False)
    stack = contextlib.ExitStack()
    nc._dev_stack = stack
    aps = _declare(nc, shared, percore0,
                   ["f1_wq", "f1_wk", "f1_wv", "f1_wo", "f1_bq", "f1_bk", "f1_bv", "f1_bo",
                    "f1_g1", "f1_b1n", "ident", "selp16", "maskh"],
                   extra={"xn1T_in": ([D, T], BF16)})
    outs7 = {"xffnT": ([D, T], BF16)}
    if dbg:
        outs7["zT_d"] = ([D, T], BF16)
        outs7["d_scr"] = ([128, 14336], BF16)
        outs7["d_sums"] = ([1, 1024], F32)
        outs7["d_q0"] = ([128, 4096], BF16)
        outs7["d_wq"] = ([128, 2048], BF16)
    aps.update(_outs(nc, outs7))
    zT = aps["zT_d"] if dbg else nc.dram_tensor("zT", [D, T], BF16, kind="Internal").ap()
    spec = _attn_sbuf_spec(16, 8, 16, 2, 8, 1024, 1024, rope=False, nmask=0, latent=False,
                           wo_stream=True, qk=False, scaled=False, wv_shared=True, bcb=False)
    spec += [("BQ", (128, 8), F32), ("BK", (128, 8), F32), ("BV", (128, 8), F32), ("BO", (128, 8), F32),
             ("G1", (128, 8), F32), ("B1N", (128, 8), F32)]
    with nc.Block() as block:
        ctx = Ctx(nc, stack, aps, consts_vals=(-8.0,), sbuf_spec=spec)
        ctx.setup()
        p = ctx.p
        S = ctx.sb
        scr = S["SCR"]
        S["ln_xstg"] = scr.bitcast(F32)[:, 0:1024]
        S["ln_sqt"] = scr[:, 2048:3072]
        S["ln_abc"] = scr[:, 3072:3584]
        S["ln_bbc"] = scr[:, 3584:4096]
        S["ln_rows"] = scr.bitcast(F32)[0:1, 3072:7168]
        for nm, key in [("BQ", "f1_bq"), ("BK", "f1_bk"), ("BV", "f1_bv"), ("BO", "f1_bo"),
                        ("G1", "f1_g1"), ("B1N", "f1_b1n")]:
            p.dma("g", S[nm][:, :], aps[key][:, :])
        _load_attn_consts(ctx, aps)
        p.barrier()
        load_rows(ctx, aps["f1_wk"], S["WK"], 8, 1024)
        p.barrier()
        src = SrcStream(ctx, aps["xn1T_in"], S["SRCB"])
        cfg = AttnCfg(nheads=16, kv_dup=False, nkv_chunks=8, nkv_heads=16, rope=None, qk=None,
                      wq=aps["f1_wq"], wq_bf=False, wk_cols=1024, wv_cols=1024,
                      bq=S["BQ"], bk=S["BK"], bv=S["BV"], bo=S["BO"],
                      exp_scale_const=0.125, exp_bias=-8.0, wo_stream=aps["f1_wo"],
                      load_wv=lambda: load_rows(ctx, aps["f1_wv"], S["WK"], 8, 1024),
                      mask_sched=_sched_full, src=src, res=lambda fc, q0: src(fc, q0))
        if dbg:
            cfg.dbg = {"qc": 0, "aps": {"scr": (S["SCR"][:, :], aps["d_scr"][:, :]),
                                        "sums": (S["sumsrow"][:, :], aps["d_sums"][:, :])},
                       "q_ap": aps["d_q0"][:, :], "wq_ap": aps["d_wq"][:, :]}
        emit_attn(ctx, cfg, zT)
        p.barrier()
        emit_ln(ctx, zT, S["KTS"], src_bf16=True, gamma=S["G1"], beta=S["B1N"], out_dram=aps["xffnT"])
        _finish(ctx)
        ctx.p.build(block)
    return nc


def build_g8(shared, percore0):
    import contextlib
    nc = bass.Bass("TRN2", target_bir_lowering=False)
    stack = contextlib.ExitStack()
    nc._dev_stack = stack
    aps = _declare(nc, shared, percore0,
                   ["f1_gw", "f1_gb", "f1_w1", "f1_b1c", "f1_w2", "f1_b2", "f1_g2", "f1_b2n",
                    "sel8", "ident"],
                   extra={"xffnT_in": ([D, T], BF16), "s1T_in": ([D, T], BF16),
                          "xnewT_in": ([D, T], F32)})
    aps.update(_outs(nc, {"outT": ([D, T], F32)}))
    spec = MOE_SBUF + [("G2N", (128, 8), F32), ("B2N", (128, 8), F32)]
    with nc.Block() as block:
        ctx = Ctx(nc, stack, aps, consts_vals=(), sbuf_spec=spec)
        ctx.setup()
        p = ctx.p
        S = ctx.sb
        p.dma("g", S["SEL8"][:, :], aps["sel8"][:, :])
        p.dma("g", S["IDENT"][:, :], aps["ident"][:, :])
        p.dma("g", S["B2S"][:, :], aps["f1_b2"][:, :])
        p.dma("g", S["G2N"][:, :], aps["f1_g2"][:, :])
        p.dma("g", S["B2N"][:, :], aps["f1_b2n"][:, :])
        p.dma("g", S["GB"][:, :], aps["f1_gb"][:, :])
        p.barrier()
        src = SrcStream(ctx, aps["xffnT_in"], S["SRCB"])
        cfg = AttnCfg(src=src, wg=aps["f1_gw"], gb=aps["f1_gb"], w1=aps["f1_w1"], w2=aps["f1_w2"],
                      b1=aps["f1_b1c"], b2=aps["f1_b2"], act="relu", renorm=False)
        W2F = S["W2S"].bitcast(F32)
        rows = lambda i: W2F[0:1, 1024 + i * 512: 1024 + (i + 1) * 512]
        abc = S["W2S"][:, 0:512]
        bbc = S["W2S"][:, 512:1024]
        sqt = S["W2S"][:, 1024 + 8192:1024 + 8192 + 512]
        W1F = S["W1S"].bitcast(F32)

        def finish(mega, tm0):
            p.barrier()
            for tc in range(4):
                t0 = tm0 + tc * 512
                tz = None
                for dc in range(NCHUNK):
                    xsl = S["XMB"][:, dc * 2048 + tc * 512: dc * 2048 + tc * 512 + 512]
                    ya = S["YA"][:, dc * 2048 + tc * 512: dc * 2048 + tc * 512 + 512]
                    tz = p.op("v", lambda h, o=xsl, a=ya, b=xsl: h.tensor_tensor(o, a, b, ALU.add), inc=1)
                t_done = []
                for dc in range(NCHUNK):
                    zsl = S["XMB"][:, dc * 2048 + tc * 512: dc * 2048 + tc * 512 + 512]
                    if dc >= 1:
                        p.wait("v", t_done[dc - 1])
                    tv = p.op("v", lambda h, o=sqt, i=zsl: h.tensor_tensor(o, i, i, ALU.mult), inc=1)
                    p.wait("t", tv)
                    p.op("t", lambda h, b=zsl, s=(dc == 0): h.matmul(ctx.pb[0][0:1, :], ctx.ones_bf[:, :], b, start=s, stop=(dc == 7)))
                    tm = p.op("t", lambda h, b=sqt, s=(dc == 0): h.matmul(ctx.pb[1][0:1, :], ctx.ones_bf[:, :], b, start=s, stop=(dc == 7)), inc=1)
                    t_done.append(tm)
                mu, m2, var, sd, rstd, brow = (rows(i) for i in range(6))
                p.wait("v", t_done[-1])
                p.op("v", lambda h: h.tensor_scalar(mu, ctx.pb[0][0:1, :], 1.0 / D, None, ALU.mult))
                p.op("v", lambda h: h.tensor_scalar(m2, ctx.pb[1][0:1, :], 1.0 / D, None, ALU.mult))
                p.op("v", lambda h: h.tensor_tensor(sd, mu, mu, ALU.mult))
                p.op("v", lambda h: h.tensor_tensor(var, m2, sd, ALU.subtract))
                tvv = p.op("v", lambda h: h.tensor_scalar(var, var, 1e-5, None, ALU.add), inc=1)
                p.wait("sc", tvv)
                tss = p.op("sc", lambda h: h.activation(sd, var, AF.Sqrt), inc=1)
                p.wait("v", tss)
                trc = p.op("v", lambda h: h.reciprocal(rstd, sd), inc=1)
                p.op("v", lambda h: h.tensor_scalar(sd, mu, -1.0, None, ALU.mult))
                p.wait("v", trc, force=True)
                tv2 = p.op("v", lambda h: h.tensor_tensor(brow, sd, rstd, ALU.mult), inc=1)
                p.wait("t", tv2)
                tb1 = p.op("t", lambda h: h.matmul(ctx.pb[2][:, :], ctx.ones_row[:, :], rstd, start=True, stop=True), inc=1)
                tb2 = p.op("t", lambda h: h.matmul(ctx.pb[3][:, :], ctx.ones_row[:, :], brow, start=True, stop=True), inc=1)
                p.wait("sc", tb2)
                p.op("sc", lambda h: h.activation(abc, ctx.pb[2][:, :], AF.Copy))
                e2 = p.op("sc", lambda h: h.activation(bbc, ctx.pb[3][:, :], AF.Copy), inc=1)
                for dc in range(NCHUNK):
                    zsl = S["XMB"][:, dc * 2048 + tc * 512: dc * 2048 + tc * 512 + 512]
                    s1s = S["W1S"][:, 0:512]
                    xns = W1F[:, 512:1024]
                    ots = W1F[:, 1024:1536]
                    gsl = S["FSTG"][:, 0:512]
                    tl1 = p.dma("sy", s1s, aps["s1T_in"][dc * 128:(dc + 1) * 128, t0:t0 + 512])
                    tl2 = p.dma("sy", xns, aps["xnewT_in"][dc * 128:(dc + 1) * 128, t0:t0 + 512])
                    p.wait("v", e2)
                    p.op("v", lambda h, a=zsl: h.tensor_tensor(a, a, abc, ALU.mult))
                    p.op("v", lambda h, a=zsl: h.tensor_tensor(a, a, bbc, ALU.add))
                    tvz = p.op("v", lambda h, a=zsl, g=S["G2N"][:, dc:dc + 1], b=S["B2N"][:, dc:dc + 1]:
                               h.tensor_scalar(a, a, g, b, ALU.mult, ALU.add), inc=1)
                    p.wait("sc", tvz)
                    tsl = p.op("sc", lambda h, o=gsl, i=zsl: h.activation(o, i, AF.Silu), inc=1)
                    p.wait("v", tsl)
                    p.wait("v", tl1)
                    p.wait("v", tl2)
                    p.op("v", lambda h, o=gsl, s2=s1s: h.tensor_tensor(o, o, s2, ALU.mult))
                    tvf = p.op("v", lambda h, o=ots, a=xns, g=gsl: h.tensor_tensor(o, a, g, ALU.add), inc=1)
                    p.wait("sy", tvf)
                    td = p.dma("sy", aps["outT"][dc * 128:(dc + 1) * 128, t0:t0 + 512], ots)
                    p.wait("v", td)
                p.barrier()

        emit_moe(ctx, cfg, finish)
        _finish(ctx)
        ctx.p.build(block)
    return nc



_G_INPUTS = {
    "g1": [],
    "g2": ["g_wq", "g_kvd", "g_kup", "g_vup", "g_wo", "w1n", "ropeg", "cmask", "ident", "selp16", "maskh"],
    "g3": ["l_wq", "l_wk", "l_wv", "l_wo", "w2n", "ropel", "lmask", "ident", "selp16", "maskh"],
    "g4": ["m0_wg", "m0_w1", "m0_w2", "sel8", "ident"],
    "g5": [],
    "g6": ["a1_wq", "a1_wk", "a1_wv", "a1_wo", "ident", "selp8", "maskh"],
    "g7": ["f1_wq", "f1_wk", "f1_wv", "f1_wo", "f1_bq", "f1_bk", "f1_bv", "f1_bo",
           "f1_g1", "f1_b1n", "ident", "selp16", "maskh"],
    "g8": ["f1_gw", "f1_gb", "f1_w1", "f1_b1c", "f1_w2", "f1_b2", "f1_g2", "f1_b2n", "sel8", "ident"],
}


def _run(nc, shared, names, extra_percore, n=8):
    maps = []
    for b in range(n):
        m = {k: shared[k] for k in names}
        m.update(extra_percore[b])
        maps.append(m)
    res = run_bass_kernel_spmd(nc, maps, core_ids=list(range(n)))
    return res.results


def kernel(**inputs):
    shared, percore = prep_inputs(inputs)
    B = 8
    r1 = _run(build_g1(shared, percore[0]), shared, _G_INPUTS["g1"],
              [{"x0": percore[b]["x0"]} for b in range(B)])
    r2 = _run(build_g2(shared, percore[0]), shared, _G_INPUTS["g2"],
              [{"xnT_in": r1[b]["xnT"], "rs1_in": r1[b]["rs1"]} for b in range(B)])
    r3 = _run(build_g3(shared, percore[0]), shared, _G_INPUTS["g3"],
              [{"x1T_in": r2[b]["x1T"]} for b in range(B)])
    r4 = _run(build_g4(shared, percore[0]), shared, _G_INPUTS["g4"],
              [{"x0": percore[b]["x0"], "xnT_in": r1[b]["xnT"], "x2T_in": r3[b]["x2T"]} for b in range(B)])
    r5 = _run(build_g5(shared, percore[0]), shared, _G_INPUTS["g5"],
              [{"xnewT_in": r4[b]["xnewT"]} for b in range(B)])
    r6 = _run(build_g6(shared, percore[0]), shared, _G_INPUTS["g6"],
              [{"xn1T_in": r5[b]["xn1T"]} for b in range(B)])
    r7 = _run(build_g7(shared, percore[0]), shared, _G_INPUTS["g7"],
              [{"xn1T_in": r5[b]["xn1T"]} for b in range(B)])
    r8 = _run(build_g8(shared, percore[0]), shared, _G_INPUTS["g8"],
              [{"xffnT_in": r7[b]["xffnT"], "s1T_in": r6[b]["s1T"], "xnewT_in": r4[b]["xnewT"]}
               for b in range(B)])
    out = np.stack([np.ascontiguousarray(r8[b]["outT"].T) for b in range(B)])
    return out.astype(np.float32)


# version 03
# speedup vs baseline: 1.0115x; 1.0115x over previous
import math
import numpy as np
import ml_dtypes
import concourse.bass as bass
import concourse.mybir as mybir
from concourse.bass_utils import run_bass_kernel_spmd

F32 = mybir.dt.float32
BF16 = mybir.dt.bfloat16
AF = mybir.ActivationFunctionType
ALU = mybir.AluOpType
AX = mybir.AxisListType

D = 1024
T = 4096
NCHUNK = D // 128
NEG = -1.0e30
import os as _os
KNORM_PARANOID = bool(_os.environ.get("KNORM_PARANOID"))

ENGINES = ("t", "v", "sc", "g", "sy")
DMA_ENG = ("sy", "g", "sc", "v")


NWAVE = 8


class Prog:

    def __init__(self, nc):
        self.nc = nc
        self.items = {e: [] for e in ENGINES}
        self.c = {e: 0 for e in ENGINES}
        self.didx = {e: 0 for e in DMA_ENG}
        self.last_wait = {e: {} for e in ENGINES}

    def op(self, e, fn, inc=0, lp=False):
        if lp:
            base = fn
            def fn(h, _b=base, _nc=self.nc):
                with _nc.allow_low_precision(reason="intentional bf16 rounding"):
                    return _b(h)
        self.items[e].append(("op", fn, inc, False, None))
        if inc:
            self.c[e] += inc
        return (e, False, self.c[e])

    def dma(self, e, out, in_, **kw):
        def fn(h):
            return h.dma_start(out=out, in_=in_, **kw)
        idx = self.didx[e]
        k = idx % NWAVE
        wave = idx // NWAVE + 1
        if idx >= NWAVE:
            self.wait(e, ("dma", e, k, 16 * (wave - 1)))
        self.items[e].append(("op", fn, 16, True, k))
        self.didx[e] = idx + 1
        return ("dma", e, k, 16 * wave)

    def tok(self, e, dma=False):
        if dma:
            idx = self.didx[e]
            if idx == 0:
                return None
            k = (idx - 1) % NWAVE
            return ("dma", e, k, 16 * ((idx - 1) // NWAVE + 1))
        return (e, False, self.c[e])

    def _dma_all_toks(self, e):
        toks = []
        idx = self.didx[e]
        for k in range(NWAVE):
            n = (idx - 1 - k) // NWAVE + 1 if idx > k else 0
            if n > 0:
                toks.append(("dma", e, k, 16 * n))
        return toks

    def wait(self, e, tok, force=False):
        if tok is None:
            return
        if tok[0] == "dma":
            _, src, k, val = tok
            key = ("dma", src, k)
        else:
            src, dma, val = tok
            if src == e and not dma and not force:
                return
            key = (src, dma)
        if val <= 0 or val <= self.last_wait[e].get(key, 0):
            return
        self.last_wait[e][key] = val
        self.items[e].append(("wait", key, val))

    def waits(self, e, toks):
        for t in toks:
            self.wait(e, t)

    def barrier(self):
        snap_c = dict(self.c)
        dtoks = {s: self._dma_all_toks(s) for s in DMA_ENG}
        for e in ENGINES:
            for s in ENGINES:
                if s != e and snap_c[s]:
                    self.wait(e, (s, False, snap_c[s]))
            for s in DMA_ENG:
                for t in dtoks[s]:
                    self.wait(e, t)

    def build(self, block):
        nc = self.nc
        sems = {}
        for e in ENGINES:
            sems[(e, False)] = nc.alloc_semaphore(name=f"c_{e}")
        for e in DMA_ENG:
            for k in range(NWAVE):
                sems[("dma", e, k)] = nc.alloc_semaphore(name=f"d_{e}{k}")
        self.sems = sems

        def runner(e):
            def run(h):
                for it in self.items[e]:
                    if it[0] == "wait":
                        _, key, val = it
                        h.wait_ge(sems[key], val)
                    else:
                        _, fn, inc, dma, slot = it
                        ins = fn(h)
                        if inc:
                            ins.then_inc(sems[("dma", e, slot) if dma else (e, False)], inc)
            return run

        block.tensor(runner("t"))
        block.vector(runner("v"))
        block.scalar(runner("sc"))
        block.gpsimd(runner("g"))
        block.sync(runner("sy"))



def ceil_div(a, b):
    return (a + b - 1) // b


class Consts:

    def __init__(self, nc, values):
        self.values = sorted(set(float(v) for v in values))
        self.tile = None

    def alloc(self, stack, nc):
        n = max(1, len(self.values))
        self.tile = stack.enter_context(nc.sbuf_tensor("consts", [128, n], F32))
        for i, v in enumerate(self.values):
            nc.const_aps.register(F32, v, self.tile[:, i:i + 1])

    def emit_init(self, p):
        def mk(i, v):
            return lambda h: h.memset(self.tile[:, i:i + 1], v)
        for i, v in enumerate(self.values):
            p.op("v", mk(i, v), inc=(1 if i == len(self.values) - 1 else 0))
        return p.tok("v")



def _eo_perm(nheads, hd=64):
    idx = []
    for h in range(nheads):
        base = h * hd
        idx.extend(base + np.arange(0, hd, 2))
        idx.extend(base + np.arange(1, hd, 2))
    return np.asarray(idx)


def _swap_perm(nheads, hd=64):
    idx = []
    for h in range(nheads):
        base = h * hd
        idx.extend(base + np.arange(1, hd, 2))
        idx.extend(base + np.arange(0, hd, 2))
    return np.asarray(idx)


def _rope_tables(base, T_, hd=64):
    j = np.arange(hd // 2, dtype=np.float64)
    inv = 1.0 / (base ** (2 * j / hd))
    t = np.arange(T_, dtype=np.float64)
    fr = t[None, :] * inv[:, None]
    cos = np.cos(fr).astype(np.float32)
    sin = np.sin(fr).astype(np.float32)
    c1 = np.tile(cos, (4, 1))
    c2 = np.tile(np.concatenate([-sin, sin], 0), (2, 1))
    return c1.astype(ml_dtypes.bfloat16), c2.astype(ml_dtypes.bfloat16)


def _chunk_major(v):
    return np.ascontiguousarray(v.reshape(-1, 128).T)


def _causal_masks():
    q = np.arange(512)[None, :]
    k = np.arange(128)[:, None]
    out = []
    for off in range(4):
        m = np.where((q - k) >= 128 * off, 0.0, NEG).astype(np.float32)
        out.append(m)
    return np.stack(out)


def _local_masks(win=256):
    q = np.arange(512)[None, :]
    k = np.arange(128)[:, None]
    out = []
    for off in range(-2, 4):
        d = (q - k) - 128 * off
        m = np.where((d >= 0) & (d < win), 0.0, NEG).astype(np.float32)
        out.append(m)
    return np.stack(out)


def _selpair(nheads):
    nch = nheads // 2
    s = np.zeros((nheads, nch * 128), np.float32)
    for c in range(nch):
        s[2 * c, c * 128: c * 128 + 64] = 1.0
        s[2 * c + 1, c * 128 + 64: (c + 1) * 128] = 1.0
    return s.astype(ml_dtypes.bfloat16)


def _maskq(nheads):
    nch = nheads // 2
    m = np.zeros((nch, 128, nheads), np.float32)
    for c in range(nch):
        m[c, 0:64, 2 * c] = 1.0
        m[c, 64:128, 2 * c + 1] = 1.0
    return np.ascontiguousarray(m.transpose(1, 0, 2).reshape(128, nch * nheads)).astype(ml_dtypes.bfloat16)


def prep_inputs(I):
    B = 8
    x = np.asarray(I["L0_attn_hidden"])
    percore = []
    for b in range(B):
        percore.append({"x0": np.ascontiguousarray(x[b].T).astype(np.float32)})

    S = {}
    f32 = lambda a: np.ascontiguousarray(np.asarray(a, dtype=np.float32))
    pq = _eo_perm(16)
    sq = _swap_perm(16)
    S["g_wq"] = f32(np.concatenate([I["L0_attn_g_wq"][:, pq], I["L0_attn_g_wq"][:, sq]], axis=1))
    S["g_kvd"] = f32(I["L0_attn_g_kv_down"])
    ku = np.asarray(I["L0_attn_g_k_up"])
    pk = _eo_perm(4)
    sk = _swap_perm(4)
    kup_d = np.concatenate([np.concatenate([ku[:, pk[g * 64:(g + 1) * 64]]] * 2, axis=1) for g in range(4)], axis=1)
    kup_s = np.concatenate([np.concatenate([ku[:, sk[g * 64:(g + 1) * 64]]] * 2, axis=1) for g in range(4)], axis=1)
    S["g_kup"] = f32(np.concatenate([kup_d, kup_s], axis=1))
    S["g_vup"] = f32(I["L0_attn_g_v_up"])
    S["g_wo"] = f32(I["L0_attn_g_wo"])
    S["l_wq"] = f32(np.concatenate([I["L0_attn_l_wq"][:, pq], I["L0_attn_l_wq"][:, sq]], axis=1))
    lk = np.asarray(I["L0_attn_l_wk"])
    lk_d = np.concatenate([np.concatenate([lk[:, pk[g * 64:(g + 1) * 64]]] * 2, axis=1) for g in range(4)], axis=1)
    lk_s = np.concatenate([np.concatenate([lk[:, sk[g * 64:(g + 1) * 64]]] * 2, axis=1) for g in range(4)], axis=1)
    S["l_wk"] = f32(np.concatenate([lk_d, lk_s], axis=1))
    S["l_wv"] = f32(I["L0_attn_l_wv"])
    S["l_wo"] = f32(I["L0_attn_l_wo"])
    S["w1n"] = f32(_chunk_major(I["L0_attn_norm1_w"]))
    S["w2n"] = f32(_chunk_major(I["L0_attn_norm2_w"]))
    S["m0_wg"] = f32(I["L0_ffn_Wg"])
    S["m0_w1"] = f32(I["L0_ffn_W1"])
    S["m0_w2"] = f32(I["L0_ffn_W2"])
    S["a1_wq"] = f32(I["L1_attn_Wq"])
    S["a1_wk"] = f32(np.asarray(I["L1_attn_Wkv"])[:, :512])
    S["a1_wv"] = f32(np.asarray(I["L1_attn_Wkv"])[:, 512:])
    S["a1_wo"] = f32(I["L1_attn_Wo"])
    for nm, key in [("f1_wq", "L1_ffn_Wq"), ("f1_wk", "L1_ffn_Wk"), ("f1_wv", "L1_ffn_Wv"), ("f1_wo", "L1_ffn_Wo")]:
        S[nm] = f32(I[key])
    for nm, key in [("f1_bq", "L1_ffn_bq"), ("f1_bk", "L1_ffn_bk"), ("f1_bv", "L1_ffn_bv"), ("f1_bo", "L1_ffn_bo"),
                    ("f1_g1", "L1_ffn_ln1_g"), ("f1_b1n", "L1_ffn_ln1_b"), ("f1_g2", "L1_ffn_ln2_g"), ("f1_b2n", "L1_ffn_ln2_b")]:
        S[nm] = f32(_chunk_major(I[key]))
    S["f1_gw"] = f32(I["L1_ffn_gW"])
    S["f1_gb"] = f32(np.asarray(I["L1_ffn_gb"]).reshape(1, 8))
    S["f1_w1"] = f32(I["L1_ffn_W1"])
    S["f1_b1c"] = f32(np.asarray(I["L1_ffn_b1"]).reshape(8, 16, 128).transpose(0, 2, 1))
    S["f1_w2"] = f32(I["L1_ffn_W2"])
    S["f1_b2"] = f32(I["L1_ffn_b2"])
    gc1, gc2 = _rope_tables(1e6, T)
    lc1, lc2 = _rope_tables(1e4, T)
    S["ropeg"] = np.ascontiguousarray(np.concatenate([gc1, gc2], axis=1))
    S["ropel"] = np.ascontiguousarray(np.concatenate([lc1, lc2], axis=1))
    S["cmask"] = _causal_masks().reshape(4 * 128, 512).astype(ml_dtypes.bfloat16)
    S["lmask"] = _local_masks().reshape(6 * 128, 512).astype(ml_dtypes.bfloat16)
    S["ident"] = np.eye(128, dtype=np.float32).astype(ml_dtypes.bfloat16)
    S["selp16"] = _selpair(16)
    S["selp8"] = _selpair(8)
    S["maskq16"] = _maskq(16)
    S["maskq8"] = _maskq(8)
    mh = np.zeros((128, 3), np.float32)
    mh[0:64, 0] = 1.0
    mh[0:64, 1] = 1.0
    mh[64:128, 2] = 1.0
    S["maskh"] = mh.astype(ml_dtypes.bfloat16)
    sel8 = np.zeros((8, 8 * 128), np.float32)
    for e in range(8):
        sel8[e, e * 128:(e + 1) * 128] = 1.0
    S["sel8"] = sel8.astype(ml_dtypes.bfloat16)
    return S, percore


_DT = {np.dtype(np.float32): F32, np.dtype(ml_dtypes.bfloat16): BF16}


def declare_inputs(nc, shared, percore0):
    aps = {}
    for nm, arr in list(shared.items()) + list(percore0.items()):
        aps[nm] = nc.dram_tensor(nm, list(arr.shape), _DT[arr.dtype], kind="ExternalInput").ap()
    return aps


class Ctx:
    def __init__(self, nc, stack, aps, consts_vals=(), sbuf_spec=()):
        self.nc = nc
        self.aps = aps
        self.p = Prog(nc)
        self.pb = [stack.enter_context(nc.psum_tensor(f"pb{i}", [128, 512], F32)) for i in range(7)]
        self.pbbf = stack.enter_context(nc.psum_tensor("pbbf", [128, 1024], BF16))
        self.ones_bf = stack.enter_context(nc.sbuf_tensor("ones_bf", [128, 1], BF16))
        self.ones_row = stack.enter_context(nc.sbuf_tensor("ones_row", [1, 128], F32))
        self.ones_row_bf = stack.enter_context(nc.sbuf_tensor("ones_row_bf", [1, 128], BF16))
        self.consts = Consts(nc, consts_vals)
        self.consts.alloc(stack, nc)
        self.sb = {}
        for (nm, shape, dt) in sbuf_spec:
            self.sb[nm] = stack.enter_context(nc.sbuf_tensor(nm, list(shape), dt))
        self.wstage_tok = None

    def setup(self):
        p = self.p
        p.op("v", lambda h: h.memset(self.ones_bf[:, :], 1.0))
        p.op("v", lambda h: h.memset(self.ones_row[:, :], 1.0))
        p.op("v", lambda h: h.memset(self.ones_row_bf[:, :], 1.0), inc=1)
        self.consts.emit_init(p)
        self.t_setup = p.tok("v")
        for e in ("t", "sc", "v", "g"):
            p.wait(e, self.t_setup)


def load_bf16(p, dram_ap, sb_ap):
    return p.dma("g", sb_ap, dram_ap)



def emit_ln(ctx, x_dram, xn_sb, *, rs1_row=None, out_dram=None, sq_eps=1e-6,
            src_bf16=False, gamma=None, beta=None):
    p = ctx.p
    nc = ctx.nc
    S = ctx.sb
    xstg, sqt = S["ln_xstg"], S["ln_sqt"]
    rows = S["ln_rows"]
    abc, bbc = S["ln_abc"], S["ln_bbc"]
    ps_s, ps_q, ps_b1, ps_b2 = ctx.pb[0], ctx.pb[1], ctx.pb[2], ctx.pb[3]

    for blk in range(8):
        t0 = blk * 512
        v_done = []
        t_done = []
        for ch in range(8):
            sl = xstg[:, (ch % 2) * 512:(ch % 2) * 512 + 512]
            sq = sqt[:, (ch % 2) * 512:(ch % 2) * 512 + 512]
            if ch >= 2:
                p.wait("sy", v_done[ch - 2])
            if src_bf16:
                sl = sl.bitcast(BF16)[:, 0:512]
            tl = p.dma("sy", sl, x_dram[ch * 128:(ch + 1) * 128, t0:t0 + 512])
            xnsl = xn_sb[:, ch * T + t0: ch * T + t0 + 512]
            p.wait("v", tl)
            if ch >= 2:
                p.wait("v", t_done[ch - 2])
            p.op("v", lambda h, a=xnsl, b=sl: h.tensor_copy(a, b))
            tv = p.op("v", lambda h, a=sq, b=xnsl: h.tensor_tensor(a, b, b, ALU.mult), inc=1)
            v_done.append(tv)
            p.wait("t", tv)
            p.op("t", lambda h, b=xnsl, s=(ch == 0): h.matmul(ps_s[0:1, :], ctx.ones_bf[:, :], b, start=s, stop=(ch == 7)))
            tok_mm = p.op("t", lambda h, b=sq, s=(ch == 0): h.matmul(ps_q[0:1, :], ctx.ones_bf[:, :], b, start=s, stop=(ch == 7)), inc=1)
            t_done.append(tok_mm)
        mu, var, sd, rstd = (rows[0:1, i * 512:(i + 1) * 512] for i in range(4))
        m2 = var
        tmp = sd
        brow = mu
        p.wait("v", tok_mm)
        p.op("v", lambda h: h.tensor_scalar(mu, ps_s[0:1, :], 1.0 / D, None, ALU.mult))
        p.op("v", lambda h: h.tensor_tensor(tmp, mu, mu, ALU.mult))
        p.op("v", lambda h: h.tensor_scalar(m2, ps_q[0:1, :], 1.0 / D, None, ALU.mult))
        p.op("v", lambda h: h.tensor_tensor(var, m2, tmp, ALU.subtract))
        tv = p.op("v", lambda h: h.tensor_scalar(var, var, 1e-5, None, ALU.add), inc=1)
        p.wait("sc", tv)
        ts = p.op("sc", lambda h: h.activation(sd, var, AF.Sqrt), inc=1)
        p.wait("v", ts)
        trc0 = p.op("v", lambda h: h.reciprocal(rstd, sd), inc=1)
        p.op("v", lambda h: h.tensor_scalar(tmp, mu, -1.0, None, ALU.mult))
        p.wait("v", trc0, force=True)
        tv2 = p.op("v", lambda h: h.tensor_tensor(brow, tmp, rstd, ALU.mult), inc=1)
        if rs1_row is not None:
            r1 = r2 = rows[0:1, 2 * 512:3 * 512]
            p.wait("v", tv, force=True)
            trr = p.op("v", lambda h: h.reciprocal(r1, var), inc=1)
            p.wait("v", trr, force=True)
            p.op("v", lambda h: h.tensor_scalar(r1, r1, -1e-5, None, ALU.mult))
            p.op("v", lambda h: h.tensor_scalar(r1, r1, 1.0, None, ALU.add))
            tv3 = p.op("v", lambda h: h.tensor_scalar(r1, r1, 1.0, sq_eps, ALU.mult, ALU.add), inc=1)
            p.wait("sc", tv3)
            ts2 = p.op("sc", lambda h: h.activation(r2, r1, AF.Sqrt), inc=1)
            p.wait("v", ts2)
            p.op("v", lambda h, o=rs1_row[0:1, t0:t0 + 512]: h.reciprocal(o, r2), inc=1, lp=True)
        p.wait("t", tv2)
        tb1 = p.op("t", lambda h: h.matmul(ps_b1[:, :], ctx.ones_row[:, :], rstd, start=True, stop=True), inc=1)
        tb2 = p.op("t", lambda h: h.matmul(ps_b2[:, :], ctx.ones_row[:, :], brow, start=True, stop=True), inc=1)
        p.wait("sc", tb1)
        p.wait("sc", tb2)
        e1 = p.op("sc", lambda h: h.activation(abc[:, :], ps_b1[:, :], AF.Copy))
        e2 = p.op("sc", lambda h: h.activation(bbc[:, :], ps_b2[:, :], AF.Copy), inc=1)
        p.wait("v", e2)
        for ch in range(8):
            xnsl = xn_sb[:, ch * T + t0: ch * T + t0 + 512]
            p.op("v", lambda h, a=xnsl: h.tensor_tensor(a, a, abc[:, :], ALU.mult))
            tv = p.op("v", lambda h, a=xnsl: h.tensor_tensor(a, a, bbc[:, :], ALU.add), inc=1)
            if gamma is not None:
                tv = p.op("v", lambda h, a=xnsl, g=gamma[:, ch:ch + 1], b=beta[:, ch:ch + 1]:
                          h.tensor_scalar(a, a, g, b, ALU.mult, ALU.add), inc=1)
            if out_dram is not None:
                p.wait("sy", tv)
                p.dma("sy", out_dram[ch * 128:(ch + 1) * 128, t0:t0 + 512], xnsl)
    return p.tok("v")



def _declare(nc, shared, percore0, names, extra=None):
    aps = {}
    for nm in names:
        arr = shared[nm] if nm in shared else percore0[nm]
        aps[nm] = nc.dram_tensor(nm, list(arr.shape), _DT[np.asarray(arr).dtype], kind="ExternalInput").ap()
    for nm, (shape, dt) in (extra or {}).items():
        aps[nm] = nc.dram_tensor(nm, list(shape), dt, kind="ExternalInput").ap()
    return aps


def _outs(nc, spec):
    return {nm: nc.dram_tensor(nm, list(shape), dt, kind="ExternalOutput").ap()
            for nm, (shape, dt) in spec.items()}


def _finish(ctx):
    ctx.p.barrier()


def build_g1(shared, percore0):
    import contextlib
    nc = bass.Bass("TRN2", target_bir_lowering=False)
    stack = contextlib.ExitStack()
    nc._dev_stack = stack
    aps = _declare(nc, shared, percore0, ["x0"])
    aps.update(_outs(nc, {"xnT": ([D, T], BF16), "rs1": ([1, T], F32)}))
    sbuf_spec = [
        ("ln_xstg", (128, 1024), F32), ("ln_sqt", (128, 1024), BF16),
        ("ln_rows", (1, 4096), F32), ("ln_abc", (128, 512), BF16), ("ln_bbc", (128, 512), BF16),
        ("XN", (128, NCHUNK * T), BF16),
        ("rs1row", (1, T), F32),
    ]
    with nc.Block() as block:
        ctx = Ctx(nc, stack, aps, consts_vals=(), sbuf_spec=sbuf_spec)
        ctx.setup()
        emit_ln(ctx, aps["x0"], ctx.sb["XN"], rs1_row=ctx.sb["rs1row"], out_dram=aps["xnT"])
        tv = ctx.p.tok("v")
        ctx.p.wait("sy", tv)
        ctx.p.dma("sy", aps["rs1"][:, :], ctx.sb["rs1row"][:, :])
        _finish(ctx)
        ctx.p.build(block)
    return nc



def emit_rms_row(ctx, src_sb, out_row, eps=1e-6):
    p = ctx.p
    sqt = ctx.sb["ln_sqt"]
    rows = ctx.sb["ln_rows"]
    ps = ctx.pb[0]
    for blk in range(8):
        t0 = blk * 512
        t_done = []
        for ch in range(8):
            sq = sqt[:, (ch % 2) * 512:(ch % 2) * 512 + 512]
            sl = src_sb[:, ch * T + t0: ch * T + t0 + 512]
            if ch >= 2:
                p.wait("v", t_done[ch - 2])
            tv = p.op("v", lambda h, a=sq, b=sl: h.tensor_tensor(a, b, b, ALU.mult), inc=1)
            p.wait("t", tv)
            tm = p.op("t", lambda h, b=sq, s=(ch == 0): h.matmul(ps[0:1, :], ctx.ones_bf[:, :], b, start=s, stop=(ch == 7)), inc=1)
            t_done.append(tm)
        r1 = rows[0:1, 0:512]
        r2 = rows[0:1, 512:1024]
        p.wait("v", t_done[-1])
        tv = p.op("v", lambda h: h.tensor_scalar(r1, ps[0:1, :], 1.0 / D, eps, ALU.mult, ALU.add), inc=1)
        p.wait("sc", tv)
        ts = p.op("sc", lambda h: h.activation(r2, r1, AF.Sqrt), inc=1)
        p.wait("v", ts)
        p.op("v", lambda h, o=out_row[0:1, t0:t0 + 512]: h.reciprocal(o, r2), inc=1)
    return p.tok("v")



def load_rows(ctx, dram, dst, nkc, cols, scale=None):
    p = ctx.p
    toks = []
    if scale is None:
        for k in range(nkc):
            toks.append(p.dma("g", dst[:, k * cols:(k + 1) * cols], dram[k * 128:(k + 1) * 128, 0:cols]))
        return toks[-1]
    stg = ctx.sb["wstage"]
    for k in range(nkc):
        s0 = 0
        if k >= 1:
            p.wait("sy", toks[k - 1])
        else:
            p.wait("sy", ctx.wstage_tok)
        tl = p.dma("sy", stg[:, s0:s0 + cols], dram[k * 128:(k + 1) * 128, 0:cols])
        p.wait("v", tl)
        tv = p.op("v", lambda h, o=dst[:, k * cols:(k + 1) * cols], i=stg[:, s0:s0 + cols], s=scale[:, k:k + 1]:
                  h.tensor_scalar(o, i, s, None, ALU.mult), inc=1)
        toks.append(tv)
    ctx.wstage_tok = toks[-1]
    return toks[-1]


def emit_w_prep(ctx, src_dram, dst_dram, nkc, cols, scale):
    p = ctx.p
    stg = ctx.sb["wstage"]
    stgb = ctx.sb["wstageb"]
    vt, dt = [], []
    for k in range(nkc):
        s0 = 0
        if k >= 1:
            p.wait("sy", vt[k - 1])
        else:
            p.wait("sy", ctx.wstage_tok)
        tl = p.dma("sy", stg[:, s0:s0 + cols], src_dram[k * 128:(k + 1) * 128, 0:cols])
        p.wait("v", tl)
        if k >= 1:
            p.wait("v", dt[k - 1])
        tv = p.op("v", lambda h, o=stgb[:, s0:s0 + cols], i=stg[:, s0:s0 + cols], s=scale[:, k:k + 1]:
                  h.tensor_scalar(o, i, s, None, ALU.mult), inc=1)
        vt.append(tv)
        p.wait("sy", tv)
        dt.append(p.dma("sy", dst_dram[k * 128:(k + 1) * 128, 0:cols], stgb[:, s0:s0 + cols]))
    ctx.wstage_tok = vt[-1]
    return dt[-1]


class AttnCfg:
    def __init__(self, **kw):
        self.__dict__.update(kw)


def emit_attn(ctx, cfg, out_dram):
    p = ctx.p
    S = ctx.sb
    nq = cfg.nheads
    NQC = nq // 2
    rope = getattr(cfg, "rope", None)
    qk = getattr(cfg, "qk", None)
    dup = getattr(cfg, "kv_dup", False)
    nkvc = cfg.nkv_chunks
    nkvh = cfg.nkv_heads
    nho = 1 if dup else 2
    SCR = S["SCR"]
    KTS = S["KTS"]
    VTOK = S["VTOK"]
    IC = NQC
    qtile = lambda c: SCR[:, c * 512:(c + 1) * 512]
    qsw = lambda c: SCR[:, 4096 + c * 512:4096 + (c + 1) * 512]
    asm = lambda c: SCR[:, 8192 + c * 512:8192 + (c + 1) * 512]
    P_t = lambda i: SCR[:, 12288 + i * 512:12288 + (i + 1) * 512]
    nskt = S.get("nskt")
    rqrows = S.get("rqrows")
    sumsrow = S["sumsrow"]
    fscr = S["fscr"]
    bank_tok = {}

    def use_bank(e, b):
        tok = bank_tok.get(b)
        if tok is not None:
            p.wait(e, tok)

    if getattr(cfg, "latent", None) is not None:
        LATN = S["LATN"]
        WD = S["WD"]
        lat = cfg.latent
        bc_tok = {}
        for tb in range(8):
            t0 = tb * 512
            sl = tb % 2
            bcb = S["bcbuf"][:, sl * 512:sl * 512 + 512]
            bb = ctx.pb[2]
            use_bank("t", 2)
            tb1 = p.op("t", lambda h, r=lat["scale_row"][0:1, t0:t0 + 512]:
                       h.matmul(bb[:, :], ctx.ones_row_bf[:, :], r, start=True, stop=True), inc=1)
            p.wait("sc", tb1)
            p.wait("sc", bc_tok.get(sl))
            te = p.op("sc", lambda h, o=bcb: h.activation(o, bb[:, :], AF.Copy), inc=1)
            bank_tok[2] = te
            for fc in range(2):
                b = fc % 2
                pb = ctx.pb[b]
                use_bank("t", b)
                for kc in range(NCHUNK):
                    xap, tok = cfg.src(kc, t0, 512)
                    p.wait("t", tok)
                    tm = p.op("t", lambda h, w=WD[:, kc * 256 + fc * 128: kc * 256 + fc * 128 + 128],
                              x=xap, o=pb, s=(kc == 0), e=(kc == NCHUNK - 1):
                              h.matmul(o[:, :], w, x, start=s, stop=e), inc=(1 if kc == NCHUNK - 1 else 0))
                p.wait("v", tm)
                p.wait("v", te)
                tv = p.op("v", lambda h, o=LATN[:, fc * T + t0: fc * T + t0 + 512], i=pb, bcx=bcb:
                          h.tensor_tensor(o, i[:, :], bcx, ALU.mult), inc=1)
                bank_tok[b] = tv
                bc_tok[sl] = tv
        lat_tok = p.tok("v")
        kv_src = lambda kc, t0, w: (LATN[:, kc * T + t0: kc * T + t0 + w], lat_tok)
        kv_kc = 2
        kv_wcols = getattr(cfg, "wk_cols")
    else:
        kv_src = cfg.src
        kv_kc = NCHUNK
        kv_wcols = getattr(cfg, "wk_cols")

    WK = S["WK"]
    ksw_tok = {}
    for tb in range(8):
        for c in range(nkvc):
            t0 = tb * 512
            b = c % 2
            pbK = ctx.pb[b]
            use_bank("t", b)
            for kc in range(kv_kc):
                xap, tok = kv_src(kc, t0, 512)
                p.wait("t", tok)
                tm = p.op("t", lambda h, w=WK[:, kc * kv_wcols + c * 128: kc * kv_wcols + c * 128 + 128],
                          x=xap, o=pbK, s=(kc == 0), e=(kc == kv_kc - 1):
                          h.matmul(o[:, :], w, x, start=s, stop=e), inc=(1 if kc == kv_kc - 1 else 0))
            ksl = KTS[:, c * T + t0: c * T + t0 + 512]
            p.wait("sc", tm)
            if getattr(cfg, "bk", None) is not None:
                te = p.op("sc", lambda h, o=ksl, i=pbK, ba=cfg.bk[:, c:c + 1]: h.activation(o, i[:, :], AF.Identity, bias=ba), inc=1)
            else:
                te = p.op("sc", lambda h, o=ksl, i=pbK: h.activation(o, i[:, :], AF.Copy), inc=1)
            bank_tok[b] = te
            if rope is not None:
                b2 = 2 + c % 2
                pbS = ctx.pb[b2]
                use_bank("t", b2)
                for kc in range(kv_kc):
                    xap, tok = kv_src(kc, t0, 512)
                    p.wait("t", tok)
                    tm2 = p.op("t", lambda h, w=WK[:, kc * kv_wcols + nkvc * 128 + c * 128: kc * kv_wcols + nkvc * 128 + c * 128 + 128],
                               x=xap, o=pbS, s=(kc == 0), e=(kc == kv_kc - 1):
                               h.matmul(o[:, :], w, x, start=s, stop=e), inc=(1 if kc == kv_kc - 1 else 0))
                ks = asm(c % 2)
                p.wait("sc", tm2)
                p.wait("sc", ksw_tok.get(c % 2))
                te2 = p.op("sc", lambda h, o=ks, i=pbS: h.activation(o, i[:, :], AF.Copy), inc=1)
                bank_tok[b2] = te2
                rt = S["ROPE"]
                p.wait("v", te)
                p.wait("v", te2)
                if getattr(cfg, "no_krope", False):
                    tv = p.op("v", lambda h, a=ksl: h.tensor_copy(a, a), inc=1)
                else:
                    p.op("v", lambda h, a=ksl, t1=rt[:, t0:t0 + 512]: h.tensor_tensor(a, a, t1, ALU.mult))
                    p.op("v", lambda h, bx=ks, t2=rt[:, T + t0:T + t0 + 512]: h.tensor_tensor(bx, bx, t2, ALU.mult))
                    tv = p.op("v", lambda h, a=ksl, bx=ks: h.tensor_tensor(a, a, bx, ALU.add), inc=1)
                ksw_tok[c % 2] = tv
                bank_tok[b2] = tv

    if qk is not None:
        for c in range(nkvc):
            sq_tok = {}
            ev_tok = None
            for tb in range(8):
                t0 = tb * 512
                dq = P_t(tb % 2)
                ksl = KTS[:, c * T + t0: c * T + t0 + 512]
                p.wait("v", sq_tok.get(tb % 2))
                tv = p.op("v", lambda h, o=dq, i=ksl: h.tensor_tensor(o, i, i, ALU.mult), inc=1)
                p.wait("t", tv)
                use_bank("t", 4)
                rh = S["MASKH"][:, 0:1] if dup else S["MASKH"][:, 1:3]
                for tt in range(4):
                    tm = p.op("t", lambda h, l=dq[:, tt * 128:(tt + 1) * 128], r=rh,
                              o=ctx.pb[4][:, tt * nho:(tt + 1) * nho]:
                              h.matmul(o, l, r, start=True, stop=True), inc=(1 if tt == 3 else 0))
                sq_tok[tb % 2] = tm
                p.wait("v", tm)
                tvd = p.op("v", lambda h: h.memset(fscr[:, 191:192], 0.0), inc=1)
                p.wait("sc", tvd)
                ev_tok = p.op("sc", lambda h, o=fscr[:, 128 + tb * 4 * nho: 128 + (tb + 1) * 4 * nho],
                              i=ctx.pb[4][:, 0:4 * nho]: h.activation(o, i, AF.Copy), inc=1)
                bank_tok[4] = ev_tok
            width = 32 * nho
            p.wait("v", ev_tok)
            f1_ = fscr[:, 0:width]
            f2_ = fscr[:, 64:64 + width]
            inv_pm = 1.0 / cfg.exp_premul
            if qk == "rms":
                tv = p.op("v", lambda h, a=f1_, b=fscr[:, 128:128 + width]: h.tensor_scalar(a, b, 1.0 / 64.0, 1e-6, ALU.mult, ALU.add), inc=1)
                p.wait("sc", tv)
                ts = p.op("sc", lambda h, a=f2_, b=f1_: h.activation(a, b, AF.Sqrt), inc=1)
                p.wait("v", ts)
                tpm = p.op("v", lambda h, a=f2_, s=inv_pm: h.tensor_scalar(a, a, s, None, ALU.mult), inc=1)
                p.wait("v", tpm, force=True)
                tv2 = p.op("v", lambda h, o=nskt[:, c * width:(c + 1) * width], b=f2_: h.reciprocal(o, b), inc=1)
            else:
                tv = p.op("v", lambda h, a=f1_, b=fscr[:, 128:128 + width]: h.tensor_copy(a, b), inc=1)
                p.wait("sc", tv)
                ts = p.op("sc", lambda h, a=f2_, b=f1_: h.activation(a, b, AF.Sqrt), inc=1)
                p.wait("v", ts)
                p.op("v", lambda h, a=f2_: h.tensor_scalar_max(a, a, 1e-12))
                tpm = p.op("v", lambda h, a=f2_, s=inv_pm: h.tensor_scalar(a, a, s, None, ALU.mult), inc=1)
                p.wait("v", tpm, force=True)
                tv2 = p.op("v", lambda h, o=nskt[:, c * width:(c + 1) * width], b=f2_: h.reciprocal(o, b), inc=1)
            bank_tok[4] = tv2

    if getattr(cfg, "dbg", None) is not None and "knorm_aps" in cfg.dbg:
        p.barrier()
        for nm, (sap, dap) in cfg.dbg["knorm_aps"].items():
            p.dma("sy", dap, sap)
        p.barrier()

    if getattr(cfg, "load_wv", None) is not None:
        p.barrier()
        cfg.load_wv()
        p.barrier()
    WV = S.get("WV") or S["WK"]
    vone = VTOK.rearrange("p (n s) -> p n s", s=65)
    tvo = p.op("v", lambda h: h.memset(vone[:, :, 64:65], 1.0), inc=1)
    nvc = nkvh * 64 // 128
    wv_cols = cfg.wv_cols
    vstg_tok = {}
    vbc_tok = {}
    for tb in range(8):
        for c in range(nvc):
            t0 = tb * 512
            b = c % 2
            pbV = ctx.pb[b]
            use_bank("t", b)
            for kc in range(kv_kc):
                xap, tok = kv_src(kc, t0, 512)
                p.wait("t", tok)
                tm = p.op("t", lambda h, w=WV[:, kc * wv_cols + c * 128: kc * wv_cols + c * 128 + 128],
                          x=xap, o=pbV, s=(kc == 0), e=(kc == kv_kc - 1):
                          h.matmul(o[:, :], w, x, start=s, stop=e), inc=(1 if kc == kv_kc - 1 else 0))
            vstg = S["VSTG"][:, b * 512: b * 512 + 512]
            p.wait("sc", tm)
            p.wait("sc", vstg_tok.get(b))
            if getattr(cfg, "bv", None) is not None:
                te = p.op("sc", lambda h, o=vstg, i=pbV, ba=cfg.bv[:, c:c + 1]: h.activation(o, i[:, :], AF.Identity, bias=ba), inc=1)
            elif getattr(cfg, "v_scale_row", None) is not None:
                sl = tb % 2
                bcb = S["bcbuf"][:, sl * 512:sl * 512 + 512]
                use_bank("t", 5)
                tb1 = p.op("t", lambda h, r=cfg.v_scale_row[0:1, t0:t0 + 512]:
                           h.matmul(ctx.pb[5][:, :], ctx.ones_row_bf[:, :], r, start=True, stop=True), inc=1)
                p.wait("sc", tb1)
                p.wait("sc", vbc_tok.get(sl))
                p.op("sc", lambda h, o=bcb: h.activation(o, ctx.pb[5][:, :], AF.Copy))
                te0 = p.op("sc", lambda h, o=vstg, i=pbV: h.activation(o, i[:, :], AF.Copy), inc=1)
                p.wait("v", te0)
                te = p.op("v", lambda h, o=vstg, bcx=bcb: h.tensor_tensor(o, o, bcx, ALU.mult), inc=1)
                bank_tok[5] = te
                vbc_tok[sl] = te
            else:
                te = p.op("sc", lambda h, o=vstg, i=pbV: h.activation(o, i[:, :], AF.Copy), inc=1)
            bank_tok[b] = te
            p.wait("t", te)
            p.wait("v", tvo)
            last_v = None
            for tt in range(4):
                kt = tb * 4 + tt
                pt = ctx.pbbf[:, 0:128]
                use_bank("t", "bf")
                ttr = p.op("t", lambda h, i=vstg[:, tt * 128:(tt + 1) * 128], o=pt:
                           h.transpose(o, i, S["IDENT"][:, :]), inc=1)
                p.wait("v", ttr)
                bA = ((2 * c) * 32 + kt) * 65
                bB = ((2 * c + 1) * 32 + kt) * 65
                p.op("v", lambda h, o=VTOK[:, bA:bA + 64], i=pt[:, 0:64]: h.tensor_copy(o, i))
                tvv = p.op("v", lambda h, o=VTOK[:, bB:bB + 64], i=pt[:, 64:128]: h.tensor_copy(o, i), inc=1)
                bank_tok["bf"] = tvv
                last_v = tvv
            vstg_tok[b] = last_v

    p.barrier()
    if getattr(cfg, "dbg", None) is not None and "kv_aps" in cfg.dbg:
        for nm, (sap, dap) in cfg.dbg["kv_aps"].items():
            p.dma("sy", dap, sap)
        p.barrier()

    WQb = S["WQB"]
    WOS = S.get("WOS")
    FQ = NQC * (2 if rope else 1)
    recips = sumsrow[0:1, 1024:2048] if False else S["recips"]
    ostg = S["OSTG"]
    p_tok = {}
    asm_tok = None
    qe_tok = {}
    for qc in range(8):
        q0 = qc * 512
        sched = cfg.mask_sched(qc)
        p.wait("sc", qe_tok.get(("o", 0)))
        p.wait("sc", qe_tok.get(("o", 1)))
        wq_toks = {}
        for fc in range(FQ):
            slot = fc % 2
            b = fc % 2
            if fc >= 2:
                p.wait("g", wq_toks[fc - 2])
            else:
                p.wait("g", qe_tok.get(("wo", fc % 2)))
            tl = p.dma("g", WQb[:, slot * 1024:(slot + 1) * 1024].rearrange("p (k c) -> p k c", c=128),
                       cfg.wq[:, fc * 128:(fc + 1) * 128].rearrange("(k p) c -> p k c", p=128))
            pbQ = ctx.pb[b]
            use_bank("t", b)
            p.wait("t", tl)
            for kc in range(NCHUNK):
                xap, tok = cfg.src(kc, q0, 512)
                p.wait("t", tok)
                tm = p.op("t", lambda h, w=WQb[:, slot * 1024 + kc * 128: slot * 1024 + kc * 128 + 128],
                          x=xap, o=pbQ, s=(kc == 0), e=(kc == NCHUNK - 1):
                          h.matmul(o[:, :], w, x, start=s, stop=e), inc=(1 if kc == NCHUNK - 1 else 0))
            wq_toks[fc] = tm
            dst = qtile(fc) if fc < NQC else qsw(fc - NQC)
            p.wait("sc", tm)
            p.wait("sc", asm_tok)
            if getattr(cfg, "bq", None) is not None and fc < NQC:
                te = p.op("sc", lambda h, o=dst, i=pbQ, ba=cfg.bq[:, fc:fc + 1]: h.activation(o, i[:, :], AF.Identity, bias=ba), inc=1)
            else:
                te = p.op("sc", lambda h, o=dst, i=pbQ: h.activation(o, i[:, :], AF.Copy), inc=1)
            bank_tok[b] = te
            qe_tok[("e", fc)] = te
        if rope is not None:
            rt = S["ROPE"]
            for c in range(NQC):
                p.wait("v", qe_tok[("e", c)])
                p.wait("v", qe_tok[("e", c + NQC)])
                p.op("v", lambda h, a=qtile(c), t1=rt[:, q0:q0 + 512]: h.tensor_tensor(a, a, t1, ALU.mult))
                p.op("v", lambda h, bx=qsw(c), t2=rt[:, T + q0:T + q0 + 512]: h.tensor_tensor(bx, bx, t2, ALU.mult))
                p.op("v", lambda h, a=qtile(c), bx=qsw(c): h.tensor_tensor(a, a, bx, ALU.add), inc=1)
                qe_tok[("r", c)] = p.tok("v")
        if qk is not None:
            use_bank("t", 4)
            for c in range(NQC):
                src_t = qe_tok.get(("r", c), qe_tok[("e", c)])
                p.wait("v", src_t)
                tv = p.op("v", lambda h, o=qsw(c), a=qtile(c): h.tensor_tensor(o, a, a, ALU.mult), inc=1)
                p.wait("t", tv)
                for tt in range(4):
                    tm = p.op("t", lambda h, l=qsw(c)[:, tt * 128:(tt + 1) * 128], r=S["MASKH"][:, 1:3],
                              o=ctx.pb[4][:, tt * 2 * NQC + c * 2: tt * 2 * NQC + c * 2 + 2]:
                              h.matmul(o, l, r, start=True, stop=True), inc=(1 if (tt == 3 and c == NQC - 1) else 0))
            qn_w = 2 * NQC * 4
            p.wait("v", tm)
            f1q = fscr[:, 0:qn_w]
            f3q = fscr[:, 64:64 + qn_w]
            if qk == "rms":
                p.op("v", lambda h, a=f1q, b=ctx.pb[4][:, 0:qn_w]: h.tensor_scalar(a, b, 1.0 / 64.0, 1e-6, ALU.mult, ALU.add), inc=1)
                p.wait("sc", p.tok("v"))
                ts = p.op("sc", lambda h, a=f1q: h.activation(a, a, AF.Sqrt), inc=1)
                p.wait("v", ts)
                tvq0 = p.op("v", lambda h, o=f3q, b=f1q: h.reciprocal(o, b), inc=1)
            else:
                p.op("v", lambda h, a=f1q, b=ctx.pb[4][:, 0:qn_w]: h.tensor_copy(a, b), inc=1)
                p.wait("sc", p.tok("v"))
                ts = p.op("sc", lambda h, a=f1q: h.activation(a, a, AF.Sqrt), inc=1)
                p.wait("v", ts)
                tmx = p.op("v", lambda h, a=f1q: h.tensor_scalar_max(a, a, 1e-12), inc=1)
                p.wait("v", tmx, force=True)
                tvq0 = p.op("v", lambda h, o=f3q, b=f1q: h.reciprocal(o, b), inc=1)
            p.wait("v", tvq0, force=True)
            tvq = p.op("v", lambda h, o=fscr.bitcast(BF16)[:, 256:256 + qn_w], b=f3q: h.tensor_copy(o, b), inc=1)
            bank_tok[4] = tvq
            rqb = fscr.bitcast(BF16)[:, 256:384]
            for tt in range(4):
                pt = ctx.pbbf[:, 0:128]
                use_bank("t", "bf")
                p.wait("t", tvq)
                ttr = p.op("t", lambda h, i=rqb[:, tt * 2 * NQC:(tt + 1) * 2 * NQC], o=pt[0:2 * NQC, 0:128]:
                           h.transpose(o, i, S["IDENT"][:, :]), inc=1)
                p.wait("sc", ttr)
                tse = p.op("sc", lambda h, o=rqrows[0:2 * NQC, tt * 128:(tt + 1) * 128], i=pt[0:2 * NQC, 0:128]:
                           h.activation(o, i, AF.Copy), inc=1)
                bank_tok["bf"] = tse
            for c in range(NQC):
                use_bank("t", 5)
                p.wait("t", tse)
                tbq = p.op("t", lambda h, l=S["SELP"][0:nq, c * 128:(c + 1) * 128], r=rqrows[0:nq, 0:512]:
                           h.matmul(ctx.pb[5][:, :], l, r, start=True, stop=True), inc=1)
                p.wait("v", tbq)
                tvm = p.op("v", lambda h, a=qtile(c): h.tensor_tensor(a, a, ctx.pb[5][:, :], ALU.mult), inc=1)
                bank_tok[5] = tvm
                qe_tok[("n", c)] = tvm
        if getattr(cfg, "dbg", None) is not None and qc == cfg.dbg["qc"] and "q_ap" in cfg.dbg:
            for fc in range(FQ):
                p.wait("sy", qe_tok[("e", fc)])
            p.dma("sy", cfg.dbg["q_ap"], SCR[:, 0:4096])
            if "wq_ap" in cfg.dbg:
                p.dma("sy", cfg.dbg["wq_ap"], WQb[:, :])
            p.barrier()
        for c in range(NQC):
            g = (c // 2) if dup else c
            q_ready = qe_tok.get(("n", c)) or qe_tok.get(("r", c)) or qe_tok[("e", c)]
            pvA, pvB = ctx.pb[4], ctx.pb[5]
            use_bank("t", 4)
            use_bank("t", 5)
            p.wait("t", q_ready)
            nsched = len(sched)
            evA = evB = None
            for idx, (kt, mi) in enumerate(sched):
                bA = idx % 2
                sA, sB = ctx.pb[0 + 2 * bA], ctx.pb[1 + 2 * bA]
                use_bank("t", 0 + 2 * bA)
                use_bank("t", 1 + 2 * bA)
                p.op("t", lambda h, l=KTS[0:64, g * T + kt * 128: g * T + kt * 128 + 128],
                     r=qtile(c)[0:64, :], o=sA: h.matmul(o[:, :], l, r, start=True, stop=True, tile_position=(0, 0)))
                tSB = p.op("t", lambda h, l=KTS[64:128, g * T + kt * 128: g * T + kt * 128 + 128],
                           r=qtile(c)[64:128, :], o=sB: h.matmul(o[:, :], l, r, start=True, stop=True, tile_position=(64, 0)), inc=1)
                if mi is not None:
                    p.wait("v", tSB)
                    p.op("v", lambda h, o=sA, m=S["MASKS"][:, mi * 512:(mi + 1) * 512]: h.tensor_tensor(o[:, :], o[:, :], m, ALU.add))
                    tSB = p.op("v", lambda h, o=sB, m=S["MASKS"][:, mi * 512:(mi + 1) * 512]: h.tensor_tensor(o[:, :], o[:, :], m, ALU.add), inc=1)
                pA, pB = P_t(2 * bA), P_t(2 * bA + 1)
                p.wait("sc", tSB)
                p.wait("sc", p_tok.get(2 * bA))
                if qk is not None:
                    colA = g * 32 * nho + kt * nho
                    colB = colA if dup else colA + 1
                    p.op("sc", lambda h, o=pA, i=sA, scl=nskt[:, colA:colA + 1]:
                         h.activation(o, i[:, :], AF.Exp, bias=cfg.exp_bias, scale=scl))
                    teP = p.op("sc", lambda h, o=pB, i=sB, scl=nskt[:, colB:colB + 1]:
                               h.activation(o, i[:, :], AF.Exp, bias=cfg.exp_bias, scale=scl), inc=1)
                else:
                    p.op("sc", lambda h, o=pA, i=sA: h.activation(o, i[:, :], AF.Exp, bias=cfg.exp_bias, scale=cfg.exp_scale_const))
                    teP = p.op("sc", lambda h, o=pB, i=sB: h.activation(o, i[:, :], AF.Exp, bias=cfg.exp_bias, scale=cfg.exp_scale_const), inc=1)
                bank_tok[0 + 2 * bA] = teP
                bank_tok[1 + 2 * bA] = teP
                hA = (g if dup else 2 * c)
                hB = (g if dup else 2 * c + 1)
                vbA = (hA * 32 + kt) * 65
                vbB = (hB * 32 + kt) * 65
                p.wait("t", teP)
                p.op("t", lambda h, l=VTOK[:, vbA:vbA + 65], r=pA, o=pvA[0:65, :], s=(idx == 0), e=(idx == nsched - 1):
                     h.matmul(o, l, r, start=s, stop=e))
                tPV = p.op("t", lambda h, l=VTOK[:, vbB:vbB + 65], r=pB, o=pvB[0:65, :], s=(idx == 0), e=(idx == nsched - 1):
                           h.matmul(o, l, r, start=s, stop=e), inc=1)
                p_tok[2 * bA] = tPV
            p.wait("sc", tPV)
            p.wait("sc", qe_tok.get("rc"))
            p.op("sc", lambda h: h.activation(sumsrow[0:1, 0:512], pvA[64:65, :], AF.Copy))
            tsum = p.op("sc", lambda h: h.activation(sumsrow[0:1, 512:1024], pvB[64:65, :], AF.Copy), inc=1)
            p.wait("v", tsum)
            if getattr(cfg, "sums_clip", None) is not None:
                tcl = p.op("v", lambda h: h.tensor_scalar_max(sumsrow[0:1, :], sumsrow[0:1, :], cfg.sums_clip), inc=1)
                p.wait("v", tcl, force=True)
            trc = p.op("v", lambda h: h.reciprocal(recips[0:1, 0:1024], sumsrow[0:1, :]), inc=1)
            qe_tok["rc"] = trc
            use_bank("t", 6)
            p.wait("t", trc)
            p.op("t", lambda h: h.matmul(ctx.pb[6][0:64, :], ctx.ones_row[0:1, 0:64], recips[0:1, 0:512],
                                         start=True, stop=True, tile_position=(0, 0)))
            tbc = p.op("t", lambda h: h.matmul(ctx.pb[6][64:128, :], ctx.ones_row[0:1, 0:64], recips[0:1, 512:1024],
                                               start=True, stop=True, tile_position=(0, 64)), inc=1)
            p.wait("sc", asm_tok)
            p.op("sc", lambda h, o=asm(c)[0:64, :], i=pvA[0:64, :]: h.activation(o, i, AF.Copy))
            tev = p.op("sc", lambda h, o=asm(c)[64:128, :], i=pvB[0:64, :]: h.activation(o, i, AF.Copy), inc=1)
            bank_tok[4] = tev
            bank_tok[5] = tev
            p.wait("v", tev)
            p.wait("v", tbc)
            p.op("v", lambda h, a=asm(c)[0:64, :]: h.tensor_tensor(a, a, ctx.pb[6][0:64, :], ALU.mult))
            tno = p.op("v", lambda h, a=asm(c)[64:128, :]: h.tensor_tensor(a, a, ctx.pb[6][64:128, :], ALU.mult), inc=1)
            bank_tok[6] = tno
            qe_tok[("a", c)] = tno
        if getattr(cfg, "dbg", None) is not None and qc == cfg.dbg["qc"]:
            for nm, ap in cfg.dbg["aps"].items():
                for c2 in range(NQC):
                    p.wait("sy", qe_tok[("a", c2)])
                p.wait("sy", p.tok("v"))
                p.wait("sy", p.tok("sc"))
                p.dma("sy", ap[1], ap[0])
        for c in range(NQC):
            p.wait("t", qe_tok[("a", c)])
        for fc in range(NCHUNK):
            b = fc % 2
            pbO = ctx.pb[b]
            use_bank("t", b)
            if getattr(cfg, "wo_stream", None) is not None:
                p.wait("g", qe_tok.get(("wo", b)))
                if fc < 2:
                    p.wait("g", p.tok("t"))
                two = p.dma("g", S["WOB"][:, b * 1024: b * 1024 + IC * 128].rearrange("p (k c) -> p k c", c=128),
                            cfg.wo_stream[:, fc * 128:(fc + 1) * 128].rearrange("(k p) c -> p k c", p=128))
                p.wait("t", two)
                wsrc = lambda ic, b=b: S["WOB"][:, b * 1024 + ic * 128: b * 1024 + ic * 128 + 128]
            else:
                wsrc = lambda ic, fc=fc: WOS[:, ic * 1024 + fc * 128: ic * 1024 + fc * 128 + 128]
            for ic in range(IC):
                tm = p.op("t", lambda h, w=wsrc(ic), x=asm(ic), o=pbO, s=(ic == 0), e=(ic == IC - 1):
                          h.matmul(o[:, :], w, x, start=s, stop=e), inc=(1 if ic == IC - 1 else 0))
            o_sl = ostg[:, b * 512: b * 512 + 512]
            res = cfg.res(fc, q0) if getattr(cfg, "res", None) is not None else None
            p.wait("sc" if getattr(cfg, "bo", None) is not None else "v", tm)
            if getattr(cfg, "bo", None) is not None:
                p.wait("sc", qe_tok.get(("o", b)))
                te = p.op("sc", lambda h, o=o_sl, i=pbO, ba=cfg.bo[:, fc:fc + 1]: h.activation(o, i[:, :], AF.Identity, bias=ba), inc=1)
                if res is not None:
                    rap, rtok = res
                    p.wait("v", te)
                    p.wait("v", rtok)
                    te = p.op("v", lambda h, o=o_sl, r=rap: h.tensor_tensor(o, o, r, ALU.add), inc=1)
            else:
                p.wait("v", tm)
                p.wait("v", qe_tok.get(("o", b)))
                if res is not None:
                    rap, rtok = res
                    p.wait("v", rtok)
                    te = p.op("v", lambda h, o=o_sl, i=pbO, r=rap: h.tensor_tensor(o, i[:, :], r, ALU.add), inc=1)
                else:
                    te = p.op("v", lambda h, o=o_sl, i=pbO: h.tensor_copy(o, i[:, :]), inc=1)
            bank_tok[b] = te
            qe_tok[("wo", b)] = tm
            p.wait("sy", te)
            td = p.dma("sy", out_dram[fc * 128:(fc + 1) * 128, q0:q0 + 512], o_sl)
            qe_tok[("o", b)] = td
        asm_tok = p.tok("t")
    p.barrier()


class SrcStream:

    def __init__(self, ctx, dram, buf, queue="sy"):
        self.ctx, self.dram, self.buf, self.q = ctx, dram, buf, queue
        self.key = {}

    def __call__(self, kc, t0, w=512):
        p = self.ctx.p
        cur = self.key.get(kc)
        if cur is not None and cur[0] == t0:
            return (self.buf[:, kc * 512:(kc + 1) * 512], cur[1])
        if cur is not None:
            p.barrier()
            p.wait(self.q, p.tok("t"))
            p.wait(self.q, p.tok("v"))
        tl = p.dma(self.q, self.buf[:, kc * 512:(kc + 1) * 512],
                   self.dram[kc * 128:(kc + 1) * 128, t0:t0 + w])
        self.key[kc] = (t0, tl)
        return (self.buf[:, kc * 512:(kc + 1) * 512], tl)


def emit_rms_row_stream(ctx, src, out_row, eps=1e-6):
    p = ctx.p
    S = ctx.sb
    rows = S["sumsrow"]
    ps = ctx.pb[6]
    for tb in range(8):
        t0 = tb * 512
        t_done = []
        for ch in range(NCHUNK):
            xap, tok = src(ch, t0)
            sq = S["VSTG"][:, (ch % 2) * 512:(ch % 2) * 512 + 512]
            p.wait("v", tok)
            if ch >= 2:
                p.wait("v", t_done[ch - 2])
            tv = p.op("v", lambda h, o=sq, i=xap: h.tensor_tensor(o, i, i, ALU.mult), inc=1)
            p.wait("t", tv)
            tm = p.op("t", lambda h, b=sq, s=(ch == 0): h.matmul(ps[0:1, :], ctx.ones_bf[:, :], b, start=s, stop=(ch == 7)), inc=1)
            t_done.append(tm)
        r1 = rows[0:1, 0:512]
        p.wait("v", t_done[-1])
        tv = p.op("v", lambda h: h.tensor_scalar(r1, ps[0:1, :], 1.0 / D, eps, ALU.mult, ALU.add), inc=1)
        p.wait("sc", tv)
        ts = p.op("sc", lambda h: h.activation(r1, r1, AF.Sqrt), inc=1)
        p.wait("v", ts)
        tvr = p.op("v", lambda h, o=out_row[0:1, t0:t0 + 512]: h.reciprocal(o, r1), inc=1, lp=True)
        p.wait("t", tvr)
    return p.tok("v")


def _attn_sbuf_spec(nq, nkvc, nkvh, nho, kv_kc, wk_cols, wv_cols, *, rope, nmask, latent,
                    wo_stream=False, qk=True, scaled=True, wv_shared=False, bcb=True):
    NQC = nq // 2
    spec = [
        ("SCR", (128, 14336), BF16),
        ("KTS", (128, nkvc * T), BF16),
        ("VTOK", (128, nkvh * 32 * 65), BF16),
        ("SRCB", (128, 4096), BF16),
        ("WK", (128, kv_kc * max(wk_cols, wv_cols if wv_shared else 0)), BF16),
        ("WQB", (128, 2048), BF16),
        ("MASKH", (128, 3), BF16),
        ("IDENT", (128, 128), BF16),
        ("sumsrow", (1, 1024), F32),
        ("recips", (1, 1024), F32),
        ("fscr", (128, 256), F32),
        ("VSTG", (128, 1024), BF16),
        ("OSTG", (128, 1024), BF16),
    ]
    if not wv_shared:
        spec.append(("WV", (128, kv_kc * wv_cols), BF16))
    if qk:
        spec += [("SELP", (nq, NQC * 128), BF16),
                 ("nskt", (128, max(1, nkvc * 32 * nho)), F32),
                 ("rqrows", (16, 512), BF16)]
    if bcb:
        spec.append(("bcbuf", (128, 1024), BF16))
    if scaled:
        spec += [("wstage", (128, 2048), F32), ("wstageb", (128, 2048), BF16)]
    spec.append(("WOB", (128, 2048), BF16) if wo_stream else ("WOS", (128, NQC * 1024), BF16))
    if rope:
        spec.append(("ROPE", (128, 2 * T), BF16))
    if nmask:
        spec.append(("MASKS", (128, nmask * 512), BF16))
    if latent:
        spec.append(("WD", (128, NCHUNK * 256), BF16))
        spec.append(("LATN", (128, 2 * T), BF16))
    return spec


def _load_attn_consts(ctx, aps, *, rope_key=None, nmask=0, mask_key=None, rs1=None):
    p = ctx.p
    S = ctx.sb
    toks = []
    toks.append(p.dma("g", S["MASKH"][:, :], aps["maskh"][:, :]))
    toks.append(p.dma("g", S["IDENT"][:, :], aps["ident"][:, :]))
    if rope_key:
        toks.append(p.dma("g", S["ROPE"][:, :], aps[rope_key][:, :]))
    for mi in range(nmask):
        toks.append(p.dma("sy", S["MASKS"][:, mi * 512:(mi + 1) * 512], aps[mask_key][mi * 128:(mi + 1) * 128, :]))
    return toks


def _sched_causal(qc):
    return [(kt, (kt - qc * 4) if kt >= qc * 4 else None) for kt in range(qc * 4 + 4)]


def _sched_local(qc):
    out = []
    for kt in range(max(0, qc * 4 - 2), qc * 4 + 4):
        out.append((kt, kt - qc * 4 + 2))
    return out


def _sched_full(qc):
    return [(kt, None) for kt in range(32)]


def build_g2(shared, percore0, dbg_qc=None):
    import contextlib
    nc = bass.Bass("TRN2", target_bir_lowering=False)
    stack = contextlib.ExitStack()
    nc._dev_stack = stack
    aps = _declare(nc, shared, percore0, ["g_wq", "g_kvd", "g_kup", "g_vup", "g_wo",
                                         "w1n", "ropeg", "cmask", "ident", "selp16", "maskh"],
                   extra={"xnT_in": ([D, T], BF16), "rs1_in": ([1, T], F32)})
    outs = {"x1T": ([D, T], BF16)}
    if dbg_qc is not None:
        outs.update({"d_scr": ([128, 14336], BF16), "d_nskt": ([128, 128], F32),
                     "d_sums": ([1, 1024], F32), "d_recips": ([1, 1024], F32)})
    aps.update(_outs(nc, outs))
    wqbf = nc.dram_tensor("wqbf", [D, 2048], BF16, kind="Internal").ap()
    spec = _attn_sbuf_spec(16, 4, 4, 1, 2, 1024, 256, rope=True, nmask=4, latent=True)
    spec += [("W1N", (128, 8), F32), ("RS1", (1, T), BF16)]
    with nc.Block() as block:
        ctx = Ctx(nc, stack, aps, consts_vals=(-8.0,), sbuf_spec=spec)
        ctx.setup()
        p = ctx.p
        S = ctx.sb
        p.dma("g", S["W1N"][:, :], aps["w1n"][:, :])
        p.dma("g", S["RS1"][:, :], aps["rs1_in"][:, :])
        p.dma("g", S["SELP"][:, :], aps["selp16"][:, :])
        _load_attn_consts(ctx, aps, rope_key="ropeg", nmask=4, mask_key="cmask")
        p.barrier()
        emit_w_prep(ctx, aps["g_wq"], wqbf, 8, 2048, S["W1N"])
        load_rows(ctx, aps["g_kvd"], S["WD"], 8, 256, scale=S["W1N"])
        load_rows(ctx, aps["g_kup"], S["WK"], 2, 1024)
        load_rows(ctx, aps["g_vup"], S["WV"], 2, 256)
        load_rows(ctx, aps["g_wo"], S["WOS"], 8, 1024)
        p.barrier()
        src = SrcStream(ctx, aps["xnT_in"], S["SRCB"])
        cfg = AttnCfg(nheads=16, kv_dup=True, nkv_chunks=4, nkv_heads=4, rope="ROPE", qk="rms",
                      wq=wqbf, wq_bf=True, wk_cols=1024, wv_cols=256,
                      latent=dict(scale_row=S["RS1"]), exp_premul=0.125, exp_bias=-8.0,
                      mask_sched=_sched_causal, src=src,
                      res=lambda fc, q0: src(fc, q0))
        if dbg_qc is not None:
            cfg.dbg = {"qc": dbg_qc, "aps": {
                "scr": (S["SCR"][:, :], aps["d_scr"][:, :]),
                "nskt": (S["nskt"][:, :], aps["d_nskt"][:, :]),
                "sums": (S["sumsrow"][:, :], aps["d_sums"][:, :]),
                "recips": (S["recips"][:, :], aps["d_recips"][:, :]),
            }}
        emit_attn(ctx, cfg, aps["x1T"])
        _finish(ctx)
        ctx.p.build(block)
    return nc


class SrcResident:

    def __init__(self, ctx, arena, tok):
        self.arena, self.tok = arena, tok

    def __call__(self, kc, t0, w=512):
        return (self.arena[:, kc * T + t0: kc * T + t0 + w], self.tok)


def build_g3(shared, percore0, dbg_qc=None, resident=False):
    import contextlib
    nc = bass.Bass("TRN2", target_bir_lowering=False)
    stack = contextlib.ExitStack()
    nc._dev_stack = stack
    aps = _declare(nc, shared, percore0, ["l_wq", "l_wk", "l_wv", "l_wo",
                                         "w2n", "ropel", "lmask", "ident", "selp16", "maskh"],
                   extra={"x1T_in": ([D, T], BF16)})
    outs3 = {"x2T": ([D, T], BF16)}
    if dbg_qc is not None:
        outs3.update({"d_scr": ([128, 14336], BF16), "d_nskt": ([128, 128], F32),
                      "d_sums": ([1, 1024], F32), "d_rs2": ([1, T], BF16),
                      "d_kts": ([128, 4 * T], BF16), "d_wk": ([128, 8192], BF16),
                      "d_nskt0": ([128, 128], F32), "d_dq": ([128, 1024], BF16), "d_fscr": ([128, 256], F32)})
    aps.update(_outs(nc, outs3))
    wqbf = nc.dram_tensor("wqbf", [D, 2048], BF16, kind="Internal").ap()
    spec = _attn_sbuf_spec(16, 4, 4, 1, 8, 1024, 256, rope=True, nmask=6, latent=False)
    spec += [("W2N", (128, 8), F32), ("RS2", (1, T), BF16)]
    if resident:
        spec = [s for s in spec if s[0] != "SRCB"] + [("XA", (128, NCHUNK * T), BF16)]
    with nc.Block() as block:
        ctx = Ctx(nc, stack, aps, consts_vals=(-8.0,), sbuf_spec=spec)
        ctx.setup()
        p = ctx.p
        S = ctx.sb
        p.dma("g", S["W2N"][:, :], aps["w2n"][:, :])
        p.dma("g", S["SELP"][:, :], aps["selp16"][:, :])
        _load_attn_consts(ctx, aps, rope_key="ropel", nmask=6, mask_key="lmask")
        p.barrier()
        emit_w_prep(ctx, aps["l_wq"], wqbf, 8, 2048, S["W2N"])
        load_rows(ctx, aps["l_wk"], S["WK"], 8, 1024, scale=S["W2N"])
        load_rows(ctx, aps["l_wv"], S["WV"], 8, 256, scale=S["W2N"])
        load_rows(ctx, aps["l_wo"], S["WOS"], 8, 1024)
        p.barrier()
        if resident:
            for kc in range(NCHUNK):
                tok = p.dma("g", S["XA"][:, kc * T:(kc + 1) * T], aps["x1T_in"][kc * 128:(kc + 1) * 128, :])
            p.barrier()
            src = SrcResident(ctx, S["XA"], None)
        else:
            src = SrcStream(ctx, aps["x1T_in"], S["SRCB"])
            import os as _os
            if _os.environ.get("PARANOID"):
                src.paranoid = True
        emit_rms_row_stream(ctx, src, S["RS2"], eps=1e-6)
        cfg = AttnCfg(nheads=16, kv_dup=True, nkv_chunks=4, nkv_heads=4, rope="ROPE", qk="rms",
                      wq=wqbf, wq_bf=True, wk_cols=1024, wv_cols=256,
                      v_scale_row=S["RS2"], exp_premul=0.125, exp_bias=-8.0,
                      mask_sched=_sched_local, src=src,
                      res=lambda fc, q0: src(fc, q0))
        if dbg_qc is not None:
            cfg.dbg = {"qc": dbg_qc, "aps": {
                "scr": (S["SCR"][:, :], aps["d_scr"][:, :]),
                "nskt": (S["nskt"][:, :], aps["d_nskt"][:, :]),
                "sums": (S["sumsrow"][:, :], aps["d_sums"][:, :]),
                "rs2": (S["RS2"][:, :], aps["d_rs2"][:, :]),
            }, "kv_aps": {
                "kts": (S["KTS"][:, :], aps["d_kts"][:, :]),
                "wk": (S["WK"][:, :], aps["d_wk"][:, :]),
            }, "knorm_aps": {
                "nskt0": (S["nskt"][:, :], aps["d_nskt0"][:, :]),
                "dq": (S["SCR"][:, 12288:13312], aps["d_dq"][:, :]),
                "fscr": (S["fscr"][:, :], aps["d_fscr"][:, :]),
            }}
        emit_attn(ctx, cfg, aps["x2T"])
        _finish(ctx)
        ctx.p.build(block)
    return nc



def emit_moe(ctx, cfg, finish):
    p = ctx.p
    S = ctx.sb
    XMB = S["XMB"]
    W1S, W2S = S["W1S"], S["W2S"]
    HS = S["HS"]
    YA = S["YA"]
    CMB = S["CMB"]
    WTR = S["WTR"]
    WBC = S["WBC"]
    GSC = S["GSC"]
    renorm = cfg.renorm
    act = cfg.act

    wgs = S["WGS"]
    load_rows(ctx, cfg.wg, wgs, NCHUNK, 8)
    if cfg.gb is not None:
        tgb = p.dma("g", S["GB"][:, :], cfg.gb[0:1, :])
    p.barrier()
    for tt in range(32):
        t0 = (tt // 4) * 512
        sub = (tt % 4) * 128
        pg = ctx.pb[6]
        tok_bank = p.tok("sc")
        for kc in range(NCHUNK):
            xap, tok = cfg.src(kc, t0)
            p.wait("t", tok)
            tm = p.op("t", lambda h, l=xap[:, sub:sub + 128], r=wgs[:, kc * 8:(kc + 1) * 8],
                      o=pg[:, 0:8], s=(kc == 0), e=(kc == NCHUNK - 1 and cfg.gb is None):
                      h.matmul(o, l, r, start=s, stop=e), inc=(1 if (kc == NCHUNK - 1 and cfg.gb is None) else 0))
        if cfg.gb is not None:
            p.wait("t", tgb)
            tm = p.op("t", lambda h: h.matmul(pg[:, 0:8], ctx.ones_row_bf[0:1, :], S["GB"][0:1, :],
                                              start=False, stop=True), inc=1)
        g8 = GSC[:, 0:8]
        mx = GSC[:, 8:9]
        nmx = GSC[:, 9:10]
        sm = GSC[:, 10:11]
        rs = GSC[:, 11:12]
        mx2 = GSC[:, 12:13]
        den = GSC[:, 13:14]
        w1c = GSC[:, 14:15]
        w2c = GSC[:, 15:16]
        oh1 = GSC[:, 16:24]
        msk = GSC[:, 24:32]
        p.wait("v", tm)

        def vs(fn):
            tk = p.op("v", fn, inc=1)
            p.wait("v", tk, force=True)
            return tk
        vs(lambda h: h.tensor_copy(g8, pg[:, 0:8]))
        vs(lambda h: h.reduce_max(mx, g8, AX.X))
        tvn = vs(lambda h: h.tensor_scalar(nmx, mx, -1.0, None, ALU.mult))
        p.wait("sc", tvn)
        tse = p.op("sc", lambda h: h.activation(g8, g8, AF.Exp, bias=nmx, scale=1.0), inc=1)
        p.wait("v", tse)
        vs(lambda h: h.reduce_sum(sm, g8, AX.X))
        vs(lambda h: h.reciprocal(rs, sm))
        vs(lambda h: h.tensor_scalar(g8, g8, rs, None, ALU.mult))
        vs(lambda h: h.reduce_max(mx, g8, AX.X))
        vs(lambda h: h.tensor_scalar(oh1, g8, mx, None, ALU.is_ge))
        vs(lambda h: h.tensor_tensor(msk, g8, oh1, ALU.mult))
        vs(lambda h: h.tensor_tensor(msk, g8, msk, ALU.subtract))
        vs(lambda h: h.reduce_max(mx2, msk, AX.X))
        if renorm:
            vs(lambda h: h.tensor_tensor(den, mx, mx2, ALU.add))
            vs(lambda h: h.tensor_scalar(den, den, 1e-8, None, ALU.add))
            vs(lambda h: h.reciprocal(rs, den))
            vs(lambda h: h.tensor_tensor(w1c, mx, rs, ALU.mult))
            vs(lambda h: h.tensor_tensor(w2c, mx2, rs, ALU.mult))
        else:
            vs(lambda h: h.tensor_copy(w1c, mx))
            vs(lambda h: h.tensor_copy(w2c, mx2))
        vs(lambda h: h.tensor_scalar(msk, msk, mx2, None, ALU.is_ge))
        vs(lambda h: h.tensor_scalar(oh1, oh1, w1c, None, ALU.mult))
        vs(lambda h: h.tensor_scalar(msk, msk, w2c, None, ALU.mult))
        tvc = vs(lambda h, o=CMB[:, tt * 8:(tt + 1) * 8]:
                 h.tensor_tensor(o, oh1, msk, ALU.add))
        p.wait("t", p.tok("v"))
    p.barrier()

    for mega in range(2):
        tm0 = mega * 2048
        for kc in range(NCHUNK):
            for half in range(4):
                xap, tok = cfg.src(kc, tm0 + half * 512)
                p.wait("v", tok)
                p.op("v", lambda h, o=XMB[:, kc * 2048 + half * 512: kc * 2048 + half * 512 + 512], i=xap:
                     h.tensor_copy(o, i), inc=(1 if half == 3 else 0))
        p.barrier()
        tr_tok = None
        for tl in range(16):
            tt = mega * 16 + tl
            p.wait("t", tr_tok)
            ttr = p.op("t", lambda h, i=CMB[:, tt * 8:(tt + 1) * 8], o=ctx.pbbf[0:8, 0:128]:
                       h.transpose(o, i, S["IDENT"][:, :]), inc=1)
            p.wait("sc", ttr)
            tr_tok = p.op("sc", lambda h, o=WTR[0:8, tl * 128:(tl + 1) * 128], i=ctx.pbbf[0:8, 0:128]:
                          h.activation(o, i, AF.Copy), inc=1)
        p.barrier()
        xm_tok = p.tok("v")
        wbc_tok = {}
        E_LIST = [int(x) for x in _os.environ.get("MOE_EXPERTS", "0,1,2,3,4,5,6,7").split(",")]
        for ei, e in enumerate(E_LIST):
            p.barrier()
            tw1 = [p.dma("g", W1S[:, kc * 2048:(kc + 1) * 2048], cfg.w1[e, kc * 128:(kc + 1) * 128, :]) for kc in range(NCHUNK)]
            tw2 = [p.dma("g", W2S[:, hc * 1024:(hc + 1) * 1024], cfg.w2[e, hc * 128:(hc + 1) * 128, :]) for hc in range(16)]
            if cfg.b1 is not None:
                tb1 = p.dma("g", S["B1E"][:, :], cfg.b1[e, :, :])
            p.barrier()
            p.wait("t", xm_tok)
            for tc in range(4):
                sl = tc % 2
                tbc = p.op("t", lambda h, l=S["SEL8"][0:8, e * 128:(e + 1) * 128],
                           r=WTR[0:8, tc * 512: tc * 512 + 512]:
                           h.matmul(ctx.pb[5][:, :], l, r, start=True, stop=True), inc=1)
                p.wait("sc", tbc)
                p.wait("sc", wbc_tok.get(sl))
                twb = p.op("sc", lambda h, o=WBC[:, sl * 512: sl * 512 + 512]:
                           h.activation(o, ctx.pb[5][:, :], AF.Copy), inc=1)
                p.wait("t", twb)
                for fc in range(16):
                    b = fc % 2
                    pb = ctx.pb[b]
                    for kc in range(NCHUNK):
                        tm = p.op("t", lambda h, l=W1S[:, kc * 2048 + fc * 128: kc * 2048 + fc * 128 + 128],
                                  r=XMB[:, kc * 2048 + tc * 512: kc * 2048 + tc * 512 + 512],
                                  o=pb, s=(kc == 0), e2=(kc == NCHUNK - 1):
                                  h.matmul(o[:, :], l, r, start=s, stop=e2), inc=(1 if kc == NCHUNK - 1 else 0))
                    hs = HS[:, fc * 512:(fc + 1) * 512]
                    p.wait("v", tm)
                    p.wait("v", twb)
                    if act == "relu2":
                        p.op("v", lambda h, o=hs, i=pb: h.tensor_scalar_max(o, i[:, :], 0.0))
                        p.op("v", lambda h, o=S["HTMP"][:, 0:512], a=hs, w=WBC[:, sl * 512: sl * 512 + 512]:
                             h.tensor_tensor(o, a, w, ALU.mult))
                        tve = p.op("v", lambda h, o=hs, a=S["HTMP"][:, 0:512], b2=hs:
                                   h.tensor_tensor(o, a, b2, ALU.mult), inc=1)
                    else:
                        p.op("v", lambda h, o=hs, i=pb, b1a=S["B1E"][:, fc:fc + 1]:
                             h.tensor_scalar(o, i[:, :], b1a, 0.0, ALU.add, ALU.max))
                        tve = p.op("v", lambda h, o=hs, w=WBC[:, sl * 512: sl * 512 + 512]:
                                   h.tensor_tensor(o, o, w, ALU.mult), inc=1)
                    p.wait("t", tve)
                    wbc_tok[sl] = tve
                for dc in range(NCHUNK):
                    b = 2 + dc % 2
                    pb = ctx.pb[b]
                    first_extra = (ei == 0 and cfg.b2 is not None)
                    for hc in range(16):
                        tm = p.op("t", lambda h, l=W2S[:, hc * 1024 + dc * 128: hc * 1024 + dc * 128 + 128],
                                  r=HS[:, hc * 512:(hc + 1) * 512],
                                  o=pb, s=(hc == 0), e2=(hc == 15 and not first_extra):
                                  h.matmul(o[:, :], l, r, start=s, stop=e2),
                                  inc=(1 if (hc == 15 and not first_extra) else 0))
                    if first_extra:
                        tm = p.op("t", lambda h, l=S["B2S"][0:8, dc * 128:(dc + 1) * 128],
                                  r=WTR[0:8, tc * 512: tc * 512 + 512], o=pb:
                                  h.matmul(o[:, :], l, r, start=False, stop=True), inc=1)
                    ya = YA[:, dc * 2048 + tc * 512: dc * 2048 + tc * 512 + 512]
                    p.wait("v", tm)
                    if ei == 0:
                        tve = p.op("v", lambda h, o=ya, i=pb: h.tensor_copy(o, i[:, :]), inc=1)
                    else:
                        tve = p.op("v", lambda h, o=ya, i=pb: h.tensor_tensor(o, o, i[:, :], ALU.add), inc=1)
                    p.wait("t", tve)
        finish(mega, tm0)
        p.barrier()


MOE_SBUF = [
    ("XMB", (128, 8 * 2048), BF16), ("W1S", (128, 8 * 2048), BF16), ("W2S", (128, 16 * 1024), BF16),
    ("HS", (128, 16 * 512), BF16), ("YA", (128, 8 * 2048), F32),
    ("CMB", (128, 8 * 32), BF16), ("WTR", (8, 2048), BF16), ("WBC", (128, 1024), BF16),
    ("GSC", (128, 32), F32), ("WGS", (128, 64), BF16), ("GB", (1, 8), BF16),
    ("B1E", (128, 16), F32), ("B2S", (8, 1024), BF16), ("HTMP", (128, 512), BF16),
    ("SEL8", (8, 8 * 128), BF16), ("IDENT", (128, 128), BF16),
    ("SRCB", (128, 4096), BF16), ("FSTG", (128, 1024), BF16), ("YSTG", (128, 512), F32),
]


def build_g4(shared, percore0, dbg=False):
    import contextlib
    nc = bass.Bass("TRN2", target_bir_lowering=False)
    stack = contextlib.ExitStack()
    nc._dev_stack = stack
    aps = _declare(nc, shared, percore0, ["m0_wg", "m0_w1", "m0_w2", "sel8", "ident", "x0"],
                   extra={"xnT_in": ([D, T], BF16), "x2T_in": ([D, T], BF16)})
    outs4 = {"xnewT": ([D, T], F32)}
    if dbg:
        outs4.update({"d_cmb": ([128, 256], BF16), "d_wtr": ([8, T], BF16),
                      "d_ya": ([128, 16384], F32), "d_hs": ([128, 16 * 512], BF16),
                      "d_xmb": ([128, 16384], BF16), "d_w1s": ([128, 16384], BF16)})
    aps.update(_outs(nc, outs4))
    with nc.Block() as block:
        ctx = Ctx(nc, stack, aps, consts_vals=(), sbuf_spec=MOE_SBUF)
        ctx.setup()
        p = ctx.p
        S = ctx.sb
        p.dma("g", S["SEL8"][:, :], aps["sel8"][:, :])
        p.dma("g", S["IDENT"][:, :], aps["ident"][:, :])
        p.barrier()
        src = SrcStream(ctx, aps["xnT_in"], S["SRCB"])
        cfg = AttnCfg(src=src, wg=aps["m0_wg"], gb=None, w1=aps["m0_w1"], w2=aps["m0_w2"],
                      b1=None, b2=None, act="relu2", renorm=True)

        def finish(mega, tm0):
            p.barrier()
            if dbg and mega == 0:
                p.dma("sy", aps["d_cmb"][:, :], S["CMB"][:, :])
                p.dma("sy", aps["d_wtr"][:, :], S["WTR"][:, :])
                p.dma("sy", aps["d_ya"][:, :], S["YA"][:, :])
                p.dma("sy", aps["d_hs"][:, :], S["HS"][:, :])
                p.dma("sy", aps["d_xmb"][:, :], S["XMB"][:, :])
                p.dma("sy", aps["d_w1s"][:, :], S["W1S"][:, :])
                p.barrier()
            for dc in range(NCHUNK):
                for tc in range(4):
                    ya = S["YA"][:, dc * 2048 + tc * 512: dc * 2048 + tc * 512 + 512]
                    gs = S["FSTG"][:, (tc % 2) * 512:(tc % 2) * 512 + 512]
                    t0 = tm0 + tc * 512
                    tls = p.dma("sy", S["HTMP"][:, 0:512], aps["x2T_in"][dc * 128:(dc + 1) * 128, t0:t0 + 512])
                    tlx = p.dma("sy", S["XMB"].bitcast(F32)[:, 0:512], aps["x0"][dc * 128:(dc + 1) * 128, t0:t0 + 512])
                    tsl = p.op("sc", lambda h, o=gs, i=ya: h.activation(o, i, AF.Silu), inc=1)
                    p.wait("v", tsl)
                    p.wait("v", tls)
                    p.wait("v", tlx)
                    p.op("v", lambda h, o=gs, s2=S["HTMP"][:, 0:512]: h.tensor_tensor(o, o, s2, ALU.mult))
                    tvf = p.op("v", lambda h, o=S["XMB"].bitcast(F32)[:, 512:1024], a=S["XMB"].bitcast(F32)[:, 0:512], g=gs:
                               h.tensor_tensor(o, a, g, ALU.add), inc=1)
                    p.wait("sy", tvf)
                    p.dma("sy", aps["xnewT"][dc * 128:(dc + 1) * 128, t0:t0 + 512], S["XMB"].bitcast(F32)[:, 512:1024])
                    p.barrier()

        emit_moe(ctx, cfg, finish)
        _finish(ctx)
        ctx.p.build(block)
    return nc


def build_g5(shared, percore0, key_in="xnewT_in", key_out="xn1T"):
    import contextlib
    nc = bass.Bass("TRN2", target_bir_lowering=False)
    stack = contextlib.ExitStack()
    nc._dev_stack = stack
    aps = _declare(nc, shared, percore0, [], extra={key_in: ([D, T], F32)})
    aps.update(_outs(nc, {key_out: ([D, T], BF16)}))
    sbuf_spec = [
        ("ln_xstg", (128, 1024), F32), ("ln_sqt", (128, 1024), BF16),
        ("ln_rows", (1, 4096), F32), ("ln_abc", (128, 512), BF16), ("ln_bbc", (128, 512), BF16),
        ("XN", (128, NCHUNK * T), BF16),
    ]
    with nc.Block() as block:
        ctx = Ctx(nc, stack, aps, consts_vals=(), sbuf_spec=sbuf_spec)
        ctx.setup()
        emit_ln(ctx, aps[key_in], ctx.sb["XN"], out_dram=aps[key_out])
        _finish(ctx)
        ctx.p.build(block)
    return nc


def build_g6(shared, percore0):
    import contextlib
    nc = bass.Bass("TRN2", target_bir_lowering=False)
    stack = contextlib.ExitStack()
    nc._dev_stack = stack
    aps = _declare(nc, shared, percore0, ["a1_wq", "a1_wk", "a1_wv", "a1_wo", "ident", "selp8", "maskh"],
                   extra={"xn1T_in": ([D, T], BF16)})
    aps.update(_outs(nc, {"s1T": ([D, T], BF16)}))
    spec = _attn_sbuf_spec(8, 4, 8, 2, 8, 512, 512, rope=False, nmask=0, latent=False, scaled=False, bcb=False)
    with nc.Block() as block:
        ctx = Ctx(nc, stack, aps, consts_vals=(-16.0,), sbuf_spec=spec)
        ctx.setup()
        p = ctx.p
        S = ctx.sb
        p.dma("g", S["SELP"][0:8, :], aps["selp8"][:, :])
        _load_attn_consts(ctx, aps)
        p.barrier()
        load_rows(ctx, aps["a1_wk"], S["WK"], 8, 512)
        load_rows(ctx, aps["a1_wv"], S["WV"], 8, 512)
        load_rows(ctx, aps["a1_wo"], S["WOS"], 4, 1024)
        p.barrier()
        src = SrcStream(ctx, aps["xn1T_in"], S["SRCB"])
        cfg = AttnCfg(nheads=8, kv_dup=False, nkv_chunks=4, nkv_heads=8, rope=None, qk="l2",
                      wq=aps["a1_wq"], wq_bf=False, wk_cols=512, wv_cols=512,
                      exp_premul=16.0, exp_bias=-16.0, sums_clip=1e-6,
                      mask_sched=_sched_full, src=src, res=None)
        emit_attn(ctx, cfg, aps["s1T"])
        _finish(ctx)
        ctx.p.build(block)
    return nc


def build_g7(shared, percore0, dbg=False):
    import contextlib
    nc = bass.Bass("TRN2", target_bir_lowering=False)
    stack = contextlib.ExitStack()
    nc._dev_stack = stack
    aps = _declare(nc, shared, percore0,
                   ["f1_wq", "f1_wk", "f1_wv", "f1_wo", "f1_bq", "f1_bk", "f1_bv", "f1_bo",
                    "f1_g1", "f1_b1n", "ident", "selp16", "maskh"],
                   extra={"xn1T_in": ([D, T], BF16)})
    outs7 = {"xffnT": ([D, T], BF16)}
    if dbg:
        outs7["zT_d"] = ([D, T], BF16)
        outs7["d_scr"] = ([128, 14336], BF16)
        outs7["d_sums"] = ([1, 1024], F32)
        outs7["d_q0"] = ([128, 4096], BF16)
        outs7["d_wq"] = ([128, 2048], BF16)
    aps.update(_outs(nc, outs7))
    zT = aps["zT_d"] if dbg else nc.dram_tensor("zT", [D, T], BF16, kind="Internal").ap()
    spec = _attn_sbuf_spec(16, 8, 16, 2, 8, 1024, 1024, rope=False, nmask=0, latent=False,
                           wo_stream=True, qk=False, scaled=False, wv_shared=True, bcb=False)
    spec += [("BQ", (128, 8), F32), ("BK", (128, 8), F32), ("BV", (128, 8), F32), ("BO", (128, 8), F32),
             ("G1", (128, 8), F32), ("B1N", (128, 8), F32)]
    with nc.Block() as block:
        ctx = Ctx(nc, stack, aps, consts_vals=(-8.0,), sbuf_spec=spec)
        ctx.setup()
        p = ctx.p
        S = ctx.sb
        scr = S["SCR"]
        S["ln_xstg"] = scr.bitcast(F32)[:, 0:1024]
        S["ln_sqt"] = scr[:, 2048:3072]
        S["ln_abc"] = scr[:, 3072:3584]
        S["ln_bbc"] = scr[:, 3584:4096]
        S["ln_rows"] = scr.bitcast(F32)[0:1, 3072:7168]
        for nm, key in [("BQ", "f1_bq"), ("BK", "f1_bk"), ("BV", "f1_bv"), ("BO", "f1_bo"),
                        ("G1", "f1_g1"), ("B1N", "f1_b1n")]:
            p.dma("g", S[nm][:, :], aps[key][:, :])
        _load_attn_consts(ctx, aps)
        p.barrier()
        load_rows(ctx, aps["f1_wk"], S["WK"], 8, 1024)
        p.barrier()
        src = SrcStream(ctx, aps["xn1T_in"], S["SRCB"])
        cfg = AttnCfg(nheads=16, kv_dup=False, nkv_chunks=8, nkv_heads=16, rope=None, qk=None,
                      wq=aps["f1_wq"], wq_bf=False, wk_cols=1024, wv_cols=1024,
                      bq=S["BQ"], bk=S["BK"], bv=S["BV"], bo=S["BO"],
                      exp_scale_const=0.125, exp_bias=-8.0, wo_stream=aps["f1_wo"],
                      load_wv=lambda: load_rows(ctx, aps["f1_wv"], S["WK"], 8, 1024),
                      mask_sched=_sched_full, src=src, res=lambda fc, q0: src(fc, q0))
        if dbg:
            cfg.dbg = {"qc": 0, "aps": {"scr": (S["SCR"][:, :], aps["d_scr"][:, :]),
                                        "sums": (S["sumsrow"][:, :], aps["d_sums"][:, :])},
                       "q_ap": aps["d_q0"][:, :], "wq_ap": aps["d_wq"][:, :]}
        emit_attn(ctx, cfg, zT)
        p.barrier()
        emit_ln(ctx, zT, S["KTS"], src_bf16=True, gamma=S["G1"], beta=S["B1N"], out_dram=aps["xffnT"])
        _finish(ctx)
        ctx.p.build(block)
    return nc


def build_g8(shared, percore0):
    import contextlib
    nc = bass.Bass("TRN2", target_bir_lowering=False)
    stack = contextlib.ExitStack()
    nc._dev_stack = stack
    aps = _declare(nc, shared, percore0,
                   ["f1_gw", "f1_gb", "f1_w1", "f1_b1c", "f1_w2", "f1_b2", "f1_g2", "f1_b2n",
                    "sel8", "ident"],
                   extra={"xffnT_in": ([D, T], BF16), "s1T_in": ([D, T], BF16),
                          "xnewT_in": ([D, T], F32)})
    outs_f = {"outT": ([D, T], F32)}
    if dbg:
        outs_f.update({"d_cmb4": ([128, 256], BF16), "d_wtr4": ([8, 2048], BF16),
                       "d_ya4": ([128, 16384], F32)})
    aps.update(_outs(nc, outs_f))
    spec = MOE_SBUF + [("G2N", (128, 8), F32), ("B2N", (128, 8), F32)]
    with nc.Block() as block:
        ctx = Ctx(nc, stack, aps, consts_vals=(), sbuf_spec=spec)
        ctx.setup()
        p = ctx.p
        S = ctx.sb
        p.dma("g", S["SEL8"][:, :], aps["sel8"][:, :])
        p.dma("g", S["IDENT"][:, :], aps["ident"][:, :])
        p.dma("g", S["B2S"][:, :], aps["f1_b2"][:, :])
        p.dma("g", S["G2N"][:, :], aps["f1_g2"][:, :])
        p.dma("g", S["B2N"][:, :], aps["f1_b2n"][:, :])
        p.dma("g", S["GB"][:, :], aps["f1_gb"][:, :])
        p.barrier()
        src = SrcStream(ctx, aps["xffnT_in"], S["SRCB"])
        cfg = AttnCfg(src=src, wg=aps["f1_gw"], gb=aps["f1_gb"], w1=aps["f1_w1"], w2=aps["f1_w2"],
                      b1=aps["f1_b1c"], b2=aps["f1_b2"], act="relu", renorm=False)
        W2F = S["W2S"].bitcast(F32)
        rows = lambda i: W2F[0:1, 1024 + i * 512: 1024 + (i + 1) * 512]
        abc = S["W2S"][:, 0:512]
        bbc = S["W2S"][:, 512:1024]
        sqt = S["W2S"][:, 1024 + 8192:1024 + 8192 + 512]
        W1F = S["W1S"].bitcast(F32)

        def finish(mega, tm0):
            p.barrier()
            for tc in range(4):
                t0 = tm0 + tc * 512
                tz = None
                for dc in range(NCHUNK):
                    xsl = S["XMB"][:, dc * 2048 + tc * 512: dc * 2048 + tc * 512 + 512]
                    ya = S["YA"][:, dc * 2048 + tc * 512: dc * 2048 + tc * 512 + 512]
                    tz = p.op("v", lambda h, o=xsl, a=ya, b=xsl: h.tensor_tensor(o, a, b, ALU.add), inc=1)
                t_done = []
                for dc in range(NCHUNK):
                    zsl = S["XMB"][:, dc * 2048 + tc * 512: dc * 2048 + tc * 512 + 512]
                    if dc >= 1:
                        p.wait("v", t_done[dc - 1])
                    tv = p.op("v", lambda h, o=sqt, i=zsl: h.tensor_tensor(o, i, i, ALU.mult), inc=1)
                    p.wait("t", tv)
                    p.op("t", lambda h, b=zsl, s=(dc == 0): h.matmul(ctx.pb[0][0:1, :], ctx.ones_bf[:, :], b, start=s, stop=(dc == 7)))
                    tm = p.op("t", lambda h, b=sqt, s=(dc == 0): h.matmul(ctx.pb[1][0:1, :], ctx.ones_bf[:, :], b, start=s, stop=(dc == 7)), inc=1)
                    t_done.append(tm)
                mu, m2, var, sd, rstd, brow = (rows(i) for i in range(6))
                p.wait("v", t_done[-1])
                p.op("v", lambda h: h.tensor_scalar(mu, ctx.pb[0][0:1, :], 1.0 / D, None, ALU.mult))
                p.op("v", lambda h: h.tensor_scalar(m2, ctx.pb[1][0:1, :], 1.0 / D, None, ALU.mult))
                p.op("v", lambda h: h.tensor_tensor(sd, mu, mu, ALU.mult))
                p.op("v", lambda h: h.tensor_tensor(var, m2, sd, ALU.subtract))
                tvv = p.op("v", lambda h: h.tensor_scalar(var, var, 1e-5, None, ALU.add), inc=1)
                p.wait("sc", tvv)
                tss = p.op("sc", lambda h: h.activation(sd, var, AF.Sqrt), inc=1)
                p.wait("v", tss)
                trc = p.op("v", lambda h: h.reciprocal(rstd, sd), inc=1)
                p.op("v", lambda h: h.tensor_scalar(sd, mu, -1.0, None, ALU.mult))
                p.wait("v", trc, force=True)
                tv2 = p.op("v", lambda h: h.tensor_tensor(brow, sd, rstd, ALU.mult), inc=1)
                p.wait("t", tv2)
                tb1 = p.op("t", lambda h: h.matmul(ctx.pb[2][:, :], ctx.ones_row[:, :], rstd, start=True, stop=True), inc=1)
                tb2 = p.op("t", lambda h: h.matmul(ctx.pb[3][:, :], ctx.ones_row[:, :], brow, start=True, stop=True), inc=1)
                p.wait("sc", tb2)
                p.op("sc", lambda h: h.activation(abc, ctx.pb[2][:, :], AF.Copy))
                e2 = p.op("sc", lambda h: h.activation(bbc, ctx.pb[3][:, :], AF.Copy), inc=1)
                for dc in range(NCHUNK):
                    zsl = S["XMB"][:, dc * 2048 + tc * 512: dc * 2048 + tc * 512 + 512]
                    s1s = S["W1S"][:, 0:512]
                    xns = W1F[:, 512:1024]
                    ots = W1F[:, 1024:1536]
                    gsl = S["FSTG"][:, 0:512]
                    tl1 = p.dma("sy", s1s, aps["s1T_in"][dc * 128:(dc + 1) * 128, t0:t0 + 512])
                    tl2 = p.dma("sy", xns, aps["xnewT_in"][dc * 128:(dc + 1) * 128, t0:t0 + 512])
                    p.wait("v", e2)
                    p.op("v", lambda h, a=zsl: h.tensor_tensor(a, a, abc, ALU.mult))
                    p.op("v", lambda h, a=zsl: h.tensor_tensor(a, a, bbc, ALU.add))
                    tvz = p.op("v", lambda h, a=zsl, g=S["G2N"][:, dc:dc + 1], b=S["B2N"][:, dc:dc + 1]:
                               h.tensor_scalar(a, a, g, b, ALU.mult, ALU.add), inc=1)
                    p.wait("sc", tvz)
                    tsl = p.op("sc", lambda h, o=gsl, i=zsl: h.activation(o, i, AF.Silu), inc=1)
                    p.wait("v", tsl)
                    p.wait("v", tl1)
                    p.wait("v", tl2)
                    p.op("v", lambda h, o=gsl, s2=s1s: h.tensor_tensor(o, o, s2, ALU.mult))
                    tvf = p.op("v", lambda h, o=ots, a=xns, g=gsl: h.tensor_tensor(o, a, g, ALU.add), inc=1)
                    p.wait("sy", tvf)
                    td = p.dma("sy", aps["outT"][dc * 128:(dc + 1) * 128, t0:t0 + 512], ots)
                    p.wait("v", td)
                p.barrier()

        emit_moe(ctx, cfg, finish)
        _finish(ctx)
        ctx.p.build(block)
    return nc



_G_INPUTS = {
    "g1": [],
    "g2": ["g_wq", "g_kvd", "g_kup", "g_vup", "g_wo", "w1n", "ropeg", "cmask", "ident", "selp16", "maskh"],
    "g3": ["l_wq", "l_wk", "l_wv", "l_wo", "w2n", "ropel", "lmask", "ident", "selp16", "maskh"],
    "g4": ["m0_wg", "m0_w1", "m0_w2", "sel8", "ident"],
    "g5": [],
    "g6": ["a1_wq", "a1_wk", "a1_wv", "a1_wo", "ident", "selp8", "maskh"],
    "g7": ["f1_wq", "f1_wk", "f1_wv", "f1_wo", "f1_bq", "f1_bk", "f1_bv", "f1_bo",
           "f1_g1", "f1_b1n", "ident", "selp16", "maskh"],
    "g8": ["f1_gw", "f1_gb", "f1_w1", "f1_b1c", "f1_w2", "f1_b2", "f1_g2", "f1_b2n", "sel8", "ident"],
}


def _run(nc, shared, names, extra_percore, n=8):
    maps = []
    for b in range(n):
        m = {k: shared[k] for k in names}
        m.update(extra_percore[b])
        maps.append(m)
    res = run_bass_kernel_spmd(nc, maps, core_ids=list(range(n)))
    return res.results


def kernel(**inputs):
    return kernel_fused(**inputs)


def kernel_multilaunch(**inputs):
    shared, percore = prep_inputs(inputs)
    B = 8
    r1 = _run(build_g1(shared, percore[0]), shared, _G_INPUTS["g1"],
              [{"x0": percore[b]["x0"]} for b in range(B)])
    r2 = _run(build_g2(shared, percore[0]), shared, _G_INPUTS["g2"],
              [{"xnT_in": r1[b]["xnT"], "rs1_in": r1[b]["rs1"]} for b in range(B)])
    r3 = _run(build_g3(shared, percore[0]), shared, _G_INPUTS["g3"],
              [{"x1T_in": r2[b]["x1T"]} for b in range(B)])
    r4 = _run(build_g4(shared, percore[0]), shared, _G_INPUTS["g4"],
              [{"x0": percore[b]["x0"], "xnT_in": r1[b]["xnT"], "x2T_in": r3[b]["x2T"]} for b in range(B)])
    r5 = _run(build_g5(shared, percore[0]), shared, _G_INPUTS["g5"],
              [{"xnewT_in": r4[b]["xnewT"]} for b in range(B)])
    r6 = _run(build_g6(shared, percore[0]), shared, _G_INPUTS["g6"],
              [{"xn1T_in": r5[b]["xn1T"]} for b in range(B)])
    r7 = _run(build_g7(shared, percore[0]), shared, _G_INPUTS["g7"],
              [{"xn1T_in": r5[b]["xn1T"]} for b in range(B)])
    r8 = _run(build_g8(shared, percore[0]), shared, _G_INPUTS["g8"],
              [{"xffnT_in": r7[b]["xffnT"], "s1T_in": r6[b]["s1T"], "xnewT_in": r4[b]["xnewT"]}
               for b in range(B)])
    out = np.stack([np.ascontiguousarray(r8[b]["outT"].T) for b in range(B)])
    return out.astype(np.float32)



def build_fused(shared, percore0, dbg=False):
    import contextlib
    nc = bass.Bass("TRN2", target_bir_lowering=False)
    stack = contextlib.ExitStack()
    nc._dev_stack = stack
    names = (_G_INPUTS["g2"] + _G_INPUTS["g3"] + _G_INPUTS["g4"] + _G_INPUTS["g6"]
             + _G_INPUTS["g7"] + _G_INPUTS["g8"])
    names = sorted(set(names))
    aps = _declare(nc, shared, percore0, names, extra={"x0": ([D, T], F32)})
    outs_f = {"outT": ([D, T], F32)}
    if dbg:
        outs_f.update({"d_cmb4": ([128, 256], BF16), "d_wtr4": ([8, 2048], BF16),
                       "d_ya4": ([128, 16384], F32)})
    aps.update(_outs(nc, outs_f))
    dram = {}
    for nm, dt in [("xnT", BF16), ("x1T", BF16), ("x2T", BF16), ("xnewT", F32),
                   ("xn1T", BF16), ("s1T", BF16), ("xffnT", BF16), ("zT", BF16),
                   ("wqbf_g", BF16), ("wqbf_l", BF16)]:
        shape = [D, 2048] if nm.startswith("wqbf") else [D, T]
        if dbg and not nm.startswith("wqbf"):
            dram[nm] = nc.dram_tensor(nm, shape, dt, kind="ExternalOutput").ap()
        else:
            dram[nm] = nc.dram_tensor(nm, shape, dt, kind="Internal").ap()

    spec = [
        ("AR1", (128, 32768), BF16), ("AR2", (128, 33280), BF16),
        ("AR3", (128, 18432), BF16), ("AR4", (128, 8192), BF16),
        ("ROWS", (1, 2048), F32), ("RS", (1, T), BF16),
        ("MASKS", (128, 3072), BF16), ("MASKH", (128, 3), BF16), ("IDENT", (128, 128), BF16),
        ("SELPA", (128, 1024), BF16), ("nskt", (128, 256), F32),
        ("fscr", (128, 192), F32),
    ]
    with nc.Block() as block:
        ctx = Ctx(nc, stack, aps, consts_vals=(-8.0, -16.0), sbuf_spec=spec)
        ctx.setup()
        p = ctx.p
        S = ctx.sb
        AR1, AR2, AR3, AR4 = S["AR1"], S["AR2"], S["AR3"], S["AR4"]
        ROWS = S["ROWS"]

        def bind_common():
            S["SCR"] = AR3[:, 0:14336]
            S["SRCB"] = AR3[:, 14336:18432]
            S["WK"] = AR3[:, 0:8192]
            S["WV"] = AR3[:, 8192:12288]
            S["VSTG"] = AR3[:, 12288:13312]
            S["WQB"] = AR3[:, 12288:14336]
            S["OSTG"] = AR3[:, 4096:5120]
            S["WOB"] = AR3[:, 12288:14336]
            S["bcbuf"] = AR3[:, 5120:6144]
            S["sumsrow"] = ROWS[0:1, 0:1024]
            S["recips"] = ROWS[0:1, 1024:2048]
            S["wstage"] = AR3.bitcast(F32)[:, 5120:7168]
            S["ln_xstg"] = AR3.bitcast(F32)[:, 5120:6144]
            S["ln_rows"] = ROWS[0:1, 0:2048]
            S["WTR"] = AR3[0:8, 16384:18432]
            S["rqrows"] = AR3[0:16, 12288:12800]
            S["GSC"] = S["fscr"][:, 0:32]
            S["W1N"] = S["fscr"][:, 160:168]
            S["W2N"] = S["fscr"][:, 168:176]
            for i, nm in enumerate(["BQ", "BK", "BV", "BO", "G1", "B1N", "G2N", "B2N"]):
                S[nm] = S["nskt"][:, i * 8:(i + 1) * 8]
            S["SELP"] = S["SELPA"][0:16, :]
            S["FSTG"] = S["SELPA"][:, :]
            S["WBC"] = S["FSTG"]
            S["SEL8"] = S["MASKS"][0:8, 0:1024]
            S["B2S"] = S["MASKS"][0:8, 1024:2048]
            S["HTMP"] = S["MASKS"][:, 2048:2560]
            S["CMB"] = AR2[:, 32768:33024]
            S["WGS"] = AR2[:, 33024:33088]
            S["GB"] = AR2[0:1, 33088:33096]
            S["B1E"] = AR2.bitcast(F32)[:, 16548:16564]
            S["wstageb"] = AR3[:, 8192:10240]
            S["ln_sqt"] = AR3[:, 0:1024]
            S["ln_abc"] = AR3[:, 1024:1536]
            S["ln_bbc"] = AR3[:, 1536:2048]
            S["ROPE"] = AR4[:, :]
            S["HS"] = AR4[:, :]

        bind_common()
        p.dma("g", S["IDENT"][:, :], aps["ident"][:, :])
        p.dma("g", S["MASKH"][:, :], aps["maskh"][:, :])
        p.dma("g", S["SELP"][:, :], aps["selp16"][:, :])
        p.dma("g", S["W1N"][:, :], aps["w1n"][:, :])
        p.dma("g", S["W2N"][:, :], aps["w2n"][:, :])
        p.barrier()

        S["XN"] = AR1[:, :]
        emit_ln(ctx, aps["x0"], S["XN"], rs1_row=S["RS"], out_dram=dram["xnT"])
        p.barrier()

        S["KTS"] = AR1[:, 0:16384]
        S["LATN"] = AR1[:, 16384:24576]
        S["WD"] = AR1[:, 24576:26624]
        S["VTOK"] = AR2[:, 0:8320]
        emit_w_prep(ctx, aps["g_wq"], dram["wqbf_g"], 8, 2048, S["W1N"])
        load_rows(ctx, aps["g_kvd"], S["WD"], 8, 256, scale=S["W1N"])
        load_rows(ctx, aps["g_kup"], S["WK"], 2, 1024)
        for mi in range(4):
            p.dma("sy", S["MASKS"][:, mi * 512:(mi + 1) * 512], aps["cmask"][mi * 128:(mi + 1) * 128, :])
        p.dma("g", S["ROPE"][:, :], aps["ropeg"][:, :])
        p.barrier()
        src2 = SrcStream(ctx, dram["xnT"], S["SRCB"])
        cfg2 = AttnCfg(nheads=16, kv_dup=True, nkv_chunks=4, nkv_heads=4, rope="ROPE", qk="rms",
                       wq=dram["wqbf_g"], wq_bf=True, wk_cols=1024, wv_cols=256,
                       latent=dict(scale_row=S["RS"]), exp_premul=0.125, exp_bias=-8.0,
                       mask_sched=_sched_causal, src=src2, res=lambda fc, q0: src2(fc, q0),
                       load_wv=lambda: load_rows(ctx, aps["g_vup"], S["WV"], 2, 256),
                       wo_stream=aps["g_wo"])
        emit_attn(ctx, cfg2, dram["x1T"])
        p.barrier()

        S["KTS"] = AR1[:, 0:16384]
        S["VTOK"] = AR2[:, 0:8320]
        emit_w_prep(ctx, aps["l_wq"], dram["wqbf_l"], 8, 2048, S["W2N"])
        load_rows(ctx, aps["l_wk"], S["WK"], 8, 1024, scale=S["W2N"])
        for mi in range(6):
            p.dma("sy", S["MASKS"][:, mi * 512:(mi + 1) * 512], aps["lmask"][mi * 128:(mi + 1) * 128, :])
        p.dma("g", S["ROPE"][:, :], aps["ropel"][:, :])
        p.barrier()
        src3 = SrcStream(ctx, dram["x1T"], S["SRCB"])
        emit_rms_row_stream(ctx, src3, S["RS"], eps=1e-6)
        cfg3 = AttnCfg(nheads=16, kv_dup=True, nkv_chunks=4, nkv_heads=4, rope="ROPE", qk="rms",
                       wq=dram["wqbf_l"], wq_bf=True, wk_cols=1024, wv_cols=256,
                       v_scale_row=S["RS"], exp_premul=0.125, exp_bias=-8.0,
                       mask_sched=_sched_local, src=src3, res=lambda fc, q0: src3(fc, q0),
                       load_wv=lambda: load_rows(ctx, aps["l_wv"], S["WV"], 8, 256, scale=S["W2N"]),
                       wo_stream=aps["l_wo"])
        emit_attn(ctx, cfg3, dram["x2T"])
        p.barrier()

        S["YA"] = AR1.bitcast(F32)[:, 0:16384]
        S["XMB"] = AR2[:, 0:16384]
        S["W1S"] = AR2[:, 16384:32768]
        S["W2S"] = AR3[:, 0:16384]
        p.dma("g", S["SEL8"][:, :], aps["sel8"][:, :])
        p.barrier()
        src4 = SrcStream(ctx, dram["xnT"], S["SRCB"])
        cfg4 = AttnCfg(src=src4, wg=aps["m0_wg"], gb=None, w1=aps["m0_w1"], w2=aps["m0_w2"],
                       b1=None, b2=None, act="relu2", renorm=True)

        def finish4(mega, tm0):
            p.barrier()
            for dc in range(NCHUNK):
                for tc in range(4):
                    ya = S["YA"][:, dc * 2048 + tc * 512: dc * 2048 + tc * 512 + 512]
                    gs = S["FSTG"][:, (tc % 2) * 512:(tc % 2) * 512 + 512]
                    t0 = tm0 + tc * 512
                    tls = p.dma("sy", S["HTMP"][:, 0:512], dram["x2T"][dc * 128:(dc + 1) * 128, t0:t0 + 512])
                    tlx = p.dma("sy", S["XMB"].bitcast(F32)[:, 0:512], aps["x0"][dc * 128:(dc + 1) * 128, t0:t0 + 512])
                    tsl = p.op("sc", lambda h, o=gs, i=ya: h.activation(o, i, AF.Silu), inc=1)
                    p.wait("v", tsl)
                    p.wait("v", tls)
                    p.wait("v", tlx)
                    p.op("v", lambda h, o=gs, s2=S["HTMP"][:, 0:512]: h.tensor_tensor(o, o, s2, ALU.mult))
                    tvf = p.op("v", lambda h, o=S["XMB"].bitcast(F32)[:, 512:1024], a=S["XMB"].bitcast(F32)[:, 0:512], g=gs:
                               h.tensor_tensor(o, a, g, ALU.add), inc=1)
                    p.wait("sy", tvf)
                    p.dma("sy", dram["xnewT"][dc * 128:(dc + 1) * 128, t0:t0 + 512], S["XMB"].bitcast(F32)[:, 512:1024])
                    p.barrier()

        emit_moe(ctx, cfg4, finish4)
        p.barrier()
        if dbg:
            p.dma("sy", aps["d_cmb4"][:, :], S["CMB"][:, :])
            p.dma("sy", aps["d_wtr4"][:, :], S["WTR"][:, :])
            p.dma("sy", aps["d_ya4"][:, :], S["YA"][:, :])
            p.barrier()

        S["XN"] = AR1[:, :]
        emit_ln(ctx, dram["xnewT"], S["XN"], out_dram=dram["xn1T"])
        p.barrier()

        S["KTS"] = AR1[:, 0:16384]
        S["VTOK"] = AR2[:, 0:16640]
        p.dma("g", S["SELP"][0:8, 0:512], aps["selp8"][:, :])
        load_rows(ctx, aps["a1_wk"], S["WK"], 8, 512)
        p.barrier()
        src6 = SrcStream(ctx, dram["xn1T"], S["SRCB"])
        cfg6 = AttnCfg(nheads=8, kv_dup=False, nkv_chunks=4, nkv_heads=8, rope=None, qk="l2",
                       wq=aps["a1_wq"], wq_bf=False, wk_cols=512, wv_cols=512,
                       exp_premul=16.0, exp_bias=-16.0, sums_clip=1e-6,
                       load_wv=lambda: load_rows(ctx, aps["a1_wv"], S["WV"], 8, 512),
                       mask_sched=_sched_full, src=src6, res=None, wo_stream=aps["a1_wo"])
        emit_attn(ctx, cfg6, dram["s1T"])
        p.barrier()

        S["KTS"] = AR1[:, 0:32768]
        S["VTOK"] = AR2[:, 0:33280]
        for nm, key in [("BQ", "f1_bq"), ("BK", "f1_bk"), ("BV", "f1_bv"), ("BO", "f1_bo"),
                        ("G1", "f1_g1"), ("B1N", "f1_b1n")]:
            p.dma("g", S[nm][:, :], aps[key][:, :])
        S["WV"] = S["WK"]
        load_rows(ctx, aps["f1_wk"], S["WK"], 8, 1024)
        p.barrier()
        src7 = SrcStream(ctx, dram["xn1T"], S["SRCB"])
        cfg7 = AttnCfg(nheads=16, kv_dup=False, nkv_chunks=8, nkv_heads=16, rope=None, qk=None,
                       wq=aps["f1_wq"], wq_bf=False, wk_cols=1024, wv_cols=1024,
                       bq=S["BQ"], bk=S["BK"], bv=S["BV"], bo=S["BO"],
                       exp_scale_const=0.125, exp_bias=-8.0, wo_stream=aps["f1_wo"],
                       load_wv=lambda: load_rows(ctx, aps["f1_wv"], S["WK"], 8, 1024),
                       mask_sched=_sched_full, src=src7, res=lambda fc, q0: src7(fc, q0))
        emit_attn(ctx, cfg7, dram["zT"])
        p.barrier()
        emit_ln(ctx, dram["zT"], AR1[:, :], src_bf16=True, gamma=S["G1"], beta=S["B1N"], out_dram=dram["xffnT"])
        p.barrier()

        S["YA"] = AR1.bitcast(F32)[:, 0:16384]
        S["XMB"] = AR2[:, 0:16384]
        S["W1S"] = AR2[:, 16384:32768]
        S["W2S"] = AR3[:, 0:16384]
        p.dma("g", S["SEL8"][:, :], aps["sel8"][:, :])
        p.dma("g", S["B2S"][:, :], aps["f1_b2"][:, :])
        p.dma("g", S["GB"][:, :], aps["f1_gb"][:, :])
        p.dma("g", S["G2N"][:, :], aps["f1_g2"][:, :])
        p.dma("g", S["B2N"][:, :], aps["f1_b2n"][:, :])
        p.barrier()
        src8 = SrcStream(ctx, dram["xffnT"], S["SRCB"])
        cfg8 = AttnCfg(src=src8, wg=aps["f1_gw"], gb=aps["f1_gb"], w1=aps["f1_w1"], w2=aps["f1_w2"],
                       b1=aps["f1_b1c"], b2=aps["f1_b2"], act="relu", renorm=False)
        W2F = S["W2S"].bitcast(F32)
        rows8 = lambda i: W2F[0:1, 1024 + i * 512: 1024 + (i + 1) * 512]
        abc8 = S["W2S"][:, 0:512]
        bbc8 = S["W2S"][:, 512:1024]
        sqt8 = S["W2S"][:, 1024 + 8192:1024 + 8192 + 512]
        W1F = S["W1S"].bitcast(F32)

        def finish8(mega, tm0):
            p.barrier()
            for tc in range(4):
                t0 = tm0 + tc * 512
                for dc in range(NCHUNK):
                    xsl = S["XMB"][:, dc * 2048 + tc * 512: dc * 2048 + tc * 512 + 512]
                    ya = S["YA"][:, dc * 2048 + tc * 512: dc * 2048 + tc * 512 + 512]
                    p.op("v", lambda h, o=xsl, a=ya, b=xsl: h.tensor_tensor(o, a, b, ALU.add), inc=1)
                t_done = []
                for dc in range(NCHUNK):
                    zsl = S["XMB"][:, dc * 2048 + tc * 512: dc * 2048 + tc * 512 + 512]
                    if dc >= 1:
                        p.wait("v", t_done[dc - 1])
                    tv = p.op("v", lambda h, o=sqt8, i=zsl: h.tensor_tensor(o, i, i, ALU.mult), inc=1)
                    p.wait("t", tv)
                    p.op("t", lambda h, b=zsl, s=(dc == 0): h.matmul(ctx.pb[0][0:1, :], ctx.ones_bf[:, :], b, start=s, stop=(dc == 7)))
                    tm = p.op("t", lambda h, b=sqt8, s=(dc == 0): h.matmul(ctx.pb[1][0:1, :], ctx.ones_bf[:, :], b, start=s, stop=(dc == 7)), inc=1)
                    t_done.append(tm)
                mu, m2, var, sd, rstd, brow = (rows8(i) for i in range(6))
                p.wait("v", t_done[-1])
                p.op("v", lambda h: h.tensor_scalar(mu, ctx.pb[0][0:1, :], 1.0 / D, None, ALU.mult))
                p.op("v", lambda h: h.tensor_scalar(m2, ctx.pb[1][0:1, :], 1.0 / D, None, ALU.mult))
                p.op("v", lambda h: h.tensor_tensor(sd, mu, mu, ALU.mult))
                p.op("v", lambda h: h.tensor_tensor(var, m2, sd, ALU.subtract))
                tvv = p.op("v", lambda h: h.tensor_scalar(var, var, 1e-5, None, ALU.add), inc=1)
                p.wait("sc", tvv)
                tss = p.op("sc", lambda h: h.activation(sd, var, AF.Sqrt), inc=1)
                p.wait("v", tss)
                trc = p.op("v", lambda h: h.reciprocal(rstd, sd), inc=1)
                p.op("v", lambda h: h.tensor_scalar(sd, mu, -1.0, None, ALU.mult))
                p.wait("v", trc, force=True)
                tv2 = p.op("v", lambda h: h.tensor_tensor(brow, sd, rstd, ALU.mult), inc=1)
                p.wait("t", tv2)
                tb1 = p.op("t", lambda h: h.matmul(ctx.pb[2][:, :], ctx.ones_row[:, :], rstd, start=True, stop=True), inc=1)
                tb2 = p.op("t", lambda h: h.matmul(ctx.pb[3][:, :], ctx.ones_row[:, :], brow, start=True, stop=True), inc=1)
                p.wait("sc", tb2)
                p.op("sc", lambda h: h.activation(abc8, ctx.pb[2][:, :], AF.Copy))
                e2 = p.op("sc", lambda h: h.activation(bbc8, ctx.pb[3][:, :], AF.Copy), inc=1)
                for dc in range(NCHUNK):
                    zsl = S["XMB"][:, dc * 2048 + tc * 512: dc * 2048 + tc * 512 + 512]
                    s1s = S["W1S"][:, 0:512]
                    xns = W1F[:, 512:1024]
                    ots = W1F[:, 1024:1536]
                    gsl = S["FSTG"][:, 0:512]
                    tl1 = p.dma("sy", s1s, dram["s1T"][dc * 128:(dc + 1) * 128, t0:t0 + 512])
                    tl2 = p.dma("sy", xns, dram["xnewT"][dc * 128:(dc + 1) * 128, t0:t0 + 512])
                    p.wait("v", e2)
                    p.op("v", lambda h, a=zsl: h.tensor_tensor(a, a, abc8, ALU.mult))
                    p.op("v", lambda h, a=zsl: h.tensor_tensor(a, a, bbc8, ALU.add))
                    tvz = p.op("v", lambda h, a=zsl, g=S["G2N"][:, dc:dc + 1], b=S["B2N"][:, dc:dc + 1]:
                               h.tensor_scalar(a, a, g, b, ALU.mult, ALU.add), inc=1)
                    p.wait("sc", tvz)
                    tsl = p.op("sc", lambda h, o=gsl, i=zsl: h.activation(o, i, AF.Silu), inc=1)
                    p.wait("v", tsl)
                    p.wait("v", tl1)
                    p.wait("v", tl2)
                    p.op("v", lambda h, o=gsl, s2=s1s: h.tensor_tensor(o, o, s2, ALU.mult))
                    tvf = p.op("v", lambda h, o=ots, a=xns, g=gsl: h.tensor_tensor(o, a, g, ALU.add), inc=1)
                    p.wait("sy", tvf)
                    td = p.dma("sy", aps["outT"][dc * 128:(dc + 1) * 128, t0:t0 + 512], ots)
                    p.wait("v", td)
                p.barrier()

        emit_moe(ctx, cfg8, finish8)
        _finish(ctx)
        ctx.p.build(block)
    return nc


def kernel_fused(**inputs):
    shared, percore = prep_inputs(inputs)
    B = 8
    names = sorted(set(_G_INPUTS["g2"] + _G_INPUTS["g3"] + _G_INPUTS["g4"] + _G_INPUTS["g6"]
                       + _G_INPUTS["g7"] + _G_INPUTS["g8"]))
    nc = build_fused(shared, percore[0])
    res = _run(nc, shared, names, [{"x0": percore[b]["x0"]} for b in range(B)])
    out = np.stack([np.ascontiguousarray(res[b]["outT"].T) for b in range(B)])
    return out.astype(np.float32)


# version 05
# speedup vs baseline: 1.5320x; 1.5146x over previous
import math
import numpy as np
import ml_dtypes
import concourse.bass as bass
import concourse.mybir as mybir
from concourse.bass_utils import run_bass_kernel_spmd

F32 = mybir.dt.float32
BF16 = mybir.dt.bfloat16
AF = mybir.ActivationFunctionType
ALU = mybir.AluOpType
AX = mybir.AxisListType

D = 1024
T = 4096
NCHUNK = D // 128
NEG = -1.0e30
import os as _os
KNORM_PARANOID = bool(_os.environ.get("KNORM_PARANOID"))

ENGINES = ("t", "v", "sc", "g", "sy")
DMA_ENG = ("sy", "g", "sc", "v")


NWAVE = 8


class Prog:

    def __init__(self, nc):
        self.nc = nc
        self.items = {e: [] for e in ENGINES}
        self.c = {e: 0 for e in ENGINES}
        self.didx = {e: 0 for e in DMA_ENG}
        self.last_wait = {e: {} for e in ENGINES}

    def op(self, e, fn, inc=0, lp=False):
        if lp:
            base = fn
            def fn(h, _b=base, _nc=self.nc):
                with _nc.allow_low_precision(reason="intentional bf16 rounding"):
                    return _b(h)
        self.items[e].append(("op", fn, inc, False, None))
        if inc:
            self.c[e] += inc
        return (e, False, self.c[e])

    def dma(self, e, out, in_, **kw):
        def fn(h):
            return h.dma_start(out=out, in_=in_, **kw)
        idx = self.didx[e]
        k = idx % NWAVE
        wave = idx // NWAVE + 1
        if idx >= NWAVE:
            self.wait(e, ("dma", e, k, 16 * (wave - 1)))
        self.items[e].append(("op", fn, 16, True, k))
        self.didx[e] = idx + 1
        return ("dma", e, k, 16 * wave)

    def tok(self, e, dma=False):
        if dma:
            idx = self.didx[e]
            if idx == 0:
                return None
            k = (idx - 1) % NWAVE
            return ("dma", e, k, 16 * ((idx - 1) // NWAVE + 1))
        return (e, False, self.c[e])

    def _dma_all_toks(self, e):
        toks = []
        idx = self.didx[e]
        for k in range(NWAVE):
            n = (idx - 1 - k) // NWAVE + 1 if idx > k else 0
            if n > 0:
                toks.append(("dma", e, k, 16 * n))
        return toks

    def wait(self, e, tok, force=False):
        if tok is None:
            return
        if tok[0] == "dma":
            _, src, k, val = tok
            key = ("dma", src, k)
        else:
            src, dma, val = tok
            if src == e and not dma and not force:
                return
            key = (src, dma)
        if val <= 0 or val <= self.last_wait[e].get(key, 0):
            return
        self.last_wait[e][key] = val
        self.items[e].append(("wait", key, val))

    def waits(self, e, toks):
        for t in toks:
            self.wait(e, t)

    def barrier(self):
        snap_c = dict(self.c)
        dtoks = {s: self._dma_all_toks(s) for s in DMA_ENG}
        for e in ENGINES:
            for s in ENGINES:
                if s != e and snap_c[s]:
                    self.wait(e, (s, False, snap_c[s]))
            for s in DMA_ENG:
                for t in dtoks[s]:
                    self.wait(e, t)

    def build(self, block):
        nc = self.nc
        sems = {}
        for e in ENGINES:
            sems[(e, False)] = nc.alloc_semaphore(name=f"c_{e}")
        for e in DMA_ENG:
            for k in range(NWAVE):
                sems[("dma", e, k)] = nc.alloc_semaphore(name=f"d_{e}{k}")
        self.sems = sems

        def runner(e):
            def run(h):
                for it in self.items[e]:
                    if it[0] == "wait":
                        _, key, val = it
                        h.wait_ge(sems[key], val)
                    else:
                        _, fn, inc, dma, slot = it
                        ins = fn(h)
                        if inc:
                            ins.then_inc(sems[("dma", e, slot) if dma else (e, False)], inc)
            return run

        block.tensor(runner("t"))
        block.vector(runner("v"))
        block.scalar(runner("sc"))
        block.gpsimd(runner("g"))
        block.sync(runner("sy"))



def ceil_div(a, b):
    return (a + b - 1) // b


class Consts:

    def __init__(self, nc, values):
        self.values = sorted(set(float(v) for v in values))
        self.tile = None

    def alloc(self, stack, nc):
        n = max(1, len(self.values))
        self.tile = stack.enter_context(nc.sbuf_tensor("consts", [128, n], F32))
        for i, v in enumerate(self.values):
            nc.const_aps.register(F32, v, self.tile[:, i:i + 1])

    def emit_init(self, p):
        def mk(i, v):
            return lambda h: h.memset(self.tile[:, i:i + 1], v)
        for i, v in enumerate(self.values):
            p.op("v", mk(i, v), inc=(1 if i == len(self.values) - 1 else 0))
        return p.tok("v")



def _eo_perm(nheads, hd=64):
    idx = []
    for h in range(nheads):
        base = h * hd
        idx.extend(base + np.arange(0, hd, 2))
        idx.extend(base + np.arange(1, hd, 2))
    return np.asarray(idx)


def _swap_perm(nheads, hd=64):
    idx = []
    for h in range(nheads):
        base = h * hd
        idx.extend(base + np.arange(1, hd, 2))
        idx.extend(base + np.arange(0, hd, 2))
    return np.asarray(idx)


def _rope_tables(base, T_, hd=64):
    j = np.arange(hd // 2, dtype=np.float64)
    inv = 1.0 / (base ** (2 * j / hd))
    t = np.arange(T_, dtype=np.float64)
    fr = t[None, :] * inv[:, None]
    cos = np.cos(fr).astype(np.float32)
    sin = np.sin(fr).astype(np.float32)
    c1 = np.tile(cos, (4, 1))
    c2 = np.tile(np.concatenate([-sin, sin], 0), (2, 1))
    return c1.astype(ml_dtypes.bfloat16), c2.astype(ml_dtypes.bfloat16)


def _chunk_major(v):
    return np.ascontiguousarray(v.reshape(-1, 128).T)


def _causal_masks():
    q = np.arange(512)[None, :]
    k = np.arange(128)[:, None]
    out = []
    for off in range(4):
        m = np.where((q - k) >= 128 * off, 0.0, NEG).astype(np.float32)
        out.append(m)
    return np.stack(out)


def _local_masks(win=256):
    q = np.arange(512)[None, :]
    k = np.arange(128)[:, None]
    out = []
    for off in range(-2, 4):
        d = (q - k) - 128 * off
        m = np.where((d >= 0) & (d < win), 0.0, NEG).astype(np.float32)
        out.append(m)
    return np.stack(out)


def _selpair(nheads):
    nch = nheads // 2
    s = np.zeros((nheads, nch * 128), np.float32)
    for c in range(nch):
        s[2 * c, c * 128: c * 128 + 64] = 1.0
        s[2 * c + 1, c * 128 + 64: (c + 1) * 128] = 1.0
    return s.astype(ml_dtypes.bfloat16)


def _maskq(nheads):
    nch = nheads // 2
    m = np.zeros((nch, 128, nheads), np.float32)
    for c in range(nch):
        m[c, 0:64, 2 * c] = 1.0
        m[c, 64:128, 2 * c + 1] = 1.0
    return np.ascontiguousarray(m.transpose(1, 0, 2).reshape(128, nch * nheads)).astype(ml_dtypes.bfloat16)


def prep_inputs(I):
    B = 8
    x = np.asarray(I["L0_attn_hidden"])
    percore = []
    for b in range(B):
        percore.append({"x0": np.ascontiguousarray(x[b].T).astype(np.float32)})

    S = {}
    f32 = lambda a: np.ascontiguousarray(np.asarray(a, dtype=np.float32))
    pq = _eo_perm(16)
    sq = _swap_perm(16)
    S["g_wq"] = f32(np.concatenate([I["L0_attn_g_wq"][:, pq], I["L0_attn_g_wq"][:, sq]], axis=1))
    S["g_kvd"] = f32(I["L0_attn_g_kv_down"])
    ku = np.asarray(I["L0_attn_g_k_up"])
    pk = _eo_perm(4)
    sk = _swap_perm(4)
    kup_d = np.concatenate([np.concatenate([ku[:, pk[g * 64:(g + 1) * 64]]] * 2, axis=1) for g in range(4)], axis=1)
    kup_s = np.concatenate([np.concatenate([ku[:, sk[g * 64:(g + 1) * 64]]] * 2, axis=1) for g in range(4)], axis=1)
    S["g_kup"] = f32(np.concatenate([kup_d, kup_s], axis=1))
    S["g_vup"] = f32(I["L0_attn_g_v_up"])
    S["g_wo"] = f32(I["L0_attn_g_wo"])
    S["l_wq"] = f32(np.concatenate([I["L0_attn_l_wq"][:, pq], I["L0_attn_l_wq"][:, sq]], axis=1))
    lk = np.asarray(I["L0_attn_l_wk"])
    lk_d = np.concatenate([np.concatenate([lk[:, pk[g * 64:(g + 1) * 64]]] * 2, axis=1) for g in range(4)], axis=1)
    lk_s = np.concatenate([np.concatenate([lk[:, sk[g * 64:(g + 1) * 64]]] * 2, axis=1) for g in range(4)], axis=1)
    S["l_wk"] = f32(np.concatenate([lk_d, lk_s], axis=1))
    S["l_wv"] = f32(I["L0_attn_l_wv"])
    S["l_wo"] = f32(I["L0_attn_l_wo"])
    S["w1n"] = f32(_chunk_major(I["L0_attn_norm1_w"]))
    S["w2n"] = f32(_chunk_major(I["L0_attn_norm2_w"]))
    S["m0_wg"] = f32(I["L0_ffn_Wg"])
    S["m0_w1"] = f32(I["L0_ffn_W1"])
    S["m0_w2"] = f32(I["L0_ffn_W2"])
    S["a1_wq"] = f32(I["L1_attn_Wq"])
    S["a1_wk"] = f32(np.asarray(I["L1_attn_Wkv"])[:, :512])
    S["a1_wv"] = f32(np.asarray(I["L1_attn_Wkv"])[:, 512:])
    S["a1_wo"] = f32(I["L1_attn_Wo"])
    for nm, key in [("f1_wq", "L1_ffn_Wq"), ("f1_wk", "L1_ffn_Wk"), ("f1_wv", "L1_ffn_Wv"), ("f1_wo", "L1_ffn_Wo")]:
        S[nm] = f32(I[key])
    for nm, key in [("f1_bq", "L1_ffn_bq"), ("f1_bk", "L1_ffn_bk"), ("f1_bv", "L1_ffn_bv"), ("f1_bo", "L1_ffn_bo"),
                    ("f1_g1", "L1_ffn_ln1_g"), ("f1_b1n", "L1_ffn_ln1_b"), ("f1_g2", "L1_ffn_ln2_g"), ("f1_b2n", "L1_ffn_ln2_b")]:
        S[nm] = f32(_chunk_major(I[key]))
    S["f1_gw"] = f32(I["L1_ffn_gW"])
    S["f1_gb"] = f32(np.asarray(I["L1_ffn_gb"]).reshape(1, 8))
    S["f1_w1"] = f32(I["L1_ffn_W1"])
    S["f1_b1c"] = f32(np.asarray(I["L1_ffn_b1"]).reshape(8, 16, 128).transpose(0, 2, 1))
    S["f1_w2"] = f32(I["L1_ffn_W2"])
    S["f1_b2"] = f32(I["L1_ffn_b2"])
    gc1, gc2 = _rope_tables(1e6, T)
    lc1, lc2 = _rope_tables(1e4, T)
    S["ropeg"] = np.ascontiguousarray(np.concatenate([gc1, gc2], axis=1))
    S["ropel"] = np.ascontiguousarray(np.concatenate([lc1, lc2], axis=1))
    S["cmask"] = _causal_masks().reshape(4 * 128, 512).astype(ml_dtypes.bfloat16)
    S["lmask"] = _local_masks().reshape(6 * 128, 512).astype(ml_dtypes.bfloat16)
    S["ident"] = np.eye(128, dtype=np.float32).astype(ml_dtypes.bfloat16)
    S["selp16"] = _selpair(16)
    S["selp8"] = _selpair(8)
    S["maskq16"] = _maskq(16)
    S["maskq8"] = _maskq(8)
    mh = np.zeros((128, 3), np.float32)
    mh[0:64, 0] = 1.0
    mh[0:64, 1] = 1.0
    mh[64:128, 2] = 1.0
    S["maskh"] = mh.astype(ml_dtypes.bfloat16)
    sel8 = np.zeros((8, 8 * 128), np.float32)
    for e in range(8):
        sel8[e, e * 128:(e + 1) * 128] = 1.0
    S["sel8"] = sel8.astype(ml_dtypes.bfloat16)
    return S, percore


_DT = {np.dtype(np.float32): F32, np.dtype(ml_dtypes.bfloat16): BF16}


def declare_inputs(nc, shared, percore0):
    aps = {}
    for nm, arr in list(shared.items()) + list(percore0.items()):
        aps[nm] = nc.dram_tensor(nm, list(arr.shape), _DT[arr.dtype], kind="ExternalInput").ap()
    return aps


class Ctx:
    def __init__(self, nc, stack, aps, consts_vals=(), sbuf_spec=()):
        self.nc = nc
        self.aps = aps
        self.p = Prog(nc)
        self.pb = [stack.enter_context(nc.psum_tensor(f"pb{i}", [128, 512], F32)) for i in range(7)]
        self.pbbf = stack.enter_context(nc.psum_tensor("pbbf", [128, 1024], BF16))
        self.ones_bf = stack.enter_context(nc.sbuf_tensor("ones_bf", [128, 1], BF16))
        self.ones_row = stack.enter_context(nc.sbuf_tensor("ones_row", [1, 128], F32))
        self.ones_row_bf = stack.enter_context(nc.sbuf_tensor("ones_row_bf", [1, 128], BF16))
        self.consts = Consts(nc, consts_vals)
        self.consts.alloc(stack, nc)
        self.sb = {}
        for (nm, shape, dt) in sbuf_spec:
            self.sb[nm] = stack.enter_context(nc.sbuf_tensor(nm, list(shape), dt))
        self.wstage_tok = None

    def setup(self):
        p = self.p
        p.op("v", lambda h: h.memset(self.ones_bf[:, :], 1.0))
        p.op("v", lambda h: h.memset(self.ones_row[:, :], 1.0))
        p.op("v", lambda h: h.memset(self.ones_row_bf[:, :], 1.0), inc=1)
        self.consts.emit_init(p)
        self.t_setup = p.tok("v")
        for e in ("t", "sc", "v", "g"):
            p.wait(e, self.t_setup)


def load_bf16(p, dram_ap, sb_ap):
    return p.dma("g", sb_ap, dram_ap)



def emit_ln(ctx, x_dram, xn_sb, *, rs1_row=None, out_dram=None, sq_eps=1e-6,
            src_bf16=False, gamma=None, beta=None):
    p = ctx.p
    nc = ctx.nc
    S = ctx.sb
    xstg, sqt = S["ln_xstg"], S["ln_sqt"]
    rows = S["ln_rows"]
    abc, bbc = S["ln_abc"], S["ln_bbc"]
    ps_s, ps_q, ps_b1, ps_b2 = ctx.pb[0], ctx.pb[1], ctx.pb[2], ctx.pb[3]

    for blk in range(8):
        t0 = blk * 512
        v_done = []
        t_done = []
        for ch in range(8):
            sl = xstg[:, (ch % 2) * 512:(ch % 2) * 512 + 512]
            sq = sqt[:, (ch % 2) * 512:(ch % 2) * 512 + 512]
            if ch >= 2:
                p.wait("sy", v_done[ch - 2])
            if src_bf16:
                sl = sl.bitcast(BF16)[:, 0:512]
            tl = p.dma("sy", sl, x_dram[ch * 128:(ch + 1) * 128, t0:t0 + 512])
            xnsl = xn_sb[:, ch * T + t0: ch * T + t0 + 512]
            p.wait("v", tl)
            if ch >= 2:
                p.wait("v", t_done[ch - 2])
            p.op("v", lambda h, a=xnsl, b=sl: h.tensor_copy(a, b))
            tv = p.op("v", lambda h, a=sq, b=xnsl: h.tensor_tensor(a, b, b, ALU.mult), inc=1)
            v_done.append(tv)
            p.wait("t", tv)
            p.op("t", lambda h, b=xnsl, s=(ch == 0): h.matmul(ps_s[0:1, :], ctx.ones_bf[:, :], b, start=s, stop=(ch == 7)))
            tok_mm = p.op("t", lambda h, b=sq, s=(ch == 0): h.matmul(ps_q[0:1, :], ctx.ones_bf[:, :], b, start=s, stop=(ch == 7)), inc=1)
            t_done.append(tok_mm)
        mu, var, sd, rstd = (rows[0:1, i * 512:(i + 1) * 512] for i in range(4))
        m2 = var
        tmp = sd
        brow = mu
        p.wait("v", tok_mm)
        p.op("v", lambda h: h.tensor_scalar(mu, ps_s[0:1, :], 1.0 / D, None, ALU.mult))
        p.op("v", lambda h: h.tensor_tensor(tmp, mu, mu, ALU.mult))
        p.op("v", lambda h: h.tensor_scalar(m2, ps_q[0:1, :], 1.0 / D, None, ALU.mult))
        p.op("v", lambda h: h.tensor_tensor(var, m2, tmp, ALU.subtract))
        tv = p.op("v", lambda h: h.tensor_scalar(var, var, 1e-5, None, ALU.add), inc=1)
        p.wait("sc", tv)
        ts = p.op("sc", lambda h: h.activation(sd, var, AF.Sqrt), inc=1)
        p.wait("v", ts)
        trc0 = p.op("v", lambda h: h.reciprocal(rstd, sd), inc=1)
        p.op("v", lambda h: h.tensor_scalar(tmp, mu, -1.0, None, ALU.mult))
        p.wait("v", trc0, force=True)
        tv2 = p.op("v", lambda h: h.tensor_tensor(brow, tmp, rstd, ALU.mult), inc=1)
        if rs1_row is not None:
            r1 = r2 = rows[0:1, 2 * 512:3 * 512]
            p.wait("v", tv, force=True)
            trr = p.op("v", lambda h: h.reciprocal(r1, var), inc=1)
            p.wait("v", trr, force=True)
            p.op("v", lambda h: h.tensor_scalar(r1, r1, -1e-5, None, ALU.mult))
            p.op("v", lambda h: h.tensor_scalar(r1, r1, 1.0, None, ALU.add))
            tv3 = p.op("v", lambda h: h.tensor_scalar(r1, r1, 1.0, sq_eps, ALU.mult, ALU.add), inc=1)
            p.wait("sc", tv3)
            ts2 = p.op("sc", lambda h: h.activation(r2, r1, AF.Sqrt), inc=1)
            p.wait("v", ts2)
            p.op("v", lambda h, o=rs1_row[0:1, t0:t0 + 512]: h.reciprocal(o, r2), inc=1, lp=True)
        p.wait("t", tv2)
        tb1 = p.op("t", lambda h: h.matmul(ps_b1[:, :], ctx.ones_row[:, :], rstd, start=True, stop=True), inc=1)
        tb2 = p.op("t", lambda h: h.matmul(ps_b2[:, :], ctx.ones_row[:, :], brow, start=True, stop=True), inc=1)
        p.wait("sc", tb1)
        p.wait("sc", tb2)
        e1 = p.op("sc", lambda h: h.activation(abc[:, :], ps_b1[:, :], AF.Copy))
        e2 = p.op("sc", lambda h: h.activation(bbc[:, :], ps_b2[:, :], AF.Copy), inc=1)
        p.wait("v", e2)
        for ch in range(8):
            xnsl = xn_sb[:, ch * T + t0: ch * T + t0 + 512]
            p.op("v", lambda h, a=xnsl: h.tensor_tensor(a, a, abc[:, :], ALU.mult))
            tv = p.op("v", lambda h, a=xnsl: h.tensor_tensor(a, a, bbc[:, :], ALU.add), inc=1)
            if gamma is not None:
                tv = p.op("v", lambda h, a=xnsl, g=gamma[:, ch:ch + 1], b=beta[:, ch:ch + 1]:
                          h.tensor_scalar(a, a, g, b, ALU.mult, ALU.add), inc=1)
            if out_dram is not None:
                p.wait("sy", tv)
                p.dma("sy", out_dram[ch * 128:(ch + 1) * 128, t0:t0 + 512], xnsl)
    return p.tok("v")



def _declare(nc, shared, percore0, names, extra=None):
    aps = {}
    for nm in names:
        arr = shared[nm] if nm in shared else percore0[nm]
        aps[nm] = nc.dram_tensor(nm, list(arr.shape), _DT[np.asarray(arr).dtype], kind="ExternalInput").ap()
    for nm, (shape, dt) in (extra or {}).items():
        aps[nm] = nc.dram_tensor(nm, list(shape), dt, kind="ExternalInput").ap()
    return aps


def _outs(nc, spec):
    return {nm: nc.dram_tensor(nm, list(shape), dt, kind="ExternalOutput").ap()
            for nm, (shape, dt) in spec.items()}


def _finish(ctx):
    ctx.p.barrier()


def build_g1(shared, percore0):
    import contextlib
    nc = bass.Bass("TRN2", target_bir_lowering=False)
    stack = contextlib.ExitStack()
    nc._dev_stack = stack
    aps = _declare(nc, shared, percore0, ["x0"])
    aps.update(_outs(nc, {"xnT": ([D, T], BF16), "rs1": ([1, T], F32)}))
    sbuf_spec = [
        ("ln_xstg", (128, 1024), F32), ("ln_sqt", (128, 1024), BF16),
        ("ln_rows", (1, 4096), F32), ("ln_abc", (128, 512), BF16), ("ln_bbc", (128, 512), BF16),
        ("XN", (128, NCHUNK * T), BF16),
        ("rs1row", (1, T), F32),
    ]
    with nc.Block() as block:
        ctx = Ctx(nc, stack, aps, consts_vals=(), sbuf_spec=sbuf_spec)
        ctx.setup()
        emit_ln(ctx, aps["x0"], ctx.sb["XN"], rs1_row=ctx.sb["rs1row"], out_dram=aps["xnT"])
        tv = ctx.p.tok("v")
        ctx.p.wait("sy", tv)
        ctx.p.dma("sy", aps["rs1"][:, :], ctx.sb["rs1row"][:, :])
        _finish(ctx)
        ctx.p.build(block)
    return nc



def emit_rms_row(ctx, src_sb, out_row, eps=1e-6):
    p = ctx.p
    sqt = ctx.sb["ln_sqt"]
    rows = ctx.sb["ln_rows"]
    ps = ctx.pb[0]
    for blk in range(8):
        t0 = blk * 512
        t_done = []
        for ch in range(8):
            sq = sqt[:, (ch % 2) * 512:(ch % 2) * 512 + 512]
            sl = src_sb[:, ch * T + t0: ch * T + t0 + 512]
            if ch >= 2:
                p.wait("v", t_done[ch - 2])
            tv = p.op("v", lambda h, a=sq, b=sl: h.tensor_tensor(a, b, b, ALU.mult), inc=1)
            p.wait("t", tv)
            tm = p.op("t", lambda h, b=sq, s=(ch == 0): h.matmul(ps[0:1, :], ctx.ones_bf[:, :], b, start=s, stop=(ch == 7)), inc=1)
            t_done.append(tm)
        r1 = rows[0:1, 0:512]
        r2 = rows[0:1, 512:1024]
        p.wait("v", t_done[-1])
        tv = p.op("v", lambda h: h.tensor_scalar(r1, ps[0:1, :], 1.0 / D, eps, ALU.mult, ALU.add), inc=1)
        p.wait("sc", tv)
        ts = p.op("sc", lambda h: h.activation(r2, r1, AF.Sqrt), inc=1)
        p.wait("v", ts)
        p.op("v", lambda h, o=out_row[0:1, t0:t0 + 512]: h.reciprocal(o, r2), inc=1)
    return p.tok("v")



def load_rows(ctx, dram, dst, nkc, cols, scale=None):
    p = ctx.p
    toks = []
    if scale is None:
        for k in range(nkc):
            toks.append(p.dma("g", dst[:, k * cols:(k + 1) * cols], dram[k * 128:(k + 1) * 128, 0:cols]))
        return toks[-1]
    stg = ctx.sb["wstage"]
    for k in range(nkc):
        s0 = 0
        if k >= 1:
            p.wait("sy", toks[k - 1])
        else:
            p.wait("sy", ctx.wstage_tok)
        tl = p.dma("sy", stg[:, s0:s0 + cols], dram[k * 128:(k + 1) * 128, 0:cols])
        p.wait("v", tl)
        tv = p.op("v", lambda h, o=dst[:, k * cols:(k + 1) * cols], i=stg[:, s0:s0 + cols], s=scale[:, k:k + 1]:
                  h.tensor_scalar(o, i, s, None, ALU.mult), inc=1)
        toks.append(tv)
    ctx.wstage_tok = toks[-1]
    return toks[-1]


def emit_w_prep(ctx, src_dram, dst_dram, nkc, cols, scale):
    p = ctx.p
    stg = ctx.sb["wstage"]
    stgb = ctx.sb["wstageb"]
    vt, dt = [], []
    for k in range(nkc):
        s0 = 0
        if k >= 1:
            p.wait("sy", vt[k - 1])
        else:
            p.wait("sy", ctx.wstage_tok)
        tl = p.dma("sy", stg[:, s0:s0 + cols], src_dram[k * 128:(k + 1) * 128, 0:cols])
        p.wait("v", tl)
        if k >= 1:
            p.wait("v", dt[k - 1])
        tv = p.op("v", lambda h, o=stgb[:, s0:s0 + cols], i=stg[:, s0:s0 + cols], s=scale[:, k:k + 1]:
                  h.tensor_scalar(o, i, s, None, ALU.mult), inc=1)
        vt.append(tv)
        p.wait("sy", tv)
        dt.append(p.dma("sy", dst_dram[k * 128:(k + 1) * 128, 0:cols], stgb[:, s0:s0 + cols]))
    ctx.wstage_tok = vt[-1]
    return dt[-1]


class AttnCfg:
    def __init__(self, **kw):
        self.__dict__.update(kw)


def emit_attn(ctx, cfg, out_dram):
    p = ctx.p
    S = ctx.sb
    nq = cfg.nheads
    NQC = nq // 2
    rope = getattr(cfg, "rope", None)
    qk = getattr(cfg, "qk", None)
    dup = getattr(cfg, "kv_dup", False)
    nkvc = cfg.nkv_chunks
    nkvh = cfg.nkv_heads
    nho = 1 if dup else 2
    SCR = S["SCR"]
    KTS = S["KTS"]
    VTOK = S["VTOK"]
    IC = NQC
    qtile = lambda c: SCR[:, c * 512:(c + 1) * 512]
    qsw = lambda c: SCR[:, 4096 + c * 512:4096 + (c + 1) * 512]
    asm = lambda c: SCR[:, 8192 + c * 512:8192 + (c + 1) * 512]
    P_t = lambda i: SCR[:, 12288 + i * 512:12288 + (i + 1) * 512]
    nskt = S.get("nskt")
    rqrows = S.get("rqrows")
    sumsrow = S["sumsrow"]
    fscr = S["fscr"]
    bank_tok = {}

    def use_bank(e, b):
        tok = bank_tok.get(b)
        if tok is not None:
            p.wait(e, tok)

    if getattr(cfg, "latent", None) is not None:
        LATN = S["LATN"]
        WD = S["WD"]
        lat = cfg.latent
        bc_tok = {}
        for tb in range(8):
            t0 = tb * 512
            sl = tb % 2
            bcb = S["bcbuf"][:, sl * 512:sl * 512 + 512]
            bb = ctx.pb[2]
            use_bank("t", 2)
            tb1 = p.op("t", lambda h, r=lat["scale_row"][0:1, t0:t0 + 512]:
                       h.matmul(bb[:, :], ctx.ones_row_bf[:, :], r, start=True, stop=True), inc=1)
            p.wait("sc", tb1)
            p.wait("sc", bc_tok.get(sl))
            te = p.op("sc", lambda h, o=bcb: h.activation(o, bb[:, :], AF.Copy), inc=1)
            bank_tok[2] = te
            for fc in range(2):
                b = fc % 2
                pb = ctx.pb[b]
                use_bank("t", b)
                for kc in range(NCHUNK):
                    xap, tok = cfg.src(kc, t0, 512)
                    p.wait("t", tok)
                    tm = p.op("t", lambda h, w=WD[:, kc * 256 + fc * 128: kc * 256 + fc * 128 + 128],
                              x=xap, o=pb, s=(kc == 0), e=(kc == NCHUNK - 1):
                              h.matmul(o[:, :], w, x, start=s, stop=e), inc=(1 if kc == NCHUNK - 1 else 0))
                p.wait("v", tm)
                p.wait("v", te)
                tv = p.op("v", lambda h, o=LATN[:, fc * T + t0: fc * T + t0 + 512], i=pb, bcx=bcb:
                          h.tensor_tensor(o, i[:, :], bcx, ALU.mult), inc=1)
                bank_tok[b] = tv
                bc_tok[sl] = tv
        lat_tok = p.tok("v")
        kv_src = lambda kc, t0, w: (LATN[:, kc * T + t0: kc * T + t0 + w], lat_tok)
        kv_kc = 2
        kv_wcols = getattr(cfg, "wk_cols")
    else:
        kv_src = cfg.src
        kv_kc = NCHUNK
        kv_wcols = getattr(cfg, "wk_cols")

    WK = S["WK"]
    ksw_tok = {}
    for tb in range(8):
        for c in range(nkvc):
            t0 = tb * 512
            b = c % 2
            pbK = ctx.pb[b]
            use_bank("t", b)
            for kc in range(kv_kc):
                xap, tok = kv_src(kc, t0, 512)
                p.wait("t", tok)
                tm = p.op("t", lambda h, w=WK[:, kc * kv_wcols + c * 128: kc * kv_wcols + c * 128 + 128],
                          x=xap, o=pbK, s=(kc == 0), e=(kc == kv_kc - 1):
                          h.matmul(o[:, :], w, x, start=s, stop=e), inc=(1 if kc == kv_kc - 1 else 0))
            ksl = KTS[:, c * T + t0: c * T + t0 + 512]
            p.wait("sc", tm)
            if getattr(cfg, "bk", None) is not None:
                te = p.op("sc", lambda h, o=ksl, i=pbK, ba=cfg.bk[:, c:c + 1]: h.activation(o, i[:, :], AF.Identity, bias=ba), inc=1)
            else:
                te = p.op("sc", lambda h, o=ksl, i=pbK: h.activation(o, i[:, :], AF.Copy), inc=1)
            bank_tok[b] = te
            if rope is not None:
                b2 = 2 + c % 2
                pbS = ctx.pb[b2]
                use_bank("t", b2)
                for kc in range(kv_kc):
                    xap, tok = kv_src(kc, t0, 512)
                    p.wait("t", tok)
                    tm2 = p.op("t", lambda h, w=WK[:, kc * kv_wcols + nkvc * 128 + c * 128: kc * kv_wcols + nkvc * 128 + c * 128 + 128],
                               x=xap, o=pbS, s=(kc == 0), e=(kc == kv_kc - 1):
                               h.matmul(o[:, :], w, x, start=s, stop=e), inc=(1 if kc == kv_kc - 1 else 0))
                ks = asm(c % 2)
                p.wait("sc", tm2)
                p.wait("sc", ksw_tok.get(c % 2))
                te2 = p.op("sc", lambda h, o=ks, i=pbS: h.activation(o, i[:, :], AF.Copy), inc=1)
                bank_tok[b2] = te2
                rt = S["ROPE"]
                p.wait("v", te)
                p.wait("v", te2)
                if getattr(cfg, "no_krope", False):
                    tv = p.op("v", lambda h, a=ksl: h.tensor_copy(a, a), inc=1)
                else:
                    p.op("v", lambda h, a=ksl, t1=rt[:, t0:t0 + 512]: h.tensor_tensor(a, a, t1, ALU.mult))
                    p.op("v", lambda h, bx=ks, t2=rt[:, T + t0:T + t0 + 512]: h.tensor_tensor(bx, bx, t2, ALU.mult))
                    tv = p.op("v", lambda h, a=ksl, bx=ks: h.tensor_tensor(a, a, bx, ALU.add), inc=1)
                ksw_tok[c % 2] = tv
                bank_tok[b2] = tv

    if qk is not None:
        for c in range(nkvc):
            sq_tok = {}
            ev_tok = None
            for tb in range(8):
                t0 = tb * 512
                dq = P_t(tb % 2)
                ksl = KTS[:, c * T + t0: c * T + t0 + 512]
                p.wait("v", sq_tok.get(tb % 2))
                tv = p.op("v", lambda h, o=dq, i=ksl: h.tensor_tensor(o, i, i, ALU.mult), inc=1)
                p.wait("t", tv)
                use_bank("t", 4)
                rh = S["MASKH"][:, 0:1] if dup else S["MASKH"][:, 1:3]
                for tt in range(4):
                    tm = p.op("t", lambda h, l=dq[:, tt * 128:(tt + 1) * 128], r=rh,
                              o=ctx.pb[4][:, tt * nho:(tt + 1) * nho]:
                              h.matmul(o, l, r, start=True, stop=True), inc=(1 if tt == 3 else 0))
                sq_tok[tb % 2] = tm
                p.wait("v", tm)
                tvd = p.op("v", lambda h: h.memset(fscr[:, 191:192], 0.0), inc=1)
                p.wait("sc", tvd)
                ev_tok = p.op("sc", lambda h, o=fscr[:, 128 + tb * 4 * nho: 128 + (tb + 1) * 4 * nho],
                              i=ctx.pb[4][:, 0:4 * nho]: h.activation(o, i, AF.Copy), inc=1)
                bank_tok[4] = ev_tok
            width = 32 * nho
            p.wait("v", ev_tok)
            f1_ = fscr[:, 0:width]
            f2_ = fscr[:, 64:64 + width]
            inv_pm = 1.0 / cfg.exp_premul
            if qk == "rms":
                tv = p.op("v", lambda h, a=f1_, b=fscr[:, 128:128 + width]: h.tensor_scalar(a, b, 1.0 / 64.0, 1e-6, ALU.mult, ALU.add), inc=1)
                p.wait("sc", tv)
                ts = p.op("sc", lambda h, a=f2_, b=f1_: h.activation(a, b, AF.Sqrt), inc=1)
                p.wait("v", ts)
                tpm = p.op("v", lambda h, a=f2_, s=inv_pm: h.tensor_scalar(a, a, s, None, ALU.mult), inc=1)
                p.wait("v", tpm, force=True)
                tv2 = p.op("v", lambda h, o=nskt[:, c * width:(c + 1) * width], b=f2_: h.reciprocal(o, b), inc=1)
            else:
                tv = p.op("v", lambda h, a=f1_, b=fscr[:, 128:128 + width]: h.tensor_copy(a, b), inc=1)
                p.wait("sc", tv)
                ts = p.op("sc", lambda h, a=f2_, b=f1_: h.activation(a, b, AF.Sqrt), inc=1)
                p.wait("v", ts)
                p.op("v", lambda h, a=f2_: h.tensor_scalar_max(a, a, 1e-12))
                tpm = p.op("v", lambda h, a=f2_, s=inv_pm: h.tensor_scalar(a, a, s, None, ALU.mult), inc=1)
                p.wait("v", tpm, force=True)
                tv2 = p.op("v", lambda h, o=nskt[:, c * width:(c + 1) * width], b=f2_: h.reciprocal(o, b), inc=1)
            bank_tok[4] = tv2

    if getattr(cfg, "dbg", None) is not None and "knorm_aps" in cfg.dbg:
        p.barrier()
        for nm, (sap, dap) in cfg.dbg["knorm_aps"].items():
            p.dma("sy", dap, sap)
        p.barrier()

    if getattr(cfg, "load_wv", None) is not None:
        p.barrier()
        cfg.load_wv()
        p.barrier()
    WV = S.get("WV") or S["WK"]
    vone = VTOK.rearrange("p (n s) -> p n s", s=65)
    tvo = p.op("v", lambda h: h.memset(vone[:, :, 64:65], 1.0), inc=1)
    nvc = nkvh * 64 // 128
    wv_cols = cfg.wv_cols
    vstg_tok = {}
    vbc_tok = {}
    for tb in range(8):
        for c in range(nvc):
            t0 = tb * 512
            b = c % 2
            pbV = ctx.pb[b]
            use_bank("t", b)
            for kc in range(kv_kc):
                xap, tok = kv_src(kc, t0, 512)
                p.wait("t", tok)
                tm = p.op("t", lambda h, w=WV[:, kc * wv_cols + c * 128: kc * wv_cols + c * 128 + 128],
                          x=xap, o=pbV, s=(kc == 0), e=(kc == kv_kc - 1):
                          h.matmul(o[:, :], w, x, start=s, stop=e), inc=(1 if kc == kv_kc - 1 else 0))
            vstg = S["VSTG"][:, b * 512: b * 512 + 512]
            p.wait("sc", tm)
            p.wait("sc", vstg_tok.get(b))
            if getattr(cfg, "bv", None) is not None:
                te = p.op("sc", lambda h, o=vstg, i=pbV, ba=cfg.bv[:, c:c + 1]: h.activation(o, i[:, :], AF.Identity, bias=ba), inc=1)
            elif getattr(cfg, "v_scale_row", None) is not None:
                sl = tb % 2
                bcb = S["bcbuf"][:, sl * 512:sl * 512 + 512]
                use_bank("t", 5)
                tb1 = p.op("t", lambda h, r=cfg.v_scale_row[0:1, t0:t0 + 512]:
                           h.matmul(ctx.pb[5][:, :], ctx.ones_row_bf[:, :], r, start=True, stop=True), inc=1)
                p.wait("sc", tb1)
                p.wait("sc", vbc_tok.get(sl))
                p.op("sc", lambda h, o=bcb: h.activation(o, ctx.pb[5][:, :], AF.Copy))
                te0 = p.op("sc", lambda h, o=vstg, i=pbV: h.activation(o, i[:, :], AF.Copy), inc=1)
                p.wait("v", te0)
                te = p.op("v", lambda h, o=vstg, bcx=bcb: h.tensor_tensor(o, o, bcx, ALU.mult), inc=1)
                bank_tok[5] = te
                vbc_tok[sl] = te
            else:
                te = p.op("sc", lambda h, o=vstg, i=pbV: h.activation(o, i[:, :], AF.Copy), inc=1)
            bank_tok[b] = te
            p.wait("t", te)
            p.wait("v", tvo)
            last_v = None
            for tt in range(4):
                kt = tb * 4 + tt
                pt = ctx.pbbf[:, 0:128]
                use_bank("t", "bf")
                ttr = p.op("t", lambda h, i=vstg[:, tt * 128:(tt + 1) * 128], o=pt:
                           h.transpose(o, i, S["IDENT"][:, :]), inc=1)
                p.wait("v", ttr)
                bA = ((2 * c) * 32 + kt) * 65
                bB = ((2 * c + 1) * 32 + kt) * 65
                p.op("v", lambda h, o=VTOK[:, bA:bA + 64], i=pt[:, 0:64]: h.tensor_copy(o, i))
                tvv = p.op("v", lambda h, o=VTOK[:, bB:bB + 64], i=pt[:, 64:128]: h.tensor_copy(o, i), inc=1)
                bank_tok["bf"] = tvv
                last_v = tvv
            vstg_tok[b] = last_v

    p.barrier()
    if getattr(cfg, "dbg", None) is not None and "kv_aps" in cfg.dbg:
        for nm, (sap, dap) in cfg.dbg["kv_aps"].items():
            p.dma("sy", dap, sap)
        p.barrier()

    WQb = S["WQB"]
    WOS = S.get("WOS")
    FQ = NQC * (2 if rope else 1)
    recips = sumsrow[0:1, 1024:2048] if False else S["recips"]
    ostg = S["OSTG"]
    p_tok = {}
    asm_tok = None
    qe_tok = {}
    for qc in range(8):
        q0 = qc * 512
        sched = cfg.mask_sched(qc)
        p.wait("sc", qe_tok.get(("o", 0)))
        p.wait("sc", qe_tok.get(("o", 1)))
        wq_toks = {}
        for fc in range(FQ):
            slot = fc % 2
            b = fc % 2
            if fc >= 2:
                p.wait("g", wq_toks[fc - 2])
            else:
                p.wait("g", qe_tok.get(("wo", fc % 2)))
            tl = p.dma("g", WQb[:, slot * 1024:(slot + 1) * 1024].rearrange("p (k c) -> p k c", c=128),
                       cfg.wq[:, fc * 128:(fc + 1) * 128].rearrange("(k p) c -> p k c", p=128))
            pbQ = ctx.pb[b]
            use_bank("t", b)
            p.wait("t", tl)
            for kc in range(NCHUNK):
                xap, tok = cfg.src(kc, q0, 512)
                p.wait("t", tok)
                tm = p.op("t", lambda h, w=WQb[:, slot * 1024 + kc * 128: slot * 1024 + kc * 128 + 128],
                          x=xap, o=pbQ, s=(kc == 0), e=(kc == NCHUNK - 1):
                          h.matmul(o[:, :], w, x, start=s, stop=e), inc=(1 if kc == NCHUNK - 1 else 0))
            wq_toks[fc] = tm
            dst = qtile(fc) if fc < NQC else qsw(fc - NQC)
            p.wait("sc", tm)
            p.wait("sc", asm_tok)
            if getattr(cfg, "bq", None) is not None and fc < NQC:
                te = p.op("sc", lambda h, o=dst, i=pbQ, ba=cfg.bq[:, fc:fc + 1]: h.activation(o, i[:, :], AF.Identity, bias=ba), inc=1)
            else:
                te = p.op("sc", lambda h, o=dst, i=pbQ: h.activation(o, i[:, :], AF.Copy), inc=1)
            bank_tok[b] = te
            qe_tok[("e", fc)] = te
        if rope is not None:
            rt = S["ROPE"]
            for c in range(NQC):
                p.wait("v", qe_tok[("e", c)])
                p.wait("v", qe_tok[("e", c + NQC)])
                p.op("v", lambda h, a=qtile(c), t1=rt[:, q0:q0 + 512]: h.tensor_tensor(a, a, t1, ALU.mult))
                p.op("v", lambda h, bx=qsw(c), t2=rt[:, T + q0:T + q0 + 512]: h.tensor_tensor(bx, bx, t2, ALU.mult))
                p.op("v", lambda h, a=qtile(c), bx=qsw(c): h.tensor_tensor(a, a, bx, ALU.add), inc=1)
                qe_tok[("r", c)] = p.tok("v")
        if qk is not None:
            use_bank("t", 4)
            for c in range(NQC):
                src_t = qe_tok.get(("r", c), qe_tok[("e", c)])
                p.wait("v", src_t)
                tv = p.op("v", lambda h, o=qsw(c), a=qtile(c): h.tensor_tensor(o, a, a, ALU.mult), inc=1)
                p.wait("t", tv)
                for tt in range(4):
                    tm = p.op("t", lambda h, l=qsw(c)[:, tt * 128:(tt + 1) * 128], r=S["MASKH"][:, 1:3],
                              o=ctx.pb[4][:, tt * 2 * NQC + c * 2: tt * 2 * NQC + c * 2 + 2]:
                              h.matmul(o, l, r, start=True, stop=True), inc=(1 if (tt == 3 and c == NQC - 1) else 0))
            qn_w = 2 * NQC * 4
            p.wait("v", tm)
            f1q = fscr[:, 0:qn_w]
            f3q = fscr[:, 64:64 + qn_w]
            if qk == "rms":
                p.op("v", lambda h, a=f1q, b=ctx.pb[4][:, 0:qn_w]: h.tensor_scalar(a, b, 1.0 / 64.0, 1e-6, ALU.mult, ALU.add), inc=1)
                p.wait("sc", p.tok("v"))
                ts = p.op("sc", lambda h, a=f1q: h.activation(a, a, AF.Sqrt), inc=1)
                p.wait("v", ts)
                tvq0 = p.op("v", lambda h, o=f3q, b=f1q: h.reciprocal(o, b), inc=1)
            else:
                p.op("v", lambda h, a=f1q, b=ctx.pb[4][:, 0:qn_w]: h.tensor_copy(a, b), inc=1)
                p.wait("sc", p.tok("v"))
                ts = p.op("sc", lambda h, a=f1q: h.activation(a, a, AF.Sqrt), inc=1)
                p.wait("v", ts)
                tmx = p.op("v", lambda h, a=f1q: h.tensor_scalar_max(a, a, 1e-12), inc=1)
                p.wait("v", tmx, force=True)
                tvq0 = p.op("v", lambda h, o=f3q, b=f1q: h.reciprocal(o, b), inc=1)
            p.wait("v", tvq0, force=True)
            tvq = p.op("v", lambda h, o=fscr.bitcast(BF16)[:, 256:256 + qn_w], b=f3q: h.tensor_copy(o, b), inc=1)
            bank_tok[4] = tvq
            rqb = fscr.bitcast(BF16)[:, 256:384]
            for tt in range(4):
                pt = ctx.pbbf[:, 0:128]
                use_bank("t", "bf")
                p.wait("t", tvq)
                ttr = p.op("t", lambda h, i=rqb[:, tt * 2 * NQC:(tt + 1) * 2 * NQC], o=pt[0:2 * NQC, 0:128]:
                           h.transpose(o, i, S["IDENT"][:, :]), inc=1)
                p.wait("sc", ttr)
                tse = p.op("sc", lambda h, o=rqrows[0:2 * NQC, tt * 128:(tt + 1) * 128], i=pt[0:2 * NQC, 0:128]:
                           h.activation(o, i, AF.Copy), inc=1)
                bank_tok["bf"] = tse
            for c in range(NQC):
                use_bank("t", 5)
                p.wait("t", tse)
                tbq = p.op("t", lambda h, l=S["SELP"][0:nq, c * 128:(c + 1) * 128], r=rqrows[0:nq, 0:512]:
                           h.matmul(ctx.pb[5][:, :], l, r, start=True, stop=True), inc=1)
                p.wait("v", tbq)
                tvm = p.op("v", lambda h, a=qtile(c): h.tensor_tensor(a, a, ctx.pb[5][:, :], ALU.mult), inc=1)
                bank_tok[5] = tvm
                qe_tok[("n", c)] = tvm
        if getattr(cfg, "dbg", None) is not None and qc == cfg.dbg["qc"] and "q_ap" in cfg.dbg:
            for fc in range(FQ):
                p.wait("sy", qe_tok[("e", fc)])
            p.dma("sy", cfg.dbg["q_ap"], SCR[:, 0:4096])
            if "wq_ap" in cfg.dbg:
                p.dma("sy", cfg.dbg["wq_ap"], WQb[:, :])
            p.barrier()
        for c in range(NQC):
            g = (c // 2) if dup else c
            q_ready = qe_tok.get(("n", c)) or qe_tok.get(("r", c)) or qe_tok[("e", c)]
            pvA, pvB = ctx.pb[4], ctx.pb[5]
            use_bank("t", 4)
            use_bank("t", 5)
            p.wait("t", q_ready)
            nsched = len(sched)
            pend = None

            def emit_pv(ent, last):
                idx0, teP0, pA0, pB0, vbA0, vbB0 = ent
                p.wait("t", teP0)
                p.op("t", lambda h, l=VTOK[:, vbA0:vbA0 + 65], r=pA0, o=pvA[0:65, :], s=(idx0 == 0), e=last:
                     h.matmul(o, l, r, start=s, stop=e))
                tPV0 = p.op("t", lambda h, l=VTOK[:, vbB0:vbB0 + 65], r=pB0, o=pvB[0:65, :], s=(idx0 == 0), e=last:
                            h.matmul(o, l, r, start=s, stop=e), inc=1)
                p_tok[2 * (idx0 % 2)] = tPV0
                return tPV0

            for idx, (kt, mi) in enumerate(sched):
                bA = idx % 2
                sA, sB = ctx.pb[0 + 2 * bA], ctx.pb[1 + 2 * bA]
                use_bank("t", 0 + 2 * bA)
                use_bank("t", 1 + 2 * bA)
                p.op("t", lambda h, l=KTS[0:64, g * T + kt * 128: g * T + kt * 128 + 128],
                     r=qtile(c)[0:64, :], o=sA: h.matmul(o[:, :], l, r, start=True, stop=True, tile_position=(0, 0)))
                tSB = p.op("t", lambda h, l=KTS[64:128, g * T + kt * 128: g * T + kt * 128 + 128],
                           r=qtile(c)[64:128, :], o=sB: h.matmul(o[:, :], l, r, start=True, stop=True, tile_position=(64, 0)), inc=1)
                if pend is not None:
                    emit_pv(pend, False)
                if mi is not None:
                    p.wait("v", tSB)
                    p.op("v", lambda h, o=sA, m=S["MASKS"][:, mi * 512:(mi + 1) * 512]: h.tensor_tensor(o[:, :], o[:, :], m, ALU.add))
                    tSB = p.op("v", lambda h, o=sB, m=S["MASKS"][:, mi * 512:(mi + 1) * 512]: h.tensor_tensor(o[:, :], o[:, :], m, ALU.add), inc=1)
                pA, pB = P_t(2 * bA), P_t(2 * bA + 1)
                p.wait("sc", tSB)
                p.wait("sc", p_tok.get(2 * bA))
                if qk is not None:
                    colA = g * 32 * nho + kt * nho
                    colB = colA if dup else colA + 1
                    p.op("sc", lambda h, o=pA, i=sA, scl=nskt[:, colA:colA + 1]:
                         h.activation(o, i[:, :], AF.Exp, bias=cfg.exp_bias, scale=scl))
                    teP = p.op("sc", lambda h, o=pB, i=sB, scl=nskt[:, colB:colB + 1]:
                               h.activation(o, i[:, :], AF.Exp, bias=cfg.exp_bias, scale=scl), inc=1)
                else:
                    p.op("sc", lambda h, o=pA, i=sA: h.activation(o, i[:, :], AF.Exp, bias=cfg.exp_bias, scale=cfg.exp_scale_const))
                    teP = p.op("sc", lambda h, o=pB, i=sB: h.activation(o, i[:, :], AF.Exp, bias=cfg.exp_bias, scale=cfg.exp_scale_const), inc=1)
                bank_tok[0 + 2 * bA] = teP
                bank_tok[1 + 2 * bA] = teP
                hA = (g if dup else 2 * c)
                hB = (g if dup else 2 * c + 1)
                pend = (idx, teP, pA, pB, (hA * 32 + kt) * 65, (hB * 32 + kt) * 65)
            tPV = emit_pv(pend, True)
            p.wait("sc", tPV)
            p.wait("sc", qe_tok.get("rc"))
            p.op("sc", lambda h: h.activation(sumsrow[0:1, 0:512], pvA[64:65, :], AF.Copy))
            tsum = p.op("sc", lambda h: h.activation(sumsrow[0:1, 512:1024], pvB[64:65, :], AF.Copy), inc=1)
            p.wait("v", tsum)
            if getattr(cfg, "sums_clip", None) is not None:
                tcl = p.op("v", lambda h: h.tensor_scalar_max(sumsrow[0:1, :], sumsrow[0:1, :], cfg.sums_clip), inc=1)
                p.wait("v", tcl, force=True)
            trc = p.op("v", lambda h: h.reciprocal(recips[0:1, 0:1024], sumsrow[0:1, :]), inc=1)
            qe_tok["rc"] = trc
            use_bank("t", 6)
            p.wait("t", trc)
            p.op("t", lambda h: h.matmul(ctx.pb[6][0:64, :], ctx.ones_row[0:1, 0:64], recips[0:1, 0:512],
                                         start=True, stop=True, tile_position=(0, 0)))
            tbc = p.op("t", lambda h: h.matmul(ctx.pb[6][64:128, :], ctx.ones_row[0:1, 0:64], recips[0:1, 512:1024],
                                               start=True, stop=True, tile_position=(0, 64)), inc=1)
            p.wait("sc", asm_tok)
            p.op("sc", lambda h, o=asm(c)[0:64, :], i=pvA[0:64, :]: h.activation(o, i, AF.Copy))
            tev = p.op("sc", lambda h, o=asm(c)[64:128, :], i=pvB[0:64, :]: h.activation(o, i, AF.Copy), inc=1)
            bank_tok[4] = tev
            bank_tok[5] = tev
            p.wait("v", tev)
            p.wait("v", tbc)
            p.op("v", lambda h, a=asm(c)[0:64, :]: h.tensor_tensor(a, a, ctx.pb[6][0:64, :], ALU.mult))
            tno = p.op("v", lambda h, a=asm(c)[64:128, :]: h.tensor_tensor(a, a, ctx.pb[6][64:128, :], ALU.mult), inc=1)
            bank_tok[6] = tno
            qe_tok[("a", c)] = tno
        if getattr(cfg, "dbg", None) is not None and qc == cfg.dbg["qc"]:
            for nm, ap in cfg.dbg["aps"].items():
                for c2 in range(NQC):
                    p.wait("sy", qe_tok[("a", c2)])
                p.wait("sy", p.tok("v"))
                p.wait("sy", p.tok("sc"))
                p.dma("sy", ap[1], ap[0])
        for c in range(NQC):
            p.wait("t", qe_tok[("a", c)])
        for fc in range(NCHUNK):
            b = fc % 2
            pbO = ctx.pb[b]
            use_bank("t", b)
            if getattr(cfg, "wo_stream", None) is not None:
                p.wait("g", qe_tok.get(("wo", b)))
                if fc < 2:
                    p.wait("g", p.tok("t"))
                two = p.dma("g", S["WOB"][:, b * 1024: b * 1024 + IC * 128].rearrange("p (k c) -> p k c", c=128),
                            cfg.wo_stream[:, fc * 128:(fc + 1) * 128].rearrange("(k p) c -> p k c", p=128))
                p.wait("t", two)
                wsrc = lambda ic, b=b: S["WOB"][:, b * 1024 + ic * 128: b * 1024 + ic * 128 + 128]
            else:
                wsrc = lambda ic, fc=fc: WOS[:, ic * 1024 + fc * 128: ic * 1024 + fc * 128 + 128]
            for ic in range(IC):
                tm = p.op("t", lambda h, w=wsrc(ic), x=asm(ic), o=pbO, s=(ic == 0), e=(ic == IC - 1):
                          h.matmul(o[:, :], w, x, start=s, stop=e), inc=(1 if ic == IC - 1 else 0))
            o_sl = ostg[:, b * 512: b * 512 + 512]
            res = cfg.res(fc, q0) if getattr(cfg, "res", None) is not None else None
            p.wait("sc" if getattr(cfg, "bo", None) is not None else "v", tm)
            if getattr(cfg, "bo", None) is not None:
                p.wait("sc", qe_tok.get(("o", b)))
                te = p.op("sc", lambda h, o=o_sl, i=pbO, ba=cfg.bo[:, fc:fc + 1]: h.activation(o, i[:, :], AF.Identity, bias=ba), inc=1)
                if res is not None:
                    rap, rtok = res
                    p.wait("v", te)
                    p.wait("v", rtok)
                    te = p.op("v", lambda h, o=o_sl, r=rap: h.tensor_tensor(o, o, r, ALU.add), inc=1)
            else:
                p.wait("v", tm)
                p.wait("v", qe_tok.get(("o", b)))
                if res is not None:
                    rap, rtok = res
                    p.wait("v", rtok)
                    te = p.op("v", lambda h, o=o_sl, i=pbO, r=rap: h.tensor_tensor(o, i[:, :], r, ALU.add), inc=1)
                else:
                    te = p.op("v", lambda h, o=o_sl, i=pbO: h.tensor_copy(o, i[:, :]), inc=1)
            bank_tok[b] = te
            qe_tok[("wo", b)] = tm
            p.wait("sy", te)
            td = p.dma("sy", out_dram[fc * 128:(fc + 1) * 128, q0:q0 + 512], o_sl)
            qe_tok[("o", b)] = td
        asm_tok = p.tok("t")
    p.barrier()


class SrcStream:

    def __init__(self, ctx, dram, buf, queue="sy", barrier=False):
        self.ctx, self.dram, self.buf, self.q = ctx, dram, buf, queue
        self.barrier = barrier
        self.key = {}

    def __call__(self, kc, t0, w=512):
        p = self.ctx.p
        cur = self.key.get(kc)
        if cur is not None and cur[0] == t0:
            return (self.buf[:, kc * 512:(kc + 1) * 512], cur[1])
        if cur is not None:
            if self.barrier or _os.environ.get("PROVIDER_BARRIER"):
                p.barrier()
            p.wait(self.q, p.tok("t"))
            p.wait(self.q, p.tok("v"))
            p.wait(self.q, p.tok("sc"))
        tl = p.dma(self.q, self.buf[:, kc * 512:(kc + 1) * 512],
                   self.dram[kc * 128:(kc + 1) * 128, t0:t0 + w])
        self.key[kc] = (t0, tl)
        return (self.buf[:, kc * 512:(kc + 1) * 512], tl)


def emit_rms_row_stream(ctx, src, out_row, eps=1e-6):
    p = ctx.p
    S = ctx.sb
    rows = S["sumsrow"]
    ps = ctx.pb[6]
    for tb in range(8):
        t0 = tb * 512
        t_done = []
        for ch in range(NCHUNK):
            xap, tok = src(ch, t0)
            sq = S["VSTG"][:, (ch % 2) * 512:(ch % 2) * 512 + 512]
            p.wait("v", tok)
            if ch >= 2:
                p.wait("v", t_done[ch - 2])
            tv = p.op("v", lambda h, o=sq, i=xap: h.tensor_tensor(o, i, i, ALU.mult), inc=1)
            p.wait("t", tv)
            tm = p.op("t", lambda h, b=sq, s=(ch == 0): h.matmul(ps[0:1, :], ctx.ones_bf[:, :], b, start=s, stop=(ch == 7)), inc=1)
            t_done.append(tm)
        r1 = rows[0:1, 0:512]
        p.wait("v", t_done[-1])
        tv = p.op("v", lambda h: h.tensor_scalar(r1, ps[0:1, :], 1.0 / D, eps, ALU.mult, ALU.add), inc=1)
        p.wait("sc", tv)
        ts = p.op("sc", lambda h: h.activation(r1, r1, AF.Sqrt), inc=1)
        p.wait("v", ts)
        tvr = p.op("v", lambda h, o=out_row[0:1, t0:t0 + 512]: h.reciprocal(o, r1), inc=1, lp=True)
        p.wait("t", tvr)
    return p.tok("v")


def _attn_sbuf_spec(nq, nkvc, nkvh, nho, kv_kc, wk_cols, wv_cols, *, rope, nmask, latent,
                    wo_stream=False, qk=True, scaled=True, wv_shared=False, bcb=True):
    NQC = nq // 2
    spec = [
        ("SCR", (128, 14336), BF16),
        ("KTS", (128, nkvc * T), BF16),
        ("VTOK", (128, nkvh * 32 * 65), BF16),
        ("SRCB", (128, 4096), BF16),
        ("WK", (128, kv_kc * max(wk_cols, wv_cols if wv_shared else 0)), BF16),
        ("WQB", (128, 2048), BF16),
        ("MASKH", (128, 3), BF16),
        ("IDENT", (128, 128), BF16),
        ("sumsrow", (1, 1024), F32),
        ("recips", (1, 1024), F32),
        ("fscr", (128, 256), F32),
        ("VSTG", (128, 1024), BF16),
        ("OSTG", (128, 1024), BF16),
    ]
    if not wv_shared:
        spec.append(("WV", (128, kv_kc * wv_cols), BF16))
    if qk:
        spec += [("SELP", (nq, NQC * 128), BF16),
                 ("nskt", (128, max(1, nkvc * 32 * nho)), F32),
                 ("rqrows", (16, 512), BF16)]
    if bcb:
        spec.append(("bcbuf", (128, 1024), BF16))
    if scaled:
        spec += [("wstage", (128, 2048), F32), ("wstageb", (128, 2048), BF16)]
    spec.append(("WOB", (128, 2048), BF16) if wo_stream else ("WOS", (128, NQC * 1024), BF16))
    if rope:
        spec.append(("ROPE", (128, 2 * T), BF16))
    if nmask:
        spec.append(("MASKS", (128, nmask * 512), BF16))
    if latent:
        spec.append(("WD", (128, NCHUNK * 256), BF16))
        spec.append(("LATN", (128, 2 * T), BF16))
    return spec


def _load_attn_consts(ctx, aps, *, rope_key=None, nmask=0, mask_key=None, rs1=None):
    p = ctx.p
    S = ctx.sb
    toks = []
    toks.append(p.dma("g", S["MASKH"][:, :], aps["maskh"][:, :]))
    toks.append(p.dma("g", S["IDENT"][:, :], aps["ident"][:, :]))
    if rope_key:
        toks.append(p.dma("g", S["ROPE"][:, :], aps[rope_key][:, :]))
    for mi in range(nmask):
        toks.append(p.dma("sy", S["MASKS"][:, mi * 512:(mi + 1) * 512], aps[mask_key][mi * 128:(mi + 1) * 128, :]))
    return toks


def _sched_causal(qc):
    return [(kt, (kt - qc * 4) if kt >= qc * 4 else None) for kt in range(qc * 4 + 4)]


def _sched_local(qc):
    out = []
    for kt in range(max(0, qc * 4 - 2), qc * 4 + 4):
        out.append((kt, kt - qc * 4 + 2))
    return out


def _sched_full(qc):
    return [(kt, None) for kt in range(32)]


def build_g2(shared, percore0, dbg_qc=None):
    import contextlib
    nc = bass.Bass("TRN2", target_bir_lowering=False)
    stack = contextlib.ExitStack()
    nc._dev_stack = stack
    aps = _declare(nc, shared, percore0, ["g_wq", "g_kvd", "g_kup", "g_vup", "g_wo",
                                         "w1n", "ropeg", "cmask", "ident", "selp16", "maskh"],
                   extra={"xnT_in": ([D, T], BF16), "rs1_in": ([1, T], F32)})
    outs = {"x1T": ([D, T], BF16)}
    if dbg_qc is not None:
        outs.update({"d_scr": ([128, 14336], BF16), "d_nskt": ([128, 128], F32),
                     "d_sums": ([1, 1024], F32), "d_recips": ([1, 1024], F32)})
    aps.update(_outs(nc, outs))
    wqbf = nc.dram_tensor("wqbf", [D, 2048], BF16, kind="Internal").ap()
    spec = _attn_sbuf_spec(16, 4, 4, 1, 2, 1024, 256, rope=True, nmask=4, latent=True)
    spec += [("W1N", (128, 8), F32), ("RS1", (1, T), BF16)]
    with nc.Block() as block:
        ctx = Ctx(nc, stack, aps, consts_vals=(-8.0,), sbuf_spec=spec)
        ctx.setup()
        p = ctx.p
        S = ctx.sb
        p.dma("g", S["W1N"][:, :], aps["w1n"][:, :])
        p.dma("g", S["RS1"][:, :], aps["rs1_in"][:, :])
        p.dma("g", S["SELP"][:, :], aps["selp16"][:, :])
        _load_attn_consts(ctx, aps, rope_key="ropeg", nmask=4, mask_key="cmask")
        p.barrier()
        emit_w_prep(ctx, aps["g_wq"], wqbf, 8, 2048, S["W1N"])
        load_rows(ctx, aps["g_kvd"], S["WD"], 8, 256, scale=S["W1N"])
        load_rows(ctx, aps["g_kup"], S["WK"], 2, 1024)
        load_rows(ctx, aps["g_vup"], S["WV"], 2, 256)
        load_rows(ctx, aps["g_wo"], S["WOS"], 8, 1024)
        p.barrier()
        src = SrcStream(ctx, aps["xnT_in"], S["SRCB"])
        cfg = AttnCfg(nheads=16, kv_dup=True, nkv_chunks=4, nkv_heads=4, rope="ROPE", qk="rms",
                      wq=wqbf, wq_bf=True, wk_cols=1024, wv_cols=256,
                      latent=dict(scale_row=S["RS1"]), exp_premul=0.125, exp_bias=-8.0,
                      mask_sched=_sched_causal, src=src,
                      res=lambda fc, q0: src(fc, q0))
        if dbg_qc is not None:
            cfg.dbg = {"qc": dbg_qc, "aps": {
                "scr": (S["SCR"][:, :], aps["d_scr"][:, :]),
                "nskt": (S["nskt"][:, :], aps["d_nskt"][:, :]),
                "sums": (S["sumsrow"][:, :], aps["d_sums"][:, :]),
                "recips": (S["recips"][:, :], aps["d_recips"][:, :]),
            }}
        emit_attn(ctx, cfg, aps["x1T"])
        _finish(ctx)
        ctx.p.build(block)
    return nc


class SrcResident:

    def __init__(self, ctx, arena, tok):
        self.arena, self.tok = arena, tok

    def __call__(self, kc, t0, w=512):
        return (self.arena[:, kc * T + t0: kc * T + t0 + w], self.tok)


def build_g3(shared, percore0, dbg_qc=None, resident=False):
    import contextlib
    nc = bass.Bass("TRN2", target_bir_lowering=False)
    stack = contextlib.ExitStack()
    nc._dev_stack = stack
    aps = _declare(nc, shared, percore0, ["l_wq", "l_wk", "l_wv", "l_wo",
                                         "w2n", "ropel", "lmask", "ident", "selp16", "maskh"],
                   extra={"x1T_in": ([D, T], BF16)})
    outs3 = {"x2T": ([D, T], BF16)}
    if dbg_qc is not None:
        outs3.update({"d_scr": ([128, 14336], BF16), "d_nskt": ([128, 128], F32),
                      "d_sums": ([1, 1024], F32), "d_rs2": ([1, T], BF16),
                      "d_kts": ([128, 4 * T], BF16), "d_wk": ([128, 8192], BF16),
                      "d_nskt0": ([128, 128], F32), "d_dq": ([128, 1024], BF16), "d_fscr": ([128, 256], F32)})
    aps.update(_outs(nc, outs3))
    wqbf = nc.dram_tensor("wqbf", [D, 2048], BF16, kind="Internal").ap()
    spec = _attn_sbuf_spec(16, 4, 4, 1, 8, 1024, 256, rope=True, nmask=6, latent=False)
    spec += [("W2N", (128, 8), F32), ("RS2", (1, T), BF16)]
    if resident:
        spec = [s for s in spec if s[0] != "SRCB"] + [("XA", (128, NCHUNK * T), BF16)]
    with nc.Block() as block:
        ctx = Ctx(nc, stack, aps, consts_vals=(-8.0,), sbuf_spec=spec)
        ctx.setup()
        p = ctx.p
        S = ctx.sb
        p.dma("g", S["W2N"][:, :], aps["w2n"][:, :])
        p.dma("g", S["SELP"][:, :], aps["selp16"][:, :])
        _load_attn_consts(ctx, aps, rope_key="ropel", nmask=6, mask_key="lmask")
        p.barrier()
        emit_w_prep(ctx, aps["l_wq"], wqbf, 8, 2048, S["W2N"])
        load_rows(ctx, aps["l_wk"], S["WK"], 8, 1024, scale=S["W2N"])
        load_rows(ctx, aps["l_wv"], S["WV"], 8, 256, scale=S["W2N"])
        load_rows(ctx, aps["l_wo"], S["WOS"], 8, 1024)
        p.barrier()
        if resident:
            for kc in range(NCHUNK):
                tok = p.dma("g", S["XA"][:, kc * T:(kc + 1) * T], aps["x1T_in"][kc * 128:(kc + 1) * 128, :])
            p.barrier()
            src = SrcResident(ctx, S["XA"], None)
        else:
            src = SrcStream(ctx, aps["x1T_in"], S["SRCB"])
            import os as _os
            if _os.environ.get("PARANOID"):
                src.paranoid = True
        emit_rms_row_stream(ctx, src, S["RS2"], eps=1e-6)
        cfg = AttnCfg(nheads=16, kv_dup=True, nkv_chunks=4, nkv_heads=4, rope="ROPE", qk="rms",
                      wq=wqbf, wq_bf=True, wk_cols=1024, wv_cols=256,
                      v_scale_row=S["RS2"], exp_premul=0.125, exp_bias=-8.0,
                      mask_sched=_sched_local, src=src,
                      res=lambda fc, q0: src(fc, q0))
        if dbg_qc is not None:
            cfg.dbg = {"qc": dbg_qc, "aps": {
                "scr": (S["SCR"][:, :], aps["d_scr"][:, :]),
                "nskt": (S["nskt"][:, :], aps["d_nskt"][:, :]),
                "sums": (S["sumsrow"][:, :], aps["d_sums"][:, :]),
                "rs2": (S["RS2"][:, :], aps["d_rs2"][:, :]),
            }, "kv_aps": {
                "kts": (S["KTS"][:, :], aps["d_kts"][:, :]),
                "wk": (S["WK"][:, :], aps["d_wk"][:, :]),
            }, "knorm_aps": {
                "nskt0": (S["nskt"][:, :], aps["d_nskt0"][:, :]),
                "dq": (S["SCR"][:, 12288:13312], aps["d_dq"][:, :]),
                "fscr": (S["fscr"][:, :], aps["d_fscr"][:, :]),
            }}
        emit_attn(ctx, cfg, aps["x2T"])
        _finish(ctx)
        ctx.p.build(block)
    return nc



def emit_moe(ctx, cfg, finish):
    p = ctx.p
    S = ctx.sb
    bank_tok = {}

    def use_bank(e, b):
        tok = bank_tok.get(b)
        if tok is not None:
            p.wait(e, tok)
    XMB = S["XMB"]
    W1S, W2S = S["W1S"], S["W2S"]
    HS = S["HS"]
    YA = S["YA"]
    CMB = S["CMB"]
    WTR = S["WTR"]
    WBC = S["WBC"]
    GSC = S["GSC"]
    renorm = cfg.renorm
    act = cfg.act

    wgs = S["WGS"]
    load_rows(ctx, cfg.wg, wgs, NCHUNK, 8)
    if cfg.gb is not None:
        tgb = p.dma("g", S["GB"][:, :], cfg.gb[0:1, :])
    p.barrier()
    for tt in range(32):
        t0 = (tt // 4) * 512
        sub = (tt % 4) * 128
        pg = ctx.pb[6]
        tok_bank = p.tok("sc")
        for kc in range(NCHUNK):
            xap, tok = cfg.src(kc, t0)
            p.wait("t", tok)
            tm = p.op("t", lambda h, l=xap[:, sub:sub + 128], r=wgs[:, kc * 8:(kc + 1) * 8],
                      o=pg[:, 0:8], s=(kc == 0), e=(kc == NCHUNK - 1 and cfg.gb is None):
                      h.matmul(o, l, r, start=s, stop=e), inc=(1 if (kc == NCHUNK - 1 and cfg.gb is None) else 0))
        if cfg.gb is not None:
            p.wait("t", tgb)
            tm = p.op("t", lambda h: h.matmul(pg[:, 0:8], ctx.ones_row_bf[0:1, :], S["GB"][0:1, :],
                                              start=False, stop=True), inc=1)
        g8 = GSC[:, 0:8]
        mx = GSC[:, 8:9]
        nmx = GSC[:, 9:10]
        sm = GSC[:, 10:11]
        rs = GSC[:, 11:12]
        mx2 = GSC[:, 12:13]
        den = GSC[:, 13:14]
        w1c = GSC[:, 14:15]
        w2c = GSC[:, 15:16]
        oh1 = GSC[:, 16:24]
        msk = GSC[:, 24:32]
        p.wait("v", tm)

        def vs(fn):
            tk = p.op("v", fn, inc=1)
            p.wait("v", tk, force=True)
            return tk
        vs(lambda h: h.tensor_copy(g8, pg[:, 0:8]))
        vs(lambda h: h.reduce_max(mx, g8, AX.X))
        tvn = vs(lambda h: h.tensor_scalar(nmx, mx, -1.0, None, ALU.mult))
        p.wait("sc", tvn)
        tse = p.op("sc", lambda h: h.activation(g8, g8, AF.Exp, bias=nmx, scale=1.0), inc=1)
        p.wait("v", tse)
        vs(lambda h: h.reduce_sum(sm, g8, AX.X))
        vs(lambda h: h.reciprocal(rs, sm))
        vs(lambda h: h.tensor_scalar(g8, g8, rs, None, ALU.mult))
        vs(lambda h: h.reduce_max(mx, g8, AX.X))
        vs(lambda h: h.tensor_scalar(oh1, g8, mx, None, ALU.is_ge))
        vs(lambda h: h.tensor_tensor(msk, g8, oh1, ALU.mult))
        vs(lambda h: h.tensor_tensor(msk, g8, msk, ALU.subtract))
        vs(lambda h: h.reduce_max(mx2, msk, AX.X))
        if renorm:
            vs(lambda h: h.tensor_tensor(den, mx, mx2, ALU.add))
            vs(lambda h: h.tensor_scalar(den, den, 1e-8, None, ALU.add))
            vs(lambda h: h.reciprocal(rs, den))
            vs(lambda h: h.tensor_tensor(w1c, mx, rs, ALU.mult))
            vs(lambda h: h.tensor_tensor(w2c, mx2, rs, ALU.mult))
        else:
            vs(lambda h: h.tensor_copy(w1c, mx))
            vs(lambda h: h.tensor_copy(w2c, mx2))
        vs(lambda h: h.tensor_scalar(msk, msk, mx2, None, ALU.is_ge))
        vs(lambda h: h.tensor_scalar(oh1, oh1, w1c, None, ALU.mult))
        vs(lambda h: h.tensor_scalar(msk, msk, w2c, None, ALU.mult))
        tvc = vs(lambda h, o=CMB[:, tt * 8:(tt + 1) * 8]:
                 h.tensor_tensor(o, oh1, msk, ALU.add))
        p.wait("t", p.tok("v"))
    p.barrier()

    for mega in range(2):
        tm0 = mega * 2048
        for kc in range(NCHUNK):
            for half in range(4):
                xap, tok = cfg.src(kc, tm0 + half * 512)
                p.wait("v", tok)
                p.op("v", lambda h, o=XMB[:, kc * 2048 + half * 512: kc * 2048 + half * 512 + 512], i=xap:
                     h.tensor_copy(o, i), inc=(1 if half == 3 else 0))
        p.barrier()
        tr_tok = None
        for tl in range(16):
            tt = mega * 16 + tl
            p.wait("t", tr_tok)
            ttr = p.op("t", lambda h, i=CMB[:, tt * 8:(tt + 1) * 8], o=ctx.pbbf[0:8, 0:128]:
                       h.transpose(o, i, S["IDENT"][:, :]), inc=1)
            p.wait("sc", ttr)
            tr_tok = p.op("sc", lambda h, o=WTR[0:8, tl * 128:(tl + 1) * 128], i=ctx.pbbf[0:8, 0:128]:
                          h.activation(o, i, AF.Copy), inc=1)
        p.barrier()
        xm_tok = p.tok("v")
        wbc_tok = {}
        E_LIST = [int(x) for x in _os.environ.get("MOE_EXPERTS", "0,1,2,3,4,5,6,7").split(",")]
        h_done_prev = None
        for ei, e in enumerate(E_LIST):
            p.wait("g", h_done_prev if h_done_prev is not None else p.tok("t"))
            tw1 = [p.dma("g", W1S[:, kc * 2048:(kc + 1) * 2048], cfg.w1[e, kc * 128:(kc + 1) * 128, :]) for kc in range(NCHUNK)]
            p.wait("g", p.tok("t"))
            tw2 = [p.dma("g", W2S[:, hc * 1024:(hc + 1) * 1024], cfg.w2[e, hc * 128:(hc + 1) * 128, :]) for hc in range(16)]
            if cfg.b1 is not None:
                p.wait("g", p.tok("v"))
                tb1 = p.dma("g", S["B1E"][:, :], cfg.b1[e, :, :])
                p.wait("v", tb1)
            p.waits("t", tw1)
            p.waits("t", tw2)
            p.wait("t", xm_tok)
            for tc in range(4):
                sl = tc % 2
                tbc = p.op("t", lambda h, l=S["SEL8"][0:8, e * 128:(e + 1) * 128],
                           r=WTR[0:8, tc * 512: tc * 512 + 512]:
                           h.matmul(ctx.pb[5][:, :], l, r, start=True, stop=True), inc=1)
                p.wait("sc", tbc)
                p.wait("sc", wbc_tok.get(sl))
                twb = p.op("sc", lambda h, o=WBC[:, sl * 512: sl * 512 + 512]:
                           h.activation(o, ctx.pb[5][:, :], AF.Copy), inc=1)
                p.wait("t", twb)
                for fc in range(16):
                    b = fc % 2
                    pb = ctx.pb[b]
                    use_bank("t", ("m", b))
                    for kc in range(NCHUNK):
                        tm = p.op("t", lambda h, l=W1S[:, kc * 2048 + fc * 128: kc * 2048 + fc * 128 + 128],
                                  r=XMB[:, kc * 2048 + tc * 512: kc * 2048 + tc * 512 + 512],
                                  o=pb, s=(kc == 0), e2=(kc == NCHUNK - 1):
                                  h.matmul(o[:, :], l, r, start=s, stop=e2), inc=(1 if kc == NCHUNK - 1 else 0))
                    hs = HS[:, fc * 512:(fc + 1) * 512]
                    p.wait("v", tm)
                    p.wait("v", twb)
                    if act == "relu2":
                        p.op("v", lambda h, o=hs, i=pb: h.tensor_scalar_max(o, i[:, :], 0.0))
                        p.op("v", lambda h, o=S["HTMP"][:, 0:512], a=hs, w=WBC[:, sl * 512: sl * 512 + 512]:
                             h.tensor_tensor(o, a, w, ALU.mult))
                        tve = p.op("v", lambda h, o=hs, a=S["HTMP"][:, 0:512], b2=hs:
                                   h.tensor_tensor(o, a, b2, ALU.mult), inc=1)
                    else:
                        p.op("v", lambda h, o=hs, i=pb, b1a=S["B1E"][:, fc:fc + 1]:
                             h.tensor_scalar(o, i[:, :], b1a, 0.0, ALU.add, ALU.max))
                        tve = p.op("v", lambda h, o=hs, w=WBC[:, sl * 512: sl * 512 + 512]:
                                   h.tensor_tensor(o, o, w, ALU.mult), inc=1)
                    bank_tok[("m", b)] = tve
                    wbc_tok[sl] = tve
                if tc == 3:
                    h_done_prev = tm
                p.wait("t", tve)
                for dc in range(NCHUNK):
                    b = 2 + dc % 2
                    pb = ctx.pb[b]
                    use_bank("t", ("m", b))
                    first_extra = (ei == 0 and cfg.b2 is not None)
                    for hc in range(16):
                        tm = p.op("t", lambda h, l=W2S[:, hc * 1024 + dc * 128: hc * 1024 + dc * 128 + 128],
                                  r=HS[:, hc * 512:(hc + 1) * 512],
                                  o=pb, s=(hc == 0), e2=(hc == 15 and not first_extra):
                                  h.matmul(o[:, :], l, r, start=s, stop=e2),
                                  inc=(1 if (hc == 15 and not first_extra) else 0))
                    if first_extra:
                        tm = p.op("t", lambda h, l=S["B2S"][0:8, dc * 128:(dc + 1) * 128],
                                  r=WTR[0:8, tc * 512: tc * 512 + 512], o=pb:
                                  h.matmul(o[:, :], l, r, start=False, stop=True), inc=1)
                    ya = YA[:, dc * 2048 + tc * 512: dc * 2048 + tc * 512 + 512]
                    p.wait("v", tm)
                    if ei == 0:
                        tve = p.op("v", lambda h, o=ya, i=pb: h.tensor_copy(o, i[:, :]), inc=1)
                    else:
                        tve = p.op("v", lambda h, o=ya, i=pb: h.tensor_tensor(o, o, i[:, :], ALU.add), inc=1)
                    bank_tok[("m", b)] = tve
                    if _os.environ.get("MOE_SERIAL"):
                        p.wait("t", tve)
        finish(mega, tm0)
        p.barrier()


MOE_SBUF = [
    ("XMB", (128, 8 * 2048), BF16), ("W1S", (128, 8 * 2048), BF16), ("W2S", (128, 16 * 1024), BF16),
    ("HS", (128, 16 * 512), BF16), ("YA", (128, 8 * 2048), F32),
    ("CMB", (128, 8 * 32), BF16), ("WTR", (8, 2048), BF16), ("WBC", (128, 1024), BF16),
    ("GSC", (128, 32), F32), ("WGS", (128, 64), BF16), ("GB", (1, 8), BF16),
    ("B1E", (128, 16), F32), ("B2S", (8, 1024), BF16), ("HTMP", (128, 512), BF16),
    ("SEL8", (8, 8 * 128), BF16), ("IDENT", (128, 128), BF16),
    ("SRCB", (128, 4096), BF16), ("FSTG", (128, 1024), BF16), ("YSTG", (128, 512), F32),
]


def build_g4(shared, percore0, dbg=False):
    import contextlib
    nc = bass.Bass("TRN2", target_bir_lowering=False)
    stack = contextlib.ExitStack()
    nc._dev_stack = stack
    aps = _declare(nc, shared, percore0, ["m0_wg", "m0_w1", "m0_w2", "sel8", "ident", "x0"],
                   extra={"xnT_in": ([D, T], BF16), "x2T_in": ([D, T], BF16)})
    outs4 = {"xnewT": ([D, T], F32)}
    if dbg:
        outs4.update({"d_cmb": ([128, 256], BF16), "d_wtr": ([8, T], BF16),
                      "d_ya": ([128, 16384], F32), "d_hs": ([128, 16 * 512], BF16),
                      "d_xmb": ([128, 16384], BF16), "d_w1s": ([128, 16384], BF16)})
    aps.update(_outs(nc, outs4))
    with nc.Block() as block:
        ctx = Ctx(nc, stack, aps, consts_vals=(), sbuf_spec=MOE_SBUF)
        ctx.setup()
        p = ctx.p
        S = ctx.sb
        p.dma("g", S["SEL8"][:, :], aps["sel8"][:, :])
        p.dma("g", S["IDENT"][:, :], aps["ident"][:, :])
        p.barrier()
        src = SrcStream(ctx, aps["xnT_in"], S["SRCB"])
        cfg = AttnCfg(src=src, wg=aps["m0_wg"], gb=None, w1=aps["m0_w1"], w2=aps["m0_w2"],
                      b1=None, b2=None, act="relu2", renorm=True)

        def finish(mega, tm0):
            p.barrier()
            if dbg and mega == 0:
                p.dma("sy", aps["d_cmb"][:, :], S["CMB"][:, :])
                p.dma("sy", aps["d_wtr"][:, :], S["WTR"][:, :])
                p.dma("sy", aps["d_ya"][:, :], S["YA"][:, :])
                p.dma("sy", aps["d_hs"][:, :], S["HS"][:, :])
                p.dma("sy", aps["d_xmb"][:, :], S["XMB"][:, :])
                p.dma("sy", aps["d_w1s"][:, :], S["W1S"][:, :])
                p.barrier()
            for dc in range(NCHUNK):
                for tc in range(4):
                    ya = S["YA"][:, dc * 2048 + tc * 512: dc * 2048 + tc * 512 + 512]
                    gs = S["FSTG"][:, (tc % 2) * 512:(tc % 2) * 512 + 512]
                    t0 = tm0 + tc * 512
                    tls = p.dma("sy", S["HTMP"][:, 0:512], aps["x2T_in"][dc * 128:(dc + 1) * 128, t0:t0 + 512])
                    tlx = p.dma("sy", S["XMB"].bitcast(F32)[:, 0:512], aps["x0"][dc * 128:(dc + 1) * 128, t0:t0 + 512])
                    tsl = p.op("sc", lambda h, o=gs, i=ya: h.activation(o, i, AF.Silu), inc=1)
                    p.wait("v", tsl)
                    p.wait("v", tls)
                    p.wait("v", tlx)
                    p.op("v", lambda h, o=gs, s2=S["HTMP"][:, 0:512]: h.tensor_tensor(o, o, s2, ALU.mult))
                    tvf = p.op("v", lambda h, o=S["XMB"].bitcast(F32)[:, 512:1024], a=S["XMB"].bitcast(F32)[:, 0:512], g=gs:
                               h.tensor_tensor(o, a, g, ALU.add), inc=1)
                    p.wait("sy", tvf)
                    p.dma("sy", aps["xnewT"][dc * 128:(dc + 1) * 128, t0:t0 + 512], S["XMB"].bitcast(F32)[:, 512:1024])
                    p.barrier()

        emit_moe(ctx, cfg, finish)
        _finish(ctx)
        ctx.p.build(block)
    return nc


def build_g5(shared, percore0, key_in="xnewT_in", key_out="xn1T"):
    import contextlib
    nc = bass.Bass("TRN2", target_bir_lowering=False)
    stack = contextlib.ExitStack()
    nc._dev_stack = stack
    aps = _declare(nc, shared, percore0, [], extra={key_in: ([D, T], F32)})
    aps.update(_outs(nc, {key_out: ([D, T], BF16)}))
    sbuf_spec = [
        ("ln_xstg", (128, 1024), F32), ("ln_sqt", (128, 1024), BF16),
        ("ln_rows", (1, 4096), F32), ("ln_abc", (128, 512), BF16), ("ln_bbc", (128, 512), BF16),
        ("XN", (128, NCHUNK * T), BF16),
    ]
    with nc.Block() as block:
        ctx = Ctx(nc, stack, aps, consts_vals=(), sbuf_spec=sbuf_spec)
        ctx.setup()
        emit_ln(ctx, aps[key_in], ctx.sb["XN"], out_dram=aps[key_out])
        _finish(ctx)
        ctx.p.build(block)
    return nc


def build_g6(shared, percore0):
    import contextlib
    nc = bass.Bass("TRN2", target_bir_lowering=False)
    stack = contextlib.ExitStack()
    nc._dev_stack = stack
    aps = _declare(nc, shared, percore0, ["a1_wq", "a1_wk", "a1_wv", "a1_wo", "ident", "selp8", "maskh"],
                   extra={"xn1T_in": ([D, T], BF16)})
    aps.update(_outs(nc, {"s1T": ([D, T], BF16)}))
    spec = _attn_sbuf_spec(8, 4, 8, 2, 8, 512, 512, rope=False, nmask=0, latent=False, scaled=False, bcb=False)
    with nc.Block() as block:
        ctx = Ctx(nc, stack, aps, consts_vals=(-16.0,), sbuf_spec=spec)
        ctx.setup()
        p = ctx.p
        S = ctx.sb
        p.dma("g", S["SELP"][0:8, :], aps["selp8"][:, :])
        _load_attn_consts(ctx, aps)
        p.barrier()
        load_rows(ctx, aps["a1_wk"], S["WK"], 8, 512)
        load_rows(ctx, aps["a1_wv"], S["WV"], 8, 512)
        load_rows(ctx, aps["a1_wo"], S["WOS"], 4, 1024)
        p.barrier()
        src = SrcStream(ctx, aps["xn1T_in"], S["SRCB"])
        cfg = AttnCfg(nheads=8, kv_dup=False, nkv_chunks=4, nkv_heads=8, rope=None, qk="l2",
                      wq=aps["a1_wq"], wq_bf=False, wk_cols=512, wv_cols=512,
                      exp_premul=16.0, exp_bias=-16.0, sums_clip=1e-6,
                      mask_sched=_sched_full, src=src, res=None)
        emit_attn(ctx, cfg, aps["s1T"])
        _finish(ctx)
        ctx.p.build(block)
    return nc


def build_g7(shared, percore0, dbg=False):
    import contextlib
    nc = bass.Bass("TRN2", target_bir_lowering=False)
    stack = contextlib.ExitStack()
    nc._dev_stack = stack
    aps = _declare(nc, shared, percore0,
                   ["f1_wq", "f1_wk", "f1_wv", "f1_wo", "f1_bq", "f1_bk", "f1_bv", "f1_bo",
                    "f1_g1", "f1_b1n", "ident", "selp16", "maskh"],
                   extra={"xn1T_in": ([D, T], BF16)})
    outs7 = {"xffnT": ([D, T], BF16)}
    if dbg:
        outs7["zT_d"] = ([D, T], BF16)
        outs7["d_scr"] = ([128, 14336], BF16)
        outs7["d_sums"] = ([1, 1024], F32)
        outs7["d_q0"] = ([128, 4096], BF16)
        outs7["d_wq"] = ([128, 2048], BF16)
    aps.update(_outs(nc, outs7))
    zT = aps["zT_d"] if dbg else nc.dram_tensor("zT", [D, T], BF16, kind="Internal").ap()
    spec = _attn_sbuf_spec(16, 8, 16, 2, 8, 1024, 1024, rope=False, nmask=0, latent=False,
                           wo_stream=True, qk=False, scaled=False, wv_shared=True, bcb=False)
    spec += [("BQ", (128, 8), F32), ("BK", (128, 8), F32), ("BV", (128, 8), F32), ("BO", (128, 8), F32),
             ("G1", (128, 8), F32), ("B1N", (128, 8), F32)]
    with nc.Block() as block:
        ctx = Ctx(nc, stack, aps, consts_vals=(-8.0,), sbuf_spec=spec)
        ctx.setup()
        p = ctx.p
        S = ctx.sb
        scr = S["SCR"]
        S["ln_xstg"] = scr.bitcast(F32)[:, 0:1024]
        S["ln_sqt"] = scr[:, 2048:3072]
        S["ln_abc"] = scr[:, 3072:3584]
        S["ln_bbc"] = scr[:, 3584:4096]
        S["ln_rows"] = scr.bitcast(F32)[0:1, 3072:7168]
        for nm, key in [("BQ", "f1_bq"), ("BK", "f1_bk"), ("BV", "f1_bv"), ("BO", "f1_bo"),
                        ("G1", "f1_g1"), ("B1N", "f1_b1n")]:
            p.dma("g", S[nm][:, :], aps[key][:, :])
        _load_attn_consts(ctx, aps)
        p.barrier()
        load_rows(ctx, aps["f1_wk"], S["WK"], 8, 1024)
        p.barrier()
        src = SrcStream(ctx, aps["xn1T_in"], S["SRCB"])
        cfg = AttnCfg(nheads=16, kv_dup=False, nkv_chunks=8, nkv_heads=16, rope=None, qk=None,
                      wq=aps["f1_wq"], wq_bf=False, wk_cols=1024, wv_cols=1024,
                      bq=S["BQ"], bk=S["BK"], bv=S["BV"], bo=S["BO"],
                      exp_scale_const=0.125, exp_bias=-8.0, wo_stream=aps["f1_wo"],
                      load_wv=lambda: load_rows(ctx, aps["f1_wv"], S["WK"], 8, 1024),
                      mask_sched=_sched_full, src=src, res=lambda fc, q0: src(fc, q0))
        if dbg:
            cfg.dbg = {"qc": 0, "aps": {"scr": (S["SCR"][:, :], aps["d_scr"][:, :]),
                                        "sums": (S["sumsrow"][:, :], aps["d_sums"][:, :])},
                       "q_ap": aps["d_q0"][:, :], "wq_ap": aps["d_wq"][:, :]}
        emit_attn(ctx, cfg, zT)
        p.barrier()
        emit_ln(ctx, zT, S["KTS"], src_bf16=True, gamma=S["G1"], beta=S["B1N"], out_dram=aps["xffnT"])
        _finish(ctx)
        ctx.p.build(block)
    return nc


def build_g8(shared, percore0):
    import contextlib
    nc = bass.Bass("TRN2", target_bir_lowering=False)
    stack = contextlib.ExitStack()
    nc._dev_stack = stack
    aps = _declare(nc, shared, percore0,
                   ["f1_gw", "f1_gb", "f1_w1", "f1_b1c", "f1_w2", "f1_b2", "f1_g2", "f1_b2n",
                    "sel8", "ident"],
                   extra={"xffnT_in": ([D, T], BF16), "s1T_in": ([D, T], BF16),
                          "xnewT_in": ([D, T], F32)})
    outs_f = {"outT": ([D, T], F32)}
    if dbg:
        outs_f.update({"d_cmb4": ([128, 256], BF16), "d_wtr4": ([8, 2048], BF16),
                       "d_ya4": ([128, 16384], F32)})
    aps.update(_outs(nc, outs_f))
    spec = MOE_SBUF + [("G2N", (128, 8), F32), ("B2N", (128, 8), F32)]
    with nc.Block() as block:
        ctx = Ctx(nc, stack, aps, consts_vals=(), sbuf_spec=spec)
        ctx.setup()
        p = ctx.p
        S = ctx.sb
        p.dma("g", S["SEL8"][:, :], aps["sel8"][:, :])
        p.dma("g", S["IDENT"][:, :], aps["ident"][:, :])
        p.dma("g", S["B2S"][:, :], aps["f1_b2"][:, :])
        p.dma("g", S["G2N"][:, :], aps["f1_g2"][:, :])
        p.dma("g", S["B2N"][:, :], aps["f1_b2n"][:, :])
        p.dma("g", S["GB"][:, :], aps["f1_gb"][:, :])
        p.barrier()
        src = SrcStream(ctx, aps["xffnT_in"], S["SRCB"])
        cfg = AttnCfg(src=src, wg=aps["f1_gw"], gb=aps["f1_gb"], w1=aps["f1_w1"], w2=aps["f1_w2"],
                      b1=aps["f1_b1c"], b2=aps["f1_b2"], act="relu", renorm=False)
        W2F = S["W2S"].bitcast(F32)
        rows = lambda i: W2F[0:1, 1024 + i * 512: 1024 + (i + 1) * 512]
        abc = S["W2S"][:, 0:512]
        bbc = S["W2S"][:, 512:1024]
        sqt = S["W2S"][:, 1024 + 8192:1024 + 8192 + 512]
        W1F = S["W1S"].bitcast(F32)

        def finish(mega, tm0):
            p.barrier()
            for tc in range(4):
                t0 = tm0 + tc * 512
                tz = None
                for dc in range(NCHUNK):
                    xsl = S["XMB"][:, dc * 2048 + tc * 512: dc * 2048 + tc * 512 + 512]
                    ya = S["YA"][:, dc * 2048 + tc * 512: dc * 2048 + tc * 512 + 512]
                    tz = p.op("v", lambda h, o=xsl, a=ya, b=xsl: h.tensor_tensor(o, a, b, ALU.add), inc=1)
                t_done = []
                for dc in range(NCHUNK):
                    zsl = S["XMB"][:, dc * 2048 + tc * 512: dc * 2048 + tc * 512 + 512]
                    if dc >= 1:
                        p.wait("v", t_done[dc - 1])
                    tv = p.op("v", lambda h, o=sqt, i=zsl: h.tensor_tensor(o, i, i, ALU.mult), inc=1)
                    p.wait("t", tv)
                    p.op("t", lambda h, b=zsl, s=(dc == 0): h.matmul(ctx.pb[0][0:1, :], ctx.ones_bf[:, :], b, start=s, stop=(dc == 7)))
                    tm = p.op("t", lambda h, b=sqt, s=(dc == 0): h.matmul(ctx.pb[1][0:1, :], ctx.ones_bf[:, :], b, start=s, stop=(dc == 7)), inc=1)
                    t_done.append(tm)
                mu, m2, var, sd, rstd, brow = (rows(i) for i in range(6))
                p.wait("v", t_done[-1])
                p.op("v", lambda h: h.tensor_scalar(mu, ctx.pb[0][0:1, :], 1.0 / D, None, ALU.mult))
                p.op("v", lambda h: h.tensor_scalar(m2, ctx.pb[1][0:1, :], 1.0 / D, None, ALU.mult))
                p.op("v", lambda h: h.tensor_tensor(sd, mu, mu, ALU.mult))
                p.op("v", lambda h: h.tensor_tensor(var, m2, sd, ALU.subtract))
                tvv = p.op("v", lambda h: h.tensor_scalar(var, var, 1e-5, None, ALU.add), inc=1)
                p.wait("sc", tvv)
                tss = p.op("sc", lambda h: h.activation(sd, var, AF.Sqrt), inc=1)
                p.wait("v", tss)
                trc = p.op("v", lambda h: h.reciprocal(rstd, sd), inc=1)
                p.op("v", lambda h: h.tensor_scalar(sd, mu, -1.0, None, ALU.mult))
                p.wait("v", trc, force=True)
                tv2 = p.op("v", lambda h: h.tensor_tensor(brow, sd, rstd, ALU.mult), inc=1)
                p.wait("t", tv2)
                tb1 = p.op("t", lambda h: h.matmul(ctx.pb[2][:, :], ctx.ones_row[:, :], rstd, start=True, stop=True), inc=1)
                tb2 = p.op("t", lambda h: h.matmul(ctx.pb[3][:, :], ctx.ones_row[:, :], brow, start=True, stop=True), inc=1)
                p.wait("sc", tb2)
                p.op("sc", lambda h: h.activation(abc, ctx.pb[2][:, :], AF.Copy))
                e2 = p.op("sc", lambda h: h.activation(bbc, ctx.pb[3][:, :], AF.Copy), inc=1)
                for dc in range(NCHUNK):
                    zsl = S["XMB"][:, dc * 2048 + tc * 512: dc * 2048 + tc * 512 + 512]
                    s1s = S["W1S"][:, 0:512]
                    xns = W1F[:, 512:1024]
                    ots = W1F[:, 1024:1536]
                    gsl = S["FSTG"][:, 0:512]
                    tl1 = p.dma("sy", s1s, aps["s1T_in"][dc * 128:(dc + 1) * 128, t0:t0 + 512])
                    tl2 = p.dma("sy", xns, aps["xnewT_in"][dc * 128:(dc + 1) * 128, t0:t0 + 512])
                    p.wait("v", e2)
                    p.op("v", lambda h, a=zsl: h.tensor_tensor(a, a, abc, ALU.mult))
                    p.op("v", lambda h, a=zsl: h.tensor_tensor(a, a, bbc, ALU.add))
                    tvz = p.op("v", lambda h, a=zsl, g=S["G2N"][:, dc:dc + 1], b=S["B2N"][:, dc:dc + 1]:
                               h.tensor_scalar(a, a, g, b, ALU.mult, ALU.add), inc=1)
                    p.wait("sc", tvz)
                    tsl = p.op("sc", lambda h, o=gsl, i=zsl: h.activation(o, i, AF.Silu), inc=1)
                    p.wait("v", tsl)
                    p.wait("v", tl1)
                    p.wait("v", tl2)
                    p.op("v", lambda h, o=gsl, s2=s1s: h.tensor_tensor(o, o, s2, ALU.mult))
                    tvf = p.op("v", lambda h, o=ots, a=xns, g=gsl: h.tensor_tensor(o, a, g, ALU.add), inc=1)
                    p.wait("sy", tvf)
                    td = p.dma("sy", aps["outT"][dc * 128:(dc + 1) * 128, t0:t0 + 512], ots)
                    p.wait("v", td)
                p.barrier()

        emit_moe(ctx, cfg, finish)
        _finish(ctx)
        ctx.p.build(block)
    return nc



_G_INPUTS = {
    "g1": [],
    "g2": ["g_wq", "g_kvd", "g_kup", "g_vup", "g_wo", "w1n", "ropeg", "cmask", "ident", "selp16", "maskh"],
    "g3": ["l_wq", "l_wk", "l_wv", "l_wo", "w2n", "ropel", "lmask", "ident", "selp16", "maskh"],
    "g4": ["m0_wg", "m0_w1", "m0_w2", "sel8", "ident"],
    "g5": [],
    "g6": ["a1_wq", "a1_wk", "a1_wv", "a1_wo", "ident", "selp8", "maskh"],
    "g7": ["f1_wq", "f1_wk", "f1_wv", "f1_wo", "f1_bq", "f1_bk", "f1_bv", "f1_bo",
           "f1_g1", "f1_b1n", "ident", "selp16", "maskh"],
    "g8": ["f1_gw", "f1_gb", "f1_w1", "f1_b1c", "f1_w2", "f1_b2", "f1_g2", "f1_b2n", "sel8", "ident"],
}


def _run(nc, shared, names, extra_percore, n=8):
    maps = []
    for b in range(n):
        m = {k: shared[k] for k in names}
        m.update(extra_percore[b])
        maps.append(m)
    res = run_bass_kernel_spmd(nc, maps, core_ids=list(range(n)))
    return res.results


def kernel(**inputs):
    return kernel_fused(**inputs)


def kernel_multilaunch(**inputs):
    shared, percore = prep_inputs(inputs)
    B = 8
    r1 = _run(build_g1(shared, percore[0]), shared, _G_INPUTS["g1"],
              [{"x0": percore[b]["x0"]} for b in range(B)])
    r2 = _run(build_g2(shared, percore[0]), shared, _G_INPUTS["g2"],
              [{"xnT_in": r1[b]["xnT"], "rs1_in": r1[b]["rs1"]} for b in range(B)])
    r3 = _run(build_g3(shared, percore[0]), shared, _G_INPUTS["g3"],
              [{"x1T_in": r2[b]["x1T"]} for b in range(B)])
    r4 = _run(build_g4(shared, percore[0]), shared, _G_INPUTS["g4"],
              [{"x0": percore[b]["x0"], "xnT_in": r1[b]["xnT"], "x2T_in": r3[b]["x2T"]} for b in range(B)])
    r5 = _run(build_g5(shared, percore[0]), shared, _G_INPUTS["g5"],
              [{"xnewT_in": r4[b]["xnewT"]} for b in range(B)])
    r6 = _run(build_g6(shared, percore[0]), shared, _G_INPUTS["g6"],
              [{"xn1T_in": r5[b]["xn1T"]} for b in range(B)])
    r7 = _run(build_g7(shared, percore[0]), shared, _G_INPUTS["g7"],
              [{"xn1T_in": r5[b]["xn1T"]} for b in range(B)])
    r8 = _run(build_g8(shared, percore[0]), shared, _G_INPUTS["g8"],
              [{"xffnT_in": r7[b]["xffnT"], "s1T_in": r6[b]["s1T"], "xnewT_in": r4[b]["xnewT"]}
               for b in range(B)])
    out = np.stack([np.ascontiguousarray(r8[b]["outT"].T) for b in range(B)])
    return out.astype(np.float32)



def build_fused(shared, percore0, dbg=False):
    import contextlib
    nc = bass.Bass("TRN2", target_bir_lowering=False)
    stack = contextlib.ExitStack()
    nc._dev_stack = stack
    names = (_G_INPUTS["g2"] + _G_INPUTS["g3"] + _G_INPUTS["g4"] + _G_INPUTS["g6"]
             + _G_INPUTS["g7"] + _G_INPUTS["g8"])
    names = sorted(set(names))
    aps = _declare(nc, shared, percore0, names, extra={"x0": ([D, T], F32)})
    outs_f = {"outT": ([D, T], F32)}
    if dbg:
        outs_f.update({"d_cmb4": ([128, 256], BF16), "d_wtr4": ([8, 2048], BF16),
                       "d_ya4": ([128, 16384], F32)})
    aps.update(_outs(nc, outs_f))
    dram = {}
    for nm, dt in [("xnT", BF16), ("x1T", BF16), ("x2T", BF16), ("xnewT", F32),
                   ("xn1T", BF16), ("s1T", BF16), ("xffnT", BF16), ("zT", BF16),
                   ("wqbf_g", BF16), ("wqbf_l", BF16)]:
        shape = [D, 2048] if nm.startswith("wqbf") else [D, T]
        if dbg and not nm.startswith("wqbf"):
            dram[nm] = nc.dram_tensor(nm, shape, dt, kind="ExternalOutput").ap()
        else:
            dram[nm] = nc.dram_tensor(nm, shape, dt, kind="Internal").ap()

    spec = [
        ("AR1", (128, 32768), BF16), ("AR2", (128, 33280), BF16),
        ("AR3", (128, 18432), BF16), ("AR4", (128, 8192), BF16),
        ("ROWS", (1, 2048), F32), ("RS", (1, T), BF16),
        ("MASKS", (128, 3072), BF16), ("MASKH", (128, 3), BF16), ("IDENT", (128, 128), BF16),
        ("SELPA", (128, 1024), BF16), ("nskt", (128, 256), F32),
        ("fscr", (128, 192), F32),
    ]
    with nc.Block() as block:
        ctx = Ctx(nc, stack, aps, consts_vals=(-8.0, -16.0), sbuf_spec=spec)
        ctx.setup()
        p = ctx.p
        S = ctx.sb
        AR1, AR2, AR3, AR4 = S["AR1"], S["AR2"], S["AR3"], S["AR4"]
        ROWS = S["ROWS"]

        def bind_common():
            S["SCR"] = AR3[:, 0:14336]
            S["SRCB"] = AR3[:, 14336:18432]
            S["WK"] = AR3[:, 0:8192]
            S["WV"] = AR3[:, 8192:12288]
            S["VSTG"] = AR3[:, 12288:13312]
            S["WQB"] = AR3[:, 12288:14336]
            S["OSTG"] = AR3[:, 4096:5120]
            S["WOB"] = AR3[:, 12288:14336]
            S["bcbuf"] = AR3[:, 5120:6144]
            S["sumsrow"] = ROWS[0:1, 0:1024]
            S["recips"] = ROWS[0:1, 1024:2048]
            S["wstage"] = AR3.bitcast(F32)[:, 5120:7168]
            S["ln_xstg"] = AR3.bitcast(F32)[:, 5120:6144]
            S["ln_rows"] = ROWS[0:1, 0:2048]
            S["WTR"] = AR3[0:8, 16384:18432]
            S["rqrows"] = AR3[0:16, 12288:12800]
            S["GSC"] = S["fscr"][:, 0:32]
            S["W1N"] = S["fscr"][:, 160:168]
            S["W2N"] = S["fscr"][:, 168:176]
            for i, nm in enumerate(["BQ", "BK", "BV", "BO", "G1", "B1N", "G2N", "B2N"]):
                S[nm] = S["nskt"][:, i * 8:(i + 1) * 8]
            S["SELP"] = S["SELPA"][0:16, :]
            S["FSTG"] = S["SELPA"][:, :]
            S["WBC"] = S["FSTG"]
            S["SEL8"] = S["MASKS"][0:8, 0:1024]
            S["B2S"] = S["MASKS"][0:8, 1024:2048]
            S["HTMP"] = S["MASKS"][:, 2048:2560]
            S["CMB"] = AR2[:, 32768:33024]
            S["WGS"] = AR2[:, 33024:33088]
            S["GB"] = AR2[0:1, 33088:33096]
            S["B1E"] = AR2.bitcast(F32)[:, 16548:16564]
            S["wstageb"] = AR3[:, 8192:10240]
            S["ln_sqt"] = AR3[:, 0:1024]
            S["ln_abc"] = AR3[:, 1024:1536]
            S["ln_bbc"] = AR3[:, 1536:2048]
            S["ROPE"] = AR4[:, :]
            S["HS"] = AR4[:, :]

        bind_common()
        p.dma("g", S["IDENT"][:, :], aps["ident"][:, :])
        p.dma("g", S["MASKH"][:, :], aps["maskh"][:, :])
        p.dma("g", S["SELP"][:, :], aps["selp16"][:, :])
        p.dma("g", S["W1N"][:, :], aps["w1n"][:, :])
        p.dma("g", S["W2N"][:, :], aps["w2n"][:, :])
        p.barrier()

        S["XN"] = AR1[:, :]
        emit_ln(ctx, aps["x0"], S["XN"], rs1_row=S["RS"], out_dram=dram["xnT"])
        p.barrier()

        S["KTS"] = AR1[:, 0:16384]
        S["LATN"] = AR1[:, 16384:24576]
        S["WD"] = AR1[:, 24576:26624]
        S["VTOK"] = AR2[:, 0:8320]
        emit_w_prep(ctx, aps["g_wq"], dram["wqbf_g"], 8, 2048, S["W1N"])
        load_rows(ctx, aps["g_kvd"], S["WD"], 8, 256, scale=S["W1N"])
        load_rows(ctx, aps["g_kup"], S["WK"], 2, 1024)
        for mi in range(4):
            p.dma("sy", S["MASKS"][:, mi * 512:(mi + 1) * 512], aps["cmask"][mi * 128:(mi + 1) * 128, :])
        p.dma("g", S["ROPE"][:, :], aps["ropeg"][:, :])
        p.barrier()
        src2 = SrcStream(ctx, dram["xnT"], S["SRCB"])
        cfg2 = AttnCfg(nheads=16, kv_dup=True, nkv_chunks=4, nkv_heads=4, rope="ROPE", qk="rms",
                       wq=dram["wqbf_g"], wq_bf=True, wk_cols=1024, wv_cols=256,
                       latent=dict(scale_row=S["RS"]), exp_premul=0.125, exp_bias=-8.0,
                       mask_sched=_sched_causal, src=src2, res=lambda fc, q0: src2(fc, q0),
                       load_wv=lambda: load_rows(ctx, aps["g_vup"], S["WV"], 2, 256),
                       wo_stream=aps["g_wo"])
        emit_attn(ctx, cfg2, dram["x1T"])
        p.barrier()

        S["KTS"] = AR1[:, 0:16384]
        S["VTOK"] = AR2[:, 0:8320]
        emit_w_prep(ctx, aps["l_wq"], dram["wqbf_l"], 8, 2048, S["W2N"])
        load_rows(ctx, aps["l_wk"], S["WK"], 8, 1024, scale=S["W2N"])
        for mi in range(6):
            p.dma("sy", S["MASKS"][:, mi * 512:(mi + 1) * 512], aps["lmask"][mi * 128:(mi + 1) * 128, :])
        p.dma("g", S["ROPE"][:, :], aps["ropel"][:, :])
        p.barrier()
        src3 = SrcStream(ctx, dram["x1T"], S["SRCB"])
        emit_rms_row_stream(ctx, src3, S["RS"], eps=1e-6)
        cfg3 = AttnCfg(nheads=16, kv_dup=True, nkv_chunks=4, nkv_heads=4, rope="ROPE", qk="rms",
                       wq=dram["wqbf_l"], wq_bf=True, wk_cols=1024, wv_cols=256,
                       v_scale_row=S["RS"], exp_premul=0.125, exp_bias=-8.0,
                       mask_sched=_sched_local, src=src3, res=lambda fc, q0: src3(fc, q0),
                       load_wv=lambda: load_rows(ctx, aps["l_wv"], S["WV"], 8, 256, scale=S["W2N"]),
                       wo_stream=aps["l_wo"])
        emit_attn(ctx, cfg3, dram["x2T"])
        p.barrier()

        S["YA"] = AR1.bitcast(F32)[:, 0:16384]
        S["XMB"] = AR2[:, 0:16384]
        S["W1S"] = AR2[:, 16384:32768]
        S["W2S"] = AR3[:, 0:16384]
        p.dma("g", S["SEL8"][:, :], aps["sel8"][:, :])
        p.barrier()
        src4 = SrcStream(ctx, dram["xnT"], S["SRCB"], barrier=True)
        cfg4 = AttnCfg(src=src4, wg=aps["m0_wg"], gb=None, w1=aps["m0_w1"], w2=aps["m0_w2"],
                       b1=None, b2=None, act="relu2", renorm=True)

        def finish4(mega, tm0):
            p.barrier()
            for dc in range(NCHUNK):
                for tc in range(4):
                    ya = S["YA"][:, dc * 2048 + tc * 512: dc * 2048 + tc * 512 + 512]
                    gs = S["FSTG"][:, (tc % 2) * 512:(tc % 2) * 512 + 512]
                    t0 = tm0 + tc * 512
                    tls = p.dma("sy", S["HTMP"][:, 0:512], dram["x2T"][dc * 128:(dc + 1) * 128, t0:t0 + 512])
                    tlx = p.dma("sy", S["XMB"].bitcast(F32)[:, 0:512], aps["x0"][dc * 128:(dc + 1) * 128, t0:t0 + 512])
                    tsl = p.op("sc", lambda h, o=gs, i=ya: h.activation(o, i, AF.Silu), inc=1)
                    p.wait("v", tsl)
                    p.wait("v", tls)
                    p.wait("v", tlx)
                    p.op("v", lambda h, o=gs, s2=S["HTMP"][:, 0:512]: h.tensor_tensor(o, o, s2, ALU.mult))
                    tvf = p.op("v", lambda h, o=S["XMB"].bitcast(F32)[:, 512:1024], a=S["XMB"].bitcast(F32)[:, 0:512], g=gs:
                               h.tensor_tensor(o, a, g, ALU.add), inc=1)
                    p.wait("sy", tvf)
                    p.dma("sy", dram["xnewT"][dc * 128:(dc + 1) * 128, t0:t0 + 512], S["XMB"].bitcast(F32)[:, 512:1024])
                    p.barrier()

        emit_moe(ctx, cfg4, finish4)
        p.barrier()
        if dbg:
            p.dma("sy", aps["d_cmb4"][:, :], S["CMB"][:, :])
            p.dma("sy", aps["d_wtr4"][:, :], S["WTR"][:, :])
            p.dma("sy", aps["d_ya4"][:, :], S["YA"][:, :])
            p.barrier()

        S["XN"] = AR1[:, :]
        emit_ln(ctx, dram["xnewT"], S["XN"], out_dram=dram["xn1T"])
        p.barrier()

        S["KTS"] = AR1[:, 0:16384]
        S["VTOK"] = AR2[:, 0:16640]
        p.dma("g", S["SELP"][0:8, 0:512], aps["selp8"][:, :])
        load_rows(ctx, aps["a1_wk"], S["WK"], 8, 512)
        p.barrier()
        src6 = SrcStream(ctx, dram["xn1T"], S["SRCB"], barrier=True)
        cfg6 = AttnCfg(nheads=8, kv_dup=False, nkv_chunks=4, nkv_heads=8, rope=None, qk="l2",
                       wq=aps["a1_wq"], wq_bf=False, wk_cols=512, wv_cols=512,
                       exp_premul=16.0, exp_bias=-16.0, sums_clip=1e-6,
                       load_wv=lambda: load_rows(ctx, aps["a1_wv"], S["WV"], 8, 512),
                       mask_sched=_sched_full, src=src6, res=None, wo_stream=aps["a1_wo"])
        emit_attn(ctx, cfg6, dram["s1T"])
        p.barrier()

        S["KTS"] = AR1[:, 0:32768]
        S["VTOK"] = AR2[:, 0:33280]
        for nm, key in [("BQ", "f1_bq"), ("BK", "f1_bk"), ("BV", "f1_bv"), ("BO", "f1_bo"),
                        ("G1", "f1_g1"), ("B1N", "f1_b1n")]:
            p.dma("g", S[nm][:, :], aps[key][:, :])
        S["WV"] = S["WK"]
        load_rows(ctx, aps["f1_wk"], S["WK"], 8, 1024)
        p.barrier()
        src7 = SrcStream(ctx, dram["xn1T"], S["SRCB"])
        cfg7 = AttnCfg(nheads=16, kv_dup=False, nkv_chunks=8, nkv_heads=16, rope=None, qk=None,
                       wq=aps["f1_wq"], wq_bf=False, wk_cols=1024, wv_cols=1024,
                       bq=S["BQ"], bk=S["BK"], bv=S["BV"], bo=S["BO"],
                       exp_scale_const=0.125, exp_bias=-8.0, wo_stream=aps["f1_wo"],
                       load_wv=lambda: load_rows(ctx, aps["f1_wv"], S["WK"], 8, 1024),
                       mask_sched=_sched_full, src=src7, res=lambda fc, q0: src7(fc, q0))
        emit_attn(ctx, cfg7, dram["zT"])
        p.barrier()
        emit_ln(ctx, dram["zT"], AR1[:, :], src_bf16=True, gamma=S["G1"], beta=S["B1N"], out_dram=dram["xffnT"])
        p.barrier()

        S["YA"] = AR1.bitcast(F32)[:, 0:16384]
        S["XMB"] = AR2[:, 0:16384]
        S["W1S"] = AR2[:, 16384:32768]
        S["W2S"] = AR3[:, 0:16384]
        p.dma("g", S["SEL8"][:, :], aps["sel8"][:, :])
        p.dma("g", S["B2S"][:, :], aps["f1_b2"][:, :])
        p.dma("g", S["GB"][:, :], aps["f1_gb"][:, :])
        p.dma("g", S["G2N"][:, :], aps["f1_g2"][:, :])
        p.dma("g", S["B2N"][:, :], aps["f1_b2n"][:, :])
        p.barrier()
        src8 = SrcStream(ctx, dram["xffnT"], S["SRCB"], barrier=True)
        cfg8 = AttnCfg(src=src8, wg=aps["f1_gw"], gb=aps["f1_gb"], w1=aps["f1_w1"], w2=aps["f1_w2"],
                       b1=aps["f1_b1c"], b2=aps["f1_b2"], act="relu", renorm=False)
        W2F = S["W2S"].bitcast(F32)
        rows8 = lambda i: W2F[0:1, 1024 + i * 512: 1024 + (i + 1) * 512]
        abc8 = S["W2S"][:, 0:512]
        bbc8 = S["W2S"][:, 512:1024]
        sqt8 = S["W2S"][:, 1024 + 8192:1024 + 8192 + 512]
        W1F = S["W1S"].bitcast(F32)

        def finish8(mega, tm0):
            p.barrier()
            for tc in range(4):
                t0 = tm0 + tc * 512
                for dc in range(NCHUNK):
                    xsl = S["XMB"][:, dc * 2048 + tc * 512: dc * 2048 + tc * 512 + 512]
                    ya = S["YA"][:, dc * 2048 + tc * 512: dc * 2048 + tc * 512 + 512]
                    p.op("v", lambda h, o=xsl, a=ya, b=xsl: h.tensor_tensor(o, a, b, ALU.add), inc=1)
                t_done = []
                for dc in range(NCHUNK):
                    zsl = S["XMB"][:, dc * 2048 + tc * 512: dc * 2048 + tc * 512 + 512]
                    if dc >= 1:
                        p.wait("v", t_done[dc - 1])
                    tv = p.op("v", lambda h, o=sqt8, i=zsl: h.tensor_tensor(o, i, i, ALU.mult), inc=1)
                    p.wait("t", tv)
                    p.op("t", lambda h, b=zsl, s=(dc == 0): h.matmul(ctx.pb[0][0:1, :], ctx.ones_bf[:, :], b, start=s, stop=(dc == 7)))
                    tm = p.op("t", lambda h, b=sqt8, s=(dc == 0): h.matmul(ctx.pb[1][0:1, :], ctx.ones_bf[:, :], b, start=s, stop=(dc == 7)), inc=1)
                    t_done.append(tm)
                mu, m2, var, sd, rstd, brow = (rows8(i) for i in range(6))
                p.wait("v", t_done[-1])
                p.op("v", lambda h: h.tensor_scalar(mu, ctx.pb[0][0:1, :], 1.0 / D, None, ALU.mult))
                p.op("v", lambda h: h.tensor_scalar(m2, ctx.pb[1][0:1, :], 1.0 / D, None, ALU.mult))
                p.op("v", lambda h: h.tensor_tensor(sd, mu, mu, ALU.mult))
                p.op("v", lambda h: h.tensor_tensor(var, m2, sd, ALU.subtract))
                tvv = p.op("v", lambda h: h.tensor_scalar(var, var, 1e-5, None, ALU.add), inc=1)
                p.wait("sc", tvv)
                tss = p.op("sc", lambda h: h.activation(sd, var, AF.Sqrt), inc=1)
                p.wait("v", tss)
                trc = p.op("v", lambda h: h.reciprocal(rstd, sd), inc=1)
                p.op("v", lambda h: h.tensor_scalar(sd, mu, -1.0, None, ALU.mult))
                p.wait("v", trc, force=True)
                tv2 = p.op("v", lambda h: h.tensor_tensor(brow, sd, rstd, ALU.mult), inc=1)
                p.wait("t", tv2)
                tb1 = p.op("t", lambda h: h.matmul(ctx.pb[2][:, :], ctx.ones_row[:, :], rstd, start=True, stop=True), inc=1)
                tb2 = p.op("t", lambda h: h.matmul(ctx.pb[3][:, :], ctx.ones_row[:, :], brow, start=True, stop=True), inc=1)
                p.wait("sc", tb2)
                p.op("sc", lambda h: h.activation(abc8, ctx.pb[2][:, :], AF.Copy))
                e2 = p.op("sc", lambda h: h.activation(bbc8, ctx.pb[3][:, :], AF.Copy), inc=1)
                for dc in range(NCHUNK):
                    zsl = S["XMB"][:, dc * 2048 + tc * 512: dc * 2048 + tc * 512 + 512]
                    s1s = S["W1S"][:, 0:512]
                    xns = W1F[:, 512:1024]
                    ots = W1F[:, 1024:1536]
                    gsl = S["FSTG"][:, 0:512]
                    tl1 = p.dma("sy", s1s, dram["s1T"][dc * 128:(dc + 1) * 128, t0:t0 + 512])
                    tl2 = p.dma("sy", xns, dram["xnewT"][dc * 128:(dc + 1) * 128, t0:t0 + 512])
                    p.wait("v", e2)
                    p.op("v", lambda h, a=zsl: h.tensor_tensor(a, a, abc8, ALU.mult))
                    p.op("v", lambda h, a=zsl: h.tensor_tensor(a, a, bbc8, ALU.add))
                    tvz = p.op("v", lambda h, a=zsl, g=S["G2N"][:, dc:dc + 1], b=S["B2N"][:, dc:dc + 1]:
                               h.tensor_scalar(a, a, g, b, ALU.mult, ALU.add), inc=1)
                    p.wait("sc", tvz)
                    tsl = p.op("sc", lambda h, o=gsl, i=zsl: h.activation(o, i, AF.Silu), inc=1)
                    p.wait("v", tsl)
                    p.wait("v", tl1)
                    p.wait("v", tl2)
                    p.op("v", lambda h, o=gsl, s2=s1s: h.tensor_tensor(o, o, s2, ALU.mult))
                    tvf = p.op("v", lambda h, o=ots, a=xns, g=gsl: h.tensor_tensor(o, a, g, ALU.add), inc=1)
                    p.wait("sy", tvf)
                    td = p.dma("sy", aps["outT"][dc * 128:(dc + 1) * 128, t0:t0 + 512], ots)
                    p.wait("v", td)
                p.barrier()

        emit_moe(ctx, cfg8, finish8)
        _finish(ctx)
        ctx.p.build(block)
    return nc


def kernel_fused(**inputs):
    shared, percore = prep_inputs(inputs)
    B = 8
    names = sorted(set(_G_INPUTS["g2"] + _G_INPUTS["g3"] + _G_INPUTS["g4"] + _G_INPUTS["g6"]
                       + _G_INPUTS["g7"] + _G_INPUTS["g8"]))
    nc = build_fused(shared, percore[0])
    res = _run(nc, shared, names, [{"x0": percore[b]["x0"]} for b in range(B)])
    out = np.stack([np.ascontiguousarray(res[b]["outT"].T) for b in range(B)])
    return out.astype(np.float32)


# version 06 (final)
# speedup vs baseline: 1.6025x; 1.0460x over previous
import math
import numpy as np
import ml_dtypes
import concourse.bass as bass
import concourse.mybir as mybir
from concourse.bass_utils import run_bass_kernel_spmd

F32 = mybir.dt.float32
BF16 = mybir.dt.bfloat16
AF = mybir.ActivationFunctionType
ALU = mybir.AluOpType
AX = mybir.AxisListType

D = 1024
T = 4096
NCHUNK = D // 128
NEG = -1.0e30
import os as _os
KNORM_PARANOID = bool(_os.environ.get("KNORM_PARANOID"))

ENGINES = ("t", "v", "sc", "g", "sy")
DMA_ENG = ("sy", "g", "sc", "v")


NWAVE = 8


class Prog:

    def __init__(self, nc):
        self.nc = nc
        self.items = {e: [] for e in ENGINES}
        self.c = {e: 0 for e in ENGINES}
        self.didx = {e: 0 for e in DMA_ENG}
        self.last_wait = {e: {} for e in ENGINES}

    def op(self, e, fn, inc=0, lp=False):
        if lp:
            base = fn
            def fn(h, _b=base, _nc=self.nc):
                with _nc.allow_low_precision(reason="intentional bf16 rounding"):
                    return _b(h)
        self.items[e].append(("op", fn, inc, False, None))
        if inc:
            self.c[e] += inc
        return (e, False, self.c[e])

    def dma(self, e, out, in_, **kw):
        def fn(h):
            return h.dma_start(out=out, in_=in_, **kw)
        idx = self.didx[e]
        k = idx % NWAVE
        wave = idx // NWAVE + 1
        if idx >= NWAVE:
            self.wait(e, ("dma", e, k, 16 * (wave - 1)))
        self.items[e].append(("op", fn, 16, True, k))
        self.didx[e] = idx + 1
        return ("dma", e, k, 16 * wave)

    def tok(self, e, dma=False):
        if dma:
            idx = self.didx[e]
            if idx == 0:
                return None
            k = (idx - 1) % NWAVE
            return ("dma", e, k, 16 * ((idx - 1) // NWAVE + 1))
        return (e, False, self.c[e])

    def _dma_all_toks(self, e):
        toks = []
        idx = self.didx[e]
        for k in range(NWAVE):
            n = (idx - 1 - k) // NWAVE + 1 if idx > k else 0
            if n > 0:
                toks.append(("dma", e, k, 16 * n))
        return toks

    def wait(self, e, tok, force=False):
        if tok is None:
            return
        if tok[0] == "dma":
            _, src, k, val = tok
            key = ("dma", src, k)
        else:
            src, dma, val = tok
            if src == e and not dma and not force:
                return
            key = (src, dma)
        if val <= 0 or val <= self.last_wait[e].get(key, 0):
            return
        self.last_wait[e][key] = val
        self.items[e].append(("wait", key, val))

    def waits(self, e, toks):
        for t in toks:
            self.wait(e, t)

    def barrier(self):
        snap_c = dict(self.c)
        dtoks = {s: self._dma_all_toks(s) for s in DMA_ENG}
        for e in ENGINES:
            for s in ENGINES:
                if s != e and snap_c[s]:
                    self.wait(e, (s, False, snap_c[s]))
            for s in DMA_ENG:
                for t in dtoks[s]:
                    self.wait(e, t)

    def build(self, block):
        nc = self.nc
        sems = {}
        for e in ENGINES:
            sems[(e, False)] = nc.alloc_semaphore(name=f"c_{e}")
        for e in DMA_ENG:
            for k in range(NWAVE):
                sems[("dma", e, k)] = nc.alloc_semaphore(name=f"d_{e}{k}")
        self.sems = sems

        def runner(e):
            def run(h):
                for it in self.items[e]:
                    if it[0] == "wait":
                        _, key, val = it
                        h.wait_ge(sems[key], val)
                    else:
                        _, fn, inc, dma, slot = it
                        ins = fn(h)
                        if inc:
                            ins.then_inc(sems[("dma", e, slot) if dma else (e, False)], inc)
            return run

        block.tensor(runner("t"))
        block.vector(runner("v"))
        block.scalar(runner("sc"))
        block.gpsimd(runner("g"))
        block.sync(runner("sy"))



def ceil_div(a, b):
    return (a + b - 1) // b


class Consts:

    def __init__(self, nc, values):
        self.values = sorted(set(float(v) for v in values))
        self.tile = None

    def alloc(self, stack, nc):
        n = max(1, len(self.values))
        self.tile = stack.enter_context(nc.sbuf_tensor("consts", [128, n], F32))
        for i, v in enumerate(self.values):
            nc.const_aps.register(F32, v, self.tile[:, i:i + 1])

    def emit_init(self, p):
        def mk(i, v):
            return lambda h: h.memset(self.tile[:, i:i + 1], v)
        for i, v in enumerate(self.values):
            p.op("v", mk(i, v), inc=(1 if i == len(self.values) - 1 else 0))
        return p.tok("v")



def _eo_perm(nheads, hd=64):
    idx = []
    for h in range(nheads):
        base = h * hd
        idx.extend(base + np.arange(0, hd, 2))
        idx.extend(base + np.arange(1, hd, 2))
    return np.asarray(idx)


def _swap_perm(nheads, hd=64):
    idx = []
    for h in range(nheads):
        base = h * hd
        idx.extend(base + np.arange(1, hd, 2))
        idx.extend(base + np.arange(0, hd, 2))
    return np.asarray(idx)


def _rope_tables(base, T_, hd=64):
    j = np.arange(hd // 2, dtype=np.float64)
    inv = 1.0 / (base ** (2 * j / hd))
    t = np.arange(T_, dtype=np.float64)
    fr = t[None, :] * inv[:, None]
    cos = np.cos(fr).astype(np.float32)
    sin = np.sin(fr).astype(np.float32)
    c1 = np.tile(cos, (4, 1))
    c2 = np.tile(np.concatenate([-sin, sin], 0), (2, 1))
    return c1.astype(ml_dtypes.bfloat16), c2.astype(ml_dtypes.bfloat16)


def _chunk_major(v):
    return np.ascontiguousarray(v.reshape(-1, 128).T)


def _causal_masks():
    q = np.arange(512)[None, :]
    k = np.arange(128)[:, None]
    out = []
    for off in range(4):
        m = np.where((q - k) >= 128 * off, 0.0, NEG).astype(np.float32)
        out.append(m)
    return np.stack(out)


def _local_masks(win=256):
    q = np.arange(512)[None, :]
    k = np.arange(128)[:, None]
    out = []
    for off in range(-2, 4):
        d = (q - k) - 128 * off
        m = np.where((d >= 0) & (d < win), 0.0, NEG).astype(np.float32)
        out.append(m)
    return np.stack(out)


def _selpair(nheads):
    nch = nheads // 2
    s = np.zeros((nheads, nch * 128), np.float32)
    for c in range(nch):
        s[2 * c, c * 128: c * 128 + 64] = 1.0
        s[2 * c + 1, c * 128 + 64: (c + 1) * 128] = 1.0
    return s.astype(ml_dtypes.bfloat16)


def _maskq(nheads):
    nch = nheads // 2
    m = np.zeros((nch, 128, nheads), np.float32)
    for c in range(nch):
        m[c, 0:64, 2 * c] = 1.0
        m[c, 64:128, 2 * c + 1] = 1.0
    return np.ascontiguousarray(m.transpose(1, 0, 2).reshape(128, nch * nheads)).astype(ml_dtypes.bfloat16)


def prep_inputs(I):
    B = 8
    x = np.asarray(I["L0_attn_hidden"])
    percore = []
    for b in range(B):
        percore.append({"x0": np.ascontiguousarray(x[b].T).astype(np.float32)})

    S = {}
    f32 = lambda a: np.ascontiguousarray(np.asarray(a, dtype=np.float32))
    pq = _eo_perm(16)
    sq = _swap_perm(16)
    S["g_wq"] = f32(np.concatenate([I["L0_attn_g_wq"][:, pq], I["L0_attn_g_wq"][:, sq]], axis=1))
    S["g_kvd"] = f32(I["L0_attn_g_kv_down"])
    ku = np.asarray(I["L0_attn_g_k_up"])
    pk = _eo_perm(4)
    sk = _swap_perm(4)
    kup_d = np.concatenate([np.concatenate([ku[:, pk[g * 64:(g + 1) * 64]]] * 2, axis=1) for g in range(4)], axis=1)
    kup_s = np.concatenate([np.concatenate([ku[:, sk[g * 64:(g + 1) * 64]]] * 2, axis=1) for g in range(4)], axis=1)
    S["g_kup"] = f32(np.concatenate([kup_d, kup_s], axis=1))
    S["g_vup"] = f32(I["L0_attn_g_v_up"])
    S["g_wo"] = f32(I["L0_attn_g_wo"])
    S["l_wq"] = f32(np.concatenate([I["L0_attn_l_wq"][:, pq], I["L0_attn_l_wq"][:, sq]], axis=1))
    lk = np.asarray(I["L0_attn_l_wk"])
    lk_d = np.concatenate([np.concatenate([lk[:, pk[g * 64:(g + 1) * 64]]] * 2, axis=1) for g in range(4)], axis=1)
    lk_s = np.concatenate([np.concatenate([lk[:, sk[g * 64:(g + 1) * 64]]] * 2, axis=1) for g in range(4)], axis=1)
    S["l_wk"] = f32(np.concatenate([lk_d, lk_s], axis=1))
    S["l_wv"] = f32(I["L0_attn_l_wv"])
    S["l_wo"] = f32(I["L0_attn_l_wo"])
    S["w1n"] = f32(_chunk_major(I["L0_attn_norm1_w"]))
    S["w2n"] = f32(_chunk_major(I["L0_attn_norm2_w"]))
    S["m0_wg"] = f32(I["L0_ffn_Wg"])
    S["m0_w1"] = f32(I["L0_ffn_W1"])
    S["m0_w2"] = f32(I["L0_ffn_W2"])
    S["a1_wq"] = f32(I["L1_attn_Wq"])
    S["a1_wk"] = f32(np.asarray(I["L1_attn_Wkv"])[:, :512])
    S["a1_wv"] = f32(np.asarray(I["L1_attn_Wkv"])[:, 512:])
    S["a1_wo"] = f32(I["L1_attn_Wo"])
    for nm, key in [("f1_wq", "L1_ffn_Wq"), ("f1_wk", "L1_ffn_Wk"), ("f1_wv", "L1_ffn_Wv"), ("f1_wo", "L1_ffn_Wo")]:
        S[nm] = f32(I[key])
    for nm, key in [("f1_bq", "L1_ffn_bq"), ("f1_bk", "L1_ffn_bk"), ("f1_bv", "L1_ffn_bv"), ("f1_bo", "L1_ffn_bo"),
                    ("f1_g1", "L1_ffn_ln1_g"), ("f1_b1n", "L1_ffn_ln1_b"), ("f1_g2", "L1_ffn_ln2_g"), ("f1_b2n", "L1_ffn_ln2_b")]:
        S[nm] = f32(_chunk_major(I[key]))
    S["f1_gw"] = f32(I["L1_ffn_gW"])
    S["f1_gb"] = f32(np.asarray(I["L1_ffn_gb"]).reshape(1, 8))
    S["f1_w1"] = f32(I["L1_ffn_W1"])
    S["f1_b1c"] = f32(np.asarray(I["L1_ffn_b1"]).reshape(8, 16, 128).transpose(0, 2, 1))
    S["f1_w2"] = f32(I["L1_ffn_W2"])
    S["f1_b2"] = f32(I["L1_ffn_b2"])
    gc1, gc2 = _rope_tables(1e6, T)
    lc1, lc2 = _rope_tables(1e4, T)
    S["ropeg"] = np.ascontiguousarray(np.concatenate([gc1, gc2], axis=1))
    S["ropel"] = np.ascontiguousarray(np.concatenate([lc1, lc2], axis=1))
    S["cmask"] = _causal_masks().reshape(4 * 128, 512).astype(ml_dtypes.bfloat16)
    S["lmask"] = _local_masks().reshape(6 * 128, 512).astype(ml_dtypes.bfloat16)
    S["ident"] = np.eye(128, dtype=np.float32).astype(ml_dtypes.bfloat16)
    S["selp16"] = _selpair(16)
    S["selp8"] = _selpair(8)
    S["maskq16"] = _maskq(16)
    S["maskq8"] = _maskq(8)
    mh = np.zeros((128, 3), np.float32)
    mh[0:64, 0] = 1.0
    mh[0:64, 1] = 1.0
    mh[64:128, 2] = 1.0
    S["maskh"] = mh.astype(ml_dtypes.bfloat16)
    sel8 = np.zeros((8, 8 * 128), np.float32)
    for e in range(8):
        sel8[e, e * 128:(e + 1) * 128] = 1.0
    S["sel8"] = sel8.astype(ml_dtypes.bfloat16)
    return S, percore


_DT = {np.dtype(np.float32): F32, np.dtype(ml_dtypes.bfloat16): BF16}


def declare_inputs(nc, shared, percore0):
    aps = {}
    for nm, arr in list(shared.items()) + list(percore0.items()):
        aps[nm] = nc.dram_tensor(nm, list(arr.shape), _DT[arr.dtype], kind="ExternalInput").ap()
    return aps


class Ctx:
    def __init__(self, nc, stack, aps, consts_vals=(), sbuf_spec=()):
        self.nc = nc
        self.aps = aps
        self.p = Prog(nc)
        self.pb = [stack.enter_context(nc.psum_tensor(f"pb{i}", [128, 512], F32)) for i in range(7)]
        self.pbbf = stack.enter_context(nc.psum_tensor("pbbf", [128, 1024], BF16))
        self.ones_bf = stack.enter_context(nc.sbuf_tensor("ones_bf", [128, 1], BF16))
        self.ones_row = stack.enter_context(nc.sbuf_tensor("ones_row", [1, 128], F32))
        self.ones_row_bf = stack.enter_context(nc.sbuf_tensor("ones_row_bf", [1, 128], BF16))
        self.consts = Consts(nc, consts_vals)
        self.consts.alloc(stack, nc)
        self.sb = {}
        for (nm, shape, dt) in sbuf_spec:
            self.sb[nm] = stack.enter_context(nc.sbuf_tensor(nm, list(shape), dt))
        self.wstage_tok = None

    def setup(self):
        p = self.p
        p.op("v", lambda h: h.memset(self.ones_bf[:, :], 1.0))
        p.op("v", lambda h: h.memset(self.ones_row[:, :], 1.0))
        p.op("v", lambda h: h.memset(self.ones_row_bf[:, :], 1.0), inc=1)
        self.consts.emit_init(p)
        self.t_setup = p.tok("v")
        for e in ("t", "sc", "v", "g"):
            p.wait(e, self.t_setup)


def load_bf16(p, dram_ap, sb_ap):
    return p.dma("g", sb_ap, dram_ap)



def emit_ln(ctx, x_dram, xn_sb, *, rs1_row=None, out_dram=None, sq_eps=1e-6,
            src_bf16=False, gamma=None, beta=None):
    p = ctx.p
    nc = ctx.nc
    S = ctx.sb
    xstg, sqt = S["ln_xstg"], S["ln_sqt"]
    rows = S["ln_rows"]
    abc, bbc = S["ln_abc"], S["ln_bbc"]
    ps_s, ps_q, ps_b1, ps_b2 = ctx.pb[0], ctx.pb[1], ctx.pb[2], ctx.pb[3]

    for blk in range(8):
        t0 = blk * 512
        v_done = []
        t_done = []
        for ch in range(8):
            sl = xstg[:, (ch % 2) * 512:(ch % 2) * 512 + 512]
            sq = sqt[:, (ch % 2) * 512:(ch % 2) * 512 + 512]
            if ch >= 2:
                p.wait("sy", v_done[ch - 2])
            if src_bf16:
                sl = sl.bitcast(BF16)[:, 0:512]
            tl = p.dma("sy", sl, x_dram[ch * 128:(ch + 1) * 128, t0:t0 + 512])
            xnsl = xn_sb[:, ch * T + t0: ch * T + t0 + 512]
            p.wait("v", tl)
            if ch >= 2:
                p.wait("v", t_done[ch - 2])
            p.op("v", lambda h, a=xnsl, b=sl: h.tensor_copy(a, b))
            tv = p.op("v", lambda h, a=sq, b=xnsl: h.tensor_tensor(a, b, b, ALU.mult), inc=1)
            v_done.append(tv)
            p.wait("t", tv)
            p.op("t", lambda h, b=xnsl, s=(ch == 0): h.matmul(ps_s[0:1, :], ctx.ones_bf[:, :], b, start=s, stop=(ch == 7)))
            tok_mm = p.op("t", lambda h, b=sq, s=(ch == 0): h.matmul(ps_q[0:1, :], ctx.ones_bf[:, :], b, start=s, stop=(ch == 7)), inc=1)
            t_done.append(tok_mm)
        mu, var, sd, rstd = (rows[0:1, i * 512:(i + 1) * 512] for i in range(4))
        m2 = var
        tmp = sd
        brow = mu
        p.wait("v", tok_mm)
        p.op("v", lambda h: h.tensor_scalar(mu, ps_s[0:1, :], 1.0 / D, None, ALU.mult))
        p.op("v", lambda h: h.tensor_tensor(tmp, mu, mu, ALU.mult))
        p.op("v", lambda h: h.tensor_scalar(m2, ps_q[0:1, :], 1.0 / D, None, ALU.mult))
        p.op("v", lambda h: h.tensor_tensor(var, m2, tmp, ALU.subtract))
        tv = p.op("v", lambda h: h.tensor_scalar(var, var, 1e-5, None, ALU.add), inc=1)
        p.wait("sc", tv)
        ts = p.op("sc", lambda h: h.activation(sd, var, AF.Sqrt), inc=1)
        p.wait("v", ts)
        trc0 = p.op("v", lambda h: h.reciprocal(rstd, sd), inc=1)
        p.op("v", lambda h: h.tensor_scalar(tmp, mu, -1.0, None, ALU.mult))
        p.wait("v", trc0, force=True)
        tv2 = p.op("v", lambda h: h.tensor_tensor(brow, tmp, rstd, ALU.mult), inc=1)
        if rs1_row is not None:
            r1 = r2 = rows[0:1, 2 * 512:3 * 512]
            p.wait("v", tv, force=True)
            trr = p.op("v", lambda h: h.reciprocal(r1, var), inc=1)
            p.wait("v", trr, force=True)
            p.op("v", lambda h: h.tensor_scalar(r1, r1, -1e-5, None, ALU.mult))
            p.op("v", lambda h: h.tensor_scalar(r1, r1, 1.0, None, ALU.add))
            tv3 = p.op("v", lambda h: h.tensor_scalar(r1, r1, 1.0, sq_eps, ALU.mult, ALU.add), inc=1)
            p.wait("sc", tv3)
            ts2 = p.op("sc", lambda h: h.activation(r2, r1, AF.Sqrt), inc=1)
            p.wait("v", ts2)
            p.op("v", lambda h, o=rs1_row[0:1, t0:t0 + 512]: h.reciprocal(o, r2), inc=1, lp=True)
        p.wait("t", tv2)
        tb1 = p.op("t", lambda h: h.matmul(ps_b1[:, :], ctx.ones_row[:, :], rstd, start=True, stop=True), inc=1)
        tb2 = p.op("t", lambda h: h.matmul(ps_b2[:, :], ctx.ones_row[:, :], brow, start=True, stop=True), inc=1)
        p.wait("sc", tb1)
        p.wait("sc", tb2)
        e1 = p.op("sc", lambda h: h.activation(abc[:, :], ps_b1[:, :], AF.Copy))
        e2 = p.op("sc", lambda h: h.activation(bbc[:, :], ps_b2[:, :], AF.Copy), inc=1)
        p.wait("v", e2)
        for ch in range(8):
            xnsl = xn_sb[:, ch * T + t0: ch * T + t0 + 512]
            p.op("v", lambda h, a=xnsl: h.tensor_tensor(a, a, abc[:, :], ALU.mult))
            tv = p.op("v", lambda h, a=xnsl: h.tensor_tensor(a, a, bbc[:, :], ALU.add), inc=1)
            if gamma is not None:
                tv = p.op("v", lambda h, a=xnsl, g=gamma[:, ch:ch + 1], b=beta[:, ch:ch + 1]:
                          h.tensor_scalar(a, a, g, b, ALU.mult, ALU.add), inc=1)
            if out_dram is not None:
                p.wait("sy", tv)
                p.dma("sy", out_dram[ch * 128:(ch + 1) * 128, t0:t0 + 512], xnsl)
    return p.tok("v")



def _declare(nc, shared, percore0, names, extra=None):
    aps = {}
    for nm in names:
        arr = shared[nm] if nm in shared else percore0[nm]
        aps[nm] = nc.dram_tensor(nm, list(arr.shape), _DT[np.asarray(arr).dtype], kind="ExternalInput").ap()
    for nm, (shape, dt) in (extra or {}).items():
        aps[nm] = nc.dram_tensor(nm, list(shape), dt, kind="ExternalInput").ap()
    return aps


def _outs(nc, spec):
    return {nm: nc.dram_tensor(nm, list(shape), dt, kind="ExternalOutput").ap()
            for nm, (shape, dt) in spec.items()}


def _finish(ctx):
    ctx.p.barrier()


def build_g1(shared, percore0):
    import contextlib
    nc = bass.Bass("TRN2", target_bir_lowering=False)
    stack = contextlib.ExitStack()
    nc._dev_stack = stack
    aps = _declare(nc, shared, percore0, ["x0"])
    aps.update(_outs(nc, {"xnT": ([D, T], BF16), "rs1": ([1, T], F32)}))
    sbuf_spec = [
        ("ln_xstg", (128, 1024), F32), ("ln_sqt", (128, 1024), BF16),
        ("ln_rows", (1, 4096), F32), ("ln_abc", (128, 512), BF16), ("ln_bbc", (128, 512), BF16),
        ("XN", (128, NCHUNK * T), BF16),
        ("rs1row", (1, T), F32),
    ]
    with nc.Block() as block:
        ctx = Ctx(nc, stack, aps, consts_vals=(), sbuf_spec=sbuf_spec)
        ctx.setup()
        emit_ln(ctx, aps["x0"], ctx.sb["XN"], rs1_row=ctx.sb["rs1row"], out_dram=aps["xnT"])
        tv = ctx.p.tok("v")
        ctx.p.wait("sy", tv)
        ctx.p.dma("sy", aps["rs1"][:, :], ctx.sb["rs1row"][:, :])
        _finish(ctx)
        ctx.p.build(block)
    return nc



def emit_rms_row(ctx, src_sb, out_row, eps=1e-6):
    p = ctx.p
    sqt = ctx.sb["ln_sqt"]
    rows = ctx.sb["ln_rows"]
    ps = ctx.pb[0]
    for blk in range(8):
        t0 = blk * 512
        t_done = []
        for ch in range(8):
            sq = sqt[:, (ch % 2) * 512:(ch % 2) * 512 + 512]
            sl = src_sb[:, ch * T + t0: ch * T + t0 + 512]
            if ch >= 2:
                p.wait("v", t_done[ch - 2])
            tv = p.op("v", lambda h, a=sq, b=sl: h.tensor_tensor(a, b, b, ALU.mult), inc=1)
            p.wait("t", tv)
            tm = p.op("t", lambda h, b=sq, s=(ch == 0): h.matmul(ps[0:1, :], ctx.ones_bf[:, :], b, start=s, stop=(ch == 7)), inc=1)
            t_done.append(tm)
        r1 = rows[0:1, 0:512]
        r2 = rows[0:1, 512:1024]
        p.wait("v", t_done[-1])
        tv = p.op("v", lambda h: h.tensor_scalar(r1, ps[0:1, :], 1.0 / D, eps, ALU.mult, ALU.add), inc=1)
        p.wait("sc", tv)
        ts = p.op("sc", lambda h: h.activation(r2, r1, AF.Sqrt), inc=1)
        p.wait("v", ts)
        p.op("v", lambda h, o=out_row[0:1, t0:t0 + 512]: h.reciprocal(o, r2), inc=1)
    return p.tok("v")



def load_rows(ctx, dram, dst, nkc, cols, scale=None):
    p = ctx.p
    toks = []
    if scale is None:
        for k in range(nkc):
            toks.append(p.dma("g", dst[:, k * cols:(k + 1) * cols], dram[k * 128:(k + 1) * 128, 0:cols]))
        return toks[-1]
    stg = ctx.sb["wstage"]
    for k in range(nkc):
        s0 = 0
        if k >= 1:
            p.wait("sy", toks[k - 1])
        else:
            p.wait("sy", ctx.wstage_tok)
        tl = p.dma("sy", stg[:, s0:s0 + cols], dram[k * 128:(k + 1) * 128, 0:cols])
        p.wait("v", tl)
        tv = p.op("v", lambda h, o=dst[:, k * cols:(k + 1) * cols], i=stg[:, s0:s0 + cols], s=scale[:, k:k + 1]:
                  h.tensor_scalar(o, i, s, None, ALU.mult), inc=1)
        toks.append(tv)
    ctx.wstage_tok = toks[-1]
    return toks[-1]


def emit_w_prep(ctx, src_dram, dst_dram, nkc, cols, scale):
    p = ctx.p
    stg = ctx.sb["wstage"]
    stgb = ctx.sb["wstageb"]
    vt, dt = [], []
    for k in range(nkc):
        s0 = 0
        if k >= 1:
            p.wait("sy", vt[k - 1])
        else:
            p.wait("sy", ctx.wstage_tok)
        tl = p.dma("sy", stg[:, s0:s0 + cols], src_dram[k * 128:(k + 1) * 128, 0:cols])
        p.wait("v", tl)
        if k >= 1:
            p.wait("v", dt[k - 1])
        tv = p.op("v", lambda h, o=stgb[:, s0:s0 + cols], i=stg[:, s0:s0 + cols], s=scale[:, k:k + 1]:
                  h.tensor_scalar(o, i, s, None, ALU.mult), inc=1)
        vt.append(tv)
        p.wait("sy", tv)
        dt.append(p.dma("sy", dst_dram[k * 128:(k + 1) * 128, 0:cols], stgb[:, s0:s0 + cols]))
    ctx.wstage_tok = vt[-1]
    return dt[-1]


class AttnCfg:
    def __init__(self, **kw):
        self.__dict__.update(kw)


def emit_attn(ctx, cfg, out_dram):
    p = ctx.p
    S = ctx.sb
    nq = cfg.nheads
    NQC = nq // 2
    rope = getattr(cfg, "rope", None)
    qk = getattr(cfg, "qk", None)
    dup = getattr(cfg, "kv_dup", False)
    nkvc = cfg.nkv_chunks
    nkvh = cfg.nkv_heads
    nho = 1 if dup else 2
    SCR = S["SCR"]
    KTS = S["KTS"]
    VTOK = S["VTOK"]
    IC = NQC
    qtile = lambda c: SCR[:, c * 512:(c + 1) * 512]
    qsw = lambda c: SCR[:, 4096 + c * 512:4096 + (c + 1) * 512]
    asm = lambda c: SCR[:, 8192 + c * 512:8192 + (c + 1) * 512]
    P_t = lambda i: SCR[:, 12288 + i * 512:12288 + (i + 1) * 512]
    nskt = S.get("nskt")
    rqrows = S.get("rqrows")
    sumsrow = S["sumsrow"]
    fscr = S["fscr"]
    bank_tok = {}

    def use_bank(e, b):
        tok = bank_tok.get(b)
        if tok is not None:
            p.wait(e, tok)

    if getattr(cfg, "latent", None) is not None:
        LATN = S["LATN"]
        WD = S["WD"]
        lat = cfg.latent
        bc_tok = {}
        for tb in range(8):
            t0 = tb * 512
            sl = tb % 2
            bcb = S["bcbuf"][:, sl * 512:sl * 512 + 512]
            bb = ctx.pb[2]
            use_bank("t", 2)
            tb1 = p.op("t", lambda h, r=lat["scale_row"][0:1, t0:t0 + 512]:
                       h.matmul(bb[:, :], ctx.ones_row_bf[:, :], r, start=True, stop=True), inc=1)
            p.wait("sc", tb1)
            p.wait("sc", bc_tok.get(sl))
            te = p.op("sc", lambda h, o=bcb: h.activation(o, bb[:, :], AF.Copy), inc=1)
            bank_tok[2] = te
            for fc in range(2):
                b = fc % 2
                pb = ctx.pb[b]
                use_bank("t", b)
                for kc in range(NCHUNK):
                    xap, tok = cfg.src(kc, t0, 512)
                    p.wait("t", tok)
                    tm = p.op("t", lambda h, w=WD[:, kc * 256 + fc * 128: kc * 256 + fc * 128 + 128],
                              x=xap, o=pb, s=(kc == 0), e=(kc == NCHUNK - 1):
                              h.matmul(o[:, :], w, x, start=s, stop=e), inc=(1 if kc == NCHUNK - 1 else 0))
                p.wait("v", tm)
                p.wait("v", te)
                tv = p.op("v", lambda h, o=LATN[:, fc * T + t0: fc * T + t0 + 512], i=pb, bcx=bcb:
                          h.tensor_tensor(o, i[:, :], bcx, ALU.mult), inc=1)
                bank_tok[b] = tv
                bc_tok[sl] = tv
        lat_tok = p.tok("v")
        kv_src = lambda kc, t0, w: (LATN[:, kc * T + t0: kc * T + t0 + w], lat_tok)
        kv_kc = 2
        kv_wcols = getattr(cfg, "wk_cols")
    else:
        kv_src = cfg.src
        kv_kc = NCHUNK
        kv_wcols = getattr(cfg, "wk_cols")

    WK = S["WK"]
    ksw_tok = {}
    for tb in range(8):
        for c in range(nkvc):
            t0 = tb * 512
            b = c % 2
            pbK = ctx.pb[b]
            use_bank("t", b)
            for kc in range(kv_kc):
                xap, tok = kv_src(kc, t0, 512)
                p.wait("t", tok)
                tm = p.op("t", lambda h, w=WK[:, kc * kv_wcols + c * 128: kc * kv_wcols + c * 128 + 128],
                          x=xap, o=pbK, s=(kc == 0), e=(kc == kv_kc - 1):
                          h.matmul(o[:, :], w, x, start=s, stop=e), inc=(1 if kc == kv_kc - 1 else 0))
            ksl = KTS[:, c * T + t0: c * T + t0 + 512]
            p.wait("sc", tm)
            if getattr(cfg, "bk", None) is not None:
                te = p.op("sc", lambda h, o=ksl, i=pbK, ba=cfg.bk[:, c:c + 1]: h.activation(o, i[:, :], AF.Identity, bias=ba), inc=1)
            else:
                te = p.op("sc", lambda h, o=ksl, i=pbK: h.activation(o, i[:, :], AF.Copy), inc=1)
            bank_tok[b] = te
            if rope is not None:
                b2 = 2 + c % 2
                pbS = ctx.pb[b2]
                use_bank("t", b2)
                for kc in range(kv_kc):
                    xap, tok = kv_src(kc, t0, 512)
                    p.wait("t", tok)
                    tm2 = p.op("t", lambda h, w=WK[:, kc * kv_wcols + nkvc * 128 + c * 128: kc * kv_wcols + nkvc * 128 + c * 128 + 128],
                               x=xap, o=pbS, s=(kc == 0), e=(kc == kv_kc - 1):
                               h.matmul(o[:, :], w, x, start=s, stop=e), inc=(1 if kc == kv_kc - 1 else 0))
                ks = asm(c % 2)
                p.wait("sc", tm2)
                p.wait("sc", ksw_tok.get(c % 2))
                te2 = p.op("sc", lambda h, o=ks, i=pbS: h.activation(o, i[:, :], AF.Copy), inc=1)
                bank_tok[b2] = te2
                rt = S["ROPE"]
                p.wait("v", te)
                p.wait("v", te2)
                if getattr(cfg, "no_krope", False):
                    tv = p.op("v", lambda h, a=ksl: h.tensor_copy(a, a), inc=1)
                else:
                    p.op("v", lambda h, a=ksl, t1=rt[:, t0:t0 + 512]: h.tensor_tensor(a, a, t1, ALU.mult))
                    p.op("v", lambda h, bx=ks, t2=rt[:, T + t0:T + t0 + 512]: h.tensor_tensor(bx, bx, t2, ALU.mult))
                    tv = p.op("v", lambda h, a=ksl, bx=ks: h.tensor_tensor(a, a, bx, ALU.add), inc=1)
                ksw_tok[c % 2] = tv
                bank_tok[b2] = tv

    if qk is not None:
        for c in range(nkvc):
            sq_tok = {}
            ev_tok = None
            for tb in range(8):
                t0 = tb * 512
                dq = P_t(tb % 2)
                ksl = KTS[:, c * T + t0: c * T + t0 + 512]
                p.wait("v", sq_tok.get(tb % 2))
                tv = p.op("v", lambda h, o=dq, i=ksl: h.tensor_tensor(o, i, i, ALU.mult), inc=1)
                p.wait("t", tv)
                use_bank("t", 4)
                rh = S["MASKH"][:, 0:1] if dup else S["MASKH"][:, 1:3]
                for tt in range(4):
                    tm = p.op("t", lambda h, l=dq[:, tt * 128:(tt + 1) * 128], r=rh,
                              o=ctx.pb[4][:, tt * nho:(tt + 1) * nho]:
                              h.matmul(o, l, r, start=True, stop=True), inc=(1 if tt == 3 else 0))
                sq_tok[tb % 2] = tm
                p.wait("v", tm)
                tvd = p.op("v", lambda h: h.memset(fscr[:, 191:192], 0.0), inc=1)
                p.wait("sc", tvd)
                ev_tok = p.op("sc", lambda h, o=fscr[:, 128 + tb * 4 * nho: 128 + (tb + 1) * 4 * nho],
                              i=ctx.pb[4][:, 0:4 * nho]: h.activation(o, i, AF.Copy), inc=1)
                bank_tok[4] = ev_tok
            width = 32 * nho
            p.wait("v", ev_tok)
            f1_ = fscr[:, 0:width]
            f2_ = fscr[:, 64:64 + width]
            inv_pm = 1.0 / cfg.exp_premul
            if qk == "rms":
                tv = p.op("v", lambda h, a=f1_, b=fscr[:, 128:128 + width]: h.tensor_scalar(a, b, 1.0 / 64.0, 1e-6, ALU.mult, ALU.add), inc=1)
                p.wait("sc", tv)
                ts = p.op("sc", lambda h, a=f2_, b=f1_: h.activation(a, b, AF.Sqrt), inc=1)
                p.wait("v", ts)
                tpm = p.op("v", lambda h, a=f2_, s=inv_pm: h.tensor_scalar(a, a, s, None, ALU.mult), inc=1)
                p.wait("v", tpm, force=True)
                tv2 = p.op("v", lambda h, o=nskt[:, c * width:(c + 1) * width], b=f2_: h.reciprocal(o, b), inc=1)
            else:
                tv = p.op("v", lambda h, a=f1_, b=fscr[:, 128:128 + width]: h.tensor_copy(a, b), inc=1)
                p.wait("sc", tv)
                ts = p.op("sc", lambda h, a=f2_, b=f1_: h.activation(a, b, AF.Sqrt), inc=1)
                p.wait("v", ts)
                p.op("v", lambda h, a=f2_: h.tensor_scalar_max(a, a, 1e-12))
                tpm = p.op("v", lambda h, a=f2_, s=inv_pm: h.tensor_scalar(a, a, s, None, ALU.mult), inc=1)
                p.wait("v", tpm, force=True)
                tv2 = p.op("v", lambda h, o=nskt[:, c * width:(c + 1) * width], b=f2_: h.reciprocal(o, b), inc=1)
            bank_tok[4] = tv2

    if getattr(cfg, "dbg", None) is not None and "knorm_aps" in cfg.dbg:
        p.barrier()
        for nm, (sap, dap) in cfg.dbg["knorm_aps"].items():
            p.dma("sy", dap, sap)
        p.barrier()

    if getattr(cfg, "load_wv", None) is not None:
        p.barrier()
        cfg.load_wv()
        p.barrier()
    WV = S.get("WV") or S["WK"]
    vone = VTOK.rearrange("p (n s) -> p n s", s=65)
    tvo = p.op("v", lambda h: h.memset(vone[:, :, 64:65], 1.0), inc=1)
    nvc = nkvh * 64 // 128
    wv_cols = cfg.wv_cols
    vstg_tok = {}
    vbc_tok = {}
    for tb in range(8):
        for c in range(nvc):
            t0 = tb * 512
            b = c % 2
            pbV = ctx.pb[b]
            use_bank("t", b)
            for kc in range(kv_kc):
                xap, tok = kv_src(kc, t0, 512)
                p.wait("t", tok)
                tm = p.op("t", lambda h, w=WV[:, kc * wv_cols + c * 128: kc * wv_cols + c * 128 + 128],
                          x=xap, o=pbV, s=(kc == 0), e=(kc == kv_kc - 1):
                          h.matmul(o[:, :], w, x, start=s, stop=e), inc=(1 if kc == kv_kc - 1 else 0))
            vstg = S["VSTG"][:, b * 512: b * 512 + 512]
            p.wait("sc", tm)
            p.wait("sc", vstg_tok.get(b))
            if getattr(cfg, "bv", None) is not None:
                te = p.op("sc", lambda h, o=vstg, i=pbV, ba=cfg.bv[:, c:c + 1]: h.activation(o, i[:, :], AF.Identity, bias=ba), inc=1)
            elif getattr(cfg, "v_scale_row", None) is not None:
                sl = tb % 2
                bcb = S["bcbuf"][:, sl * 512:sl * 512 + 512]
                use_bank("t", 5)
                tb1 = p.op("t", lambda h, r=cfg.v_scale_row[0:1, t0:t0 + 512]:
                           h.matmul(ctx.pb[5][:, :], ctx.ones_row_bf[:, :], r, start=True, stop=True), inc=1)
                p.wait("sc", tb1)
                p.wait("sc", vbc_tok.get(sl))
                p.op("sc", lambda h, o=bcb: h.activation(o, ctx.pb[5][:, :], AF.Copy))
                te0 = p.op("sc", lambda h, o=vstg, i=pbV: h.activation(o, i[:, :], AF.Copy), inc=1)
                p.wait("v", te0)
                te = p.op("v", lambda h, o=vstg, bcx=bcb: h.tensor_tensor(o, o, bcx, ALU.mult), inc=1)
                bank_tok[5] = te
                vbc_tok[sl] = te
            else:
                te = p.op("sc", lambda h, o=vstg, i=pbV: h.activation(o, i[:, :], AF.Copy), inc=1)
            bank_tok[b] = te
            p.wait("t", te)
            p.wait("v", tvo)
            last_v = None
            for tt in range(4):
                kt = tb * 4 + tt
                pt = ctx.pbbf[:, 0:128]
                use_bank("t", "bf")
                ttr = p.op("t", lambda h, i=vstg[:, tt * 128:(tt + 1) * 128], o=pt:
                           h.transpose(o, i, S["IDENT"][:, :]), inc=1)
                p.wait("v", ttr)
                bA = ((2 * c) * 32 + kt) * 65
                bB = ((2 * c + 1) * 32 + kt) * 65
                p.op("v", lambda h, o=VTOK[:, bA:bA + 64], i=pt[:, 0:64]: h.tensor_copy(o, i))
                tvv = p.op("v", lambda h, o=VTOK[:, bB:bB + 64], i=pt[:, 64:128]: h.tensor_copy(o, i), inc=1)
                bank_tok["bf"] = tvv
                last_v = tvv
            vstg_tok[b] = last_v

    p.barrier()
    if getattr(cfg, "dbg", None) is not None and "kv_aps" in cfg.dbg:
        for nm, (sap, dap) in cfg.dbg["kv_aps"].items():
            p.dma("sy", dap, sap)
        p.barrier()

    WQb = S["WQB"]
    WOS = S.get("WOS")
    FQ = NQC * (2 if rope else 1)
    recips = sumsrow[0:1, 1024:2048] if False else S["recips"]
    ostg = S["OSTG"]
    p_tok = {}
    asm_tok = None
    qe_tok = {}
    for qc in range(8):
        q0 = qc * 512
        sched = cfg.mask_sched(qc)
        p.wait("sc", qe_tok.get(("o", 0)))
        p.wait("sc", qe_tok.get(("o", 1)))
        wq_toks = {}
        for fc in range(FQ):
            slot = fc % 2
            b = fc % 2
            if fc >= 2:
                p.wait("g", wq_toks[fc - 2])
            else:
                p.wait("g", qe_tok.get(("wo", fc % 2)))
            tl = p.dma("g", WQb[:, slot * 1024:(slot + 1) * 1024].rearrange("p (k c) -> p k c", c=128),
                       cfg.wq[:, fc * 128:(fc + 1) * 128].rearrange("(k p) c -> p k c", p=128))
            pbQ = ctx.pb[b]
            use_bank("t", b)
            p.wait("t", tl)
            for kc in range(NCHUNK):
                xap, tok = cfg.src(kc, q0, 512)
                p.wait("t", tok)
                tm = p.op("t", lambda h, w=WQb[:, slot * 1024 + kc * 128: slot * 1024 + kc * 128 + 128],
                          x=xap, o=pbQ, s=(kc == 0), e=(kc == NCHUNK - 1):
                          h.matmul(o[:, :], w, x, start=s, stop=e), inc=(1 if kc == NCHUNK - 1 else 0))
            wq_toks[fc] = tm
            dst = qtile(fc) if fc < NQC else qsw(fc - NQC)
            p.wait("sc", tm)
            p.wait("sc", asm_tok)
            if getattr(cfg, "bq", None) is not None and fc < NQC:
                te = p.op("sc", lambda h, o=dst, i=pbQ, ba=cfg.bq[:, fc:fc + 1]: h.activation(o, i[:, :], AF.Identity, bias=ba), inc=1)
            else:
                te = p.op("sc", lambda h, o=dst, i=pbQ: h.activation(o, i[:, :], AF.Copy), inc=1)
            bank_tok[b] = te
            qe_tok[("e", fc)] = te
        if rope is not None:
            rt = S["ROPE"]
            for c in range(NQC):
                p.wait("v", qe_tok[("e", c)])
                p.wait("v", qe_tok[("e", c + NQC)])
                p.op("v", lambda h, a=qtile(c), t1=rt[:, q0:q0 + 512]: h.tensor_tensor(a, a, t1, ALU.mult))
                p.op("v", lambda h, bx=qsw(c), t2=rt[:, T + q0:T + q0 + 512]: h.tensor_tensor(bx, bx, t2, ALU.mult))
                p.op("v", lambda h, a=qtile(c), bx=qsw(c): h.tensor_tensor(a, a, bx, ALU.add), inc=1)
                qe_tok[("r", c)] = p.tok("v")
        if qk is not None:
            use_bank("t", 4)
            for c in range(NQC):
                src_t = qe_tok.get(("r", c), qe_tok[("e", c)])
                p.wait("v", src_t)
                tv = p.op("v", lambda h, o=qsw(c), a=qtile(c): h.tensor_tensor(o, a, a, ALU.mult), inc=1)
                p.wait("t", tv)
                for tt in range(4):
                    tm = p.op("t", lambda h, l=qsw(c)[:, tt * 128:(tt + 1) * 128], r=S["MASKH"][:, 1:3],
                              o=ctx.pb[4][:, tt * 2 * NQC + c * 2: tt * 2 * NQC + c * 2 + 2]:
                              h.matmul(o, l, r, start=True, stop=True), inc=(1 if (tt == 3 and c == NQC - 1) else 0))
            qn_w = 2 * NQC * 4
            p.wait("v", tm)
            f1q = fscr[:, 0:qn_w]
            f3q = fscr[:, 64:64 + qn_w]
            if qk == "rms":
                p.op("v", lambda h, a=f1q, b=ctx.pb[4][:, 0:qn_w]: h.tensor_scalar(a, b, 1.0 / 64.0, 1e-6, ALU.mult, ALU.add), inc=1)
                p.wait("sc", p.tok("v"))
                ts = p.op("sc", lambda h, a=f1q: h.activation(a, a, AF.Sqrt), inc=1)
                p.wait("v", ts)
                tvq0 = p.op("v", lambda h, o=f3q, b=f1q: h.reciprocal(o, b), inc=1)
            else:
                p.op("v", lambda h, a=f1q, b=ctx.pb[4][:, 0:qn_w]: h.tensor_copy(a, b), inc=1)
                p.wait("sc", p.tok("v"))
                ts = p.op("sc", lambda h, a=f1q: h.activation(a, a, AF.Sqrt), inc=1)
                p.wait("v", ts)
                tmx = p.op("v", lambda h, a=f1q: h.tensor_scalar_max(a, a, 1e-12), inc=1)
                p.wait("v", tmx, force=True)
                tvq0 = p.op("v", lambda h, o=f3q, b=f1q: h.reciprocal(o, b), inc=1)
            p.wait("v", tvq0, force=True)
            tvq = p.op("v", lambda h, o=fscr.bitcast(BF16)[:, 256:256 + qn_w], b=f3q: h.tensor_copy(o, b), inc=1)
            bank_tok[4] = tvq
            rqb = fscr.bitcast(BF16)[:, 256:384]
            for tt in range(4):
                pt = ctx.pbbf[:, 0:128]
                use_bank("t", "bf")
                p.wait("t", tvq)
                ttr = p.op("t", lambda h, i=rqb[:, tt * 2 * NQC:(tt + 1) * 2 * NQC], o=pt[0:2 * NQC, 0:128]:
                           h.transpose(o, i, S["IDENT"][:, :]), inc=1)
                p.wait("sc", ttr)
                tse = p.op("sc", lambda h, o=rqrows[0:2 * NQC, tt * 128:(tt + 1) * 128], i=pt[0:2 * NQC, 0:128]:
                           h.activation(o, i, AF.Copy), inc=1)
                bank_tok["bf"] = tse
            for c in range(NQC):
                use_bank("t", 5)
                p.wait("t", tse)
                tbq = p.op("t", lambda h, l=S["SELP"][0:nq, c * 128:(c + 1) * 128], r=rqrows[0:nq, 0:512]:
                           h.matmul(ctx.pb[5][:, :], l, r, start=True, stop=True), inc=1)
                p.wait("v", tbq)
                tvm = p.op("v", lambda h, a=qtile(c): h.tensor_tensor(a, a, ctx.pb[5][:, :], ALU.mult), inc=1)
                bank_tok[5] = tvm
                qe_tok[("n", c)] = tvm
        if getattr(cfg, "dbg", None) is not None and qc == cfg.dbg["qc"] and "q_ap" in cfg.dbg:
            for fc in range(FQ):
                p.wait("sy", qe_tok[("e", fc)])
            p.dma("sy", cfg.dbg["q_ap"], SCR[:, 0:4096])
            if "wq_ap" in cfg.dbg:
                p.dma("sy", cfg.dbg["wq_ap"], WQb[:, :])
            p.barrier()
        for c in range(NQC):
            g = (c // 2) if dup else c
            q_ready = qe_tok.get(("n", c)) or qe_tok.get(("r", c)) or qe_tok[("e", c)]
            pvA, pvB = ctx.pb[4], ctx.pb[5]
            use_bank("t", 4)
            use_bank("t", 5)
            p.wait("t", q_ready)
            nsched = len(sched)
            pend = None

            def emit_pv(ent, last):
                idx0, teP0, pA0, pB0, vbA0, vbB0 = ent
                p.wait("t", teP0)
                p.op("t", lambda h, l=VTOK[:, vbA0:vbA0 + 65], r=pA0, o=pvA[0:65, :], s=(idx0 == 0), e=last:
                     h.matmul(o, l, r, start=s, stop=e))
                tPV0 = p.op("t", lambda h, l=VTOK[:, vbB0:vbB0 + 65], r=pB0, o=pvB[0:65, :], s=(idx0 == 0), e=last:
                            h.matmul(o, l, r, start=s, stop=e), inc=1)
                p_tok[2 * (idx0 % 2)] = tPV0
                return tPV0

            for idx, (kt, mi) in enumerate(sched):
                bA = idx % 2
                sA, sB = ctx.pb[0 + 2 * bA], ctx.pb[1 + 2 * bA]
                use_bank("t", 0 + 2 * bA)
                use_bank("t", 1 + 2 * bA)
                p.op("t", lambda h, l=KTS[0:64, g * T + kt * 128: g * T + kt * 128 + 128],
                     r=qtile(c)[0:64, :], o=sA: h.matmul(o[:, :], l, r, start=True, stop=True, tile_position=(0, 0)))
                tSB = p.op("t", lambda h, l=KTS[64:128, g * T + kt * 128: g * T + kt * 128 + 128],
                           r=qtile(c)[64:128, :], o=sB: h.matmul(o[:, :], l, r, start=True, stop=True, tile_position=(64, 0)), inc=1)
                if pend is not None:
                    emit_pv(pend, False)
                if mi is not None:
                    p.wait("v", tSB)
                    p.op("v", lambda h, o=sA, m=S["MASKS"][:, mi * 512:(mi + 1) * 512]: h.tensor_tensor(o[:, :], o[:, :], m, ALU.add))
                    tSB = p.op("v", lambda h, o=sB, m=S["MASKS"][:, mi * 512:(mi + 1) * 512]: h.tensor_tensor(o[:, :], o[:, :], m, ALU.add), inc=1)
                pA, pB = P_t(2 * bA), P_t(2 * bA + 1)
                p.wait("sc", tSB)
                p.wait("sc", p_tok.get(2 * bA))
                if qk is not None:
                    colA = g * 32 * nho + kt * nho
                    colB = colA if dup else colA + 1
                    p.op("sc", lambda h, o=pA, i=sA, scl=nskt[:, colA:colA + 1]:
                         h.activation(o, i[:, :], AF.Exp, bias=cfg.exp_bias, scale=scl))
                    teP = p.op("sc", lambda h, o=pB, i=sB, scl=nskt[:, colB:colB + 1]:
                               h.activation(o, i[:, :], AF.Exp, bias=cfg.exp_bias, scale=scl), inc=1)
                else:
                    p.op("sc", lambda h, o=pA, i=sA: h.activation(o, i[:, :], AF.Exp, bias=cfg.exp_bias, scale=cfg.exp_scale_const))
                    teP = p.op("sc", lambda h, o=pB, i=sB: h.activation(o, i[:, :], AF.Exp, bias=cfg.exp_bias, scale=cfg.exp_scale_const), inc=1)
                bank_tok[0 + 2 * bA] = teP
                bank_tok[1 + 2 * bA] = teP
                hA = (g if dup else 2 * c)
                hB = (g if dup else 2 * c + 1)
                pend = (idx, teP, pA, pB, (hA * 32 + kt) * 65, (hB * 32 + kt) * 65)
            tPV = emit_pv(pend, True)
            p.wait("sc", tPV)
            p.wait("sc", qe_tok.get("rc"))
            p.op("sc", lambda h: h.activation(sumsrow[0:1, 0:512], pvA[64:65, :], AF.Copy))
            tsum = p.op("sc", lambda h: h.activation(sumsrow[0:1, 512:1024], pvB[64:65, :], AF.Copy), inc=1)
            p.wait("v", tsum)
            if getattr(cfg, "sums_clip", None) is not None:
                tcl = p.op("v", lambda h: h.tensor_scalar_max(sumsrow[0:1, :], sumsrow[0:1, :], cfg.sums_clip), inc=1)
                p.wait("v", tcl, force=True)
            trc = p.op("v", lambda h: h.reciprocal(recips[0:1, 0:1024], sumsrow[0:1, :]), inc=1)
            qe_tok["rc"] = trc
            use_bank("t", 6)
            p.wait("t", trc)
            p.op("t", lambda h: h.matmul(ctx.pb[6][0:64, :], ctx.ones_row[0:1, 0:64], recips[0:1, 0:512],
                                         start=True, stop=True, tile_position=(0, 0)))
            tbc = p.op("t", lambda h: h.matmul(ctx.pb[6][64:128, :], ctx.ones_row[0:1, 0:64], recips[0:1, 512:1024],
                                               start=True, stop=True, tile_position=(0, 64)), inc=1)
            p.wait("sc", asm_tok)
            p.op("sc", lambda h, o=asm(c)[0:64, :], i=pvA[0:64, :]: h.activation(o, i, AF.Copy))
            tev = p.op("sc", lambda h, o=asm(c)[64:128, :], i=pvB[0:64, :]: h.activation(o, i, AF.Copy), inc=1)
            bank_tok[4] = tev
            bank_tok[5] = tev
            p.wait("v", tev)
            p.wait("v", tbc)
            p.op("v", lambda h, a=asm(c)[0:64, :]: h.tensor_tensor(a, a, ctx.pb[6][0:64, :], ALU.mult))
            tno = p.op("v", lambda h, a=asm(c)[64:128, :]: h.tensor_tensor(a, a, ctx.pb[6][64:128, :], ALU.mult), inc=1)
            bank_tok[6] = tno
            qe_tok[("a", c)] = tno
        if getattr(cfg, "dbg", None) is not None and qc == cfg.dbg["qc"]:
            for nm, ap in cfg.dbg["aps"].items():
                for c2 in range(NQC):
                    p.wait("sy", qe_tok[("a", c2)])
                p.wait("sy", p.tok("v"))
                p.wait("sy", p.tok("sc"))
                p.dma("sy", ap[1], ap[0])
        for c in range(NQC):
            p.wait("t", qe_tok[("a", c)])
        for fc in range(NCHUNK):
            b = fc % 2
            pbO = ctx.pb[b]
            use_bank("t", b)
            if getattr(cfg, "wo_stream", None) is not None:
                p.wait("g", qe_tok.get(("wo", b)))
                if fc < 2:
                    p.wait("g", p.tok("t"))
                two = p.dma("g", S["WOB"][:, b * 1024: b * 1024 + IC * 128].rearrange("p (k c) -> p k c", c=128),
                            cfg.wo_stream[:, fc * 128:(fc + 1) * 128].rearrange("(k p) c -> p k c", p=128))
                p.wait("t", two)
                wsrc = lambda ic, b=b: S["WOB"][:, b * 1024 + ic * 128: b * 1024 + ic * 128 + 128]
            else:
                wsrc = lambda ic, fc=fc: WOS[:, ic * 1024 + fc * 128: ic * 1024 + fc * 128 + 128]
            for ic in range(IC):
                tm = p.op("t", lambda h, w=wsrc(ic), x=asm(ic), o=pbO, s=(ic == 0), e=(ic == IC - 1):
                          h.matmul(o[:, :], w, x, start=s, stop=e), inc=(1 if ic == IC - 1 else 0))
            o_sl = ostg[:, b * 512: b * 512 + 512]
            res = cfg.res(fc, q0) if getattr(cfg, "res", None) is not None else None
            p.wait("sc" if getattr(cfg, "bo", None) is not None else "v", tm)
            if getattr(cfg, "bo", None) is not None:
                p.wait("sc", qe_tok.get(("o", b)))
                te = p.op("sc", lambda h, o=o_sl, i=pbO, ba=cfg.bo[:, fc:fc + 1]: h.activation(o, i[:, :], AF.Identity, bias=ba), inc=1)
                if res is not None:
                    rap, rtok = res
                    p.wait("v", te)
                    p.wait("v", rtok)
                    te = p.op("v", lambda h, o=o_sl, r=rap: h.tensor_tensor(o, o, r, ALU.add), inc=1)
            else:
                p.wait("v", tm)
                p.wait("v", qe_tok.get(("o", b)))
                if res is not None:
                    rap, rtok = res
                    p.wait("v", rtok)
                    te = p.op("v", lambda h, o=o_sl, i=pbO, r=rap: h.tensor_tensor(o, i[:, :], r, ALU.add), inc=1)
                else:
                    te = p.op("v", lambda h, o=o_sl, i=pbO: h.tensor_copy(o, i[:, :]), inc=1)
            bank_tok[b] = te
            qe_tok[("wo", b)] = tm
            p.wait("sy", te)
            td = p.dma("sy", out_dram[fc * 128:(fc + 1) * 128, q0:q0 + 512], o_sl)
            qe_tok[("o", b)] = td
        asm_tok = p.tok("t")
    p.barrier()


class SrcStream:

    def __init__(self, ctx, dram, buf, queue="sy", barrier=False):
        self.ctx, self.dram, self.buf, self.q = ctx, dram, buf, queue
        self.barrier = barrier
        self.key = {}

    def __call__(self, kc, t0, w=512):
        p = self.ctx.p
        cur = self.key.get(kc)
        if cur is not None and cur[0] == t0:
            return (self.buf[:, kc * 512:(kc + 1) * 512], cur[1])
        if cur is not None:
            if self.barrier or _os.environ.get("PROVIDER_BARRIER"):
                p.barrier()
            p.wait(self.q, p.tok("t"))
            p.wait(self.q, p.tok("v"))
            p.wait(self.q, p.tok("sc"))
        tl = p.dma(self.q, self.buf[:, kc * 512:(kc + 1) * 512],
                   self.dram[kc * 128:(kc + 1) * 128, t0:t0 + w])
        self.key[kc] = (t0, tl)
        return (self.buf[:, kc * 512:(kc + 1) * 512], tl)


def emit_rms_row_stream(ctx, src, out_row, eps=1e-6):
    p = ctx.p
    S = ctx.sb
    rows = S["sumsrow"]
    ps = ctx.pb[6]
    for tb in range(8):
        t0 = tb * 512
        t_done = []
        for ch in range(NCHUNK):
            xap, tok = src(ch, t0)
            sq = S["VSTG"][:, (ch % 2) * 512:(ch % 2) * 512 + 512]
            p.wait("v", tok)
            if ch >= 2:
                p.wait("v", t_done[ch - 2])
            tv = p.op("v", lambda h, o=sq, i=xap: h.tensor_tensor(o, i, i, ALU.mult), inc=1)
            p.wait("t", tv)
            tm = p.op("t", lambda h, b=sq, s=(ch == 0): h.matmul(ps[0:1, :], ctx.ones_bf[:, :], b, start=s, stop=(ch == 7)), inc=1)
            t_done.append(tm)
        r1 = rows[0:1, 0:512]
        p.wait("v", t_done[-1])
        tv = p.op("v", lambda h: h.tensor_scalar(r1, ps[0:1, :], 1.0 / D, eps, ALU.mult, ALU.add), inc=1)
        p.wait("sc", tv)
        ts = p.op("sc", lambda h: h.activation(r1, r1, AF.Sqrt), inc=1)
        p.wait("v", ts)
        tvr = p.op("v", lambda h, o=out_row[0:1, t0:t0 + 512]: h.reciprocal(o, r1), inc=1, lp=True)
        p.wait("t", tvr)
    return p.tok("v")


def _attn_sbuf_spec(nq, nkvc, nkvh, nho, kv_kc, wk_cols, wv_cols, *, rope, nmask, latent,
                    wo_stream=False, qk=True, scaled=True, wv_shared=False, bcb=True):
    NQC = nq // 2
    spec = [
        ("SCR", (128, 14336), BF16),
        ("KTS", (128, nkvc * T), BF16),
        ("VTOK", (128, nkvh * 32 * 65), BF16),
        ("SRCB", (128, 4096), BF16),
        ("WK", (128, kv_kc * max(wk_cols, wv_cols if wv_shared else 0)), BF16),
        ("WQB", (128, 2048), BF16),
        ("MASKH", (128, 3), BF16),
        ("IDENT", (128, 128), BF16),
        ("sumsrow", (1, 1024), F32),
        ("recips", (1, 1024), F32),
        ("fscr", (128, 256), F32),
        ("VSTG", (128, 1024), BF16),
        ("OSTG", (128, 1024), BF16),
    ]
    if not wv_shared:
        spec.append(("WV", (128, kv_kc * wv_cols), BF16))
    if qk:
        spec += [("SELP", (nq, NQC * 128), BF16),
                 ("nskt", (128, max(1, nkvc * 32 * nho)), F32),
                 ("rqrows", (16, 512), BF16)]
    if bcb:
        spec.append(("bcbuf", (128, 1024), BF16))
    if scaled:
        spec += [("wstage", (128, 2048), F32), ("wstageb", (128, 2048), BF16)]
    spec.append(("WOB", (128, 2048), BF16) if wo_stream else ("WOS", (128, NQC * 1024), BF16))
    if rope:
        spec.append(("ROPE", (128, 2 * T), BF16))
    if nmask:
        spec.append(("MASKS", (128, nmask * 512), BF16))
    if latent:
        spec.append(("WD", (128, NCHUNK * 256), BF16))
        spec.append(("LATN", (128, 2 * T), BF16))
    return spec


def _load_attn_consts(ctx, aps, *, rope_key=None, nmask=0, mask_key=None, rs1=None):
    p = ctx.p
    S = ctx.sb
    toks = []
    toks.append(p.dma("g", S["MASKH"][:, :], aps["maskh"][:, :]))
    toks.append(p.dma("g", S["IDENT"][:, :], aps["ident"][:, :]))
    if rope_key:
        toks.append(p.dma("g", S["ROPE"][:, :], aps[rope_key][:, :]))
    for mi in range(nmask):
        toks.append(p.dma("sy", S["MASKS"][:, mi * 512:(mi + 1) * 512], aps[mask_key][mi * 128:(mi + 1) * 128, :]))
    return toks


def _sched_causal(qc):
    return [(kt, (kt - qc * 4) if kt >= qc * 4 else None) for kt in range(qc * 4 + 4)]


def _sched_local(qc):
    out = []
    for kt in range(max(0, qc * 4 - 2), qc * 4 + 4):
        out.append((kt, kt - qc * 4 + 2))
    return out


def _sched_full(qc):
    return [(kt, None) for kt in range(32)]


def build_g2(shared, percore0, dbg_qc=None):
    import contextlib
    nc = bass.Bass("TRN2", target_bir_lowering=False)
    stack = contextlib.ExitStack()
    nc._dev_stack = stack
    aps = _declare(nc, shared, percore0, ["g_wq", "g_kvd", "g_kup", "g_vup", "g_wo",
                                         "w1n", "ropeg", "cmask", "ident", "selp16", "maskh"],
                   extra={"xnT_in": ([D, T], BF16), "rs1_in": ([1, T], F32)})
    outs = {"x1T": ([D, T], BF16)}
    if dbg_qc is not None:
        outs.update({"d_scr": ([128, 14336], BF16), "d_nskt": ([128, 128], F32),
                     "d_sums": ([1, 1024], F32), "d_recips": ([1, 1024], F32)})
    aps.update(_outs(nc, outs))
    wqbf = nc.dram_tensor("wqbf", [D, 2048], BF16, kind="Internal").ap()
    spec = _attn_sbuf_spec(16, 4, 4, 1, 2, 1024, 256, rope=True, nmask=4, latent=True)
    spec += [("W1N", (128, 8), F32), ("RS1", (1, T), BF16)]
    with nc.Block() as block:
        ctx = Ctx(nc, stack, aps, consts_vals=(-8.0,), sbuf_spec=spec)
        ctx.setup()
        p = ctx.p
        S = ctx.sb
        p.dma("g", S["W1N"][:, :], aps["w1n"][:, :])
        p.dma("g", S["RS1"][:, :], aps["rs1_in"][:, :])
        p.dma("g", S["SELP"][:, :], aps["selp16"][:, :])
        _load_attn_consts(ctx, aps, rope_key="ropeg", nmask=4, mask_key="cmask")
        p.barrier()
        emit_w_prep(ctx, aps["g_wq"], wqbf, 8, 2048, S["W1N"])
        load_rows(ctx, aps["g_kvd"], S["WD"], 8, 256, scale=S["W1N"])
        load_rows(ctx, aps["g_kup"], S["WK"], 2, 1024)
        load_rows(ctx, aps["g_vup"], S["WV"], 2, 256)
        load_rows(ctx, aps["g_wo"], S["WOS"], 8, 1024)
        p.barrier()
        src = SrcStream(ctx, aps["xnT_in"], S["SRCB"])
        cfg = AttnCfg(nheads=16, kv_dup=True, nkv_chunks=4, nkv_heads=4, rope="ROPE", qk="rms",
                      wq=wqbf, wq_bf=True, wk_cols=1024, wv_cols=256,
                      latent=dict(scale_row=S["RS1"]), exp_premul=0.125, exp_bias=-8.0,
                      mask_sched=_sched_causal, src=src,
                      res=lambda fc, q0: src(fc, q0))
        if dbg_qc is not None:
            cfg.dbg = {"qc": dbg_qc, "aps": {
                "scr": (S["SCR"][:, :], aps["d_scr"][:, :]),
                "nskt": (S["nskt"][:, :], aps["d_nskt"][:, :]),
                "sums": (S["sumsrow"][:, :], aps["d_sums"][:, :]),
                "recips": (S["recips"][:, :], aps["d_recips"][:, :]),
            }}
        emit_attn(ctx, cfg, aps["x1T"])
        _finish(ctx)
        ctx.p.build(block)
    return nc


class SrcResident:

    def __init__(self, ctx, arena, tok):
        self.arena, self.tok = arena, tok

    def __call__(self, kc, t0, w=512):
        return (self.arena[:, kc * T + t0: kc * T + t0 + w], self.tok)


def build_g3(shared, percore0, dbg_qc=None, resident=False):
    import contextlib
    nc = bass.Bass("TRN2", target_bir_lowering=False)
    stack = contextlib.ExitStack()
    nc._dev_stack = stack
    aps = _declare(nc, shared, percore0, ["l_wq", "l_wk", "l_wv", "l_wo",
                                         "w2n", "ropel", "lmask", "ident", "selp16", "maskh"],
                   extra={"x1T_in": ([D, T], BF16)})
    outs3 = {"x2T": ([D, T], BF16)}
    if dbg_qc is not None:
        outs3.update({"d_scr": ([128, 14336], BF16), "d_nskt": ([128, 128], F32),
                      "d_sums": ([1, 1024], F32), "d_rs2": ([1, T], BF16),
                      "d_kts": ([128, 4 * T], BF16), "d_wk": ([128, 8192], BF16),
                      "d_nskt0": ([128, 128], F32), "d_dq": ([128, 1024], BF16), "d_fscr": ([128, 256], F32)})
    aps.update(_outs(nc, outs3))
    wqbf = nc.dram_tensor("wqbf", [D, 2048], BF16, kind="Internal").ap()
    spec = _attn_sbuf_spec(16, 4, 4, 1, 8, 1024, 256, rope=True, nmask=6, latent=False)
    spec += [("W2N", (128, 8), F32), ("RS2", (1, T), BF16)]
    if resident:
        spec = [s for s in spec if s[0] != "SRCB"] + [("XA", (128, NCHUNK * T), BF16)]
    with nc.Block() as block:
        ctx = Ctx(nc, stack, aps, consts_vals=(-8.0,), sbuf_spec=spec)
        ctx.setup()
        p = ctx.p
        S = ctx.sb
        p.dma("g", S["W2N"][:, :], aps["w2n"][:, :])
        p.dma("g", S["SELP"][:, :], aps["selp16"][:, :])
        _load_attn_consts(ctx, aps, rope_key="ropel", nmask=6, mask_key="lmask")
        p.barrier()
        emit_w_prep(ctx, aps["l_wq"], wqbf, 8, 2048, S["W2N"])
        load_rows(ctx, aps["l_wk"], S["WK"], 8, 1024, scale=S["W2N"])
        load_rows(ctx, aps["l_wv"], S["WV"], 8, 256, scale=S["W2N"])
        load_rows(ctx, aps["l_wo"], S["WOS"], 8, 1024)
        p.barrier()
        if resident:
            for kc in range(NCHUNK):
                tok = p.dma("g", S["XA"][:, kc * T:(kc + 1) * T], aps["x1T_in"][kc * 128:(kc + 1) * 128, :])
            p.barrier()
            src = SrcResident(ctx, S["XA"], None)
        else:
            src = SrcStream(ctx, aps["x1T_in"], S["SRCB"])
            import os as _os
            if _os.environ.get("PARANOID"):
                src.paranoid = True
        emit_rms_row_stream(ctx, src, S["RS2"], eps=1e-6)
        cfg = AttnCfg(nheads=16, kv_dup=True, nkv_chunks=4, nkv_heads=4, rope="ROPE", qk="rms",
                      wq=wqbf, wq_bf=True, wk_cols=1024, wv_cols=256,
                      v_scale_row=S["RS2"], exp_premul=0.125, exp_bias=-8.0,
                      mask_sched=_sched_local, src=src,
                      res=lambda fc, q0: src(fc, q0))
        if dbg_qc is not None:
            cfg.dbg = {"qc": dbg_qc, "aps": {
                "scr": (S["SCR"][:, :], aps["d_scr"][:, :]),
                "nskt": (S["nskt"][:, :], aps["d_nskt"][:, :]),
                "sums": (S["sumsrow"][:, :], aps["d_sums"][:, :]),
                "rs2": (S["RS2"][:, :], aps["d_rs2"][:, :]),
            }, "kv_aps": {
                "kts": (S["KTS"][:, :], aps["d_kts"][:, :]),
                "wk": (S["WK"][:, :], aps["d_wk"][:, :]),
            }, "knorm_aps": {
                "nskt0": (S["nskt"][:, :], aps["d_nskt0"][:, :]),
                "dq": (S["SCR"][:, 12288:13312], aps["d_dq"][:, :]),
                "fscr": (S["fscr"][:, :], aps["d_fscr"][:, :]),
            }}
        emit_attn(ctx, cfg, aps["x2T"])
        _finish(ctx)
        ctx.p.build(block)
    return nc



def emit_moe(ctx, cfg, finish):
    p = ctx.p
    S = ctx.sb
    bank_tok = {}

    def use_bank(e, b):
        tok = bank_tok.get(b)
        if tok is not None:
            p.wait(e, tok)
    XMB = S["XMB"]
    W1S, W2S = S["W1S"], S["W2S"]
    HS = S["HS"]
    YA = S["YA"]
    CMB = S["CMB"]
    WTR = S["WTR"]
    WBC = S["WBC"]
    GSC = S["GSC"]
    renorm = cfg.renorm
    act = cfg.act
    src_dram = cfg.src_dram

    wgs = S["WGS"]
    load_rows(ctx, cfg.wg, wgs, NCHUNK, 8)
    if cfg.gb is not None:
        p.dma("g", S["GB"][:, :], cfg.gb[0:1, :])
    p.barrier()

    E_LIST = [int(x) for x in _os.environ.get("MOE_EXPERTS", "0,1,2,3,4,5,6,7").split(",")]
    for mega in range(2):
        tm0 = mega * 2048
        p.wait("g", p.tok("t"))
        xmb_toks = [p.dma("g", XMB[:, kc * 2048:(kc + 1) * 2048],
                          src_dram[kc * 128:(kc + 1) * 128, tm0:tm0 + 2048]) for kc in range(NCHUNK)]
        e0 = E_LIST[0]
        tw1_pre = [p.dma("g", W1S[:, kc * 2048:(kc + 1) * 2048], cfg.w1[e0, kc * 128:(kc + 1) * 128, :]) for kc in range(NCHUNK)]
        tw2_pre = [p.dma("g", W2S[:, hc * 1024:(hc + 1) * 1024], cfg.w2[e0, hc * 128:(hc + 1) * 128, :]) for hc in range(16)]
        tb1_pre = None
        if cfg.b1 is not None:
            p.wait("g", p.tok("v"))
            tb1_pre = p.dma("g", S["B1E"][:, :], cfg.b1[e0, :, :])
        for tl in range(16):
            tt = mega * 16 + tl
            t0r = (tl // 4) * 512
            sub = (tl % 4) * 128
            pg = ctx.pb[6]
            for kc in range(NCHUNK):
                p.wait("t", xmb_toks[kc])
                tm = p.op("t", lambda h, l=XMB[:, kc * 2048 + t0r + sub: kc * 2048 + t0r + sub + 128],
                          r=wgs[:, kc * 8:(kc + 1) * 8],
                          o=pg[:, 0:8], s=(kc == 0), e=(kc == NCHUNK - 1 and cfg.gb is None):
                          h.matmul(o, l, r, start=s, stop=e), inc=(1 if (kc == NCHUNK - 1 and cfg.gb is None) else 0))
            if cfg.gb is not None:
                tm = p.op("t", lambda h: h.matmul(pg[:, 0:8], ctx.ones_row_bf[0:1, :], S["GB"][0:1, :],
                                                  start=False, stop=True), inc=1)
            g8 = GSC[:, 0:8]
            mx = GSC[:, 8:9]
            sm = GSC[:, 10:11]
            rs = GSC[:, 11:12]
            mx2 = GSC[:, 12:13]
            w1c = GSC[:, 14:15]
            w2c = GSC[:, 15:16]
            oh1 = GSC[:, 16:24]
            msk = GSC[:, 24:32]
            p.wait("v", tm)

            def vs(fn):
                tk = p.op("v", fn, inc=1)
                p.wait("v", tk, force=True)
                return tk
            tcopy = vs(lambda h: h.tensor_copy(g8, pg[:, 0:8]))
            if renorm:
                vs(lambda h: h.reduce_max(mx, g8, AX.X))
                vs(lambda h: h.tensor_scalar(oh1, g8, mx, None, ALU.is_ge))
                vs(lambda h: h.tensor_scalar(msk, oh1, NEG, None, ALU.mult))
                vs(lambda h: h.tensor_tensor(msk, g8, msk, ALU.add))
                vs(lambda h: h.reduce_max(mx2, msk, AX.X))
                tvd = vs(lambda h: h.tensor_tensor(sm, mx, mx2, ALU.subtract))
                p.wait("sc", tvd)
                tse = p.op("sc", lambda h: h.activation(w1c, sm, AF.Sigmoid, bias=0.0, scale=1.0), inc=1)
                p.wait("v", tse)
                vs(lambda h: h.tensor_scalar(w2c, w1c, -1.0, 1.0, ALU.mult, ALU.add))
                vs(lambda h: h.tensor_scalar(msk, msk, mx2, None, ALU.is_ge))
            else:
                p.wait("sc", tcopy)
                tse = p.op("sc", lambda h: h.activation(g8, g8, AF.Exp, bias=0.0, scale=1.0), inc=1)
                p.wait("v", tse)
                vs(lambda h: h.reduce_sum(sm, g8, AX.X))
                vs(lambda h: h.reciprocal(rs, sm))
                vs(lambda h: h.tensor_scalar(g8, g8, rs, None, ALU.mult))
                vs(lambda h: h.reduce_max(mx, g8, AX.X))
                vs(lambda h: h.tensor_scalar(oh1, g8, mx, None, ALU.is_ge))
                vs(lambda h: h.tensor_tensor(msk, g8, oh1, ALU.mult))
                vs(lambda h: h.tensor_tensor(msk, g8, msk, ALU.subtract))
                vs(lambda h: h.reduce_max(mx2, msk, AX.X))
                vs(lambda h: h.tensor_copy(w1c, mx))
                vs(lambda h: h.tensor_copy(w2c, mx2))
                vs(lambda h: h.tensor_scalar(msk, msk, mx2, None, ALU.is_ge))
            vs(lambda h: h.tensor_scalar(oh1, oh1, w1c, None, ALU.mult))
            vs(lambda h: h.tensor_scalar(msk, msk, w2c, None, ALU.mult))
            vs(lambda h, o=CMB[:, tt * 8:(tt + 1) * 8]:
               h.tensor_tensor(o, oh1, msk, ALU.add))
            p.wait("t", tcopy)
        p.wait("t", p.tok("v"))
        tr_tok = None
        for tl in range(16):
            tt = mega * 16 + tl
            p.wait("t", tr_tok)
            ttr = p.op("t", lambda h, i=CMB[:, tt * 8:(tt + 1) * 8], o=ctx.pbbf[0:8, 0:128]:
                       h.transpose(o, i, S["IDENT"][:, :]), inc=1)
            p.wait("sc", ttr)
            tr_tok = p.op("sc", lambda h, o=WTR[0:8, tl * 128:(tl + 1) * 128], i=ctx.pbbf[0:8, 0:128]:
                          h.activation(o, i, AF.Copy), inc=1)
        p.barrier()
        wbc_tok = {}
        h_done_prev = None
        for ei, e in enumerate(E_LIST):
            if ei == 0:
                tw1, tw2 = tw1_pre, tw2_pre
                if tb1_pre is not None:
                    p.wait("v", tb1_pre)
            else:
                p.wait("g", h_done_prev if h_done_prev is not None else p.tok("t"))
                tw1 = [p.dma("g", W1S[:, kc * 2048:(kc + 1) * 2048], cfg.w1[e, kc * 128:(kc + 1) * 128, :]) for kc in range(NCHUNK)]
                p.wait("g", p.tok("t"))
                tw2 = [p.dma("g", W2S[:, hc * 1024:(hc + 1) * 1024], cfg.w2[e, hc * 128:(hc + 1) * 128, :]) for hc in range(16)]
                if cfg.b1 is not None:
                    p.wait("g", p.tok("v"))
                    tb1 = p.dma("g", S["B1E"][:, :], cfg.b1[e, :, :])
                    p.wait("v", tb1)
            p.waits("t", tw1)
            p.waits("t", tw2)
            for tc in range(4):
                sl = tc % 2
                tbc = p.op("t", lambda h, l=S["SEL8"][0:8, e * 128:(e + 1) * 128],
                           r=WTR[0:8, tc * 512: tc * 512 + 512]:
                           h.matmul(ctx.pb[5][:, :], l, r, start=True, stop=True), inc=1)
                p.wait("sc", tbc)
                p.wait("sc", wbc_tok.get(sl))
                twb = p.op("sc", lambda h, o=WBC[:, sl * 512: sl * 512 + 512]:
                           h.activation(o, ctx.pb[5][:, :], AF.Copy), inc=1)
                p.wait("t", twb)
                for fc in range(16):
                    b = fc % 2
                    pb = ctx.pb[b]
                    use_bank("t", ("m", b))
                    for kc in range(NCHUNK):
                        tm = p.op("t", lambda h, l=W1S[:, kc * 2048 + fc * 128: kc * 2048 + fc * 128 + 128],
                                  r=XMB[:, kc * 2048 + tc * 512: kc * 2048 + tc * 512 + 512],
                                  o=pb, s=(kc == 0), e2=(kc == NCHUNK - 1):
                                  h.matmul(o[:, :], l, r, start=s, stop=e2), inc=(1 if kc == NCHUNK - 1 else 0))
                    hs = HS[:, fc * 512:(fc + 1) * 512]
                    p.wait("v", tm)
                    p.wait("v", twb)
                    if act == "relu2":
                        p.op("v", lambda h, o=hs, i=pb: h.tensor_scalar_max(o, i[:, :], 0.0))
                        p.op("v", lambda h, o=S["HTMP"][:, 0:512], a=hs, w=WBC[:, sl * 512: sl * 512 + 512]:
                             h.tensor_tensor(o, a, w, ALU.mult))
                        tve = p.op("v", lambda h, o=hs, a=S["HTMP"][:, 0:512], b2=hs:
                                   h.tensor_tensor(o, a, b2, ALU.mult), inc=1)
                    else:
                        p.op("v", lambda h, o=hs, i=pb, b1a=S["B1E"][:, fc:fc + 1]:
                             h.tensor_scalar(o, i[:, :], b1a, 0.0, ALU.add, ALU.max))
                        tve = p.op("v", lambda h, o=hs, w=WBC[:, sl * 512: sl * 512 + 512]:
                                   h.tensor_tensor(o, o, w, ALU.mult), inc=1)
                    bank_tok[("m", b)] = tve
                    wbc_tok[sl] = tve
                if tc == 3:
                    h_done_prev = tm
                p.wait("t", tve)
                for dc in range(NCHUNK):
                    b = 2 + dc % 2
                    pb = ctx.pb[b]
                    use_bank("t", ("m", b))
                    first_extra = (ei == 0 and cfg.b2 is not None)
                    for hc in range(16):
                        tm = p.op("t", lambda h, l=W2S[:, hc * 1024 + dc * 128: hc * 1024 + dc * 128 + 128],
                                  r=HS[:, hc * 512:(hc + 1) * 512],
                                  o=pb, s=(hc == 0), e2=(hc == 15 and not first_extra):
                                  h.matmul(o[:, :], l, r, start=s, stop=e2),
                                  inc=(1 if (hc == 15 and not first_extra) else 0))
                    if first_extra:
                        tm = p.op("t", lambda h, l=S["B2S"][0:8, dc * 128:(dc + 1) * 128],
                                  r=WTR[0:8, tc * 512: tc * 512 + 512], o=pb:
                                  h.matmul(o[:, :], l, r, start=False, stop=True), inc=1)
                    ya = YA[:, dc * 2048 + tc * 512: dc * 2048 + tc * 512 + 512]
                    p.wait("v", tm)
                    if ei == 0:
                        tve = p.op("v", lambda h, o=ya, i=pb: h.tensor_copy(o, i[:, :]), inc=1)
                    else:
                        tve = p.op("v", lambda h, o=ya, i=pb: h.tensor_tensor(o, o, i[:, :], ALU.add), inc=1)
                    bank_tok[("m", b)] = tve
                    if _os.environ.get("MOE_SERIAL"):
                        p.wait("t", tve)
        finish(mega, tm0)
        p.barrier()


MOE_SBUF = [
    ("XMB", (128, 8 * 2048), BF16), ("W1S", (128, 8 * 2048), BF16), ("W2S", (128, 16 * 1024), BF16),
    ("HS", (128, 16 * 512), BF16), ("YA", (128, 8 * 2048), F32),
    ("CMB", (128, 8 * 32), BF16), ("WTR", (8, 2048), BF16), ("WBC", (128, 1024), BF16),
    ("GSC", (128, 32), F32), ("WGS", (128, 64), BF16), ("GB", (1, 8), BF16),
    ("B1E", (128, 16), F32), ("B2S", (8, 1024), BF16), ("HTMP", (128, 512), BF16),
    ("SEL8", (8, 8 * 128), BF16), ("IDENT", (128, 128), BF16),
    ("SRCB", (128, 4096), BF16), ("FSTG", (128, 1024), BF16), ("YSTG", (128, 512), F32),
]


def build_g4(shared, percore0, dbg=False):
    import contextlib
    nc = bass.Bass("TRN2", target_bir_lowering=False)
    stack = contextlib.ExitStack()
    nc._dev_stack = stack
    aps = _declare(nc, shared, percore0, ["m0_wg", "m0_w1", "m0_w2", "sel8", "ident", "x0"],
                   extra={"xnT_in": ([D, T], BF16), "x2T_in": ([D, T], BF16)})
    outs4 = {"xnewT": ([D, T], F32)}
    if dbg:
        outs4.update({"d_cmb": ([128, 256], BF16), "d_wtr": ([8, T], BF16),
                      "d_ya": ([128, 16384], F32), "d_hs": ([128, 16 * 512], BF16),
                      "d_xmb": ([128, 16384], BF16), "d_w1s": ([128, 16384], BF16)})
    aps.update(_outs(nc, outs4))
    with nc.Block() as block:
        ctx = Ctx(nc, stack, aps, consts_vals=(0.0,), sbuf_spec=MOE_SBUF)
        ctx.setup()
        p = ctx.p
        S = ctx.sb
        p.dma("g", S["SEL8"][:, :], aps["sel8"][:, :])
        p.dma("g", S["IDENT"][:, :], aps["ident"][:, :])
        p.barrier()
        cfg = AttnCfg(src_dram=aps["xnT_in"], wg=aps["m0_wg"], gb=None, w1=aps["m0_w1"], w2=aps["m0_w2"],
                      b1=None, b2=None, act="relu2", renorm=True)

        def finish(mega, tm0):
            p.barrier()
            if dbg and mega == 0:
                p.dma("sy", aps["d_cmb"][:, :], S["CMB"][:, :])
                p.dma("sy", aps["d_wtr"][:, :], S["WTR"][:, :])
                p.dma("sy", aps["d_ya"][:, :], S["YA"][:, :])
                p.dma("sy", aps["d_hs"][:, :], S["HS"][:, :])
                p.dma("sy", aps["d_xmb"][:, :], S["XMB"][:, :])
                p.dma("sy", aps["d_w1s"][:, :], S["W1S"][:, :])
                p.barrier()
            for dc in range(NCHUNK):
                for tc in range(4):
                    ya = S["YA"][:, dc * 2048 + tc * 512: dc * 2048 + tc * 512 + 512]
                    gs = S["FSTG"][:, (tc % 2) * 512:(tc % 2) * 512 + 512]
                    t0 = tm0 + tc * 512
                    tls = p.dma("sy", S["HTMP"][:, 0:512], aps["x2T_in"][dc * 128:(dc + 1) * 128, t0:t0 + 512])
                    tlx = p.dma("sy", S["XMB"].bitcast(F32)[:, 0:512], aps["x0"][dc * 128:(dc + 1) * 128, t0:t0 + 512])
                    tsl = p.op("sc", lambda h, o=gs, i=ya: h.activation(o, i, AF.Silu), inc=1)
                    p.wait("v", tsl)
                    p.wait("v", tls)
                    p.wait("v", tlx)
                    p.op("v", lambda h, o=gs, s2=S["HTMP"][:, 0:512]: h.tensor_tensor(o, o, s2, ALU.mult))
                    tvf = p.op("v", lambda h, o=S["XMB"].bitcast(F32)[:, 512:1024], a=S["XMB"].bitcast(F32)[:, 0:512], g=gs:
                               h.tensor_tensor(o, a, g, ALU.add), inc=1)
                    p.wait("sy", tvf)
                    p.dma("sy", aps["xnewT"][dc * 128:(dc + 1) * 128, t0:t0 + 512], S["XMB"].bitcast(F32)[:, 512:1024])
                    p.barrier()

        emit_moe(ctx, cfg, finish)
        _finish(ctx)
        ctx.p.build(block)
    return nc


def build_g5(shared, percore0, key_in="xnewT_in", key_out="xn1T"):
    import contextlib
    nc = bass.Bass("TRN2", target_bir_lowering=False)
    stack = contextlib.ExitStack()
    nc._dev_stack = stack
    aps = _declare(nc, shared, percore0, [], extra={key_in: ([D, T], F32)})
    aps.update(_outs(nc, {key_out: ([D, T], BF16)}))
    sbuf_spec = [
        ("ln_xstg", (128, 1024), F32), ("ln_sqt", (128, 1024), BF16),
        ("ln_rows", (1, 4096), F32), ("ln_abc", (128, 512), BF16), ("ln_bbc", (128, 512), BF16),
        ("XN", (128, NCHUNK * T), BF16),
    ]
    with nc.Block() as block:
        ctx = Ctx(nc, stack, aps, consts_vals=(), sbuf_spec=sbuf_spec)
        ctx.setup()
        emit_ln(ctx, aps[key_in], ctx.sb["XN"], out_dram=aps[key_out])
        _finish(ctx)
        ctx.p.build(block)
    return nc


def build_g6(shared, percore0):
    import contextlib
    nc = bass.Bass("TRN2", target_bir_lowering=False)
    stack = contextlib.ExitStack()
    nc._dev_stack = stack
    aps = _declare(nc, shared, percore0, ["a1_wq", "a1_wk", "a1_wv", "a1_wo", "ident", "selp8", "maskh"],
                   extra={"xn1T_in": ([D, T], BF16)})
    aps.update(_outs(nc, {"s1T": ([D, T], BF16)}))
    spec = _attn_sbuf_spec(8, 4, 8, 2, 8, 512, 512, rope=False, nmask=0, latent=False, scaled=False, bcb=False)
    with nc.Block() as block:
        ctx = Ctx(nc, stack, aps, consts_vals=(-16.0,), sbuf_spec=spec)
        ctx.setup()
        p = ctx.p
        S = ctx.sb
        p.dma("g", S["SELP"][0:8, :], aps["selp8"][:, :])
        _load_attn_consts(ctx, aps)
        p.barrier()
        load_rows(ctx, aps["a1_wk"], S["WK"], 8, 512)
        load_rows(ctx, aps["a1_wv"], S["WV"], 8, 512)
        load_rows(ctx, aps["a1_wo"], S["WOS"], 4, 1024)
        p.barrier()
        src = SrcStream(ctx, aps["xn1T_in"], S["SRCB"])
        cfg = AttnCfg(nheads=8, kv_dup=False, nkv_chunks=4, nkv_heads=8, rope=None, qk="l2",
                      wq=aps["a1_wq"], wq_bf=False, wk_cols=512, wv_cols=512,
                      exp_premul=16.0, exp_bias=-16.0, sums_clip=1e-6,
                      mask_sched=_sched_full, src=src, res=None)
        emit_attn(ctx, cfg, aps["s1T"])
        _finish(ctx)
        ctx.p.build(block)
    return nc


def build_g7(shared, percore0, dbg=False):
    import contextlib
    nc = bass.Bass("TRN2", target_bir_lowering=False)
    stack = contextlib.ExitStack()
    nc._dev_stack = stack
    aps = _declare(nc, shared, percore0,
                   ["f1_wq", "f1_wk", "f1_wv", "f1_wo", "f1_bq", "f1_bk", "f1_bv", "f1_bo",
                    "f1_g1", "f1_b1n", "ident", "selp16", "maskh"],
                   extra={"xn1T_in": ([D, T], BF16)})
    outs7 = {"xffnT": ([D, T], BF16)}
    if dbg:
        outs7["zT_d"] = ([D, T], BF16)
        outs7["d_scr"] = ([128, 14336], BF16)
        outs7["d_sums"] = ([1, 1024], F32)
        outs7["d_q0"] = ([128, 4096], BF16)
        outs7["d_wq"] = ([128, 2048], BF16)
    aps.update(_outs(nc, outs7))
    zT = aps["zT_d"] if dbg else nc.dram_tensor("zT", [D, T], BF16, kind="Internal").ap()
    spec = _attn_sbuf_spec(16, 8, 16, 2, 8, 1024, 1024, rope=False, nmask=0, latent=False,
                           wo_stream=True, qk=False, scaled=False, wv_shared=True, bcb=False)
    spec += [("BQ", (128, 8), F32), ("BK", (128, 8), F32), ("BV", (128, 8), F32), ("BO", (128, 8), F32),
             ("G1", (128, 8), F32), ("B1N", (128, 8), F32)]
    with nc.Block() as block:
        ctx = Ctx(nc, stack, aps, consts_vals=(-8.0,), sbuf_spec=spec)
        ctx.setup()
        p = ctx.p
        S = ctx.sb
        scr = S["SCR"]
        S["ln_xstg"] = scr.bitcast(F32)[:, 0:1024]
        S["ln_sqt"] = scr[:, 2048:3072]
        S["ln_abc"] = scr[:, 3072:3584]
        S["ln_bbc"] = scr[:, 3584:4096]
        S["ln_rows"] = scr.bitcast(F32)[0:1, 3072:7168]
        for nm, key in [("BQ", "f1_bq"), ("BK", "f1_bk"), ("BV", "f1_bv"), ("BO", "f1_bo"),
                        ("G1", "f1_g1"), ("B1N", "f1_b1n")]:
            p.dma("g", S[nm][:, :], aps[key][:, :])
        _load_attn_consts(ctx, aps)
        p.barrier()
        load_rows(ctx, aps["f1_wk"], S["WK"], 8, 1024)
        p.barrier()
        src = SrcStream(ctx, aps["xn1T_in"], S["SRCB"])
        cfg = AttnCfg(nheads=16, kv_dup=False, nkv_chunks=8, nkv_heads=16, rope=None, qk=None,
                      wq=aps["f1_wq"], wq_bf=False, wk_cols=1024, wv_cols=1024,
                      bq=S["BQ"], bk=S["BK"], bv=S["BV"], bo=S["BO"],
                      exp_scale_const=0.125, exp_bias=-8.0, wo_stream=aps["f1_wo"],
                      load_wv=lambda: load_rows(ctx, aps["f1_wv"], S["WK"], 8, 1024),
                      mask_sched=_sched_full, src=src, res=lambda fc, q0: src(fc, q0))
        if dbg:
            cfg.dbg = {"qc": 0, "aps": {"scr": (S["SCR"][:, :], aps["d_scr"][:, :]),
                                        "sums": (S["sumsrow"][:, :], aps["d_sums"][:, :])},
                       "q_ap": aps["d_q0"][:, :], "wq_ap": aps["d_wq"][:, :]}
        emit_attn(ctx, cfg, zT)
        p.barrier()
        emit_ln(ctx, zT, S["KTS"], src_bf16=True, gamma=S["G1"], beta=S["B1N"], out_dram=aps["xffnT"])
        _finish(ctx)
        ctx.p.build(block)
    return nc


def build_g8(shared, percore0):
    import contextlib
    nc = bass.Bass("TRN2", target_bir_lowering=False)
    stack = contextlib.ExitStack()
    nc._dev_stack = stack
    aps = _declare(nc, shared, percore0,
                   ["f1_gw", "f1_gb", "f1_w1", "f1_b1c", "f1_w2", "f1_b2", "f1_g2", "f1_b2n",
                    "sel8", "ident"],
                   extra={"xffnT_in": ([D, T], BF16), "s1T_in": ([D, T], BF16),
                          "xnewT_in": ([D, T], F32)})
    outs_f = {"outT": ([D, T], F32)}
    if dbg:
        outs_f.update({"d_cmb4": ([128, 256], BF16), "d_wtr4": ([8, 2048], BF16),
                       "d_ya4": ([128, 16384], F32)})
    aps.update(_outs(nc, outs_f))
    spec = MOE_SBUF + [("G2N", (128, 8), F32), ("B2N", (128, 8), F32)]
    with nc.Block() as block:
        ctx = Ctx(nc, stack, aps, consts_vals=(0.0,), sbuf_spec=spec)
        ctx.setup()
        p = ctx.p
        S = ctx.sb
        p.dma("g", S["SEL8"][:, :], aps["sel8"][:, :])
        p.dma("g", S["IDENT"][:, :], aps["ident"][:, :])
        p.dma("g", S["B2S"][:, :], aps["f1_b2"][:, :])
        p.dma("g", S["G2N"][:, :], aps["f1_g2"][:, :])
        p.dma("g", S["B2N"][:, :], aps["f1_b2n"][:, :])
        p.dma("g", S["GB"][:, :], aps["f1_gb"][:, :])
        p.barrier()
        cfg = AttnCfg(src_dram=aps["xffnT_in"], wg=aps["f1_gw"], gb=aps["f1_gb"], w1=aps["f1_w1"], w2=aps["f1_w2"],
                      b1=aps["f1_b1c"], b2=aps["f1_b2"], act="relu", renorm=False)
        W2F = S["W2S"].bitcast(F32)
        rows = lambda i: W2F[0:1, 1024 + i * 512: 1024 + (i + 1) * 512]
        abc = S["W2S"][:, 0:512]
        bbc = S["W2S"][:, 512:1024]
        sqt = S["W2S"][:, 1024 + 8192:1024 + 8192 + 512]
        W1F = S["W1S"].bitcast(F32)

        def finish(mega, tm0):
            p.barrier()
            for tc in range(4):
                t0 = tm0 + tc * 512
                tz = None
                for dc in range(NCHUNK):
                    xsl = S["XMB"][:, dc * 2048 + tc * 512: dc * 2048 + tc * 512 + 512]
                    ya = S["YA"][:, dc * 2048 + tc * 512: dc * 2048 + tc * 512 + 512]
                    tz = p.op("v", lambda h, o=xsl, a=ya, b=xsl: h.tensor_tensor(o, a, b, ALU.add), inc=1)
                t_done = []
                for dc in range(NCHUNK):
                    zsl = S["XMB"][:, dc * 2048 + tc * 512: dc * 2048 + tc * 512 + 512]
                    if dc >= 1:
                        p.wait("v", t_done[dc - 1])
                    tv = p.op("v", lambda h, o=sqt, i=zsl: h.tensor_tensor(o, i, i, ALU.mult), inc=1)
                    p.wait("t", tv)
                    p.op("t", lambda h, b=zsl, s=(dc == 0): h.matmul(ctx.pb[0][0:1, :], ctx.ones_bf[:, :], b, start=s, stop=(dc == 7)))
                    tm = p.op("t", lambda h, b=sqt, s=(dc == 0): h.matmul(ctx.pb[1][0:1, :], ctx.ones_bf[:, :], b, start=s, stop=(dc == 7)), inc=1)
                    t_done.append(tm)
                mu, m2, var, sd, rstd, brow = (rows(i) for i in range(6))
                p.wait("v", t_done[-1])
                p.op("v", lambda h: h.tensor_scalar(mu, ctx.pb[0][0:1, :], 1.0 / D, None, ALU.mult))
                p.op("v", lambda h: h.tensor_scalar(m2, ctx.pb[1][0:1, :], 1.0 / D, None, ALU.mult))
                p.op("v", lambda h: h.tensor_tensor(sd, mu, mu, ALU.mult))
                p.op("v", lambda h: h.tensor_tensor(var, m2, sd, ALU.subtract))
                tvv = p.op("v", lambda h: h.tensor_scalar(var, var, 1e-5, None, ALU.add), inc=1)
                p.wait("sc", tvv)
                tss = p.op("sc", lambda h: h.activation(sd, var, AF.Sqrt), inc=1)
                p.wait("v", tss)
                trc = p.op("v", lambda h: h.reciprocal(rstd, sd), inc=1)
                p.op("v", lambda h: h.tensor_scalar(sd, mu, -1.0, None, ALU.mult))
                p.wait("v", trc, force=True)
                tv2 = p.op("v", lambda h: h.tensor_tensor(brow, sd, rstd, ALU.mult), inc=1)
                p.wait("t", tv2)
                tb1 = p.op("t", lambda h: h.matmul(ctx.pb[2][:, :], ctx.ones_row[:, :], rstd, start=True, stop=True), inc=1)
                tb2 = p.op("t", lambda h: h.matmul(ctx.pb[3][:, :], ctx.ones_row[:, :], brow, start=True, stop=True), inc=1)
                p.wait("sc", tb2)
                p.op("sc", lambda h: h.activation(abc, ctx.pb[2][:, :], AF.Copy))
                e2 = p.op("sc", lambda h: h.activation(bbc, ctx.pb[3][:, :], AF.Copy), inc=1)
                for dc in range(NCHUNK):
                    zsl = S["XMB"][:, dc * 2048 + tc * 512: dc * 2048 + tc * 512 + 512]
                    s1s = S["W1S"][:, 0:512]
                    xns = W1F[:, 512:1024]
                    ots = W1F[:, 1024:1536]
                    gsl = S["FSTG"][:, 0:512]
                    tl1 = p.dma("sy", s1s, aps["s1T_in"][dc * 128:(dc + 1) * 128, t0:t0 + 512])
                    tl2 = p.dma("sy", xns, aps["xnewT_in"][dc * 128:(dc + 1) * 128, t0:t0 + 512])
                    p.wait("v", e2)
                    p.op("v", lambda h, a=zsl: h.tensor_tensor(a, a, abc, ALU.mult))
                    p.op("v", lambda h, a=zsl: h.tensor_tensor(a, a, bbc, ALU.add))
                    tvz = p.op("v", lambda h, a=zsl, g=S["G2N"][:, dc:dc + 1], b=S["B2N"][:, dc:dc + 1]:
                               h.tensor_scalar(a, a, g, b, ALU.mult, ALU.add), inc=1)
                    p.wait("sc", tvz)
                    tsl = p.op("sc", lambda h, o=gsl, i=zsl: h.activation(o, i, AF.Silu), inc=1)
                    p.wait("v", tsl)
                    p.wait("v", tl1)
                    p.wait("v", tl2)
                    p.op("v", lambda h, o=gsl, s2=s1s: h.tensor_tensor(o, o, s2, ALU.mult))
                    tvf = p.op("v", lambda h, o=ots, a=xns, g=gsl: h.tensor_tensor(o, a, g, ALU.add), inc=1)
                    p.wait("sy", tvf)
                    td = p.dma("sy", aps["outT"][dc * 128:(dc + 1) * 128, t0:t0 + 512], ots)
                    p.wait("v", td)
                p.barrier()

        emit_moe(ctx, cfg, finish)
        _finish(ctx)
        ctx.p.build(block)
    return nc



_G_INPUTS = {
    "g1": [],
    "g2": ["g_wq", "g_kvd", "g_kup", "g_vup", "g_wo", "w1n", "ropeg", "cmask", "ident", "selp16", "maskh"],
    "g3": ["l_wq", "l_wk", "l_wv", "l_wo", "w2n", "ropel", "lmask", "ident", "selp16", "maskh"],
    "g4": ["m0_wg", "m0_w1", "m0_w2", "sel8", "ident"],
    "g5": [],
    "g6": ["a1_wq", "a1_wk", "a1_wv", "a1_wo", "ident", "selp8", "maskh"],
    "g7": ["f1_wq", "f1_wk", "f1_wv", "f1_wo", "f1_bq", "f1_bk", "f1_bv", "f1_bo",
           "f1_g1", "f1_b1n", "ident", "selp16", "maskh"],
    "g8": ["f1_gw", "f1_gb", "f1_w1", "f1_b1c", "f1_w2", "f1_b2", "f1_g2", "f1_b2n", "sel8", "ident"],
}


def _run(nc, shared, names, extra_percore, n=8):
    maps = []
    for b in range(n):
        m = {k: shared[k] for k in names}
        m.update(extra_percore[b])
        maps.append(m)
    res = run_bass_kernel_spmd(nc, maps, core_ids=list(range(n)))
    return res.results


def kernel(**inputs):
    return kernel_fused(**inputs)


def kernel_multilaunch(**inputs):
    shared, percore = prep_inputs(inputs)
    B = 8
    r1 = _run(build_g1(shared, percore[0]), shared, _G_INPUTS["g1"],
              [{"x0": percore[b]["x0"]} for b in range(B)])
    r2 = _run(build_g2(shared, percore[0]), shared, _G_INPUTS["g2"],
              [{"xnT_in": r1[b]["xnT"], "rs1_in": r1[b]["rs1"]} for b in range(B)])
    r3 = _run(build_g3(shared, percore[0]), shared, _G_INPUTS["g3"],
              [{"x1T_in": r2[b]["x1T"]} for b in range(B)])
    r4 = _run(build_g4(shared, percore[0]), shared, _G_INPUTS["g4"],
              [{"x0": percore[b]["x0"], "xnT_in": r1[b]["xnT"], "x2T_in": r3[b]["x2T"]} for b in range(B)])
    r5 = _run(build_g5(shared, percore[0]), shared, _G_INPUTS["g5"],
              [{"xnewT_in": r4[b]["xnewT"]} for b in range(B)])
    r6 = _run(build_g6(shared, percore[0]), shared, _G_INPUTS["g6"],
              [{"xn1T_in": r5[b]["xn1T"]} for b in range(B)])
    r7 = _run(build_g7(shared, percore[0]), shared, _G_INPUTS["g7"],
              [{"xn1T_in": r5[b]["xn1T"]} for b in range(B)])
    r8 = _run(build_g8(shared, percore[0]), shared, _G_INPUTS["g8"],
              [{"xffnT_in": r7[b]["xffnT"], "s1T_in": r6[b]["s1T"], "xnewT_in": r4[b]["xnewT"]}
               for b in range(B)])
    out = np.stack([np.ascontiguousarray(r8[b]["outT"].T) for b in range(B)])
    return out.astype(np.float32)



def build_fused(shared, percore0, dbg=False):
    import contextlib
    nc = bass.Bass("TRN2", target_bir_lowering=False)
    stack = contextlib.ExitStack()
    nc._dev_stack = stack
    names = (_G_INPUTS["g2"] + _G_INPUTS["g3"] + _G_INPUTS["g4"] + _G_INPUTS["g6"]
             + _G_INPUTS["g7"] + _G_INPUTS["g8"])
    names = sorted(set(names))
    aps = _declare(nc, shared, percore0, names, extra={"x0": ([D, T], F32)})
    outs_f = {"outT": ([D, T], F32)}
    if dbg:
        outs_f.update({"d_cmb4": ([128, 256], BF16), "d_wtr4": ([8, 2048], BF16),
                       "d_ya4": ([128, 16384], F32)})
    aps.update(_outs(nc, outs_f))
    dram = {}
    for nm, dt in [("xnT", BF16), ("x1T", BF16), ("x2T", BF16), ("xnewT", F32),
                   ("xn1T", BF16), ("s1T", BF16), ("xffnT", BF16), ("zT", BF16),
                   ("wqbf_g", BF16), ("wqbf_l", BF16)]:
        shape = [D, 2048] if nm.startswith("wqbf") else [D, T]
        if dbg and not nm.startswith("wqbf"):
            dram[nm] = nc.dram_tensor(nm, shape, dt, kind="ExternalOutput").ap()
        else:
            dram[nm] = nc.dram_tensor(nm, shape, dt, kind="Internal").ap()

    spec = [
        ("AR1", (128, 32768), BF16), ("AR2", (128, 33280), BF16),
        ("AR3", (128, 18432), BF16), ("AR4", (128, 8192), BF16),
        ("ROWS", (1, 2048), F32), ("RS", (1, T), BF16),
        ("MASKS", (128, 3072), BF16), ("MASKH", (128, 3), BF16), ("IDENT", (128, 128), BF16),
        ("SELPA", (128, 1024), BF16), ("nskt", (128, 256), F32),
        ("fscr", (128, 192), F32),
    ]
    with nc.Block() as block:
        ctx = Ctx(nc, stack, aps, consts_vals=(-8.0, -16.0, 0.0), sbuf_spec=spec)
        ctx.setup()
        p = ctx.p
        S = ctx.sb
        AR1, AR2, AR3, AR4 = S["AR1"], S["AR2"], S["AR3"], S["AR4"]
        ROWS = S["ROWS"]

        def bind_common():
            S["SCR"] = AR3[:, 0:14336]
            S["SRCB"] = AR3[:, 14336:18432]
            S["WK"] = AR3[:, 0:8192]
            S["WV"] = AR3[:, 8192:12288]
            S["VSTG"] = AR3[:, 12288:13312]
            S["WQB"] = AR3[:, 12288:14336]
            S["OSTG"] = AR3[:, 4096:5120]
            S["WOB"] = AR3[:, 12288:14336]
            S["bcbuf"] = AR3[:, 5120:6144]
            S["sumsrow"] = ROWS[0:1, 0:1024]
            S["recips"] = ROWS[0:1, 1024:2048]
            S["wstage"] = AR3.bitcast(F32)[:, 5120:7168]
            S["ln_xstg"] = AR3.bitcast(F32)[:, 5120:6144]
            S["ln_rows"] = ROWS[0:1, 0:2048]
            S["WTR"] = AR3[0:8, 16384:18432]
            S["rqrows"] = AR3[0:16, 12288:12800]
            S["GSC"] = S["fscr"][:, 0:32]
            S["W1N"] = S["fscr"][:, 160:168]
            S["W2N"] = S["fscr"][:, 168:176]
            for i, nm in enumerate(["BQ", "BK", "BV", "BO", "G1", "B1N", "G2N", "B2N"]):
                S[nm] = S["nskt"][:, i * 8:(i + 1) * 8]
            S["SELP"] = S["SELPA"][0:16, :]
            S["FSTG"] = S["SELPA"][:, :]
            S["WBC"] = S["FSTG"]
            S["SEL8"] = S["MASKS"][0:8, 0:1024]
            S["B2S"] = S["MASKS"][0:8, 1024:2048]
            S["HTMP"] = S["MASKS"][:, 2048:2560]
            S["CMB"] = AR2[:, 32768:33024]
            S["WGS"] = AR2[:, 33024:33088]
            S["GB"] = AR2[0:1, 33088:33096]
            S["B1E"] = AR2.bitcast(F32)[:, 16548:16564]
            S["wstageb"] = AR3[:, 8192:10240]
            S["ln_sqt"] = AR3[:, 0:1024]
            S["ln_abc"] = AR3[:, 1024:1536]
            S["ln_bbc"] = AR3[:, 1536:2048]
            S["ROPE"] = AR4[:, :]
            S["HS"] = AR4[:, :]

        bind_common()
        p.dma("g", S["IDENT"][:, :], aps["ident"][:, :])
        p.dma("g", S["MASKH"][:, :], aps["maskh"][:, :])
        p.dma("g", S["SELP"][:, :], aps["selp16"][:, :])
        p.dma("g", S["W1N"][:, :], aps["w1n"][:, :])
        p.dma("g", S["W2N"][:, :], aps["w2n"][:, :])
        p.barrier()

        S["XN"] = AR1[:, :]
        emit_ln(ctx, aps["x0"], S["XN"], rs1_row=S["RS"], out_dram=dram["xnT"])
        p.barrier()

        S["KTS"] = AR1[:, 0:16384]
        S["LATN"] = AR1[:, 16384:24576]
        S["WD"] = AR1[:, 24576:26624]
        S["VTOK"] = AR2[:, 0:8320]
        emit_w_prep(ctx, aps["g_wq"], dram["wqbf_g"], 8, 2048, S["W1N"])
        load_rows(ctx, aps["g_kvd"], S["WD"], 8, 256, scale=S["W1N"])
        load_rows(ctx, aps["g_kup"], S["WK"], 2, 1024)
        for mi in range(4):
            p.dma("sy", S["MASKS"][:, mi * 512:(mi + 1) * 512], aps["cmask"][mi * 128:(mi + 1) * 128, :])
        p.dma("g", S["ROPE"][:, :], aps["ropeg"][:, :])
        p.barrier()
        src2 = SrcStream(ctx, dram["xnT"], S["SRCB"])
        cfg2 = AttnCfg(nheads=16, kv_dup=True, nkv_chunks=4, nkv_heads=4, rope="ROPE", qk="rms",
                       wq=dram["wqbf_g"], wq_bf=True, wk_cols=1024, wv_cols=256,
                       latent=dict(scale_row=S["RS"]), exp_premul=0.125, exp_bias=-8.0,
                       mask_sched=_sched_causal, src=src2, res=lambda fc, q0: src2(fc, q0),
                       load_wv=lambda: load_rows(ctx, aps["g_vup"], S["WV"], 2, 256),
                       wo_stream=aps["g_wo"])
        emit_attn(ctx, cfg2, dram["x1T"])
        p.barrier()

        S["KTS"] = AR1[:, 0:16384]
        S["VTOK"] = AR2[:, 0:8320]
        emit_w_prep(ctx, aps["l_wq"], dram["wqbf_l"], 8, 2048, S["W2N"])
        load_rows(ctx, aps["l_wk"], S["WK"], 8, 1024, scale=S["W2N"])
        for mi in range(6):
            p.dma("sy", S["MASKS"][:, mi * 512:(mi + 1) * 512], aps["lmask"][mi * 128:(mi + 1) * 128, :])
        p.dma("g", S["ROPE"][:, :], aps["ropel"][:, :])
        p.barrier()
        src3 = SrcStream(ctx, dram["x1T"], S["SRCB"])
        emit_rms_row_stream(ctx, src3, S["RS"], eps=1e-6)
        cfg3 = AttnCfg(nheads=16, kv_dup=True, nkv_chunks=4, nkv_heads=4, rope="ROPE", qk="rms",
                       wq=dram["wqbf_l"], wq_bf=True, wk_cols=1024, wv_cols=256,
                       v_scale_row=S["RS"], exp_premul=0.125, exp_bias=-8.0,
                       mask_sched=_sched_local, src=src3, res=lambda fc, q0: src3(fc, q0),
                       load_wv=lambda: load_rows(ctx, aps["l_wv"], S["WV"], 8, 256, scale=S["W2N"]),
                       wo_stream=aps["l_wo"])
        emit_attn(ctx, cfg3, dram["x2T"])
        p.barrier()

        S["YA"] = AR1.bitcast(F32)[:, 0:16384]
        S["XMB"] = AR2[:, 0:16384]
        S["W1S"] = AR2[:, 16384:32768]
        S["W2S"] = AR3[:, 0:16384]
        p.dma("g", S["SEL8"][:, :], aps["sel8"][:, :])
        p.barrier()
        cfg4 = AttnCfg(src_dram=dram["xnT"], wg=aps["m0_wg"], gb=None, w1=aps["m0_w1"], w2=aps["m0_w2"],
                       b1=None, b2=None, act="relu2", renorm=True)

        def finish4(mega, tm0):
            p.barrier()
            XF = S["XMB"].bitcast(F32)
            slot_mul, slot_add, slot_out = {}, {}, {}
            for dc in range(NCHUNK):
                for tc in range(4):
                    s = (dc * 4 + tc) % 2
                    ya = S["YA"][:, dc * 2048 + tc * 512: dc * 2048 + tc * 512 + 512]
                    gs = S["FSTG"][:, s * 512:s * 512 + 512]
                    hsl = S["HS"][:, s * 512:(s + 1) * 512]
                    xld = XF[:, s * 512:s * 512 + 512]
                    ost = XF[:, 1024 + s * 512:1024 + s * 512 + 512]
                    t0 = tm0 + tc * 512
                    p.wait("sy", slot_mul.get(s))
                    tls = p.dma("sy", hsl, dram["x2T"][dc * 128:(dc + 1) * 128, t0:t0 + 512])
                    p.wait("sy", slot_add.get(s))
                    tlx = p.dma("sy", xld, aps["x0"][dc * 128:(dc + 1) * 128, t0:t0 + 512])
                    p.wait("sc", slot_add.get(s))
                    tsl = p.op("sc", lambda h, o=gs, i=ya: h.activation(o, i, AF.Silu), inc=1)
                    p.wait("v", tsl)
                    p.wait("v", tls)
                    p.wait("v", tlx)
                    p.wait("v", slot_out.get(s))
                    tmul = p.op("v", lambda h, o=gs, s2=hsl: h.tensor_tensor(o, o, s2, ALU.mult), inc=1)
                    tvf = p.op("v", lambda h, o=ost, a=xld, g=gs:
                               h.tensor_tensor(o, a, g, ALU.add), inc=1)
                    slot_mul[s], slot_add[s] = tmul, tvf
                    p.wait("sy", tvf)
                    slot_out[s] = p.dma("sy", dram["xnewT"][dc * 128:(dc + 1) * 128, t0:t0 + 512], ost)

        emit_moe(ctx, cfg4, finish4)
        p.barrier()
        if dbg:
            p.dma("sy", aps["d_cmb4"][:, :], S["CMB"][:, :])
            p.dma("sy", aps["d_wtr4"][:, :], S["WTR"][:, :])
            p.dma("sy", aps["d_ya4"][:, :], S["YA"][:, :])
            p.barrier()

        S["XN"] = AR1[:, :]
        emit_ln(ctx, dram["xnewT"], S["XN"], out_dram=dram["xn1T"])
        p.barrier()

        S["KTS"] = AR1[:, 0:16384]
        S["VTOK"] = AR2[:, 0:16640]
        p.dma("g", S["SELP"][0:8, 0:512], aps["selp8"][:, :])
        load_rows(ctx, aps["a1_wk"], S["WK"], 8, 512)
        p.barrier()
        src6 = SrcStream(ctx, dram["xn1T"], S["SRCB"], barrier=True)
        cfg6 = AttnCfg(nheads=8, kv_dup=False, nkv_chunks=4, nkv_heads=8, rope=None, qk="l2",
                       wq=aps["a1_wq"], wq_bf=False, wk_cols=512, wv_cols=512,
                       exp_premul=16.0, exp_bias=-16.0, sums_clip=1e-6,
                       load_wv=lambda: load_rows(ctx, aps["a1_wv"], S["WV"], 8, 512),
                       mask_sched=_sched_full, src=src6, res=None, wo_stream=aps["a1_wo"])
        emit_attn(ctx, cfg6, dram["s1T"])
        p.barrier()

        S["KTS"] = AR1[:, 0:32768]
        S["VTOK"] = AR2[:, 0:33280]
        for nm, key in [("BQ", "f1_bq"), ("BK", "f1_bk"), ("BV", "f1_bv"), ("BO", "f1_bo"),
                        ("G1", "f1_g1"), ("B1N", "f1_b1n")]:
            p.dma("g", S[nm][:, :], aps[key][:, :])
        S["WV"] = S["WK"]
        load_rows(ctx, aps["f1_wk"], S["WK"], 8, 1024)
        p.barrier()
        src7 = SrcStream(ctx, dram["xn1T"], S["SRCB"])
        cfg7 = AttnCfg(nheads=16, kv_dup=False, nkv_chunks=8, nkv_heads=16, rope=None, qk=None,
                       wq=aps["f1_wq"], wq_bf=False, wk_cols=1024, wv_cols=1024,
                       bq=S["BQ"], bk=S["BK"], bv=S["BV"], bo=S["BO"],
                       exp_scale_const=0.125, exp_bias=-8.0, wo_stream=aps["f1_wo"],
                       load_wv=lambda: load_rows(ctx, aps["f1_wv"], S["WK"], 8, 1024),
                       mask_sched=_sched_full, src=src7, res=lambda fc, q0: src7(fc, q0))
        emit_attn(ctx, cfg7, dram["zT"])
        p.barrier()
        emit_ln(ctx, dram["zT"], AR1[:, :], src_bf16=True, gamma=S["G1"], beta=S["B1N"], out_dram=dram["xffnT"])
        p.barrier()

        S["YA"] = AR1.bitcast(F32)[:, 0:16384]
        S["XMB"] = AR2[:, 0:16384]
        S["W1S"] = AR2[:, 16384:32768]
        S["W2S"] = AR3[:, 0:16384]
        p.dma("g", S["SEL8"][:, :], aps["sel8"][:, :])
        p.dma("g", S["B2S"][:, :], aps["f1_b2"][:, :])
        p.dma("g", S["GB"][:, :], aps["f1_gb"][:, :])
        p.dma("g", S["G2N"][:, :], aps["f1_g2"][:, :])
        p.dma("g", S["B2N"][:, :], aps["f1_b2n"][:, :])
        p.barrier()
        cfg8 = AttnCfg(src_dram=dram["xffnT"], wg=aps["f1_gw"], gb=aps["f1_gb"], w1=aps["f1_w1"], w2=aps["f1_w2"],
                       b1=aps["f1_b1c"], b2=aps["f1_b2"], act="relu", renorm=False)
        W2F = S["W2S"].bitcast(F32)
        rows8 = lambda i: W2F[0:1, 1024 + i * 512: 1024 + (i + 1) * 512]
        abc8 = S["W2S"][:, 0:512]
        bbc8 = S["W2S"][:, 512:1024]
        sqt8 = S["W2S"][:, 1024 + 8192:1024 + 8192 + 512]
        W1F = S["W1S"].bitcast(F32)

        def finish8(mega, tm0):
            p.barrier()
            for tc in range(4):
                t0 = tm0 + tc * 512
                for dc in range(NCHUNK):
                    xsl = S["XMB"][:, dc * 2048 + tc * 512: dc * 2048 + tc * 512 + 512]
                    ya = S["YA"][:, dc * 2048 + tc * 512: dc * 2048 + tc * 512 + 512]
                    p.op("v", lambda h, o=xsl, a=ya, b=xsl: h.tensor_tensor(o, a, b, ALU.add), inc=1)
                t_done = []
                for dc in range(NCHUNK):
                    zsl = S["XMB"][:, dc * 2048 + tc * 512: dc * 2048 + tc * 512 + 512]
                    if dc >= 1:
                        p.wait("v", t_done[dc - 1])
                    tv = p.op("v", lambda h, o=sqt8, i=zsl: h.tensor_tensor(o, i, i, ALU.mult), inc=1)
                    p.wait("t", tv)
                    p.op("t", lambda h, b=zsl, s=(dc == 0): h.matmul(ctx.pb[0][0:1, :], ctx.ones_bf[:, :], b, start=s, stop=(dc == 7)))
                    tm = p.op("t", lambda h, b=sqt8, s=(dc == 0): h.matmul(ctx.pb[1][0:1, :], ctx.ones_bf[:, :], b, start=s, stop=(dc == 7)), inc=1)
                    t_done.append(tm)
                mu, m2, var, sd, rstd, brow = (rows8(i) for i in range(6))
                p.wait("v", t_done[-1])
                p.op("v", lambda h: h.tensor_scalar(mu, ctx.pb[0][0:1, :], 1.0 / D, None, ALU.mult))
                p.op("v", lambda h: h.tensor_scalar(m2, ctx.pb[1][0:1, :], 1.0 / D, None, ALU.mult))
                p.op("v", lambda h: h.tensor_tensor(sd, mu, mu, ALU.mult))
                p.op("v", lambda h: h.tensor_tensor(var, m2, sd, ALU.subtract))
                tvv = p.op("v", lambda h: h.tensor_scalar(var, var, 1e-5, None, ALU.add), inc=1)
                p.wait("sc", tvv)
                tss = p.op("sc", lambda h: h.activation(sd, var, AF.Sqrt), inc=1)
                p.wait("v", tss)
                trc = p.op("v", lambda h: h.reciprocal(rstd, sd), inc=1)
                p.op("v", lambda h: h.tensor_scalar(sd, mu, -1.0, None, ALU.mult))
                p.wait("v", trc, force=True)
                tv2 = p.op("v", lambda h: h.tensor_tensor(brow, sd, rstd, ALU.mult), inc=1)
                p.wait("t", tv2)
                tb1 = p.op("t", lambda h: h.matmul(ctx.pb[2][:, :], ctx.ones_row[:, :], rstd, start=True, stop=True), inc=1)
                tb2 = p.op("t", lambda h: h.matmul(ctx.pb[3][:, :], ctx.ones_row[:, :], brow, start=True, stop=True), inc=1)
                p.wait("sc", tb2)
                p.op("sc", lambda h: h.activation(abc8, ctx.pb[2][:, :], AF.Copy))
                e2 = p.op("sc", lambda h: h.activation(bbc8, ctx.pb[3][:, :], AF.Copy), inc=1)
                slot_mul, slot_add, slot_out = {}, {}, {}
                for dc in range(NCHUNK):
                    s = dc % 2
                    zsl = S["XMB"][:, dc * 2048 + tc * 512: dc * 2048 + tc * 512 + 512]
                    s1s = S["W1S"][:, s * 512:(s + 1) * 512]
                    xns = W1F[:, 1024 + s * 512:1024 + s * 512 + 512]
                    ots = W1F[:, 2048 + s * 512:2048 + s * 512 + 512]
                    gsl = S["FSTG"][:, s * 512:s * 512 + 512]
                    p.wait("sy", slot_mul.get(s))
                    tl1 = p.dma("sy", s1s, dram["s1T"][dc * 128:(dc + 1) * 128, t0:t0 + 512])
                    p.wait("sy", slot_add.get(s))
                    tl2 = p.dma("sy", xns, dram["xnewT"][dc * 128:(dc + 1) * 128, t0:t0 + 512])
                    p.wait("v", e2)
                    p.op("v", lambda h, a=zsl: h.tensor_tensor(a, a, abc8, ALU.mult))
                    p.op("v", lambda h, a=zsl: h.tensor_tensor(a, a, bbc8, ALU.add))
                    tvz = p.op("v", lambda h, a=zsl, g=S["G2N"][:, dc:dc + 1], b=S["B2N"][:, dc:dc + 1]:
                               h.tensor_scalar(a, a, g, b, ALU.mult, ALU.add), inc=1)
                    p.wait("sc", tvz)
                    p.wait("sc", slot_add.get(s))
                    tsl = p.op("sc", lambda h, o=gsl, i=zsl: h.activation(o, i, AF.Silu), inc=1)
                    p.wait("v", tsl)
                    p.wait("v", tl1)
                    p.wait("v", tl2)
                    p.wait("v", slot_out.get(s))
                    tmul = p.op("v", lambda h, o=gsl, s2=s1s: h.tensor_tensor(o, o, s2, ALU.mult), inc=1)
                    tvf = p.op("v", lambda h, o=ots, a=xns, g=gsl: h.tensor_tensor(o, a, g, ALU.add), inc=1)
                    slot_mul[s], slot_add[s] = tmul, tvf
                    p.wait("sy", tvf)
                    slot_out[s] = p.dma("sy", aps["outT"][dc * 128:(dc + 1) * 128, t0:t0 + 512], ots)
                p.barrier()

        emit_moe(ctx, cfg8, finish8)
        _finish(ctx)
        ctx.p.build(block)
    return nc


def kernel_fused(**inputs):
    shared, percore = prep_inputs(inputs)
    B = 8
    names = sorted(set(_G_INPUTS["g2"] + _G_INPUTS["g3"] + _G_INPUTS["g4"] + _G_INPUTS["g6"]
                       + _G_INPUTS["g7"] + _G_INPUTS["g8"]))
    nc = build_fused(shared, percore[0])
    res = _run(nc, shared, names, [{"x0": percore[b]["x0"]} for b in range(B)])
    out = np.stack([np.ascontiguousarray(res[b]["outT"].T) for b in range(B)])
    return out.astype(np.float32)
